# version 02
# baseline (speedup 1.0000x reference)
import contextlib
import numpy as np
import ml_dtypes
import concourse.bass as bass
import concourse.mybir as mybir
from concourse.bass_utils import run_bass_kernel_spmd

F32 = mybir.dt.float32
BF16 = mybir.dt.bfloat16
AF = mybir.ActivationFunctionType
ALU = mybir.AluOpType
bfloat16 = ml_dtypes.bfloat16

T = 2048
NCORE = 8
BLOC = 2
NCH = 8
DM = 1024
GH = 8
IH = 16
WIN = 256
NWIN = T // WIN
DEPTH = 6
DI2 = 5460
DI = 2730
NPAIR = 22
NJT = 22
EPS = 1e-5

ENGS = ("tensor", "vector", "scalar", "gpsimd", "sync")
DEBUG_BUILD = False
DEDUP_WAITS = True
ATTN_PARTS = ("qkv", "attn", "wo")


def _rope_tables_global():
    half = 64
    af = np.power(np.float32(10000.0), -(np.arange(half, dtype=np.float32) / np.float32(half)))
    pos = np.arange(T, dtype=np.float32)[:, None]
    rad = (pos * af[None, :]).astype(np.float32)
    cg = np.tile(np.cos(rad).astype(np.float32).T, (2, 1))
    sg = np.tile(np.sin(rad).astype(np.float32).T, (2, 1))
    return cg.astype(np.float32), sg.astype(np.float32)


def _rope_tables_inner():
    half = np.arange(0, 64, 2, dtype=np.float32)
    inv_freq = (np.float32(1.0) / np.power(np.float32(10000.0), half / np.float32(64))).astype(np.float32)
    t = np.arange(T, dtype=np.float32)
    freqs = (t[:, None] * inv_freq[None, :]).astype(np.float32)
    base_scale = ((half + np.float32(0.4 * 64)) / np.float32(1.4 * 64)).astype(np.float32)
    power = ((t - T // 2) / np.float32(WIN // 2)).astype(np.float32)
    scale = np.power(base_scale[None, :], power[:, None]).astype(np.float32)
    cos = np.cos(freqs).astype(np.float32)
    sin = np.sin(freqs).astype(np.float32)
    inv = (np.float32(1.0) / scale).astype(np.float32)

    def pat(m):
        return np.tile(m.T, (4, 1)).astype(np.float32)
    return pat(cos * scale), pat(sin * scale), pat(cos * inv), pat(sin * inv)


def _perm_global_qk():
    perm = np.zeros(1024, dtype=np.int64)
    for h in range(GH):
        for d in range(128):
            newc = (d // 64) * 512 + (h // 2) * 128 + (h % 2) * 64 + (d % 64)
            perm[newc] = h * 128 + d
    return perm


def _perm_inner_qk():
    perm = np.zeros(1024, dtype=np.int64)
    for h in range(IH):
        for d in range(64):
            newc = (d // 32) * 512 + (h // 4) * 128 + (h % 4) * 32 + (d % 32)
            perm[newc] = h * 64 + d
    return perm


def _w1_tile_cols():
    idx = []
    tcols = []
    off = 0
    for p in range(NPAIR):
        w = 128 if p < 21 else 42
        idx.extend(range(128 * p, 128 * p + w))
        tcols.append((off, w)); off += w
        idx.extend(range(DI + 128 * p, DI + 128 * p + w))
        tcols.append((off, w)); off += w
    return np.array(idx, dtype=np.int64), tcols


def host_constants():
    c = {}
    cg, sg = _rope_tables_global()
    c["tbl_g"] = np.stack([cg, sg])
    csq, ssq, csk, ssk = _rope_tables_inner()
    c["tbl_i"] = np.stack([csq, ssq, csk, ssk])
    c["identity"] = np.eye(128, dtype=np.float32)
    p = np.arange(128)[:, None]
    f = np.arange(128)[None, :]
    tri1 = (p <= f).astype(bfloat16)
    tri2 = (f <= p).astype(bfloat16)
    c["trimask4"] = np.concatenate([tri2, tri2, tri1, tri1], axis=1).astype(bfloat16)
    c["trimask2"] = np.concatenate([tri1, tri1], axis=1).astype(bfloat16)
    c["tri1one"] = tri1
    eden = np.zeros((128, 4), dtype=bfloat16)
    eden[:, 0] = bfloat16(1.0)
    eden[:, 3] = bfloat16(1.0)
    c["eden"] = eden
    en = np.zeros((4, 128, 16), dtype=bfloat16)
    for g in range(4):
        for q in range(128):
            en[g, q, 4 * g + q // 32] = bfloat16(1.0)
    c["enorm"] = en
    fn = np.zeros((4, 16, 128), dtype=np.float32)
    for g in range(4):
        for j in range(4):
            fn[g, 4 * g + j, 32 * j:32 * j + 32] = 1.0
    c["fnorm"] = fn
    fp = np.zeros((2, 128), dtype=np.float32)
    fp[0, 0:64] = 1.0
    fp[1, 64:128] = 1.0
    c["fpair"] = fp
    c["ones_col_bf"] = np.ones((128, 1), dtype=bfloat16)
    c["eps_col"] = np.full((128, 1), EPS, dtype=np.float32)
    c["ones_col_f32"] = np.ones((128, 1), dtype=np.float32)
    c["ones_row_f32"] = np.ones((1, 128), dtype=np.float32)
    return c


class Sem:
    __slots__ = ("h", "n", "name", "last_eng", "is_dma", "engs")

    def __init__(self, h, name, is_dma=False):
        self.h = h; self.n = 0; self.name = name
        self.last_eng = None; self.is_dma = is_dma
        self.engs = set()


class Plan:
    def __init__(self, nc, stack):
        self.nc = nc
        self.q = {e: [] for e in ENGS}
        self.maxwait = {e: {} for e in ENGS}
        self.stack = stack

    def sem(self, name, is_dma=False):
        return Sem(self.stack.enter_context(self.nc.semaphore(name)), name, is_dma)

    def op(self, eng, fn, waits=(), inc=None, inc_n=1):
        real = []
        mw = self.maxwait[eng]
        for s, v in waits:
            if s is None or v is None or v <= 0:
                continue
            if DEDUP_WAITS:
                if (not s.is_dma) and s.engs == {eng} and v <= s.n:
                    continue
                if mw.get(id(s), 0) >= v:
                    continue
                mw[id(s)] = v
            real.append((s, v))
        val = None
        if inc is not None:
            inc.n += inc_n
            inc.last_eng = eng
            inc.engs.add(eng)
            val = inc.n
        self.q[eng].append((tuple(real), fn, inc, inc_n))
        return val

    def dma(self, eng, fn, waits=(), inc=None):
        return self.op(eng, fn, waits, inc, inc_n=16)

    def barrier(self, bar, dma_sems=()):
        for e in ENGS:
            self.op(e, lambda E: E.nop(), inc=bar)
        tgt = bar.n
        wl = tuple((s, s.n) for s in dma_sems)
        for e in ENGS:
            self.op(e, None, waits=((bar, tgt),) + wl)

    def emit(self):
        def body(name):
            def run(E):
                for waits, fn, inc, inc_n in self.q[name]:
                    for s, v in waits:
                        E.wait_ge(s.h, v)
                    if fn is None:
                        continue
                    inst = fn(E)
                    if inc is not None:
                        inst.then_inc(inc.h, inc_n)
            return run
        with self.nc.Block() as block:
            block.tensor(body("tensor"))
            block.vector(body("vector"))
            block.scalar(body("scalar"))
            block.gpsimd(body("gpsimd"))
            block.sync(body("sync"))


class Tracker:
    def __init__(self):
        self.w = {}
        self.r = {}
        self.rb = {}

    def wait_read_bank(self, key, eng):
        out = list(self.wait_read(key))
        last = self.rb.get(key)
        if last is not None and last[0] != eng:
            out.append((last[1], last[2]))
        return out

    def note_read_bank(self, key, eng, sem, val):
        if val is None:
            return
        self.rb[key] = (eng, sem, val)
        self.note_read(key, sem, val)

    def wait_read(self, *keys):
        return [self.w[k] for k in keys if k in self.w]

    def wait_write(self, *keys):
        out = []
        for k in keys:
            if k in self.w:
                out.append(self.w[k])
            for s, v in self.r.get(k, {}).values():
                out.append((s, v))
        return out

    def note_write(self, key, sem, val):
        if val is None:
            return
        self.w[key] = (sem, val)
        self.r.pop(key, None)
        self.rb.pop(key, None)

    def note_read(self, key, sem, val):
        if val is None:
            return
        d = self.r.setdefault(key, {})
        old = d.get(id(sem))
        if old is None or old[1] < val:
            d[id(sem)] = (sem, val)


ARENA_BYTES = 195 * 1024


class Arena:
    def __init__(self, t):
        self.t = t

    def f32(self, off, n):
        assert off % 4 == 0 and off + 4 * n <= ARENA_BYTES, (off, n)
        return self.t.ap()[:, off // 4: off // 4 + n]

    def bf16(self, off, n):
        assert off % 4 == 0 and off + 2 * n <= ARENA_BYTES, (off, n)
        m = (n + 1) // 2
        return self.t.ap()[:, off // 4: off // 4 + m].bitcast(BF16)[:, :n]


def build_program(n_layers=DEPTH, n_batches=BLOC, dump_stage=None):
    nc = bass.Bass("TRN2", target_bir_lowering=False, debug=DEBUG_BUILD,
                   detect_race_conditions=False)
    stack = contextlib.ExitStack()
    nc._dev_stack = stack

    x_in = nc.dram_tensor("x_in", [BLOC, T, DM], F32, kind="ExternalInput").ap()
    out = nc.dram_tensor("out", [BLOC, T, DM], F32, kind="ExternalOutput").ap()
    gwq = nc.dram_tensor("gwq", [DM, DM], F32, kind="ExternalInput").ap()
    gwk = nc.dram_tensor("gwk", [DM, DM], F32, kind="ExternalInput").ap()
    gwv = nc.dram_tensor("gwv", [DM, DM], F32, kind="ExternalInput").ap()
    gwo = nc.dram_tensor("gwo", [DM, DM], F32, kind="ExternalInput").ap()
    iwqkv = nc.dram_tensor("iwqkv", [DEPTH, DM, 3 * DM], F32, kind="ExternalInput").ap()
    iwo = nc.dram_tensor("iwo", [DEPTH, DM, DM], F32, kind="ExternalInput").ap()
    iw1 = nc.dram_tensor("iw1", [DEPTH, DM, DI2], F32, kind="ExternalInput").ap()
    iw2 = nc.dram_tensor("iw2", [DEPTH, DI, DM], F32, kind="ExternalInput").ap()
    qscol = nc.dram_tensor("qscol", [DEPTH, 128, 4], F32, kind="ExternalInput").ap()
    lng_col = nc.dram_tensor("lng_col", [DEPTH, 128, NCH], F32, kind="ExternalInput").ap()
    lnb_col = nc.dram_tensor("lnb_col", [DEPTH, 128, NCH], F32, kind="ExternalInput").ap()
    b1_col = nc.dram_tensor("b1_col", [DEPTH, 128, 2 * NPAIR], F32, kind="ExternalInput").ap()
    b2_col = nc.dram_tensor("b2_col", [DEPTH, 128, NCH], F32, kind="ExternalInput").ap()
    tbl_g = nc.dram_tensor("tbl_g", [2, 128, T], F32, kind="ExternalInput").ap()
    tbl_i = nc.dram_tensor("tbl_i", [4, 128, T], F32, kind="ExternalInput").ap()
    identity_d = nc.dram_tensor("identity", [128, 128], F32, kind="ExternalInput").ap()
    trimask4_d = nc.dram_tensor("trimask4", [128, 512], BF16, kind="ExternalInput").ap()
    trimask2_d = nc.dram_tensor("trimask2", [128, 256], BF16, kind="ExternalInput").ap()
    tri1_d = nc.dram_tensor("tri1one", [128, 128], BF16, kind="ExternalInput").ap()
    eden_d = nc.dram_tensor("eden", [128, 4], BF16, kind="ExternalInput").ap()
    enorm_d = nc.dram_tensor("enorm", [4, 128, 16], BF16, kind="ExternalInput").ap()
    fnorm_d = nc.dram_tensor("fnorm", [4, 16, 128], F32, kind="ExternalInput").ap()
    fpair_d = nc.dram_tensor("fpair", [2, 128], F32, kind="ExternalInput").ap()
    onescb_d = nc.dram_tensor("ones_col_bf", [128, 1], BF16, kind="ExternalInput").ap()
    onescf_d = nc.dram_tensor("ones_col_f32", [128, 1], F32, kind="ExternalInput").ap()
    onesr_d = nc.dram_tensor("ones_row_f32", [1, 128], F32, kind="ExternalInput").ap()
    eps_d = nc.dram_tensor("eps_col", [128, 1], F32, kind="ExternalInput").ap()
    x1sp = nc.dram_tensor("x1sp", [DM, T], F32).ap()

    arena_t = stack.enter_context(nc.sbuf_tensor("arena", [128, ARENA_BYTES // 4], F32))
    A = Arena(arena_t)
    csb = stack.enter_context(nc.sbuf_tensor("csb", [128, 3072], F32))

    C_ID = 0
    C_TRIM4 = 128
    C_TRIM2 = 384
    C_TRI1 = 512
    C_EDEN = 576
    C_ENORM = 704
    C_FNORM = 736
    C_FPAIR = 1248
    C_ONESCB = 2272
    C_ONESCF = 2273
    C_ONESR = 2274
    C_EPS = 2528
    C_QS = 2402
    C_GCOL = 2408
    C_BLNF = 2416
    C_BLNB = 2424
    C_B1 = 2428
    C_B2 = 2472
    C_U1 = 2480
    C_END = 2529
    assert C_END <= 3072

    def cs_f32(off, n):
        return csb.ap()[:, off:off + n]

    def cs_bf(off, n):
        m = (n + 1) // 2
        return csb.ap()[:, off:off + m].bitcast(BF16)[:, :n]

    P = Plan(nc, stack)
    TK = Tracker()
    s_bar = P.sem("bar")
    s_pe = P.sem("pe")
    s_dve = P.sem("dve")
    s_act = P.sem("act")
    s_pool = P.sem("pool")
    s_ld = P.sem("ld", is_dma=True)
    s_x = P.sem("x", is_dma=True)
    s_tb0 = P.sem("tb0", is_dma=True)
    s_tb1 = P.sem("tb1", is_dma=True)
    s_wA = P.sem("wA", is_dma=True)
    s_wB = P.sem("wB", is_dma=True)
    s_w1 = [P.sem("w1a", is_dma=True), P.sem("w1b", is_dma=True)]
    s_w2 = [P.sem("w2a", is_dma=True), P.sem("w2b", is_dma=True)]
    s_vec = P.sem("vec", is_dma=True)
    s_x1r = P.sem("x1r", is_dma=True)
    s_st = P.sem("st", is_dma=True)
    ALL_DMA_SEMS = (s_ld, s_x, s_tb0, s_tb1, s_wA, s_wB, s_w1[0], s_w1[1],
                    s_w2[0], s_w2[1], s_vec, s_x1r, s_st)
    s_tbs = (s_tb0, s_tb1)

    ps_t = stack.enter_context(nc.psum_tensor("ps", [128, 4096], F32))

    def PS(bank, rows=128, cols=512, coff=0, r0=0):
        return ps_t.ap()[r0:r0 + rows, 512 * bank + coff: 512 * bank + coff + cols]

    pend_r = []
    pend_w = []

    def mm(psum_ap, lhs_ap, rhs_ap, start, stop, wbank=(), rkeys=(), waits=(), inc=None,
           tile_position=None):
        wl = list(waits)
        if start:
            for b in wbank:
                wl += TK.wait_write(("ps", b))
        for k in rkeys:
            wl += TK.wait_read(k)
        pend_r.extend(rkeys)
        pend_w.extend(("ps", b) for b in wbank)

        def f(E, a=psum_ap, b=lhs_ap, c=rhs_ap, st=start, sp=stop, tp=tile_position):
            return E.matmul(a, b, c, start=st, stop=sp, tile_position=tp)
        v = P.op("tensor", f, waits=wl, inc=inc)
        if inc is not None:
            for k in pend_w:
                TK.note_write(k, s_pe, v)
            for k in pend_r:
                TK.note_read(k, s_pe, v)
            pend_r.clear()
            pend_w.clear()
        return v

    def generic(eng, sem, fn, rkeys=(), wkeys=(), rbanks=(), wbanks=(), waits=(), inc=True):
        wl = list(waits)
        for k in rkeys:
            wl += TK.wait_read(k)
        for b in rbanks:
            wl += TK.wait_read_bank(("ps", b), eng)
        for k in wkeys:
            wl += TK.wait_write(k)
        for b in wbanks:
            wl += TK.wait_write(("ps", b))
        v = P.op(eng, fn, waits=wl, inc=sem if inc else None)
        if inc:
            for k in rkeys:
                TK.note_read(k, sem, v)
            for b in rbanks:
                TK.note_read_bank(("ps", b), eng, sem, v)
            for k in wkeys:
                TK.note_write(k, sem, v)
            for b in wbanks:
                TK.note_write(("ps", b), sem, v)
        return v

    def v_tt(o, a, b, op, **kw):
        return generic("vector", s_dve, lambda E: E.tensor_tensor(o, a, b, op), **kw)

    def v_stt(o, a, sc, b, op0, op1, **kw):
        return generic("vector", s_dve, lambda E: E.scalar_tensor_tensor(o, a, sc, b, op0, op1), **kw)

    def v_ts(o, a, s1, s2, op0, op1=None, **kw):
        if op1 is None:
            return generic("vector", s_dve, lambda E: E.tensor_scalar(o, a, s1, None, op0), **kw)
        return generic("vector", s_dve, lambda E: E.tensor_scalar(o, a, s1, s2, op0, op1), **kw)

    def v_copy(o, a, **kw):
        return generic("vector", s_dve, lambda E: E.tensor_copy(o, a), **kw)

    def v_recip(o, a, **kw):
        return generic("vector", s_dve, lambda E: E.reciprocal(o, a), **kw)

    def v_memset(o, val, **kw):
        return generic("vector", s_dve, lambda E: E.memset(o, val), **kw)

    def a_act(o, a, func, bias=None, scale=None, accum_out=None, **kw):
        def f(E):
            kk = {}
            if bias is not None:
                kk["bias"] = bias
            if scale is not None:
                kk["scale"] = scale
            if accum_out is not None:
                kk["accum_out"] = accum_out
            return E.activation(o, a, func, **kk)
        return generic("scalar", s_act, f, **kw)

    def p_tt(o, a, b, op, **kw):
        return generic("gpsimd", s_pool, lambda E: E.tensor_tensor(o, a, b, op), **kw)

    def _dma(eng, sem, o, a, **kw):
        wl = list(kw.pop("waits", ()))
        rkeys = kw.pop("rkeys", ())
        wkeys = kw.pop("wkeys", ())
        for k in rkeys:
            wl += TK.wait_read(k)
        for k in wkeys:
            wl += TK.wait_write(k)
        v = P.dma(eng, lambda E: E.dma_start(out=o, in_=a), waits=wl, inc=sem)
        for k in rkeys:
            TK.note_read(k, sem, v)
        for k in wkeys:
            TK.note_write(k, sem, v)
        return v

    def dma_sync(o, a, sem, **kw):
        return _dma("sync", sem, o, a, **kw)

    def dma_pool(o, a, sem, **kw):
        return _dma("gpsimd", sem, o, a, **kw)

    def dma_store(o, a, **kw):
        wl = list(kw.pop("waits", ()))
        for k in kw.pop("rkeys", ()):
            wl += TK.wait_read(k)
        v = P.dma("sync", lambda E: E.dma_start(out=o, in_=a), waits=wl, inc=s_st)
        for k in kw.get("note_rkeys", ()):
            TK.note_read(k, s_st, v)
        return v

    def transpose(psum_ap, in_ap, bank, rkey=None, waits=(), start_clear=True,
                  stop=False):
        wl = list(waits)
        if start_clear:
            wl += TK.wait_write(("ps", bank))
        if rkey is not None:
            wl += TK.wait_read(rkey)
        ident = cs_f32(C_ID, 128)

        def f(E, o=psum_ap, i=in_ap, idn=ident, sp=stop):
            return E.matmul(o, i, idn, start=True, stop=sp, is_transpose=True,
                            skip_group_check=True)
        v = P.op("tensor", f, waits=wl, inc=s_pe)
        pend_w.append(("ps", bank))
        if rkey is not None:
            pend_r.append(rkey)
        for k in pend_w:
            TK.note_write(k, s_pe, v)
        for k in pend_r:
            TK.note_read(k, s_pe, v)
        pend_r.clear()
        pend_w.clear()
        return v

    def barrier():
        P.barrier(s_bar, dma_sems=ALL_DMA_SEMS)
        TK.w.clear()
        TK.r.clear()
        TK.rb.clear()

    dma_sync(cs_f32(C_ID, 128), identity_d, s_ld)
    dma_sync(cs_bf(C_TRIM4, 512), trimask4_d, s_ld)
    dma_sync(cs_bf(C_TRIM2, 256), trimask2_d, s_ld)
    dma_sync(cs_bf(C_TRI1, 128), tri1_d, s_ld)
    dma_sync(cs_bf(C_EDEN, 4), eden_d, s_ld)
    dma_sync(cs_bf(C_ENORM, 64).rearrange("p (g n) -> p g n", g=4),
             enorm_d.rearrange("g p n -> p g n"), s_ld)
    dma_sync(csb.ap()[0:16, C_FNORM:C_FNORM + 512].rearrange("p (g n) -> p g n", g=4),
             fnorm_d.rearrange("g p n -> p g n"), s_ld)
    dma_sync(csb.ap()[0:2, C_FPAIR:C_FPAIR + 128], fpair_d, s_ld)
    dma_sync(cs_bf(C_ONESCB, 1), onescb_d, s_ld)
    dma_sync(cs_f32(C_ONESCF, 1), onescf_d, s_ld)
    dma_sync(csb.ap()[0:1, C_ONESR:C_ONESR + 128], onesr_d, s_ld)
    dma_sync(cs_f32(C_EPS, 1), eps_d, s_ld)
    barrier()

    G_XBF, G_KH, G_VH = 0, 32768, 65536
    G_W1, G_W2 = 98304, 114688
    G_TBL = 131072
    G_TT = 139264
    G_QS = 147456
    G_PT = 155648
    G_XST = 159744
    G_OS = 176128
    G_BC = 184320
    G_RC = 186368
    G_MU = 188416
    G_M2 = 190464
    G_STG = 192512
    assert G_STG + 2048 <= ARENA_BYTES

    def xg_bf(ch, t0, n):
        return A.bf16(G_XBF + 2 * 2048 * ch, 2048)[:, t0:t0 + n]

    def khat_g(r0, rows, ch, c0, n):
        return A.bf16(G_KH + 2 * 2048 * ch, 2048)[r0:r0 + rows, c0:c0 + n]

    def vhat_g(tt, c0, n):
        return A.bf16(G_VH + 2 * 1024 * tt, 1024)[:, c0:c0 + n]

    def rope_pass_global(bA, bB, outA, outB, tb_c, tb_s, t1b, t2b, keys):
        for cc, ub, us, sgn, o in ((0, bA, bB, -1.0, outA), (1, bB, bA, 1.0, outB)):
            v_tt(t1b, PS(ub), tb_c, ALU.mult,
                 rbanks=(ub,), rkeys=(keys["tbl"],), wkeys=(keys["t1"],))
            v_stt(t2b, PS(us), sgn, tb_s, ALU.mult, ALU.mult,
                  rbanks=(us,), rkeys=(keys["tbl"],), wkeys=(keys["t2"],))
            p_tt(o, t1b, t2b, ALU.add,
                 rkeys=(keys["t1"], keys["t2"]), wkeys=(keys["out"],))

    def emit_global(b):
        stg = A.f32(G_STG, 512)
        for ch in range(NCH):
            for tg in range(4):
                src = x_in[b, 512 * tg:512 * (tg + 1), 128 * ch:128 * (ch + 1)] \
                    .rearrange("(tt p) c -> p tt c", p=128)
                dma_sync(stg.rearrange("p (tt c) -> p tt c", tt=4), src, s_x, wkeys=(("gstg",),))
                bank = tg % 2
                for i in range(4):
                    transpose(PS(bank, 128, 128, 128 * i), stg[:, 128 * i:128 * (i + 1)],
                              bank, rkey=("gstg",), start_clear=(i == 0), stop=(i == 3))
                v_copy(xg_bf(ch, 512 * tg, 512), PS(bank), rbanks=(bank,),
                       wkeys=(("gxbf", ch, tg),))
        barrier()

        wk_bf = A.bf16(G_W1, 8 * 1024)
        wv_bf = A.bf16(G_W2, 8 * 1024)
        dma_pool(wk_bf.rearrange("p (c n) -> p c n", c=8), gwk.rearrange("(c p) n -> p c n", p=128), s_wA, wkeys=(("gw1",),))
        dma_pool(wv_bf.rearrange("p (c n) -> p c n", c=8), gwv.rearrange("(c p) n -> p c n", p=128), s_wB, wkeys=(("gw2",),))
        tbc = [A.f32(G_TBL + 4096 * i, 512) for i in range(2)]
        tbs = [A.f32(G_TBL + 2048 + 4096 * i, 512) for i in range(2)]
        t1b = [A.f32(G_TT + 4096 * i, 512) for i in range(2)]
        t2b = [A.f32(G_TT + 2048 + 4096 * i, 512) for i in range(2)]
        for ss in range(4):
            t0 = 512 * ss
            sl = ss % 2
            dma_sync(tbc[sl], tbl_g[0][:, t0:t0 + 512], s_tbs[sl], wkeys=(("gtb", sl),))
            dma_sync(tbs[sl], tbl_g[1][:, t0:t0 + 512], s_tbs[sl], wkeys=(("gtb", sl),))
            for g in range(4):
                for cc, bank in ((g, 4), (4 + g, 5)):
                    for kc in range(NCH):
                        mm(PS(bank), wk_bf[:, 1024 * kc + 128 * cc:1024 * kc + 128 * cc + 128],
                           xg_bf(kc, t0, 512), start=(kc == 0), stop=(kc == NCH - 1),
                           wbank=(bank,), rkeys=(("gw1",), ("gxbf", kc, ss)),
                           inc=s_pe if kc == NCH - 1 else None)
                rope_pass_global(4, 5,
                                 khat_g(0, 128, g, t0, 512), khat_g(0, 128, 4 + g, t0, 512),
                                 tbc[sl], tbs[sl], t1b[sl], t2b[sl],
                                 {"t1": ("gt1", sl), "t2": ("gt2", sl),
                                  "tbl": ("gtb", sl), "out": ("gkh",)})
            for tt in range(4 * ss, 4 * ss + 4):
                for nt in range(2):
                    bank = 6 + nt
                    for kc in range(NCH):
                        mm(PS(bank), xg_bf(kc, 128 * tt - t0 + t0, 128) if False else
                           xg_bf(kc, 128 * tt, 128),
                           wv_bf[:, 1024 * kc + 512 * nt:1024 * kc + 512 * nt + 512],
                           start=(kc == 0), stop=(kc == NCH - 1),
                           wbank=(bank,), rkeys=(("gw2",), ("gxbf", kc, ss)),
                           inc=s_pe if kc == NCH - 1 else None)
                    v_copy(vhat_g(tt, 512 * nt, 512), PS(bank), rbanks=(bank,),
                           wkeys=(("gvh",),))
        barrier()

        wq_bf = A.bf16(G_W1, 8 * 1024)
        wo_bf = A.bf16(G_W2, 8 * 1024)
        dma_pool(wq_bf.rearrange("p (c n) -> p c n", c=8), gwq.rearrange("(c p) n -> p c n", p=128), s_wA, wkeys=(("gw1",),))
        dma_pool(wo_bf.rearrange("p (c n) -> p c n", c=8), gwo.rearrange("(c p) n -> p c n", p=128), s_wB, wkeys=(("gw2",),))
        qsb = A.bf16(G_QS, 8 * 512)
        ptb = A.bf16(G_PT, 2048)
        xst = A.f32(G_XST, 8 * 512)
        sqb = A.bf16(G_OS, 8 * 512)
        obuf = A.bf16(G_OS, 8 * 512)
        bce = A.f32(G_BC, 512)
        rcp = A.f32(G_RC, 512)[0:1, :]
        mu = A.f32(G_MU, 512)[0:1, :]
        m2 = A.f32(G_M2, 512)[0:1, :]
        for ss in range(4):
            t0 = 512 * ss
            sl = ss % 2
            dma_sync(tbc[sl], tbl_g[0][:, t0:t0 + 512], s_tbs[sl], wkeys=(("gtb", sl),))
            dma_sync(tbs[sl], tbl_g[1][:, t0:t0 + 512], s_tbs[sl], wkeys=(("gtb", sl),))
            for g in range(4):
                for cc, bank in ((g, 4), (4 + g, 5)):
                    for kc in range(NCH):
                        mm(PS(bank), wq_bf[:, 1024 * kc + 128 * cc:1024 * kc + 128 * cc + 128],
                           xg_bf(kc, t0, 512), start=(kc == 0), stop=(kc == NCH - 1),
                           wbank=(bank,), rkeys=(("gw1",),),
                           inc=s_pe if kc == NCH - 1 else None)
                rope_pass_global(4, 5,
                                 A.bf16(G_QS + 2 * 512 * g, 512),
                                 A.bf16(G_QS + 2 * 512 * (4 + g), 512),
                                 tbc[sl], tbs[sl], t1b[sl], t2b[sl],
                                 {"t1": ("gt1", sl), "t2": ("gt2", sl),
                                  "tbl": ("gtb", sl), "out": ("gqs",)})
            for ch in range(NCH):
                src = x_in[b, t0:t0 + 512, 128 * ch:128 * (ch + 1)] \
                    .rearrange("(tt p) c -> p tt c", p=128)
                dma_sync(stg.rearrange("p (tt c) -> p tt c", tt=4), src, s_x, wkeys=(("gstg",),))
                for i in range(4):
                    transpose(PS(6, 128, 128, 128 * i), stg[:, 128 * i:128 * (i + 1)],
                              6, rkey=("gstg",), start_clear=(i == 0), stop=(i == 3))
                v_copy(xst[:, 512 * ch:512 * (ch + 1)], PS(6), rbanks=(6,),
                       wkeys=(("gxst", ch),))
            nkt = 4 * (ss + 1)
            for h in range(GH):
                r0 = 64 * (h % 2)
                cA, cB = h // 2, 4 + h // 2
                pv_started = False
                dn_started = False
                for ktg in range(0, nkt, 4):
                    gsz = min(4, nkt - ktg)
                    for i in range(gsz):
                        kt = ktg + i
                        qlo = max(0, 128 * kt - t0)
                        nq = 512 - qlo
                        for half, cc in ((0, cA), (1, cB)):
                            mm(PS(i, 128, nq, qlo),
                               khat_g(r0, 64, cc, 128 * kt, 128),
                               A.bf16(G_QS + 2 * 512 * cc, 512)[r0:r0 + 64, qlo:qlo + nq],
                               start=(half == 0), stop=(half == 1),
                               wbank=(i,), rkeys=(("gkh",), ("gqs",)),
                               tile_position=(r0, 0),
                               inc=s_pe if half == 1 else None)
                    if 128 * (ktg + gsz) <= t0:
                        a_act(ptb[:, 0:512 * gsz], ps_t.ap()[:, 0:512 * gsz], AF.Exp,
                              scale=1.0 / 128.0,
                              rbanks=tuple(range(gsz)), wkeys=(("gpt",),))
                    else:
                        for i in range(gsz):
                            kt = ktg + i
                            qlo = max(0, 128 * kt - t0)
                            a_act(ptb[:, 512 * i + qlo:512 * (i + 1)],
                                  PS(i, 128, 512 - qlo, qlo), AF.Exp, scale=1.0 / 128.0,
                                  rbanks=(i,), wkeys=(("gpt",),))
                            if 128 * kt >= t0:
                                v_tt(ptb[:, 512 * i + qlo:512 * i + qlo + 128],
                                     ptb[:, 512 * i + qlo:512 * i + qlo + 128],
                                     cs_bf(C_TRI1, 128), ALU.mult,
                                     rkeys=(("gpt",),), wkeys=(("gpt",),))
                    for i in range(gsz):
                        kt = ktg + i
                        qlo = max(0, 128 * kt - t0)
                        nq = 512 - qlo
                        last = (kt == nkt - 1)
                        mm(PS(6, 1, nq, qlo), cs_bf(C_ONESCB, 1),
                           ptb[:, 512 * i + qlo:512 * (i + 1)],
                           start=not dn_started, stop=last,
                           wbank=(6,), rkeys=(("gpt",),),
                           inc=s_pe if last else None)
                        dn_started = True
                        mm(PS(7, 128, nq, qlo), vhat_g(kt, 128 * h, 128),
                           ptb[:, 512 * i + qlo:512 * (i + 1)],
                           start=not pv_started, stop=last,
                           wbank=(7,),
                           rkeys=(("gvh",), ("gpt",)),
                           inc=s_pe if last else None)
                        pv_started = True
                v_recip(rcp, PS(6, 1, 512), rbanks=(6,), wkeys=(("grcp",),))
                mm(PS(5), csb.ap()[0:1, C_ONESR:C_ONESR + 128], rcp, start=True, stop=True,
                   wbank=(5,), rkeys=(("grcp",),), inc=s_pe)
                v_copy(bce, PS(5), rbanks=(5,), wkeys=(("gbce",),))
                v_tt(obuf[:, 512 * h:512 * (h + 1)], PS(7), bce, ALU.mult,
                     rbanks=(7,), rkeys=(("gbce",),), wkeys=(("gos", h),))
            for mt in range(NCH):
                for hc in range(NCH):
                    mm(PS(mt), wo_bf[:, 1024 * hc + 128 * mt:1024 * hc + 128 * mt + 128],
                       obuf[:, 512 * hc:512 * (hc + 1)],
                       start=(hc == 0), stop=(hc == NCH - 1),
                       wbank=(mt,), rkeys=(("gw2",), ("gos", hc)),
                       inc=s_pe if hc == NCH - 1 else None)
                v_tt(xst[:, 512 * mt:512 * (mt + 1)], PS(mt), xst[:, 512 * mt:512 * (mt + 1)],
                     ALU.add, rbanks=(mt,), rkeys=(("gxst", mt),), wkeys=(("gxst", mt),))
            if getattr(build_program, "_dbg", False) and ss in (0,):
                def _dbgx(E, t=xst, tag=(b, ss)):
                    import numpy as _np
                    E.dbg_aps(lambda a, g, tg=tag: print("XSTDBG", tg, _np.asarray(a), _np.asarray(g)),
                              t[0:1, 0:4], t[5:6, 512 * 3:512 * 3 + 4])
                    return E.nop()
                P.op("vector", _dbgx,
                     waits=sum((TK.wait_read(("gxst", c)) + TK.wait_write(("gxst", c))
                                for c in range(NCH)), []))
            for ch in range(NCH):
                a_act(sqb[:, 512 * ch:512 * (ch + 1)], xst[:, 512 * ch:512 * (ch + 1)],
                      AF.Square, rkeys=(("gxst", ch),), wkeys=(("gos", ch),))
            for ch in range(NCH):
                mm(PS(4, 1, 512), cs_f32(C_ONESCF, 1), xst[:, 512 * ch:512 * (ch + 1)],
                   start=(ch == 0), stop=(ch == NCH - 1),
                   wbank=(4,), rkeys=(("gxst", ch),),
                   inc=s_pe if ch == NCH - 1 else None)
            for ch in range(NCH):
                mm(PS(5, 1, 512), cs_bf(C_ONESCB, 1), sqb[:, 512 * ch:512 * (ch + 1)],
                   start=(ch == 0), stop=(ch == NCH - 1),
                   wbank=(5,), rkeys=(("gos", ch),),
                   inc=s_pe if ch == NCH - 1 else None)
            a_act(mu, PS(4, 1, 512), AF.Copy, scale=1.0 / DM, rbanks=(4,), wkeys=(("gmu",),))
            a_act(m2, PS(4, 1, 512), AF.Square, scale=1.0 / DM, rbanks=(4,), wkeys=(("gm2",),))
            v_stt(m2, PS(5, 1, 512), 1.0 / DM, m2, ALU.mult, ALU.subtract,
                  rbanks=(5,), rkeys=(("gm2",),), wkeys=(("gm2",),))
            a_act(m2, m2, AF.Sqrt, bias=cs_f32(C_EPS, 1)[0:1, :], rkeys=(("gm2",),), wkeys=(("gm2",),))
            v_recip(m2, m2, rkeys=(("gm2",),), wkeys=(("gm2",),))
            v_stt(mu, mu, -1.0, m2, ALU.mult, ALU.mult,
                  rkeys=(("gmu",), ("gm2",)), wkeys=(("gmu",),))
            mm(PS(4), csb.ap()[0:1, C_ONESR:C_ONESR + 128], m2, start=True, stop=True,
               wbank=(4,), rkeys=(("gm2",),), inc=s_pe)
            mm(PS(5), csb.ap()[0:1, C_ONESR:C_ONESR + 128], mu, start=True, stop=True,
               wbank=(5,), rkeys=(("gmu",),), inc=s_pe)
            for ch in range(NCH):
                v_tt(xst[:, 512 * ch:512 * (ch + 1)], xst[:, 512 * ch:512 * (ch + 1)],
                     PS(4), ALU.mult, rbanks=(4,), rkeys=(("gxst", ch),), wkeys=(("gxst", ch),))
                v_tt(xst[:, 512 * ch:512 * (ch + 1)], xst[:, 512 * ch:512 * (ch + 1)],
                     PS(5), ALU.add, rbanks=(5,), rkeys=(("gxst", ch),), wkeys=(("gxst", ch),))
            dma_store(x1sp[:, t0:t0 + 512].rearrange("(c p) t -> p c t", p=128),
                      xst.rearrange("p (c t) -> p c t", c=8),
                      rkeys=tuple(("gxst", ch) for ch in range(NCH)),
                      note_rkeys=tuple(("gxst", ch) for ch in range(NCH)))
        barrier()

    I_X = 0
    I_WQKV = 65568
    I_WO = 114720
    I_XBFW = 131104
    I_QPRE = 139328
    I_QBF = 147520
    I_KBF = 151616
    I_VBF = 159808
    I_USQ = 168000
    I_PT = 169024
    I_TBL = 172096
    I_TT = 180288
    I_OBUF = 182336
    I_BCE = 186432
    I_NRM = 187456
    I_END = 189504
    assert I_END <= ARENA_BYTES
    F_H = I_WQKV
    F_XSQ = I_WO
    F_W1 = I_WO + 8224
    F_XN = 139328
    F_W2 = 155712
    F_GEL = 163904
    F_STA = 168000
    F_END = 174144
    assert F_END <= ARENA_BYTES

    def x_sl(ch, c0, n):
        return A.f32(I_X + 4 * 2049 * ch, 2049)[:, c0:c0 + n]

    def inner_qk(ten, w, slot, outfn, sA, sB, nsB, wqkv, xbfw, tb, t1b, t2b, qpre, usq,
                 nrm, recd, layer_key):
        base = 1024 * ten
        t0 = WIN * w
        _full = "qkv" in ATTN_PARTS or "q" in ATTN_PARTS or "k" in ATTN_PARTS
        _p_mm = _full or "qk_mm" in ATTN_PARTS
        _p_sq = _p_mm and (_full or "qk_sq" in ATTN_PARTS)
        _p_rot = _p_mm and (_full or "qk_rot" in ATTN_PARTS)
        _p_bc = _p_sq and (_full or "qk_bc" in ATTN_PARTS)
        for g in range(4):
            if _p_mm:
                for cc, bank in ((g, 4), (4 + g, 5)):
                    for kc in range(NCH):
                        o = 257 * kc + (1 if kc < 4 else 0)
                        mm(PS(bank, 128, 256),
                           wqkv[:, 3072 * kc + base + 128 * cc:3072 * kc + base + 128 * cc + 128],
                           xbfw[slot][:, o:o + 256],
                           start=(kc == 0), stop=(kc == NCH - 1),
                           wbank=(bank,), rkeys=(layer_key, ("xbfw", slot)),
                           inc=s_pe if kc == NCH - 1 else None)
            if _p_sq:
                a_act(usq[:, 0:256], PS(4, 128, 256), AF.Square, rbanks=(4,), wkeys=(("usq", 0),))
                a_act(usq[:, 256:512], PS(5, 128, 256), AF.Square, rbanks=(5,), wkeys=(("usq", 1),))
                for j in range(2):
                    mm(PS(6, 16, 256), cs_bf(C_ENORM, 64)[:, 16 * g:16 * (g + 1)],
                       usq[:, 256 * j:256 * (j + 1)],
                       start=(g == 0 and j == 0), stop=(g == 3 and j == 1),
                       wbank=(6,), rkeys=(("usq", j),),
                       inc=s_pe if (g == 3 and j == 1) else None)
            if _p_rot:
                for cc, ub, us, s1, s2 in ((g, 4, 5, sA, nsB), (4 + g, 5, 4, sB, sA)):
                    v_stt(t1b, PS(ub, 128, 256), s1, tb[2 * ten], ALU.mult, ALU.mult,
                          rbanks=(ub,), rkeys=(("itb", slot),), wkeys=(("it1",),))
                    v_stt(t2b, PS(us, 128, 256), s2, tb[2 * ten + 1], ALU.mult, ALU.mult,
                          rbanks=(us,), rkeys=(("itb", slot),), wkeys=(("it2",),))
                    p_tt(qpre[:, 256 * cc:256 * (cc + 1)], t1b, t2b, ALU.add,
                         rkeys=(("it1",), ("it2",)), wkeys=(("qpre", cc),))
        if _p_sq:
            a_act(nrm, PS(6, 16, 256), AF.Sqrt, rbanks=(6,), wkeys=(("nrm",),))
            v_ts(nrm, nrm, 1e-12, None, ALU.max, rkeys=(("nrm",),), wkeys=(("nrm",),))
            v_recip(recd, nrm, rkeys=(("nrm",),), wkeys=(("recd",),))
        if _p_bc:
            for g in range(4):
                mm(PS(7, 128, 256), csb.ap()[0:16, C_FNORM + 128 * g:C_FNORM + 128 * (g + 1)],
                   recd, start=True, stop=True, wbank=(7,), rkeys=(("recd",),), inc=s_pe)
                for cc in (g, 4 + g):
                    v_tt(outfn(cc), qpre[:, 256 * cc:256 * (cc + 1)], PS(7, 128, 256), ALU.mult,
                         rbanks=(7,), rkeys=(("qpre", cc),), wkeys=(("qk_out", ten),))

    def emit_inner_attn(b, l):
        wqkv = A.bf16(I_WQKV, 8 * 3072)
        wo_bf = A.bf16(I_WO, 8 * 1024)
        lk = ("iw", l)
        dma_pool(wqkv.rearrange("p (c n) -> p c n", c=8),
                 iwqkv[l].rearrange("(c p) n -> p c n", p=128), s_wA, wkeys=(lk,))
        dma_pool(wo_bf.rearrange("p (c n) -> p c n", c=8),
                 iwo[l].rearrange("(c p) n -> p c n", p=128), s_wB, wkeys=(lk,))
        dma_sync(cs_f32(C_QS, 4), qscol[l], s_vec, wkeys=(("qs",),))
        v_ts(cs_f32(C_QS + 4, 1), cs_f32(C_QS + 1, 1), -1.0, None, ALU.mult,
             rkeys=(("qs",),), wkeys=(("qs",),))
        v_ts(cs_f32(C_QS + 5, 1), cs_f32(C_QS + 3, 1), -1.0, None, ALU.mult,
             rkeys=(("qs",),), wkeys=(("qs",),))
        qsA, qsB = cs_f32(C_QS, 1), cs_f32(C_QS + 1, 1)
        ksA, ksB = cs_f32(C_QS + 2, 1), cs_f32(C_QS + 3, 1)
        nqsB, nksB = cs_f32(C_QS + 4, 1), cs_f32(C_QS + 5, 1)

        xbfw = [A.bf16(I_XBFW + 4112 * i, 8 * 257) for i in range(2)]
        qpre = A.f32(I_QPRE, 8 * 256)
        qbf = A.bf16(I_QBF, 8 * 256)
        kbf = A.bf16(I_KBF, 8 * 512)
        vbf = A.bf16(I_VBF, 4 * 1024)
        usq = A.bf16(I_USQ, 512)
        ptb = [A.bf16(I_PT + 1536 * i, 768) for i in range(2)]
        tb = [[A.f32(I_TBL + 1024 * j + 4096 * i, 256) for j in range(4)] for i in range(2)]
        t1b = A.f32(I_TT, 256)
        t2b = A.f32(I_TT + 1024, 256)
        ob = A.bf16(I_OBUF, 8 * 256)
        bce = A.f32(I_BCE, 256)
        nrm = A.f32(I_NRM, 256)[0:16, :]
        recd = A.f32(I_NRM + 1024, 256)[0:16, :]

        def cast_win(w):
            slot = w % 2
            t0 = WIN * w
            xsrc = A.f32(I_X, 8 * 2049).rearrange("p (c t) -> p c t", c=8)[:, :, t0:t0 + 257]
            xdst = xbfw[slot].rearrange("p (c t) -> p c t", c=8)
            v_copy(xdst, xsrc,
                   rkeys=tuple(("x", ch) for ch in range(NCH)), wkeys=(("xbfw", slot),))

        _cast_on = "qkv" in ATTN_PARTS or "cast" in ATTN_PARTS
        _tbl_on = "qkv" in ATTN_PARTS or "tbl" in ATTN_PARTS
        if _cast_on:
            cast_win(0)
        for w in range(NWIN):
            t0 = WIN * w
            slot = w % 2
            pslot = (w - 1) % 2
            for j in (range(4) if _tbl_on else ()):
                dma_sync(tb[slot][j], tbl_i[j][:, t0:t0 + 256], s_tbs[slot], wkeys=(("itb", slot),))
            if ("qkv" in ATTN_PARTS or "q" in ATTN_PARTS
                    or any(p.startswith("qk_") for p in ATTN_PARTS)):
                inner_qk(0, w, slot, lambda cc: A.bf16(I_QBF + 2 * 256 * cc, 256),
                         qsA, qsB, nqsB, wqkv, xbfw, tb[slot], t1b, t2b, qpre, usq, nrm, recd, lk)
            if "qkv" in ATTN_PARTS or "k" in ATTN_PARTS:
                inner_qk(1, w, slot, lambda cc: A.bf16(I_KBF + 2 * (512 * cc + 256 * slot), 256),
                         ksA, ksB, nksB, wqkv, xbfw, tb[slot], t1b, t2b, qpre, usq, nrm, recd, lk)
            if "qkv" in ATTN_PARTS or "v" in ATTN_PARTS:
                for tt in range(2):
                    for nt in range(2):
                        bank = 4 + nt
                        for kc in range(NCH):
                            o = 257 * kc + (1 if kc < 4 else 0) + 128 * tt
                            mm(PS(bank), xbfw[slot][:, o:o + 128],
                               wqkv[:, 3072 * kc + 2048 + 512 * nt:3072 * kc + 2048 + 512 * nt + 512],
                               start=(kc == 0), stop=(kc == NCH - 1),
                               wbank=(bank,), rkeys=(lk, ("xbfw", slot)),
                               inc=s_pe if kc == NCH - 1 else None)
                        v_copy(A.bf16(I_VBF + 2 * (1024 * (2 * slot + tt) + 512 * nt), 512),
                               PS(bank), rbanks=(bank,), wkeys=(("vbf", slot),))
            if w == 0:
                slots = [(0, (slot, 0), 1), (1, (slot, 1), 1), (1, (slot, 0), 0)]
                maskc = cs_bf(C_TRIM2, 256)
                nmask = 2
            else:
                slots = [(0, (pslot, 0), 2), (1, (pslot, 1), 2),
                         (0, (slot, 0), 1), (1, (slot, 1), 1),
                         (0, (pslot, 1), 0), (1, (slot, 0), 0)]
                maskc = cs_bf(C_TRIM4, 512)
                nmask = 4
            nslot = len(slots)
            for h in (range(IH) if "attn" in ATTN_PARTS else ()):
                r0 = 32 * (h % 4)
                g = h // 4
                cA, cB = g, 4 + g
                pts = ptb[h % 2]
                sb0 = 2 * (h % 2)
                pvb = 4 + ((h // 2) % 2)
                pp = h // 2
                for si, (a, (ks, kt), mk) in enumerate(slots):
                    for half, cc in ((0, cA), (1, cB)):
                        kcol = 512 * cc + 256 * ks + 128 * kt
                        mm(ps_t.ap()[:, 512 * sb0 + 128 * si:512 * sb0 + 128 * (si + 1)],
                           kbf[r0:r0 + 32, kcol:kcol + 128],
                           qbf[r0:r0 + 32, 256 * cA + 128 * a:256 * cA + 128 * a + 128]
                           if False else
                           A.bf16(I_QBF + 2 * 256 * cc, 256)[r0:r0 + 32, 128 * a:128 * (a + 1)],
                           start=(half == 0), stop=(half == 1),
                           wbank=(sb0, sb0 + 1),
                           rkeys=(("qk_out", 0), ("qk_out", 1)),
                           tile_position=(r0, 0),
                           inc=s_pe if half == 1 else None)
                a_act(pts[:, 0:128 * nslot], ps_t.ap()[:, 512 * sb0:512 * sb0 + 128 * nslot],
                      AF.Exp, scale=8.0,
                      rbanks=(sb0, sb0 + 1), wkeys=(("pt", h % 2),))
                v_tt(pts[:, 0:128 * nmask], pts[:, 0:128 * nmask], maskc, ALU.mult,
                     rkeys=(("pt", h % 2),), wkeys=(("pt", h % 2),))
                for si, (a, (ks, kt), mk) in enumerate(slots):
                    mm(PS(6, 2, 128, 128 * a), cs_bf(C_EDEN, 4)[:, 2 * (h % 2):2 * (h % 2) + 2],
                       pts[:, 128 * si:128 * (si + 1)],
                       start=(si == 0 and h % 2 == 0), stop=(si == nslot - 1 and h % 2 == 1),
                       wbank=(6,),
                       rkeys=(("pt", h % 2),),
                       inc=s_pe if (si == nslot - 1 and h % 2 == 1) else None)
                for si, (a, (ks, kt), mk) in enumerate(slots):
                    vcol = 1024 * (2 * ks + kt) + 64 * h
                    mm(PS(pvb, 64, 128, 128 * a, r0=64 * (h % 2)),
                       A.bf16(I_VBF + 2 * vcol, 64),
                       pts[:, 128 * si:128 * (si + 1)],
                       start=(si == 0), stop=(si == nslot - 1),
                       wbank=(pvb,),
                       rkeys=(("vbf", ks), ("pt", h % 2)),
                       tile_position=(0, 64 * (h % 2)),
                       inc=s_pe if si == nslot - 1 else None)
                if h % 2 == 1:
                    v_recip(recd[0:2, :], PS(6, 2, 256), rbanks=(6,), wkeys=(("recd",),))
                    mm(PS(7, 128, 256), csb.ap()[0:2, C_FPAIR:C_FPAIR + 128],
                       recd[0:2, :], start=True, stop=True, wbank=(7,), rkeys=(("recd",),),
                       inc=s_pe)
                    v_copy(bce, PS(7, 128, 256), rbanks=(7,), wkeys=(("bce",),))
                    v_tt(ob[:, 256 * pp:256 * (pp + 1)], PS(pvb, 128, 256), bce, ALU.mult,
                         rbanks=(pvb,), rkeys=(("bce",),), wkeys=(("ob", pp),))
            if w + 1 < NWIN and _cast_on:
                cast_win(w + 1)
            for mt in (range(NCH) if "wo" in ATTN_PARTS else ()):
                for hc in range(NCH):
                    mm(PS(mt, 128, 256), wo_bf[:, 1024 * hc + 128 * mt:1024 * hc + 128 * mt + 128],
                       ob[:, 256 * hc:256 * (hc + 1)],
                       start=(hc == 0), stop=(hc == NCH - 1),
                       wbank=(mt,),
                       rkeys=(lk, ("ob", hc)),
                       inc=s_pe if hc == NCH - 1 else None)
                v_tt(x_sl(mt, t0 + 1, 256), PS(mt, 128, 256), x_sl(mt, t0 + 1, 256), ALU.add,
                     rbanks=(mt,), rkeys=(("x", mt),), wkeys=(("x", mt),))
        barrier()

    def emit_inner_ffn(b, l):
        hbf = A.bf16(F_H, NJT * 1024)
        xsq = A.bf16(F_XSQ, 8 * 513)
        w1s = [A.bf16(F_W1 + 8192 * i, 8 * 512) for i in range(2)]
        xn = [A.bf16(F_XN + 8192 * i, 8 * 512) for i in range(2)]
        w2s = [A.bf16(F_W2 + 4096 * i, 2 * 1024) for i in range(2)]
        gel = A.f32(F_GEL, 1024)
        stA = A.f32(F_STA, 512)[0:1, :]
        stB = A.f32(F_STA + 2048, 512)[0:1, :]
        stM = A.f32(F_STA + 4096, 512)[0:1, :]
        dma_sync(cs_f32(C_GCOL, 8), lng_col[l], s_vec, wkeys=(("lv",),))
        dma_sync(cs_f32(C_BLNF, 8), lnb_col[l], s_vec, wkeys=(("lv",),))
        dma_sync(cs_f32(C_B1, 44), b1_col[l], s_vec, wkeys=(("lv",),))
        dma_sync(cs_f32(C_B2, 8), b2_col[l], s_vec, wkeys=(("lv",),))
        v_copy(cs_bf(C_BLNB, 8), cs_f32(C_BLNF, 8), rkeys=(("lv",),), wkeys=(("lv",),))

        def w1cols(p):
            return (256 * p, 128 if p < 21 else 42) if p < 21 else (5376, 42)

        def emit_xn(qq):
            t0 = 512 * qq
            sl = qq % 2
            for ch in range(NCH):
                a_act(xsq[:, 513 * ch:513 * ch + 513], x_sl(ch, t0, 513), AF.Square,
                      rkeys=(("x", ch),), wkeys=(("xsq", ch),))
            for ch in range(NCH):
                o = 513 * ch + (1 if ch < 4 else 0)
                mm(PS(6, 1, 512), cs_f32(C_ONESCF, 1),
                   x_sl(ch, t0 + (1 if ch < 4 else 0), 512),
                   start=(ch == 0), stop=(ch == NCH - 1),
                   wbank=(6,), rkeys=(("x", ch),),
                   inc=s_pe if ch == NCH - 1 else None)
            for ch in range(NCH):
                o = 513 * ch + (1 if ch < 4 else 0)
                mm(PS(7, 1, 512), cs_bf(C_ONESCB, 1), xsq[:, o:o + 512],
                   start=(ch == 0), stop=(ch == NCH - 1),
                   wbank=(7,), rkeys=(("xsq", ch),),
                   inc=s_pe if ch == NCH - 1 else None)
            a_act(stM, PS(6, 1, 512), AF.Copy, scale=1.0 / DM, rbanks=(6,), wkeys=(("stM",),))
            a_act(stA, PS(6, 1, 512), AF.Square, scale=1.0 / DM, rbanks=(6,), wkeys=(("stA",),))
            v_stt(stA, PS(7, 1, 512), 1.0 / DM, stA, ALU.mult, ALU.subtract,
                  rbanks=(7,), rkeys=(("stA",),), wkeys=(("stA",),))
            a_act(stA, stA, AF.Sqrt, bias=cs_f32(C_EPS, 1)[0:1, :], rkeys=(("stA",),), wkeys=(("stA",),))
            v_recip(stA, stA, rkeys=(("stA",),), wkeys=(("stA",),))
            v_stt(stB, stM, -1.0, stA, ALU.mult, ALU.mult,
                  rkeys=(("stM",), ("stA",)), wkeys=(("stB",),))
            mm(PS(6), csb.ap()[0:1, C_ONESR:C_ONESR + 128], stA, start=True, stop=True,
               wbank=(6,), rkeys=(("stA",),), inc=s_pe)
            mm(PS(7), csb.ap()[0:1, C_ONESR:C_ONESR + 128], stB, start=True, stop=True,
               wbank=(7,), rkeys=(("stB",),), inc=s_pe)
            for ch in range(NCH):
                sh = 1 if ch < 4 else 0
                v_stt(gel[:, 0:512], x_sl(ch, t0 + sh, 512), cs_f32(C_GCOL + ch, 1),
                      PS(6), ALU.mult, ALU.mult,
                      rbanks=(6,), rkeys=(("x", ch), ("lv",)), wkeys=(("gel",),))
                v_stt(xn[sl][:, 512 * ch:512 * (ch + 1)], PS(7), cs_f32(C_GCOL + ch, 1),
                      gel[:, 0:512], ALU.mult, ALU.add,
                      rbanks=(7,), rkeys=(("gel",), ("lv",)), wkeys=(("xn", sl),))

        def emit_w1_half(hh, first):
            for jg in range(11):
                sl = jg % 2
                c0 = 512 * jg
                cw = 512 if jg < 10 else 340
                dma_pool(w1s[sl][:, 0:0 + 8 * 512] if False else w1s[sl],
                         None, wkeys=()) if False else None
                src = iw1[l][:, c0:c0 + cw].rearrange("(c p) n -> p c n", p=128)
                dma_pool(w1s[sl].rearrange("p (c n) -> p c n", c=8)[:, :, 0:cw],
                         src, s_w1[sl], wkeys=(("w1s", sl),))
                for pi in range(2):
                    p = 2 * jg + pi
                    if p >= NPAIR:
                        continue
                    wdt = 128 if p < 21 else 42
                    la = 256 * pi if jg < 10 else 256 * pi
                    if jg < 10:
                        loa, log = 256 * pi, 256 * pi + 128
                    else:
                        loa, log = 256 * pi, 256 * pi + wdt
                    if first:
                        for ti, lo in ((2 * p, loa), (2 * p + 1, log)):
                            for kc in range(NCH):
                                mm(PS(6, wdt, 1), w1s[sl][:, 512 * kc + lo:512 * kc + lo + wdt],
                                   cs_bf(C_BLNB + kc // 2, 0)[:, 0:0] if False else
                                   cs_bf(C_BLNB, 8)[:, kc:kc + 1],
                                   start=(kc == 0), stop=(kc == NCH - 1),
                                   wbank=(6,),
                                   rkeys=(("w1s", sl), ("lv",)),
                                   inc=s_pe if kc == NCH - 1 else None)
                            v_stt(cs_f32(C_U1 + ti, 1)[0:wdt, :], PS(6, wdt, 1),
                                  1.0, cs_f32(C_B1 + ti, 1)[0:wdt, :], ALU.mult, ALU.add,
                                  rbanks=(6,), rkeys=(("lv",),), wkeys=(("u1", ti),))
                    for bb, lo in ((0, loa), (2, log)):
                        for nt in range(2):
                            bank = bb + nt
                            for kc in range(NCH):
                                mm(PS(bank, wdt, 512), w1s[sl][:, 512 * kc + lo:512 * kc + lo + wdt],
                                   xn[nt][:, 512 * kc:512 * (kc + 1)],
                                   start=(kc == 0), stop=(kc == NCH - 1),
                                   wbank=(bank,),
                                   rkeys=(("w1s", sl), ("xn", nt)),
                                   inc=s_pe if kc == NCH - 1 else None)
                    a_act(gel[0:wdt, :], ps_t.ap()[0:wdt, 1024:2048], AF.Gelu,
                          bias=cs_f32(C_U1 + 2 * p + 1, 1)[0:wdt, :],
                          rbanks=(2, 3), rkeys=(("u1", 2 * p + 1),), wkeys=(("gel",),))
                    v_stt(hbf[0:wdt, 1024 * p:1024 * (p + 1)], ps_t.ap()[0:wdt, 0:1024],
                          cs_f32(C_U1 + 2 * p, 1)[0:wdt, :], gel[0:wdt, :],
                          ALU.add, ALU.mult,
                          rbanks=(0, 1), rkeys=(("gel",), ("u1", 2 * p)), wkeys=(("h", p),))

        def emit_w2_half(hh):
            for nt in range(2):
                t0 = 1024 * hh + 512 * nt
                for jg in range(11):
                    sl = jg % 2
                    r0 = 256 * jg
                    rh = 256 if jg < 10 else 170
                    if jg < 10:
                        dma_pool(w2s[sl].rearrange("p (j m) -> p j m", j=2),
                                 iw2[l][r0:r0 + 256, :].rearrange("(j p) m -> p j m", p=128),
                                 s_w2[sl], wkeys=(("w2s", sl),))
                    else:
                        dma_pool(w2s[sl][0:128, 0:1024],
                                 iw2[l][2560:2688, :], s_w2[sl], wkeys=(("w2s", sl),))
                        dma_pool(w2s[sl][0:42, 1024:2048],
                                 iw2[l][2688:2730, :], s_w2[sl], wkeys=(("w2s", sl),))
                    for ji in range(2):
                        j = 2 * jg + ji
                        if j >= NJT:
                            continue
                        kj = 128 if j < 21 else 42
                        for mt in range(NCH):
                            mm(PS(mt, 128, 512),
                               w2s[sl][0:kj, 1024 * ji + 128 * mt:1024 * ji + 128 * (mt + 1)],
                               hbf[0:kj, 1024 * j + 512 * nt:1024 * j + 512 * nt + 512],
                               start=(j == 0), stop=(j == NJT - 1),
                               wbank=(mt,),
                               rkeys=(("w2s", sl), ("h", j)),
                               inc=s_pe if (ji == 1 and mt == NCH - 1) else None)
                for mt in range(NCH):
                    v_stt(x_sl(mt, t0 + 1, 512), PS(mt, 128, 512), cs_f32(C_B2 + mt, 1),
                          x_sl(mt, t0 + 1, 512), ALU.add, ALU.add,
                          rbanks=(mt,), rkeys=(("x", mt), ("lv",)), wkeys=(("x", mt),))

        emit_xn(0)
        emit_xn(1)
        emit_w1_half(0, first=True)
        emit_xn(2)
        emit_xn(3)
        emit_w2_half(0)
        emit_w1_half(1, first=False)
        emit_w2_half(1)
        barrier()

    def emit_final(b):
        xst = A.f32(G_XST, 8 * 512)
        sqb = A.bf16(G_OS, 8 * 512)
        mu = A.f32(G_MU, 512)[0:1, :]
        m2 = A.f32(G_M2, 512)[0:1, :]
        tmst = A.f32(G_BC, 1024)
        for ss in range(4):
            t0 = 512 * ss
            dma_sync(xst.rearrange("p (c t) -> p c t", c=8),
                     x1sp[:, t0:t0 + 512].rearrange("(c p) t -> p c t", p=128),
                     s_x1r, waits=((s_st, s_st.n),),
                     wkeys=tuple(("fx1", ch) for ch in range(NCH)))
            for ch in range(NCH):
                v_tt(xst[:, 512 * ch:512 * (ch + 1)], xst[:, 512 * ch:512 * (ch + 1)],
                     x_sl(ch, t0 + 1, 512), ALU.add,
                     rkeys=(("x", ch), ("fx1", ch)), wkeys=(("fx1", ch),))
            for ch in range(NCH):
                a_act(sqb[:, 512 * ch:512 * (ch + 1)], xst[:, 512 * ch:512 * (ch + 1)],
                      AF.Square, rkeys=(("fx1", ch),), wkeys=(("fsq", ch),))
            for ch in range(NCH):
                mm(PS(4, 1, 512), cs_f32(C_ONESCF, 1), xst[:, 512 * ch:512 * (ch + 1)],
                   start=(ch == 0), stop=(ch == NCH - 1),
                   wbank=(4,), rkeys=(("fx1", ch),),
                   inc=s_pe if ch == NCH - 1 else None)
            for ch in range(NCH):
                mm(PS(5, 1, 512), cs_bf(C_ONESCB, 1), sqb[:, 512 * ch:512 * (ch + 1)],
                   start=(ch == 0), stop=(ch == NCH - 1),
                   wbank=(5,), rkeys=(("fsq", ch),),
                   inc=s_pe if ch == NCH - 1 else None)
            a_act(mu, PS(4, 1, 512), AF.Copy, scale=1.0 / DM, rbanks=(4,), wkeys=(("gmu",),))
            a_act(m2, PS(4, 1, 512), AF.Square, scale=1.0 / DM, rbanks=(4,), wkeys=(("gm2",),))
            v_stt(m2, PS(5, 1, 512), 1.0 / DM, m2, ALU.mult, ALU.subtract,
                  rbanks=(5,), rkeys=(("gm2",),), wkeys=(("gm2",),))
            a_act(m2, m2, AF.Sqrt, bias=cs_f32(C_EPS, 1)[0:1, :], rkeys=(("gm2",),), wkeys=(("gm2",),))
            v_recip(m2, m2, rkeys=(("gm2",),), wkeys=(("gm2",),))
            v_stt(mu, mu, -1.0, m2, ALU.mult, ALU.mult,
                  rkeys=(("gmu",), ("gm2",)), wkeys=(("gmu",),))
            mm(PS(4), csb.ap()[0:1, C_ONESR:C_ONESR + 128], m2, start=True, stop=True,
               wbank=(4,), rkeys=(("gm2",),), inc=s_pe)
            mm(PS(5), csb.ap()[0:1, C_ONESR:C_ONESR + 128], mu, start=True, stop=True,
               wbank=(5,), rkeys=(("gmu",),), inc=s_pe)
            for ch in range(NCH):
                v_tt(xst[:, 512 * ch:512 * (ch + 1)], xst[:, 512 * ch:512 * (ch + 1)],
                     PS(4), ALU.mult, rbanks=(4,), rkeys=(("fx1", ch),), wkeys=(("fx1", ch),))
                v_tt(xst[:, 512 * ch:512 * (ch + 1)], xst[:, 512 * ch:512 * (ch + 1)],
                     PS(5), ALU.add, rbanks=(5,), rkeys=(("fx1", ch),), wkeys=(("fx1", ch),))
            for tt in range(4):
                bank = tt % 2
                for ch in range(NCH):
                    transpose(PS(bank, 128, 128, 128 * (ch % 4), r0=0)
                              if False else
                              ps_t.ap()[:, 512 * (2 * bank + ch // 4) + 128 * (ch % 4):
                                        512 * (2 * bank + ch // 4) + 128 * (ch % 4) + 128],
                              xst[:, 512 * ch + 128 * tt:512 * ch + 128 * (tt + 1)],
                              2 * bank + ch // 4, rkey=("fx1", ch),
                              start_clear=(ch % 4 == 0))
                v_copy(tmst, ps_t.ap()[:, 1024 * bank:1024 * bank + 1024],
                       rbanks=(2 * bank, 2 * bank + 1), wkeys=(("ftm",),))
                dma_store(out[b, t0 + 128 * tt:t0 + 128 * (tt + 1), :], tmst,
                          waits=TK.wait_read(("ftm",)) + TK.wait_write(("ftm",)),
                          note_rkeys=())
                TK.note_read(("ftm",), s_st, s_st.n)
        barrier()

    for b in range(n_batches):
        emit_global(b)
        if dump_stage == "x1":
            xst = A.f32(G_XST, 8 * 512)
            tmst = A.f32(G_BC, 1024)
            for ss in range(4):
                t0 = 512 * ss
                dma_sync(xst.rearrange("p (c t) -> p c t", c=8),
                         x1sp[:, t0:t0 + 512].rearrange("(c p) t -> p c t", p=128),
                         s_x1r, waits=((s_st, s_st.n),),
                         wkeys=tuple(("dx", ch) for ch in range(NCH)))
                for tt in range(4):
                    bank = tt % 2
                    for ch in range(NCH):
                        transpose(ps_t.ap()[:, 512 * (2 * bank + ch // 4) + 128 * (ch % 4):
                                            512 * (2 * bank + ch // 4) + 128 * (ch % 4) + 128],
                                  xst[:, 512 * ch + 128 * tt:512 * ch + 128 * (tt + 1)],
                                  2 * bank + ch // 4, rkey=("dx", ch),
                                  start_clear=(ch % 4 == 0), stop=(ch % 4 == 3))
                    v_copy(tmst, ps_t.ap()[:, 1024 * bank:1024 * bank + 1024],
                           rbanks=(2 * bank, 2 * bank + 1), wkeys=(("ftm",),))
                    dma_store(out[b, t0 + 128 * tt:t0 + 128 * (tt + 1), :], tmst,
                              waits=TK.wait_read(("ftm",)) + [TK.w[("ftm",)]])
                    TK.note_read(("ftm",), s_st, s_st.n)
            barrier()
            continue
        for ch in range(NCH):
            v_memset(x_sl(ch, 0, 1), 0.0, wkeys=(("x", ch),))
        for ch in range(NCH):
            dma_sync(x_sl(ch, 1, 2048), x1sp[128 * ch:128 * (ch + 1), :],
                     s_x1r, wkeys=(("x", ch),), waits=((s_st, s_st.n),))
        barrier()
        for l in range(n_layers):
            emit_inner_attn(b, l)
            if dump_stage == ("L%da" % (l + 1)):
                break
            emit_inner_ffn(b, l)
            if dump_stage == ("L%d" % (l + 1)):
                break
        if dump_stage and dump_stage != "full" and dump_stage.startswith("L"):
            tmst = A.f32(G_BC, 1024)
            for ss in range(4):
                t0 = 512 * ss
                for tt in range(4):
                    bank = tt % 2
                    for ch in range(NCH):
                        transpose(ps_t.ap()[:, 512 * (2 * bank + ch // 4) + 128 * (ch % 4):
                                            512 * (2 * bank + ch // 4) + 128 * (ch % 4) + 128],
                                  x_sl(ch, t0 + 128 * tt + 1, 128),
                                  2 * bank + ch // 4, rkey=("x", ch),
                                  start_clear=(ch % 4 == 0), stop=(ch % 4 == 3))
                    v_copy(tmst, ps_t.ap()[:, 1024 * bank:1024 * bank + 1024],
                           rbanks=(2 * bank, 2 * bank + 1), wkeys=(("ftm",),))
                    dma_store(out[b, t0 + 128 * tt:t0 + 128 * (tt + 1), :], tmst,
                              waits=TK.wait_read(("ftm",)) + [TK.w[("ftm",)]])
                    TK.note_read(("ftm",), s_st, s_st.n)
            barrier()
        elif dump_stage is None or dump_stage == "full":
            emit_final(b)

    P.op("sync", None, waits=((s_st, s_st.n),))
    P.emit()
    return nc


def _prep_inputs(inputs):
    c = host_constants()
    pg = _perm_global_qk()
    pi = _perm_inner_qk()
    pw1, tcols = _w1_tile_cols()
    wq = np.ascontiguousarray(inputs["L0_attn_wq"].reshape(DM, DM)[:, pg], dtype=np.float32)
    wk = np.ascontiguousarray(inputs["L0_attn_wk"].reshape(DM, DM)[:, pg], dtype=np.float32)
    wv = np.ascontiguousarray(inputs["L0_attn_wv"].reshape(DM, DM), dtype=np.float32)
    wo = np.ascontiguousarray(inputs["L0_attn_wo"].reshape(DM, DM), dtype=np.float32)
    Wqkv = np.asarray(inputs["L0_ffn_Wqkv"], dtype=np.float32)
    Wqkv_p = np.concatenate([Wqkv[:, :, :DM][:, :, pi], Wqkv[:, :, DM:2 * DM][:, :, pi],
                             Wqkv[:, :, 2 * DM:]], axis=2)
    W1p = np.ascontiguousarray(np.asarray(inputs["L0_ffn_W1"], dtype=np.float32)[:, :, pw1])
    qs = np.asarray(inputs["L0_ffn_q_scale"], dtype=np.float32)
    ks = np.asarray(inputs["L0_ffn_k_scale"], dtype=np.float32)
    qscol_a = np.zeros((DEPTH, 128, 4), dtype=np.float32)
    for l in range(DEPTH):
        qscol_a[l, :, 0] = np.tile(qs[l, 0:32], 4)
        qscol_a[l, :, 1] = np.tile(qs[l, 32:64], 4)
        qscol_a[l, :, 2] = np.tile(ks[l, 0:32], 4)
        qscol_a[l, :, 3] = np.tile(ks[l, 32:64], 4)
    b1p = np.asarray(inputs["L0_ffn_b1"], dtype=np.float32)[:, pw1]
    b1c = np.zeros((DEPTH, 128, 2 * NPAIR), dtype=np.float32)
    off = 0
    tlist = []
    for p in range(NPAIR):
        w = 128 if p < 21 else 42
        tlist.append((off, w)); off += w
        tlist.append((off, w)); off += w
    for l in range(DEPTH):
        for ti, (o, w) in enumerate(tlist):
            b1c[l, :w, ti] = b1p[l, o:o + w]
    lng_c = np.zeros((DEPTH, 128, NCH), dtype=np.float32)
    lnb_c = np.zeros((DEPTH, 128, NCH), dtype=np.float32)
    b2_c = np.zeros((DEPTH, 128, NCH), dtype=np.float32)
    for l in range(DEPTH):
        lng_c[l] = np.asarray(inputs["L0_ffn_ln_g"][l], dtype=np.float32).reshape(NCH, 128).T
        lnb_c[l] = np.asarray(inputs["L0_ffn_ln_b"][l], dtype=np.float32).reshape(NCH, 128).T
        b2_c[l] = np.asarray(inputs["L0_ffn_b2"][l], dtype=np.float32).reshape(NCH, 128).T
    return {
        "gwq": wq, "gwk": wk, "gwv": wv, "gwo": wo,
        "iwqkv": np.ascontiguousarray(Wqkv_p),
        "iwo": np.ascontiguousarray(np.asarray(inputs["L0_ffn_Wo"], dtype=np.float32)),
        "iw1": W1p,
        "iw2": np.ascontiguousarray(np.asarray(inputs["L0_ffn_W2"], dtype=np.float32)),
        "qscol": qscol_a, "lng_col": lng_c, "lnb_col": lnb_c, "b1_col": b1c, "b2_col": b2_c,
        "tbl_g": c["tbl_g"], "tbl_i": c["tbl_i"], "identity": c["identity"],
        "trimask4": c["trimask4"], "trimask2": c["trimask2"], "tri1one": c["tri1one"],
        "eden": c["eden"], "enorm": c["enorm"], "fnorm": c["fnorm"], "fpair": c["fpair"],
        "ones_col_bf": c["ones_col_bf"], "ones_col_f32": c["ones_col_f32"],
        "ones_row_f32": c["ones_row_f32"], "eps_col": c["eps_col"],
    }


def kernel(**inputs):
    x = np.ascontiguousarray(inputs["L0_attn_x"], dtype=np.float32)
    base = _prep_inputs(inputs)
    nc = build_program()
    in_maps = []
    for cidx in range(NCORE):
        m = dict(base)
        m["x_in"] = np.ascontiguousarray(x[BLOC * cidx:BLOC * (cidx + 1)])
        in_maps.append(m)
    res = run_bass_kernel_spmd(nc, in_maps, core_ids=list(range(NCORE)))
    return np.concatenate([r["out"] for r in res.results], axis=0)


# version 03
# speedup vs baseline: 1.0400x; 1.0400x over previous
import contextlib
import numpy as np
import ml_dtypes
import concourse.bass as bass
import concourse.mybir as mybir
from concourse.bass_utils import run_bass_kernel_spmd

F32 = mybir.dt.float32
BF16 = mybir.dt.bfloat16
AF = mybir.ActivationFunctionType
ALU = mybir.AluOpType
bfloat16 = ml_dtypes.bfloat16

T = 2048
NCORE = 8
BLOC = 2
NCH = 8
DM = 1024
GH = 8
IH = 16
WIN = 256
NWIN = T // WIN
DEPTH = 6
DI2 = 5460
DI = 2730
NPAIR = 22
NJT = 22
EPS = 1e-5

ENGS = ("tensor", "vector", "scalar", "gpsimd", "sync")
DEBUG_BUILD = False
DEDUP_WAITS = True
ATTN_PARTS = ("qkv", "attn", "wo")


def _rope_tables_global():
    half = 64
    af = np.power(np.float32(10000.0), -(np.arange(half, dtype=np.float32) / np.float32(half)))
    pos = np.arange(T, dtype=np.float32)[:, None]
    rad = (pos * af[None, :]).astype(np.float32)
    cg = np.tile(np.cos(rad).astype(np.float32).T, (2, 1))
    sg = np.tile(np.sin(rad).astype(np.float32).T, (2, 1))
    return cg.astype(np.float32), sg.astype(np.float32)


def _rope_tables_inner():
    half = np.arange(0, 64, 2, dtype=np.float32)
    inv_freq = (np.float32(1.0) / np.power(np.float32(10000.0), half / np.float32(64))).astype(np.float32)
    t = np.arange(T, dtype=np.float32)
    freqs = (t[:, None] * inv_freq[None, :]).astype(np.float32)
    base_scale = ((half + np.float32(0.4 * 64)) / np.float32(1.4 * 64)).astype(np.float32)
    power = ((t - T // 2) / np.float32(WIN // 2)).astype(np.float32)
    scale = np.power(base_scale[None, :], power[:, None]).astype(np.float32)
    cos = np.cos(freqs).astype(np.float32)
    sin = np.sin(freqs).astype(np.float32)
    inv = (np.float32(1.0) / scale).astype(np.float32)

    def pat(m):
        return np.tile(m.T, (4, 1)).astype(np.float32)
    return pat(cos * scale), pat(sin * scale), pat(cos * inv), pat(sin * inv)


def _perm_global_qk():
    perm = np.zeros(1024, dtype=np.int64)
    for h in range(GH):
        for d in range(128):
            newc = (d // 64) * 512 + (h // 2) * 128 + (h % 2) * 64 + (d % 64)
            perm[newc] = h * 128 + d
    return perm


def _perm_inner_qk():
    perm = np.zeros(1024, dtype=np.int64)
    for h in range(IH):
        for d in range(64):
            newc = (d // 32) * 512 + (h // 4) * 128 + (h % 4) * 32 + (d % 32)
            perm[newc] = h * 64 + d
    return perm


def _w1_tile_cols():
    idx = []
    tcols = []
    off = 0
    for p in range(NPAIR):
        w = 128 if p < 21 else 42
        idx.extend(range(128 * p, 128 * p + w))
        tcols.append((off, w)); off += w
        idx.extend(range(DI + 128 * p, DI + 128 * p + w))
        tcols.append((off, w)); off += w
    return np.array(idx, dtype=np.int64), tcols


def host_constants():
    c = {}
    cg, sg = _rope_tables_global()
    c["tbl_g"] = np.stack([cg, sg])
    csq, ssq, csk, ssk = _rope_tables_inner()
    c["tbl_i"] = np.stack([csq, ssq, csk, ssk])
    c["identity"] = np.eye(128, dtype=np.float32)
    p = np.arange(128)[:, None]
    f = np.arange(128)[None, :]
    tri1 = (p <= f).astype(bfloat16)
    tri2 = (f <= p).astype(bfloat16)
    c["trimask4"] = np.concatenate([tri2, tri2, tri1, tri1], axis=1).astype(bfloat16)
    c["trimask2"] = np.concatenate([tri1, tri1], axis=1).astype(bfloat16)
    c["tri1one"] = tri1
    eden = np.zeros((128, 4), dtype=bfloat16)
    eden[:, 0] = bfloat16(1.0)
    eden[:, 3] = bfloat16(1.0)
    c["eden"] = eden
    en = np.zeros((4, 128, 16), dtype=bfloat16)
    for g in range(4):
        for q in range(128):
            en[g, q, 4 * g + q // 32] = bfloat16(1.0)
    c["enorm"] = en
    fn = np.zeros((4, 16, 128), dtype=np.float32)
    for g in range(4):
        for j in range(4):
            fn[g, 4 * g + j, 32 * j:32 * j + 32] = 1.0
    c["fnorm"] = fn
    fp = np.zeros((2, 128), dtype=np.float32)
    fp[0, 0:64] = 1.0
    fp[1, 64:128] = 1.0
    c["fpair"] = fp
    c["ones_col_bf"] = np.ones((128, 1), dtype=bfloat16)
    c["eps_col"] = np.full((128, 1), EPS, dtype=np.float32)
    c["ones_col_f32"] = np.ones((128, 1), dtype=np.float32)
    c["ones_row_f32"] = np.ones((1, 128), dtype=np.float32)
    return c


class Sem:
    __slots__ = ("h", "n", "name", "last_eng", "is_dma", "engs")

    def __init__(self, h, name, is_dma=False):
        self.h = h; self.n = 0; self.name = name
        self.last_eng = None; self.is_dma = is_dma
        self.engs = set()


class Plan:
    def __init__(self, nc, stack):
        self.nc = nc
        self.q = {e: [] for e in ENGS}
        self.maxwait = {e: {} for e in ENGS}
        self.stack = stack
        self.phase = "init"

    def mark(self, name):
        self.phase = name

    def sem(self, name, is_dma=False):
        return Sem(self.stack.enter_context(self.nc.semaphore(name)), name, is_dma)

    def op(self, eng, fn, waits=(), inc=None, inc_n=1):
        real = []
        mw = self.maxwait[eng]
        for s, v in waits:
            if s is None or v is None or v <= 0:
                continue
            if DEDUP_WAITS:
                if (not s.is_dma) and s.engs == {eng} and v <= s.n:
                    continue
                if mw.get(id(s), 0) >= v:
                    continue
                mw[id(s)] = v
            real.append((s, v))
        val = None
        if inc is not None:
            inc.n += inc_n
            inc.last_eng = eng
            inc.engs.add(eng)
            val = inc.n
        self.q[eng].append((tuple(real), fn, inc, inc_n, self.phase))
        return val

    def dma(self, eng, fn, waits=(), inc=None):
        return self.op(eng, fn, waits, inc, inc_n=16)

    def barrier(self, bar, dma_sems=()):
        for e in ENGS:
            self.op(e, lambda E: E.nop(), inc=bar)
        tgt = bar.n
        wl = tuple((s, s.n) for s in dma_sems)
        for e in ENGS:
            self.op(e, None, waits=((bar, tgt),) + wl)

    def emit(self):
        phase_of = self.nc._phase_of_inst = {}

        def body(name):
            def run(E):
                for waits, fn, inc, inc_n, phase in self.q[name]:
                    for s, v in waits:
                        E.wait_ge(s.h, v)
                    if fn is None:
                        continue
                    n0 = self.nc.n_instructions()
                    inst = fn(E)
                    phase_of[n0] = (phase, name)
                    if inc is not None:
                        inst.then_inc(inc.h, inc_n)
            return run
        with self.nc.Block() as block:
            block.tensor(body("tensor"))
            block.vector(body("vector"))
            block.scalar(body("scalar"))
            block.gpsimd(body("gpsimd"))
            block.sync(body("sync"))


class Tracker:
    def __init__(self):
        self.w = {}
        self.r = {}
        self.rb = {}

    def wait_read_bank(self, key, eng):
        out = list(self.wait_read(key))
        last = self.rb.get(key)
        if last is not None and last[0] != eng:
            out.append((last[1], last[2]))
        return out

    def note_read_bank(self, key, eng, sem, val):
        if val is None:
            return
        self.rb[key] = (eng, sem, val)
        self.note_read(key, sem, val)

    def wait_read(self, *keys):
        return [self.w[k] for k in keys if k in self.w]

    def wait_write(self, *keys):
        out = []
        for k in keys:
            if k in self.w:
                out.append(self.w[k])
            for s, v in self.r.get(k, {}).values():
                out.append((s, v))
        return out

    def note_write(self, key, sem, val):
        if val is None:
            return
        self.w[key] = (sem, val)
        self.r.pop(key, None)
        self.rb.pop(key, None)

    def note_read(self, key, sem, val):
        if val is None:
            return
        d = self.r.setdefault(key, {})
        old = d.get(id(sem))
        if old is None or old[1] < val:
            d[id(sem)] = (sem, val)


ARENA_BYTES = 195 * 1024


class Arena:
    def __init__(self, t):
        self.t = t

    def f32(self, off, n):
        assert off % 4 == 0 and off + 4 * n <= ARENA_BYTES, (off, n)
        return self.t.ap()[:, off // 4: off // 4 + n]

    def bf16(self, off, n):
        assert off % 4 == 0 and off + 2 * n <= ARENA_BYTES, (off, n)
        m = (n + 1) // 2
        return self.t.ap()[:, off // 4: off // 4 + m].bitcast(BF16)[:, :n]


def build_program(n_layers=DEPTH, n_batches=BLOC, dump_stage=None):
    nc = bass.Bass("TRN2", target_bir_lowering=False, debug=DEBUG_BUILD,
                   detect_race_conditions=False)
    stack = contextlib.ExitStack()
    nc._dev_stack = stack

    x_in = nc.dram_tensor("x_in", [BLOC, T, DM], F32, kind="ExternalInput").ap()
    out = nc.dram_tensor("out", [BLOC, T, DM], F32, kind="ExternalOutput").ap()
    gwq = nc.dram_tensor("gwq", [DM, DM], F32, kind="ExternalInput").ap()
    gwk = nc.dram_tensor("gwk", [DM, DM], F32, kind="ExternalInput").ap()
    gwv = nc.dram_tensor("gwv", [DM, DM], F32, kind="ExternalInput").ap()
    gwo = nc.dram_tensor("gwo", [DM, DM], F32, kind="ExternalInput").ap()
    iwqkv = nc.dram_tensor("iwqkv", [DEPTH, DM, 3 * DM], F32, kind="ExternalInput").ap()
    iwo = nc.dram_tensor("iwo", [DEPTH, DM, DM], F32, kind="ExternalInput").ap()
    iw1 = nc.dram_tensor("iw1", [DEPTH, DM, DI2], F32, kind="ExternalInput").ap()
    iw2 = nc.dram_tensor("iw2", [DEPTH, DI, DM], F32, kind="ExternalInput").ap()
    qscol = nc.dram_tensor("qscol", [DEPTH, 128, 4], F32, kind="ExternalInput").ap()
    lng_col = nc.dram_tensor("lng_col", [DEPTH, 128, NCH], F32, kind="ExternalInput").ap()
    lnb_col = nc.dram_tensor("lnb_col", [DEPTH, 128, NCH], F32, kind="ExternalInput").ap()
    b1_col = nc.dram_tensor("b1_col", [DEPTH, 128, 2 * NPAIR], F32, kind="ExternalInput").ap()
    b2_col = nc.dram_tensor("b2_col", [DEPTH, 128, NCH], F32, kind="ExternalInput").ap()
    tbl_g = nc.dram_tensor("tbl_g", [2, 128, T], F32, kind="ExternalInput").ap()
    tbl_i = nc.dram_tensor("tbl_i", [4, 128, T], F32, kind="ExternalInput").ap()
    identity_d = nc.dram_tensor("identity", [128, 128], F32, kind="ExternalInput").ap()
    trimask4_d = nc.dram_tensor("trimask4", [128, 512], BF16, kind="ExternalInput").ap()
    trimask2_d = nc.dram_tensor("trimask2", [128, 256], BF16, kind="ExternalInput").ap()
    tri1_d = nc.dram_tensor("tri1one", [128, 128], BF16, kind="ExternalInput").ap()
    eden_d = nc.dram_tensor("eden", [128, 4], BF16, kind="ExternalInput").ap()
    enorm_d = nc.dram_tensor("enorm", [4, 128, 16], BF16, kind="ExternalInput").ap()
    fnorm_d = nc.dram_tensor("fnorm", [4, 16, 128], F32, kind="ExternalInput").ap()
    fpair_d = nc.dram_tensor("fpair", [2, 128], F32, kind="ExternalInput").ap()
    onescb_d = nc.dram_tensor("ones_col_bf", [128, 1], BF16, kind="ExternalInput").ap()
    onescf_d = nc.dram_tensor("ones_col_f32", [128, 1], F32, kind="ExternalInput").ap()
    onesr_d = nc.dram_tensor("ones_row_f32", [1, 128], F32, kind="ExternalInput").ap()
    eps_d = nc.dram_tensor("eps_col", [128, 1], F32, kind="ExternalInput").ap()
    x1sp = nc.dram_tensor("x1sp", [DM, T], F32).ap()

    arena_t = stack.enter_context(nc.sbuf_tensor("arena", [128, ARENA_BYTES // 4], F32))
    A = Arena(arena_t)
    csb = stack.enter_context(nc.sbuf_tensor("csb", [128, 3072], F32))

    C_ID = 0
    C_TRIM4 = 128
    C_TRIM2 = 384
    C_TRI1 = 512
    C_EDEN = 576
    C_ENORM = 704
    C_FNORM = 736
    C_FPAIR = 1248
    C_ONESCB = 2272
    C_ONESCF = 2273
    C_ONESR = 2274
    C_EPS = 2528
    C_EPS2 = 2529
    C_QS = 2402
    C_GCOL = 2408
    C_BLNF = 2416
    C_BLNB = 2424
    C_B1 = 2428
    C_B2 = 2472
    C_U1 = 2480
    C_END = 2530
    assert C_END <= 3072

    def cs_f32(off, n):
        return csb.ap()[:, off:off + n]

    def cs_bf(off, n):
        m = (n + 1) // 2
        return csb.ap()[:, off:off + m].bitcast(BF16)[:, :n]

    P = Plan(nc, stack)
    TK = Tracker()
    s_bar = P.sem("bar")
    s_pe = P.sem("pe")
    s_dve = P.sem("dve")
    s_act = P.sem("act")
    s_pool = P.sem("pool")
    s_ld = P.sem("ld", is_dma=True)
    s_x = P.sem("x", is_dma=True)
    s_tb0 = P.sem("tb0", is_dma=True)
    s_tb1 = P.sem("tb1", is_dma=True)
    s_wA = P.sem("wA", is_dma=True)
    s_wB = P.sem("wB", is_dma=True)
    s_w1 = [P.sem("w1a", is_dma=True), P.sem("w1b", is_dma=True)]
    s_w2 = [P.sem("w2a", is_dma=True), P.sem("w2b", is_dma=True)]
    s_vec = P.sem("vec", is_dma=True)
    s_x1r = P.sem("x1r", is_dma=True)
    s_st = P.sem("st", is_dma=True)
    ALL_DMA_SEMS = (s_ld, s_x, s_tb0, s_tb1, s_wA, s_wB, s_w1[0], s_w1[1],
                    s_w2[0], s_w2[1], s_vec, s_x1r, s_st)
    s_tbs = (s_tb0, s_tb1)

    ps_t = stack.enter_context(nc.psum_tensor("ps", [128, 4096], F32))

    def PS(bank, rows=128, cols=512, coff=0, r0=0):
        return ps_t.ap()[r0:r0 + rows, 512 * bank + coff: 512 * bank + coff + cols]

    pend_r = []
    pend_w = []

    def mm(psum_ap, lhs_ap, rhs_ap, start, stop, wbank=(), rkeys=(), waits=(), inc=None,
           tile_position=None):
        wl = list(waits)
        if start:
            for b in wbank:
                wl += TK.wait_write(("ps", b))
        for k in rkeys:
            wl += TK.wait_read(k)
        pend_r.extend(rkeys)
        pend_w.extend(("ps", b) for b in wbank)

        def f(E, a=psum_ap, b=lhs_ap, c=rhs_ap, st=start, sp=stop, tp=tile_position):
            return E.matmul(a, b, c, start=st, stop=sp, tile_position=tp)
        v = P.op("tensor", f, waits=wl, inc=inc)
        if inc is not None:
            for k in pend_w:
                TK.note_write(k, s_pe, v)
            for k in pend_r:
                TK.note_read(k, s_pe, v)
            pend_r.clear()
            pend_w.clear()
        return v

    def generic(eng, sem, fn, rkeys=(), wkeys=(), rbanks=(), wbanks=(), waits=(), inc=True):
        wl = list(waits)
        for k in rkeys:
            wl += TK.wait_read(k)
        for b in rbanks:
            wl += TK.wait_read_bank(("ps", b), eng)
        for k in wkeys:
            wl += TK.wait_write(k)
        for b in wbanks:
            wl += TK.wait_write(("ps", b))
        v = P.op(eng, fn, waits=wl, inc=sem if inc else None)
        if inc:
            for k in rkeys:
                TK.note_read(k, sem, v)
            for b in rbanks:
                TK.note_read_bank(("ps", b), eng, sem, v)
            for k in wkeys:
                TK.note_write(k, sem, v)
            for b in wbanks:
                TK.note_write(("ps", b), sem, v)
        return v

    def v_tt(o, a, b, op, **kw):
        return generic("vector", s_dve, lambda E: E.tensor_tensor(o, a, b, op), **kw)

    def v_stt(o, a, sc, b, op0, op1, **kw):
        return generic("vector", s_dve, lambda E: E.scalar_tensor_tensor(o, a, sc, b, op0, op1), **kw)

    def v_ts(o, a, s1, s2, op0, op1=None, **kw):
        if op1 is None:
            return generic("vector", s_dve, lambda E: E.tensor_scalar(o, a, s1, None, op0), **kw)
        return generic("vector", s_dve, lambda E: E.tensor_scalar(o, a, s1, s2, op0, op1), **kw)

    def v_copy(o, a, **kw):
        return generic("vector", s_dve, lambda E: E.tensor_copy(o, a), **kw)

    def v_recip(o, a, **kw):
        return generic("vector", s_dve, lambda E: E.reciprocal(o, a), **kw)

    def v_memset(o, val, **kw):
        return generic("vector", s_dve, lambda E: E.memset(o, val), **kw)

    def a_act(o, a, func, bias=None, scale=None, accum_out=None, **kw):
        def f(E):
            kk = {}
            if bias is not None:
                kk["bias"] = bias
            if scale is not None:
                kk["scale"] = scale
            if accum_out is not None:
                kk["accum_out"] = accum_out
            return E.activation(o, a, func, **kk)
        return generic("scalar", s_act, f, **kw)

    def p_tt(o, a, b, op, **kw):
        return generic("gpsimd", s_pool, lambda E: E.tensor_tensor(o, a, b, op), **kw)

    def _dma(eng, sem, o, a, **kw):
        wl = list(kw.pop("waits", ()))
        rkeys = kw.pop("rkeys", ())
        wkeys = kw.pop("wkeys", ())
        for k in rkeys:
            wl += TK.wait_read(k)
        for k in wkeys:
            wl += TK.wait_write(k)
        v = P.dma(eng, lambda E: E.dma_start(out=o, in_=a), waits=wl, inc=sem)
        for k in rkeys:
            TK.note_read(k, sem, v)
        for k in wkeys:
            TK.note_write(k, sem, v)
        return v

    def dma_sync(o, a, sem, **kw):
        return _dma("sync", sem, o, a, **kw)

    def dma_pool(o, a, sem, **kw):
        return _dma("gpsimd", sem, o, a, **kw)

    def dma_store(o, a, **kw):
        wl = list(kw.pop("waits", ()))
        for k in kw.pop("rkeys", ()):
            wl += TK.wait_read(k)
        v = P.dma("sync", lambda E: E.dma_start(out=o, in_=a), waits=wl, inc=s_st)
        for k in kw.get("note_rkeys", ()):
            TK.note_read(k, s_st, v)
        return v

    def transpose(psum_ap, in_ap, bank, rkey=None, waits=(), start_clear=True,
                  stop=False):
        wl = list(waits)
        if start_clear:
            wl += TK.wait_write(("ps", bank))
        if rkey is not None:
            wl += TK.wait_read(rkey)
        ident = cs_f32(C_ID, 128)

        def f(E, o=psum_ap, i=in_ap, idn=ident, sp=stop):
            return E.matmul(o, i, idn, start=True, stop=sp, is_transpose=True,
                            skip_group_check=True)
        v = P.op("tensor", f, waits=wl, inc=s_pe)
        pend_w.append(("ps", bank))
        if rkey is not None:
            pend_r.append(rkey)
        for k in pend_w:
            TK.note_write(k, s_pe, v)
        for k in pend_r:
            TK.note_read(k, s_pe, v)
        pend_r.clear()
        pend_w.clear()
        return v

    def barrier():
        P.barrier(s_bar, dma_sems=ALL_DMA_SEMS)
        TK.w.clear()
        TK.r.clear()
        TK.rb.clear()

    dma_sync(cs_f32(C_ID, 128), identity_d, s_ld)
    dma_sync(cs_bf(C_TRIM4, 512), trimask4_d, s_ld)
    dma_sync(cs_bf(C_TRIM2, 256), trimask2_d, s_ld)
    dma_sync(cs_bf(C_TRI1, 128), tri1_d, s_ld)
    dma_sync(cs_bf(C_EDEN, 4), eden_d, s_ld)
    dma_sync(cs_bf(C_ENORM, 64).rearrange("p (g n) -> p g n", g=4),
             enorm_d.rearrange("g p n -> p g n"), s_ld)
    dma_sync(csb.ap()[0:16, C_FNORM:C_FNORM + 512].rearrange("p (g n) -> p g n", g=4),
             fnorm_d.rearrange("g p n -> p g n"), s_ld)
    dma_sync(csb.ap()[0:2, C_FPAIR:C_FPAIR + 128], fpair_d, s_ld)
    dma_sync(cs_bf(C_ONESCB, 1), onescb_d, s_ld)
    dma_sync(cs_f32(C_ONESCF, 1), onescf_d, s_ld)
    dma_sync(csb.ap()[0:1, C_ONESR:C_ONESR + 128], onesr_d, s_ld)
    dma_sync(cs_f32(C_EPS, 1), eps_d, s_ld)
    barrier()
    v_ts(cs_f32(C_EPS2, 1), cs_f32(C_EPS, 1), 1e-19, None, ALU.mult,
         wkeys=(("eps2",),))
    barrier()

    G_XBF, G_KH, G_VH = 0, 32768, 65536
    G_W1, G_W2 = 98304, 114688
    G_TBL = 131072
    G_TT = 139264
    G_QS = 147456
    G_PT = 155648
    G_XST = 159744
    G_OS = 176128
    G_BC = 184320
    G_RC = 186368
    G_MU = 188416
    G_M2 = 190464
    G_STG = 192512
    assert G_STG + 2048 <= ARENA_BYTES

    def xg_bf(ch, t0, n):
        return A.bf16(G_XBF + 2 * 2048 * ch, 2048)[:, t0:t0 + n]

    def khat_g(r0, rows, ch, c0, n):
        return A.bf16(G_KH + 2 * 2048 * ch, 2048)[r0:r0 + rows, c0:c0 + n]

    def vhat_g(tt, c0, n):
        return A.bf16(G_VH + 2 * 1024 * tt, 1024)[:, c0:c0 + n]

    def rope_pass_global(bA, bB, outA, outB, tb_c, tb_s, t1b, t2b, keys):
        for cc, ub, us, sgn, o in ((0, bA, bB, -1.0, outA), (1, bB, bA, 1.0, outB)):
            v_tt(t1b, PS(ub), tb_c, ALU.mult,
                 rbanks=(ub,), rkeys=(keys["tbl"],), wkeys=(keys["t1"],))
            v_stt(t2b, PS(us), sgn, tb_s, ALU.mult, ALU.mult,
                  rbanks=(us,), rkeys=(keys["tbl"],), wkeys=(keys["t2"],))
            p_tt(o, t1b, t2b, ALU.add,
                 rkeys=(keys["t1"], keys["t2"]), wkeys=(keys["out"],))

    def emit_global(b):
        P.mark("g0")
        stg = A.f32(G_STG, 512)
        for ch in range(NCH):
            for tg in range(4):
                src = x_in[b, 512 * tg:512 * (tg + 1), 128 * ch:128 * (ch + 1)] \
                    .rearrange("(tt p) c -> p tt c", p=128)
                dma_sync(stg.rearrange("p (tt c) -> p tt c", tt=4), src, s_x, wkeys=(("gstg",),))
                bank = tg % 2
                for i in range(4):
                    transpose(PS(bank, 128, 128, 128 * i), stg[:, 128 * i:128 * (i + 1)],
                              bank, rkey=("gstg",), start_clear=(i == 0), stop=(i == 3))
                v_copy(xg_bf(ch, 512 * tg, 512), PS(bank), rbanks=(bank,),
                       wkeys=(("gxbf", ch, tg),))
        barrier()

        P.mark("g1")
        wk_bf = A.bf16(G_W1, 8 * 1024)
        wv_bf = A.bf16(G_W2, 8 * 1024)
        dma_pool(wk_bf.rearrange("p (c n) -> p c n", c=8), gwk.rearrange("(c p) n -> p c n", p=128), s_wA, wkeys=(("gw1",),))
        dma_pool(wv_bf.rearrange("p (c n) -> p c n", c=8), gwv.rearrange("(c p) n -> p c n", p=128), s_wB, wkeys=(("gw2",),))
        tbc = [A.f32(G_TBL + 4096 * i, 512) for i in range(2)]
        tbs = [A.f32(G_TBL + 2048 + 4096 * i, 512) for i in range(2)]
        t1b = [A.f32(G_TT + 4096 * i, 512) for i in range(2)]
        t2b = [A.f32(G_TT + 2048 + 4096 * i, 512) for i in range(2)]
        for ss in range(4):
            t0 = 512 * ss
            sl = ss % 2
            dma_sync(tbc[sl], tbl_g[0][:, t0:t0 + 512], s_tbs[sl], wkeys=(("gtb", sl),))
            dma_sync(tbs[sl], tbl_g[1][:, t0:t0 + 512], s_tbs[sl], wkeys=(("gtb", sl),))
            for g in range(4):
                for cc, bank in ((g, 4), (4 + g, 5)):
                    for kc in range(NCH):
                        mm(PS(bank), wk_bf[:, 1024 * kc + 128 * cc:1024 * kc + 128 * cc + 128],
                           xg_bf(kc, t0, 512), start=(kc == 0), stop=(kc == NCH - 1),
                           wbank=(bank,), rkeys=(("gw1",), ("gxbf", kc, ss)),
                           inc=s_pe if kc == NCH - 1 else None)
                rope_pass_global(4, 5,
                                 khat_g(0, 128, g, t0, 512), khat_g(0, 128, 4 + g, t0, 512),
                                 tbc[sl], tbs[sl], t1b[sl], t2b[sl],
                                 {"t1": ("gt1", sl), "t2": ("gt2", sl),
                                  "tbl": ("gtb", sl), "out": ("gkh",)})
            for tt in range(4 * ss, 4 * ss + 4):
                for nt in range(2):
                    bank = 6 + nt
                    for kc in range(NCH):
                        mm(PS(bank), xg_bf(kc, 128 * tt - t0 + t0, 128) if False else
                           xg_bf(kc, 128 * tt, 128),
                           wv_bf[:, 1024 * kc + 512 * nt:1024 * kc + 512 * nt + 512],
                           start=(kc == 0), stop=(kc == NCH - 1),
                           wbank=(bank,), rkeys=(("gw2",), ("gxbf", kc, ss)),
                           inc=s_pe if kc == NCH - 1 else None)
                    v_copy(vhat_g(tt, 512 * nt, 512), PS(bank), rbanks=(bank,),
                           wkeys=(("gvh",),))
        barrier()

        P.mark("g2")
        wq_bf = A.bf16(G_W1, 8 * 1024)
        wo_bf = A.bf16(G_W2, 8 * 1024)
        dma_pool(wq_bf.rearrange("p (c n) -> p c n", c=8), gwq.rearrange("(c p) n -> p c n", p=128), s_wA, wkeys=(("gw1",),))
        dma_pool(wo_bf.rearrange("p (c n) -> p c n", c=8), gwo.rearrange("(c p) n -> p c n", p=128), s_wB, wkeys=(("gw2",),))
        qsb = A.bf16(G_QS, 8 * 512)
        ptb = A.bf16(G_PT, 2048)
        xst = A.f32(G_XST, 8 * 512)
        sqb = A.bf16(G_OS, 8 * 512)
        obuf = A.bf16(G_OS, 8 * 512)
        bce = A.f32(G_BC, 512)
        rcp = A.f32(G_RC, 512)[0:1, :]
        mu = A.f32(G_MU, 512)[0:1, :]
        m2 = A.f32(G_M2, 512)[0:1, :]
        for ss in range(4):
            t0 = 512 * ss
            sl = ss % 2
            dma_sync(tbc[sl], tbl_g[0][:, t0:t0 + 512], s_tbs[sl], wkeys=(("gtb", sl),))
            dma_sync(tbs[sl], tbl_g[1][:, t0:t0 + 512], s_tbs[sl], wkeys=(("gtb", sl),))
            for g in range(4):
                for cc, bank in ((g, 4), (4 + g, 5)):
                    for kc in range(NCH):
                        mm(PS(bank), wq_bf[:, 1024 * kc + 128 * cc:1024 * kc + 128 * cc + 128],
                           xg_bf(kc, t0, 512), start=(kc == 0), stop=(kc == NCH - 1),
                           wbank=(bank,), rkeys=(("gw1",),),
                           inc=s_pe if kc == NCH - 1 else None)
                rope_pass_global(4, 5,
                                 A.bf16(G_QS + 2 * 512 * g, 512),
                                 A.bf16(G_QS + 2 * 512 * (4 + g), 512),
                                 tbc[sl], tbs[sl], t1b[sl], t2b[sl],
                                 {"t1": ("gt1", sl), "t2": ("gt2", sl),
                                  "tbl": ("gtb", sl), "out": ("gqs",)})
            for ch in range(NCH):
                src = x_in[b, t0:t0 + 512, 128 * ch:128 * (ch + 1)] \
                    .rearrange("(tt p) c -> p tt c", p=128)
                dma_sync(stg.rearrange("p (tt c) -> p tt c", tt=4), src, s_x, wkeys=(("gstg",),))
                for i in range(4):
                    transpose(PS(6, 128, 128, 128 * i), stg[:, 128 * i:128 * (i + 1)],
                              6, rkey=("gstg",), start_clear=(i == 0), stop=(i == 3))
                v_copy(xst[:, 512 * ch:512 * (ch + 1)], PS(6), rbanks=(6,),
                       wkeys=(("gxst", ch),))
            nkt = 4 * (ss + 1)
            for h in range(GH):
                r0 = 64 * (h % 2)
                cA, cB = h // 2, 4 + h // 2
                pv_started = False
                dn_started = False
                for ktg in range(0, nkt, 4):
                    gsz = min(4, nkt - ktg)
                    for i in range(gsz):
                        kt = ktg + i
                        qlo = max(0, 128 * kt - t0)
                        nq = 512 - qlo
                        for half, cc in ((0, cA), (1, cB)):
                            mm(PS(i, 128, nq, qlo),
                               khat_g(r0, 64, cc, 128 * kt, 128),
                               A.bf16(G_QS + 2 * 512 * cc, 512)[r0:r0 + 64, qlo:qlo + nq],
                               start=(half == 0), stop=(half == 1),
                               wbank=(i,), rkeys=(("gkh",), ("gqs",)),
                               tile_position=(r0, 0),
                               inc=s_pe if half == 1 else None)
                    if 128 * (ktg + gsz) <= t0:
                        a_act(ptb[:, 0:512 * gsz], ps_t.ap()[:, 0:512 * gsz], AF.Exp,
                              scale=1.0 / 128.0,
                              rbanks=tuple(range(gsz)), wkeys=(("gpt",),))
                    else:
                        for i in range(gsz):
                            kt = ktg + i
                            qlo = max(0, 128 * kt - t0)
                            a_act(ptb[:, 512 * i + qlo:512 * (i + 1)],
                                  PS(i, 128, 512 - qlo, qlo), AF.Exp, scale=1.0 / 128.0,
                                  rbanks=(i,), wkeys=(("gpt",),))
                            if 128 * kt >= t0:
                                v_tt(ptb[:, 512 * i + qlo:512 * i + qlo + 128],
                                     ptb[:, 512 * i + qlo:512 * i + qlo + 128],
                                     cs_bf(C_TRI1, 128), ALU.mult,
                                     rkeys=(("gpt",),), wkeys=(("gpt",),))
                    for i in range(gsz):
                        kt = ktg + i
                        qlo = max(0, 128 * kt - t0)
                        nq = 512 - qlo
                        last = (kt == nkt - 1)
                        mm(PS(6, 1, nq, qlo), cs_bf(C_ONESCB, 1),
                           ptb[:, 512 * i + qlo:512 * (i + 1)],
                           start=not dn_started, stop=last,
                           wbank=(6,), rkeys=(("gpt",),),
                           inc=s_pe if last else None)
                        dn_started = True
                        mm(PS(7, 128, nq, qlo), vhat_g(kt, 128 * h, 128),
                           ptb[:, 512 * i + qlo:512 * (i + 1)],
                           start=not pv_started, stop=last,
                           wbank=(7,),
                           rkeys=(("gvh",), ("gpt",)),
                           inc=s_pe if last else None)
                        pv_started = True
                a_act(rcp, PS(6, 1, 512), AF.Ln, rbanks=(6,), wkeys=(("grcp",),))
                a_act(rcp, rcp, AF.Exp, scale=-1.0, rkeys=(("grcp",),), wkeys=(("grcp",),))
                mm(PS(5), csb.ap()[0:1, C_ONESR:C_ONESR + 128], rcp, start=True, stop=True,
                   wbank=(5,), rkeys=(("grcp",),), inc=s_pe)
                v_copy(bce, PS(5), rbanks=(5,), wkeys=(("gbce",),))
                v_tt(obuf[:, 512 * h:512 * (h + 1)], PS(7), bce, ALU.mult,
                     rbanks=(7,), rkeys=(("gbce",),), wkeys=(("gos", h),))
            for mt in range(NCH):
                for hc in range(NCH):
                    mm(PS(mt), wo_bf[:, 1024 * hc + 128 * mt:1024 * hc + 128 * mt + 128],
                       obuf[:, 512 * hc:512 * (hc + 1)],
                       start=(hc == 0), stop=(hc == NCH - 1),
                       wbank=(mt,), rkeys=(("gw2",), ("gos", hc)),
                       inc=s_pe if hc == NCH - 1 else None)
                v_tt(xst[:, 512 * mt:512 * (mt + 1)], PS(mt), xst[:, 512 * mt:512 * (mt + 1)],
                     ALU.add, rbanks=(mt,), rkeys=(("gxst", mt),), wkeys=(("gxst", mt),))
            if getattr(build_program, "_dbg", False) and ss in (0,):
                def _dbgx(E, t=xst, tag=(b, ss)):
                    import numpy as _np
                    E.dbg_aps(lambda a, g, tg=tag: print("XSTDBG", tg, _np.asarray(a), _np.asarray(g)),
                              t[0:1, 0:4], t[5:6, 512 * 3:512 * 3 + 4])
                    return E.nop()
                P.op("vector", _dbgx,
                     waits=sum((TK.wait_read(("gxst", c)) + TK.wait_write(("gxst", c))
                                for c in range(NCH)), []))
            for ch in range(NCH):
                a_act(sqb[:, 512 * ch:512 * (ch + 1)], xst[:, 512 * ch:512 * (ch + 1)],
                      AF.Square, rkeys=(("gxst", ch),), wkeys=(("gos", ch),))
            for ch in range(NCH):
                mm(PS(4, 1, 512), cs_f32(C_ONESCF, 1), xst[:, 512 * ch:512 * (ch + 1)],
                   start=(ch == 0), stop=(ch == NCH - 1),
                   wbank=(4,), rkeys=(("gxst", ch),),
                   inc=s_pe if ch == NCH - 1 else None)
            for ch in range(NCH):
                mm(PS(5, 1, 512), cs_bf(C_ONESCB, 1), sqb[:, 512 * ch:512 * (ch + 1)],
                   start=(ch == 0), stop=(ch == NCH - 1),
                   wbank=(5,), rkeys=(("gos", ch),),
                   inc=s_pe if ch == NCH - 1 else None)
            a_act(mu, PS(4, 1, 512), AF.Copy, scale=1.0 / DM, rbanks=(4,), wkeys=(("gmu",),))
            a_act(m2, PS(4, 1, 512), AF.Square, scale=1.0 / DM, rbanks=(4,), wkeys=(("gm2",),))
            v_stt(m2, PS(5, 1, 512), 1.0 / DM, m2, ALU.mult, ALU.subtract,
                  rbanks=(5,), rkeys=(("gm2",),), wkeys=(("gm2",),))
            a_act(m2, m2, AF.Ln, bias=cs_f32(C_EPS, 1)[0:1, :], rkeys=(("gm2",),), wkeys=(("gm2",),))
            a_act(m2, m2, AF.Exp, scale=-0.5, rkeys=(("gm2",),), wkeys=(("gm2",),))
            v_stt(mu, mu, -1.0, m2, ALU.mult, ALU.mult,
                  rkeys=(("gmu",), ("gm2",)), wkeys=(("gmu",),))
            mm(PS(4), csb.ap()[0:1, C_ONESR:C_ONESR + 128], m2, start=True, stop=True,
               wbank=(4,), rkeys=(("gm2",),), inc=s_pe)
            mm(PS(5), csb.ap()[0:1, C_ONESR:C_ONESR + 128], mu, start=True, stop=True,
               wbank=(5,), rkeys=(("gmu",),), inc=s_pe)
            for ch in range(NCH):
                v_tt(xst[:, 512 * ch:512 * (ch + 1)], xst[:, 512 * ch:512 * (ch + 1)],
                     PS(4), ALU.mult, rbanks=(4,), rkeys=(("gxst", ch),), wkeys=(("gxst", ch),))
                v_tt(xst[:, 512 * ch:512 * (ch + 1)], xst[:, 512 * ch:512 * (ch + 1)],
                     PS(5), ALU.add, rbanks=(5,), rkeys=(("gxst", ch),), wkeys=(("gxst", ch),))
            dma_store(x1sp[:, t0:t0 + 512].rearrange("(c p) t -> p c t", p=128),
                      xst.rearrange("p (c t) -> p c t", c=8),
                      rkeys=tuple(("gxst", ch) for ch in range(NCH)),
                      note_rkeys=tuple(("gxst", ch) for ch in range(NCH)))
        barrier()

    I_X = 0
    I_WQKV = 65568
    I_WO = 114720
    I_XBFW = 131104
    I_QPRE = 139328
    I_QBF = 147520
    I_KBF = 151616
    I_VBF = 159808
    I_USQ = 168000
    I_PT = 169024
    I_TBL = 175168
    I_TT = 183360
    I_OBUF = 185408
    I_BCE = 189504
    I_NRM = 190528
    I_END = 192576
    assert I_END <= ARENA_BYTES
    F_H = I_WQKV
    F_XSQ = I_WO
    F_W1 = I_WO + 8224
    F_XN = 139328
    F_W2 = 155712
    F_GEL = 163904
    F_STA = 168000
    F_END = 174144
    assert F_END <= ARENA_BYTES

    def x_sl(ch, c0, n):
        return A.f32(I_X + 4 * 2049 * ch, 2049)[:, c0:c0 + n]

    def inner_qk(ten, w, slot, outfn, sA, sB, nsB, wqkv, xbfw, tb, t1b, t2b, qpre, usq,
                 nrm, recd, layer_key):
        base = 1024 * ten
        t0 = WIN * w
        _full = "qkv" in ATTN_PARTS or "q" in ATTN_PARTS or "k" in ATTN_PARTS
        _p_mm = _full or "qk_mm" in ATTN_PARTS
        _p_sq = _p_mm and (_full or "qk_sq" in ATTN_PARTS)
        _p_rot = _p_mm and (_full or "qk_rot" in ATTN_PARTS)
        _p_bc = _p_sq and (_full or "qk_bc" in ATTN_PARTS)
        for g in range(4):
            if _p_mm:
                for cc, bank in ((g, 4), (4 + g, 5)):
                    for kc in range(NCH):
                        o = 257 * kc + (1 if kc < 4 else 0)
                        mm(PS(bank, 128, 256),
                           wqkv[:, 3072 * kc + base + 128 * cc:3072 * kc + base + 128 * cc + 128],
                           xbfw[slot][:, o:o + 256],
                           start=(kc == 0), stop=(kc == NCH - 1),
                           wbank=(bank,), rkeys=(layer_key, ("xbfw", slot)),
                           inc=s_pe if kc == NCH - 1 else None)
            if _p_sq:
                a_act(usq[:, 0:256], PS(4, 128, 256), AF.Square, rbanks=(4,), wkeys=(("usq", 0),))
                a_act(usq[:, 256:512], PS(5, 128, 256), AF.Square, rbanks=(5,), wkeys=(("usq", 1),))
                for j in range(2):
                    mm(PS(6, 16, 256), cs_bf(C_ENORM, 64)[:, 16 * g:16 * (g + 1)],
                       usq[:, 256 * j:256 * (j + 1)],
                       start=(g == 0 and j == 0), stop=(g == 3 and j == 1),
                       wbank=(6,), rkeys=(("usq", j),),
                       inc=s_pe if (g == 3 and j == 1) else None)
            if _p_rot:
                for cc, ub, us, s1, s2 in ((g, 4, 5, sA, nsB), (4 + g, 5, 4, sB, sA)):
                    v_stt(t1b, PS(ub, 128, 256), s1, tb[2 * ten], ALU.mult, ALU.mult,
                          rbanks=(ub,), rkeys=(("itb", slot),), wkeys=(("it1",),))
                    v_stt(t2b, PS(us, 128, 256), s2, tb[2 * ten + 1], ALU.mult, ALU.mult,
                          rbanks=(us,), rkeys=(("itb", slot),), wkeys=(("it2",),))
                    p_tt(qpre[:, 256 * cc:256 * (cc + 1)], t1b, t2b, ALU.add,
                         rkeys=(("it1",), ("it2",)), wkeys=(("qpre", cc),))
        if _p_sq:
            a_act(nrm, PS(6, 16, 256), AF.Ln, bias=cs_f32(C_EPS2, 1)[0:16, :],
                  rbanks=(6,), wkeys=(("nrm",),))
            a_act(recd, nrm, AF.Exp, scale=-0.5, rkeys=(("nrm",),), wkeys=(("recd",),))
        if _p_bc:
            for g in range(4):
                mm(PS(7, 128, 256), csb.ap()[0:16, C_FNORM + 128 * g:C_FNORM + 128 * (g + 1)],
                   recd, start=True, stop=True, wbank=(7,), rkeys=(("recd",),), inc=s_pe)
                for cc in (g, 4 + g):
                    v_tt(outfn(cc), qpre[:, 256 * cc:256 * (cc + 1)], PS(7, 128, 256), ALU.mult,
                         rbanks=(7,), rkeys=(("qpre", cc),), wkeys=(("qk_out", ten),))

    def emit_inner_attn(b, l):
        P.mark("L%da" % l)
        wqkv = A.bf16(I_WQKV, 8 * 3072)
        wo_bf = A.bf16(I_WO, 8 * 1024)
        lk = ("iw", l)
        dma_pool(wqkv.rearrange("p (c n) -> p c n", c=8),
                 iwqkv[l].rearrange("(c p) n -> p c n", p=128), s_wA, wkeys=(lk,))
        dma_pool(wo_bf.rearrange("p (c n) -> p c n", c=8),
                 iwo[l].rearrange("(c p) n -> p c n", p=128), s_wB, wkeys=(lk,))
        dma_sync(cs_f32(C_QS, 4), qscol[l], s_vec, wkeys=(("qs",),))
        v_ts(cs_f32(C_QS + 4, 1), cs_f32(C_QS + 1, 1), -1.0, None, ALU.mult,
             rkeys=(("qs",),), wkeys=(("qs",),))
        v_ts(cs_f32(C_QS + 5, 1), cs_f32(C_QS + 3, 1), -1.0, None, ALU.mult,
             rkeys=(("qs",),), wkeys=(("qs",),))
        qsA, qsB = cs_f32(C_QS, 1), cs_f32(C_QS + 1, 1)
        ksA, ksB = cs_f32(C_QS + 2, 1), cs_f32(C_QS + 3, 1)
        nqsB, nksB = cs_f32(C_QS + 4, 1), cs_f32(C_QS + 5, 1)

        xbfw = [A.bf16(I_XBFW + 4112 * i, 8 * 257) for i in range(2)]
        qpre = A.f32(I_QPRE, 8 * 256)
        qbf = A.bf16(I_QBF, 8 * 256)
        kbf = A.bf16(I_KBF, 8 * 512)
        vbf = A.bf16(I_VBF, 4 * 1024)
        usq = A.bf16(I_USQ, 512)
        ptb = [A.bf16(I_PT + 1536 * i, 768) for i in range(4)]
        tb = [[A.f32(I_TBL + 1024 * j + 4096 * i, 256) for j in range(4)] for i in range(2)]
        t1b = A.f32(I_TT, 256)
        t2b = A.f32(I_TT + 1024, 256)
        ob = A.bf16(I_OBUF, 8 * 256)
        bce = A.f32(I_BCE, 256)
        nrm = A.f32(I_NRM, 256)[0:16, :]
        recd = A.f32(I_NRM + 1024, 256)[0:16, :]

        def cast_win(w):
            slot = w % 2
            t0 = WIN * w
            xsrc = A.f32(I_X, 8 * 2049).rearrange("p (c t) -> p c t", c=8)[:, :, t0:t0 + 257]
            xdst = xbfw[slot].rearrange("p (c t) -> p c t", c=8)
            v_copy(xdst, xsrc,
                   rkeys=tuple(("x", ch) for ch in range(NCH)), wkeys=(("xbfw", slot),))

        _cast_on = "qkv" in ATTN_PARTS or "cast" in ATTN_PARTS
        _tbl_on = "qkv" in ATTN_PARTS or "tbl" in ATTN_PARTS
        if _cast_on:
            cast_win(0)
        for w in range(NWIN):
            t0 = WIN * w
            slot = w % 2
            pslot = (w - 1) % 2
            for j in (range(4) if _tbl_on else ()):
                dma_sync(tb[slot][j], tbl_i[j][:, t0:t0 + 256], s_tbs[slot], wkeys=(("itb", slot),))
            if ("qkv" in ATTN_PARTS or "q" in ATTN_PARTS
                    or any(p.startswith("qk_") for p in ATTN_PARTS)):
                inner_qk(0, w, slot, lambda cc: A.bf16(I_QBF + 2 * 256 * cc, 256),
                         qsA, qsB, nqsB, wqkv, xbfw, tb[slot], t1b, t2b, qpre, usq, nrm, recd, lk)
            if "qkv" in ATTN_PARTS or "k" in ATTN_PARTS:
                inner_qk(1, w, slot, lambda cc: A.bf16(I_KBF + 2 * (512 * cc + 256 * slot), 256),
                         ksA, ksB, nksB, wqkv, xbfw, tb[slot], t1b, t2b, qpre, usq, nrm, recd, lk)
            if "qkv" in ATTN_PARTS or "v" in ATTN_PARTS:
                for tt in range(2):
                    for nt in range(2):
                        bank = 4 + nt
                        for kc in range(NCH):
                            o = 257 * kc + (1 if kc < 4 else 0) + 128 * tt
                            mm(PS(bank), xbfw[slot][:, o:o + 128],
                               wqkv[:, 3072 * kc + 2048 + 512 * nt:3072 * kc + 2048 + 512 * nt + 512],
                               start=(kc == 0), stop=(kc == NCH - 1),
                               wbank=(bank,), rkeys=(lk, ("xbfw", slot)),
                               inc=s_pe if kc == NCH - 1 else None)
                        v_copy(A.bf16(I_VBF + 2 * (1024 * (2 * slot + tt) + 512 * nt), 512),
                               PS(bank), rbanks=(bank,), wkeys=(("vbf", slot),))
            if w == 0:
                slots = [(0, (slot, 0), 1), (1, (slot, 1), 1), (1, (slot, 0), 0)]
                maskc = cs_bf(C_TRIM2, 256)
                nmask = 2
            else:
                slots = [(0, (pslot, 0), 2), (1, (pslot, 1), 2),
                         (0, (slot, 0), 1), (1, (slot, 1), 1),
                         (0, (pslot, 1), 0), (1, (slot, 0), 0)]
                maskc = cs_bf(C_TRIM4, 512)
                nmask = 4
            nslot = len(slots)
            for h in (range(IH) if "attn" in ATTN_PARTS else ()):
                r0 = 32 * (h % 4)
                g = h // 4
                cA, cB = g, 4 + g
                pts = ptb[h % 4]
                sb0 = 2 * (h % 2)
                pvb = 4 + ((h // 2) % 2)
                pp = h // 2
                for si, (a, (ks, kt), mk) in enumerate(slots):
                    for half, cc in ((0, cA), (1, cB)):
                        kcol = 512 * cc + 256 * ks + 128 * kt
                        mm(ps_t.ap()[:, 512 * sb0 + 128 * si:512 * sb0 + 128 * (si + 1)],
                           kbf[r0:r0 + 32, kcol:kcol + 128],
                           qbf[r0:r0 + 32, 256 * cA + 128 * a:256 * cA + 128 * a + 128]
                           if False else
                           A.bf16(I_QBF + 2 * 256 * cc, 256)[r0:r0 + 32, 128 * a:128 * (a + 1)],
                           start=(half == 0), stop=(half == 1),
                           wbank=(sb0, sb0 + 1),
                           rkeys=(("qk_out", 0), ("qk_out", 1)),
                           tile_position=(r0, 0),
                           inc=s_pe if half == 1 else None)
                a_act(pts[:, 0:128 * nslot], ps_t.ap()[:, 512 * sb0:512 * sb0 + 128 * nslot],
                      AF.Exp, scale=8.0,
                      rbanks=(sb0, sb0 + 1), wkeys=(("pt", h % 4),))
                v_tt(pts[:, 0:128 * nmask], pts[:, 0:128 * nmask], maskc, ALU.mult,
                     rkeys=(("pt", h % 4),), wkeys=(("pt", h % 4),))
                for si, (a, (ks, kt), mk) in enumerate(slots):
                    mm(PS(6, 2, 128, 128 * a), cs_bf(C_EDEN, 4)[:, 2 * (h % 2):2 * (h % 2) + 2],
                       pts[:, 128 * si:128 * (si + 1)],
                       start=(si == 0 and h % 2 == 0), stop=(si == nslot - 1 and h % 2 == 1),
                       wbank=(6,),
                       rkeys=(("pt", h % 4),),
                       inc=s_pe if (si == nslot - 1 and h % 2 == 1) else None)
                for si, (a, (ks, kt), mk) in enumerate(slots):
                    vcol = 1024 * (2 * ks + kt) + 64 * h
                    mm(PS(pvb, 64, 128, 128 * a, r0=64 * (h % 2)),
                       A.bf16(I_VBF + 2 * vcol, 64),
                       pts[:, 128 * si:128 * (si + 1)],
                       start=(si == 0), stop=(si == nslot - 1),
                       wbank=(pvb,),
                       rkeys=(("vbf", ks), ("pt", h % 4)),
                       tile_position=(0, 64 * (h % 2)),
                       inc=s_pe if si == nslot - 1 else None)
                if h % 2 == 1:
                    a_act(recd[0:2, :], PS(6, 2, 256), AF.Ln, rbanks=(6,), wkeys=(("recd",),))
                    a_act(recd[0:2, :], recd[0:2, :], AF.Exp, scale=-1.0,
                          rkeys=(("recd",),), wkeys=(("recd",),))
                    mm(PS(7, 128, 256), csb.ap()[0:2, C_FPAIR:C_FPAIR + 128],
                       recd[0:2, :], start=True, stop=True, wbank=(7,), rkeys=(("recd",),),
                       inc=s_pe)
                    v_copy(bce, PS(7, 128, 256), rbanks=(7,), wkeys=(("bce",),))
                    v_tt(ob[:, 256 * pp:256 * (pp + 1)], PS(pvb, 128, 256), bce, ALU.mult,
                         rbanks=(pvb,), rkeys=(("bce",),), wkeys=(("ob", pp),))
            if w + 1 < NWIN and _cast_on:
                cast_win(w + 1)
            for mt in (range(NCH) if "wo" in ATTN_PARTS else ()):
                for hc in range(NCH):
                    mm(PS(mt, 128, 256), wo_bf[:, 1024 * hc + 128 * mt:1024 * hc + 128 * mt + 128],
                       ob[:, 256 * hc:256 * (hc + 1)],
                       start=(hc == 0), stop=(hc == NCH - 1),
                       wbank=(mt,),
                       rkeys=(lk, ("ob", hc)),
                       inc=s_pe if hc == NCH - 1 else None)
                v_tt(x_sl(mt, t0 + 1, 256), PS(mt, 128, 256), x_sl(mt, t0 + 1, 256), ALU.add,
                     rbanks=(mt,), rkeys=(("x", mt),), wkeys=(("x", mt),))
        barrier()

    def emit_inner_ffn(b, l):
        P.mark("L%df" % l)
        hbf = A.bf16(F_H, NJT * 1024)
        xsq = A.bf16(F_XSQ, 8 * 513)
        w1s = [A.bf16(F_W1 + 8192 * i, 8 * 512) for i in range(2)]
        xn = [A.bf16(F_XN + 8192 * i, 8 * 512) for i in range(2)]
        w2s = [A.bf16(F_W2 + 4096 * i, 2 * 1024) for i in range(2)]
        gel = A.f32(F_GEL, 1024)
        stA = A.f32(F_STA, 512)[0:1, :]
        stB = A.f32(F_STA + 2048, 512)[0:1, :]
        stM = A.f32(F_STA + 4096, 512)[0:1, :]
        dma_sync(cs_f32(C_GCOL, 8), lng_col[l], s_vec, wkeys=(("lv",),))
        dma_sync(cs_f32(C_BLNF, 8), lnb_col[l], s_vec, wkeys=(("lv",),))
        dma_sync(cs_f32(C_B1, 44), b1_col[l], s_vec, wkeys=(("lv",),))
        dma_sync(cs_f32(C_B2, 8), b2_col[l], s_vec, wkeys=(("lv",),))
        v_copy(cs_bf(C_BLNB, 8), cs_f32(C_BLNF, 8), rkeys=(("lv",),), wkeys=(("lv",),))

        def w1cols(p):
            return (256 * p, 128 if p < 21 else 42) if p < 21 else (5376, 42)

        def emit_xn(qq):
            t0 = 512 * qq
            sl = qq % 2
            for ch in range(NCH):
                a_act(xsq[:, 513 * ch:513 * ch + 513], x_sl(ch, t0, 513), AF.Square,
                      rkeys=(("x", ch),), wkeys=(("xsq", ch),))
            for ch in range(NCH):
                o = 513 * ch + (1 if ch < 4 else 0)
                mm(PS(6, 1, 512), cs_f32(C_ONESCF, 1),
                   x_sl(ch, t0 + (1 if ch < 4 else 0), 512),
                   start=(ch == 0), stop=(ch == NCH - 1),
                   wbank=(6,), rkeys=(("x", ch),),
                   inc=s_pe if ch == NCH - 1 else None)
            for ch in range(NCH):
                o = 513 * ch + (1 if ch < 4 else 0)
                mm(PS(7, 1, 512), cs_bf(C_ONESCB, 1), xsq[:, o:o + 512],
                   start=(ch == 0), stop=(ch == NCH - 1),
                   wbank=(7,), rkeys=(("xsq", ch),),
                   inc=s_pe if ch == NCH - 1 else None)
            a_act(stM, PS(6, 1, 512), AF.Copy, scale=1.0 / DM, rbanks=(6,), wkeys=(("stM",),))
            a_act(stA, PS(6, 1, 512), AF.Square, scale=1.0 / DM, rbanks=(6,), wkeys=(("stA",),))
            v_stt(stA, PS(7, 1, 512), 1.0 / DM, stA, ALU.mult, ALU.subtract,
                  rbanks=(7,), rkeys=(("stA",),), wkeys=(("stA",),))
            a_act(stA, stA, AF.Ln, bias=cs_f32(C_EPS, 1)[0:1, :], rkeys=(("stA",),), wkeys=(("stA",),))
            a_act(stA, stA, AF.Exp, scale=-0.5, rkeys=(("stA",),), wkeys=(("stA",),))
            v_stt(stB, stM, -1.0, stA, ALU.mult, ALU.mult,
                  rkeys=(("stM",), ("stA",)), wkeys=(("stB",),))
            mm(PS(6), csb.ap()[0:1, C_ONESR:C_ONESR + 128], stA, start=True, stop=True,
               wbank=(6,), rkeys=(("stA",),), inc=s_pe)
            mm(PS(7), csb.ap()[0:1, C_ONESR:C_ONESR + 128], stB, start=True, stop=True,
               wbank=(7,), rkeys=(("stB",),), inc=s_pe)
            for ch in range(NCH):
                sh = 1 if ch < 4 else 0
                v_stt(gel[:, 0:512], x_sl(ch, t0 + sh, 512), cs_f32(C_GCOL + ch, 1),
                      PS(6), ALU.mult, ALU.mult,
                      rbanks=(6,), rkeys=(("x", ch), ("lv",)), wkeys=(("gel",),))
                v_stt(xn[sl][:, 512 * ch:512 * (ch + 1)], PS(7), cs_f32(C_GCOL + ch, 1),
                      gel[:, 0:512], ALU.mult, ALU.add,
                      rbanks=(7,), rkeys=(("gel",), ("lv",)), wkeys=(("xn", sl),))

        def emit_w1_half(hh, first):
            for jg in range(11):
                sl = jg % 2
                c0 = 512 * jg
                cw = 512 if jg < 10 else 340
                dma_pool(w1s[sl][:, 0:0 + 8 * 512] if False else w1s[sl],
                         None, wkeys=()) if False else None
                src = iw1[l][:, c0:c0 + cw].rearrange("(c p) n -> p c n", p=128)
                dma_pool(w1s[sl].rearrange("p (c n) -> p c n", c=8)[:, :, 0:cw],
                         src, s_w1[sl], wkeys=(("w1s", sl),))
                for pi in range(2):
                    p = 2 * jg + pi
                    if p >= NPAIR:
                        continue
                    wdt = 128 if p < 21 else 42
                    la = 256 * pi if jg < 10 else 256 * pi
                    if jg < 10:
                        loa, log = 256 * pi, 256 * pi + 128
                    else:
                        loa, log = 256 * pi, 256 * pi + wdt
                    if first:
                        for ti, lo in ((2 * p, loa), (2 * p + 1, log)):
                            for kc in range(NCH):
                                mm(PS(6, wdt, 1), w1s[sl][:, 512 * kc + lo:512 * kc + lo + wdt],
                                   cs_bf(C_BLNB + kc // 2, 0)[:, 0:0] if False else
                                   cs_bf(C_BLNB, 8)[:, kc:kc + 1],
                                   start=(kc == 0), stop=(kc == NCH - 1),
                                   wbank=(6,),
                                   rkeys=(("w1s", sl), ("lv",)),
                                   inc=s_pe if kc == NCH - 1 else None)
                            v_stt(cs_f32(C_U1 + ti, 1)[0:wdt, :], PS(6, wdt, 1),
                                  1.0, cs_f32(C_B1 + ti, 1)[0:wdt, :], ALU.mult, ALU.add,
                                  rbanks=(6,), rkeys=(("lv",),), wkeys=(("u1", ti),))
                    for bb, lo in ((0, loa), (2, log)):
                        for nt in range(2):
                            bank = bb + nt
                            for kc in range(NCH):
                                mm(PS(bank, wdt, 512), w1s[sl][:, 512 * kc + lo:512 * kc + lo + wdt],
                                   xn[nt][:, 512 * kc:512 * (kc + 1)],
                                   start=(kc == 0), stop=(kc == NCH - 1),
                                   wbank=(bank,),
                                   rkeys=(("w1s", sl), ("xn", nt)),
                                   inc=s_pe if kc == NCH - 1 else None)
                    a_act(gel[0:wdt, :], ps_t.ap()[0:wdt, 1024:2048], AF.Gelu,
                          bias=cs_f32(C_U1 + 2 * p + 1, 1)[0:wdt, :],
                          rbanks=(2, 3), rkeys=(("u1", 2 * p + 1),), wkeys=(("gel",),))
                    v_stt(hbf[0:wdt, 1024 * p:1024 * (p + 1)], ps_t.ap()[0:wdt, 0:1024],
                          cs_f32(C_U1 + 2 * p, 1)[0:wdt, :], gel[0:wdt, :],
                          ALU.add, ALU.mult,
                          rbanks=(0, 1), rkeys=(("gel",), ("u1", 2 * p)), wkeys=(("h", p),))

        def emit_w2_half(hh):
            for nt in range(2):
                t0 = 1024 * hh + 512 * nt
                for jg in range(11):
                    sl = jg % 2
                    r0 = 256 * jg
                    rh = 256 if jg < 10 else 170
                    if jg < 10:
                        dma_pool(w2s[sl].rearrange("p (j m) -> p j m", j=2),
                                 iw2[l][r0:r0 + 256, :].rearrange("(j p) m -> p j m", p=128),
                                 s_w2[sl], wkeys=(("w2s", sl),))
                    else:
                        dma_pool(w2s[sl][0:128, 0:1024],
                                 iw2[l][2560:2688, :], s_w2[sl], wkeys=(("w2s", sl),))
                        dma_pool(w2s[sl][0:42, 1024:2048],
                                 iw2[l][2688:2730, :], s_w2[sl], wkeys=(("w2s", sl),))
                    for ji in range(2):
                        j = 2 * jg + ji
                        if j >= NJT:
                            continue
                        kj = 128 if j < 21 else 42
                        for mt in range(NCH):
                            mm(PS(mt, 128, 512),
                               w2s[sl][0:kj, 1024 * ji + 128 * mt:1024 * ji + 128 * (mt + 1)],
                               hbf[0:kj, 1024 * j + 512 * nt:1024 * j + 512 * nt + 512],
                               start=(j == 0), stop=(j == NJT - 1),
                               wbank=(mt,),
                               rkeys=(("w2s", sl), ("h", j)),
                               inc=s_pe if (ji == 1 and mt == NCH - 1) else None)
                for mt in range(NCH):
                    v_stt(x_sl(mt, t0 + 1, 512), PS(mt, 128, 512), cs_f32(C_B2 + mt, 1),
                          x_sl(mt, t0 + 1, 512), ALU.add, ALU.add,
                          rbanks=(mt,), rkeys=(("x", mt), ("lv",)), wkeys=(("x", mt),))

        emit_xn(0)
        emit_xn(1)
        emit_w1_half(0, first=True)
        emit_xn(2)
        emit_xn(3)
        emit_w2_half(0)
        emit_w1_half(1, first=False)
        emit_w2_half(1)
        barrier()

    def emit_final(b):
        P.mark("final")
        xst = A.f32(G_XST, 8 * 512)
        sqb = A.bf16(G_OS, 8 * 512)
        mu = A.f32(G_MU, 512)[0:1, :]
        m2 = A.f32(G_M2, 512)[0:1, :]
        tmst = A.f32(G_BC, 1024)
        for ss in range(4):
            t0 = 512 * ss
            dma_sync(xst.rearrange("p (c t) -> p c t", c=8),
                     x1sp[:, t0:t0 + 512].rearrange("(c p) t -> p c t", p=128),
                     s_x1r, waits=((s_st, s_st.n),),
                     wkeys=tuple(("fx1", ch) for ch in range(NCH)))
            for ch in range(NCH):
                v_tt(xst[:, 512 * ch:512 * (ch + 1)], xst[:, 512 * ch:512 * (ch + 1)],
                     x_sl(ch, t0 + 1, 512), ALU.add,
                     rkeys=(("x", ch), ("fx1", ch)), wkeys=(("fx1", ch),))
            for ch in range(NCH):
                a_act(sqb[:, 512 * ch:512 * (ch + 1)], xst[:, 512 * ch:512 * (ch + 1)],
                      AF.Square, rkeys=(("fx1", ch),), wkeys=(("fsq", ch),))
            for ch in range(NCH):
                mm(PS(4, 1, 512), cs_f32(C_ONESCF, 1), xst[:, 512 * ch:512 * (ch + 1)],
                   start=(ch == 0), stop=(ch == NCH - 1),
                   wbank=(4,), rkeys=(("fx1", ch),),
                   inc=s_pe if ch == NCH - 1 else None)
            for ch in range(NCH):
                mm(PS(5, 1, 512), cs_bf(C_ONESCB, 1), sqb[:, 512 * ch:512 * (ch + 1)],
                   start=(ch == 0), stop=(ch == NCH - 1),
                   wbank=(5,), rkeys=(("fsq", ch),),
                   inc=s_pe if ch == NCH - 1 else None)
            a_act(mu, PS(4, 1, 512), AF.Copy, scale=1.0 / DM, rbanks=(4,), wkeys=(("gmu",),))
            a_act(m2, PS(4, 1, 512), AF.Square, scale=1.0 / DM, rbanks=(4,), wkeys=(("gm2",),))
            v_stt(m2, PS(5, 1, 512), 1.0 / DM, m2, ALU.mult, ALU.subtract,
                  rbanks=(5,), rkeys=(("gm2",),), wkeys=(("gm2",),))
            a_act(m2, m2, AF.Ln, bias=cs_f32(C_EPS, 1)[0:1, :], rkeys=(("gm2",),), wkeys=(("gm2",),))
            a_act(m2, m2, AF.Exp, scale=-0.5, rkeys=(("gm2",),), wkeys=(("gm2",),))
            v_stt(mu, mu, -1.0, m2, ALU.mult, ALU.mult,
                  rkeys=(("gmu",), ("gm2",)), wkeys=(("gmu",),))
            mm(PS(4), csb.ap()[0:1, C_ONESR:C_ONESR + 128], m2, start=True, stop=True,
               wbank=(4,), rkeys=(("gm2",),), inc=s_pe)
            mm(PS(5), csb.ap()[0:1, C_ONESR:C_ONESR + 128], mu, start=True, stop=True,
               wbank=(5,), rkeys=(("gmu",),), inc=s_pe)
            for ch in range(NCH):
                v_tt(xst[:, 512 * ch:512 * (ch + 1)], xst[:, 512 * ch:512 * (ch + 1)],
                     PS(4), ALU.mult, rbanks=(4,), rkeys=(("fx1", ch),), wkeys=(("fx1", ch),))
                v_tt(xst[:, 512 * ch:512 * (ch + 1)], xst[:, 512 * ch:512 * (ch + 1)],
                     PS(5), ALU.add, rbanks=(5,), rkeys=(("fx1", ch),), wkeys=(("fx1", ch),))
            for tt in range(4):
                bank = tt % 2
                for ch in range(NCH):
                    transpose(PS(bank, 128, 128, 128 * (ch % 4), r0=0)
                              if False else
                              ps_t.ap()[:, 512 * (2 * bank + ch // 4) + 128 * (ch % 4):
                                        512 * (2 * bank + ch // 4) + 128 * (ch % 4) + 128],
                              xst[:, 512 * ch + 128 * tt:512 * ch + 128 * (tt + 1)],
                              2 * bank + ch // 4, rkey=("fx1", ch),
                              start_clear=(ch % 4 == 0))
                v_copy(tmst, ps_t.ap()[:, 1024 * bank:1024 * bank + 1024],
                       rbanks=(2 * bank, 2 * bank + 1), wkeys=(("ftm",),))
                dma_store(out[b, t0 + 128 * tt:t0 + 128 * (tt + 1), :], tmst,
                          waits=TK.wait_read(("ftm",)) + TK.wait_write(("ftm",)),
                          note_rkeys=())
                TK.note_read(("ftm",), s_st, s_st.n)
        barrier()

    for b in range(n_batches):
        emit_global(b)
        if dump_stage == "x1":
            xst = A.f32(G_XST, 8 * 512)
            tmst = A.f32(G_BC, 1024)
            for ss in range(4):
                t0 = 512 * ss
                dma_sync(xst.rearrange("p (c t) -> p c t", c=8),
                         x1sp[:, t0:t0 + 512].rearrange("(c p) t -> p c t", p=128),
                         s_x1r, waits=((s_st, s_st.n),),
                         wkeys=tuple(("dx", ch) for ch in range(NCH)))
                for tt in range(4):
                    bank = tt % 2
                    for ch in range(NCH):
                        transpose(ps_t.ap()[:, 512 * (2 * bank + ch // 4) + 128 * (ch % 4):
                                            512 * (2 * bank + ch // 4) + 128 * (ch % 4) + 128],
                                  xst[:, 512 * ch + 128 * tt:512 * ch + 128 * (tt + 1)],
                                  2 * bank + ch // 4, rkey=("dx", ch),
                                  start_clear=(ch % 4 == 0), stop=(ch % 4 == 3))
                    v_copy(tmst, ps_t.ap()[:, 1024 * bank:1024 * bank + 1024],
                           rbanks=(2 * bank, 2 * bank + 1), wkeys=(("ftm",),))
                    dma_store(out[b, t0 + 128 * tt:t0 + 128 * (tt + 1), :], tmst,
                              waits=TK.wait_read(("ftm",)) + [TK.w[("ftm",)]])
                    TK.note_read(("ftm",), s_st, s_st.n)
            barrier()
            continue
        P.mark("reload")
        for ch in range(NCH):
            v_memset(x_sl(ch, 0, 1), 0.0, wkeys=(("x", ch),))
        for ch in range(NCH):
            dma_sync(x_sl(ch, 1, 2048), x1sp[128 * ch:128 * (ch + 1), :],
                     s_x1r, wkeys=(("x", ch),), waits=((s_st, s_st.n),))
        barrier()
        for l in range(n_layers):
            emit_inner_attn(b, l)
            if dump_stage == ("L%da" % (l + 1)):
                break
            emit_inner_ffn(b, l)
            if dump_stage == ("L%d" % (l + 1)):
                break
        if dump_stage and dump_stage != "full" and dump_stage.startswith("L"):
            tmst = A.f32(G_BC, 1024)
            for ss in range(4):
                t0 = 512 * ss
                for tt in range(4):
                    bank = tt % 2
                    for ch in range(NCH):
                        transpose(ps_t.ap()[:, 512 * (2 * bank + ch // 4) + 128 * (ch % 4):
                                            512 * (2 * bank + ch // 4) + 128 * (ch % 4) + 128],
                                  x_sl(ch, t0 + 128 * tt + 1, 128),
                                  2 * bank + ch // 4, rkey=("x", ch),
                                  start_clear=(ch % 4 == 0), stop=(ch % 4 == 3))
                    v_copy(tmst, ps_t.ap()[:, 1024 * bank:1024 * bank + 1024],
                           rbanks=(2 * bank, 2 * bank + 1), wkeys=(("ftm",),))
                    dma_store(out[b, t0 + 128 * tt:t0 + 128 * (tt + 1), :], tmst,
                              waits=TK.wait_read(("ftm",)) + [TK.w[("ftm",)]])
                    TK.note_read(("ftm",), s_st, s_st.n)
            barrier()
        elif dump_stage is None or dump_stage == "full":
            emit_final(b)

    P.op("sync", None, waits=((s_st, s_st.n),))
    P.emit()
    return nc


def _prep_inputs(inputs):
    c = host_constants()
    pg = _perm_global_qk()
    pi = _perm_inner_qk()
    pw1, tcols = _w1_tile_cols()
    wq = np.ascontiguousarray(inputs["L0_attn_wq"].reshape(DM, DM)[:, pg], dtype=np.float32)
    wk = np.ascontiguousarray(inputs["L0_attn_wk"].reshape(DM, DM)[:, pg], dtype=np.float32)
    wv = np.ascontiguousarray(inputs["L0_attn_wv"].reshape(DM, DM), dtype=np.float32)
    wo = np.ascontiguousarray(inputs["L0_attn_wo"].reshape(DM, DM), dtype=np.float32)
    Wqkv = np.asarray(inputs["L0_ffn_Wqkv"], dtype=np.float32)
    Wqkv_p = np.concatenate([Wqkv[:, :, :DM][:, :, pi], Wqkv[:, :, DM:2 * DM][:, :, pi],
                             Wqkv[:, :, 2 * DM:]], axis=2)
    W1p = np.ascontiguousarray(np.asarray(inputs["L0_ffn_W1"], dtype=np.float32)[:, :, pw1])
    qs = np.asarray(inputs["L0_ffn_q_scale"], dtype=np.float32)
    ks = np.asarray(inputs["L0_ffn_k_scale"], dtype=np.float32)
    qscol_a = np.zeros((DEPTH, 128, 4), dtype=np.float32)
    for l in range(DEPTH):
        qscol_a[l, :, 0] = np.tile(qs[l, 0:32], 4)
        qscol_a[l, :, 1] = np.tile(qs[l, 32:64], 4)
        qscol_a[l, :, 2] = np.tile(ks[l, 0:32], 4)
        qscol_a[l, :, 3] = np.tile(ks[l, 32:64], 4)
    b1p = np.asarray(inputs["L0_ffn_b1"], dtype=np.float32)[:, pw1]
    b1c = np.zeros((DEPTH, 128, 2 * NPAIR), dtype=np.float32)
    off = 0
    tlist = []
    for p in range(NPAIR):
        w = 128 if p < 21 else 42
        tlist.append((off, w)); off += w
        tlist.append((off, w)); off += w
    for l in range(DEPTH):
        for ti, (o, w) in enumerate(tlist):
            b1c[l, :w, ti] = b1p[l, o:o + w]
    lng_c = np.zeros((DEPTH, 128, NCH), dtype=np.float32)
    lnb_c = np.zeros((DEPTH, 128, NCH), dtype=np.float32)
    b2_c = np.zeros((DEPTH, 128, NCH), dtype=np.float32)
    for l in range(DEPTH):
        lng_c[l] = np.asarray(inputs["L0_ffn_ln_g"][l], dtype=np.float32).reshape(NCH, 128).T
        lnb_c[l] = np.asarray(inputs["L0_ffn_ln_b"][l], dtype=np.float32).reshape(NCH, 128).T
        b2_c[l] = np.asarray(inputs["L0_ffn_b2"][l], dtype=np.float32).reshape(NCH, 128).T
    return {
        "gwq": wq, "gwk": wk, "gwv": wv, "gwo": wo,
        "iwqkv": np.ascontiguousarray(Wqkv_p),
        "iwo": np.ascontiguousarray(np.asarray(inputs["L0_ffn_Wo"], dtype=np.float32)),
        "iw1": W1p,
        "iw2": np.ascontiguousarray(np.asarray(inputs["L0_ffn_W2"], dtype=np.float32)),
        "qscol": qscol_a, "lng_col": lng_c, "lnb_col": lnb_c, "b1_col": b1c, "b2_col": b2_c,
        "tbl_g": c["tbl_g"], "tbl_i": c["tbl_i"], "identity": c["identity"],
        "trimask4": c["trimask4"], "trimask2": c["trimask2"], "tri1one": c["tri1one"],
        "eden": c["eden"], "enorm": c["enorm"], "fnorm": c["fnorm"], "fpair": c["fpair"],
        "ones_col_bf": c["ones_col_bf"], "ones_col_f32": c["ones_col_f32"],
        "ones_row_f32": c["ones_row_f32"], "eps_col": c["eps_col"],
    }


def kernel(**inputs):
    x = np.ascontiguousarray(inputs["L0_attn_x"], dtype=np.float32)
    base = _prep_inputs(inputs)
    nc = build_program()
    in_maps = []
    for cidx in range(NCORE):
        m = dict(base)
        m["x_in"] = np.ascontiguousarray(x[BLOC * cidx:BLOC * (cidx + 1)])
        in_maps.append(m)
    res = run_bass_kernel_spmd(nc, in_maps, core_ids=list(range(NCORE)))
    return np.concatenate([r["out"] for r in res.results], axis=0)


# version 04
# speedup vs baseline: 1.0510x; 1.0105x over previous
import contextlib
import numpy as np
import ml_dtypes
import concourse.bass as bass
import concourse.mybir as mybir
from concourse.bass_utils import run_bass_kernel_spmd

F32 = mybir.dt.float32
BF16 = mybir.dt.bfloat16
AF = mybir.ActivationFunctionType
ALU = mybir.AluOpType
bfloat16 = ml_dtypes.bfloat16

T = 2048
NCORE = 8
BLOC = 2
NCH = 8
DM = 1024
GH = 8
IH = 16
WIN = 256
NWIN = T // WIN
DEPTH = 6
DI2 = 5460
DI = 2730
NPAIR = 22
NJT = 22
EPS = 1e-5

ENGS = ("tensor", "vector", "scalar", "gpsimd", "sync")
DEBUG_BUILD = False
DEDUP_WAITS = True
ATTN_PARTS = ("qkv", "attn", "wo")


def _rope_tables_global():
    half = 64
    af = np.power(np.float32(10000.0), -(np.arange(half, dtype=np.float32) / np.float32(half)))
    pos = np.arange(T, dtype=np.float32)[:, None]
    rad = (pos * af[None, :]).astype(np.float32)
    cg = np.tile(np.cos(rad).astype(np.float32).T, (2, 1))
    sg = np.tile(np.sin(rad).astype(np.float32).T, (2, 1))
    return cg.astype(np.float32), sg.astype(np.float32)


def _rope_tables_inner():
    half = np.arange(0, 64, 2, dtype=np.float32)
    inv_freq = (np.float32(1.0) / np.power(np.float32(10000.0), half / np.float32(64))).astype(np.float32)
    t = np.arange(T, dtype=np.float32)
    freqs = (t[:, None] * inv_freq[None, :]).astype(np.float32)
    base_scale = ((half + np.float32(0.4 * 64)) / np.float32(1.4 * 64)).astype(np.float32)
    power = ((t - T // 2) / np.float32(WIN // 2)).astype(np.float32)
    scale = np.power(base_scale[None, :], power[:, None]).astype(np.float32)
    cos = np.cos(freqs).astype(np.float32)
    sin = np.sin(freqs).astype(np.float32)
    inv = (np.float32(1.0) / scale).astype(np.float32)

    def pat(m):
        return np.tile(m.T, (4, 1)).astype(np.float32)
    return pat(cos * scale), pat(sin * scale), pat(cos * inv), pat(sin * inv)


def _perm_global_qk():
    perm = np.zeros(1024, dtype=np.int64)
    for h in range(GH):
        for d in range(128):
            newc = (d // 64) * 512 + (h // 2) * 128 + (h % 2) * 64 + (d % 64)
            perm[newc] = h * 128 + d
    return perm


def _perm_inner_qk():
    perm = np.zeros(1024, dtype=np.int64)
    for h in range(IH):
        for d in range(64):
            newc = (d // 32) * 512 + (h // 4) * 128 + (h % 4) * 32 + (d % 32)
            perm[newc] = h * 64 + d
    return perm


def _w1_tile_cols():
    idx = []
    tcols = []
    off = 0
    for p in range(NPAIR):
        w = 128 if p < 21 else 42
        idx.extend(range(128 * p, 128 * p + w))
        tcols.append((off, w)); off += w
        idx.extend(range(DI + 128 * p, DI + 128 * p + w))
        tcols.append((off, w)); off += w
    return np.array(idx, dtype=np.int64), tcols


def host_constants():
    c = {}
    cg, sg = _rope_tables_global()
    c["tbl_g"] = np.stack([cg, sg])
    csq, ssq, csk, ssk = _rope_tables_inner()
    c["tbl_i"] = np.stack([csq, ssq, csk, ssk])
    c["identity"] = np.eye(128, dtype=np.float32)
    p = np.arange(128)[:, None]
    f = np.arange(128)[None, :]
    tri1 = (p <= f).astype(bfloat16)
    tri2 = (f <= p).astype(bfloat16)
    c["trimask4"] = np.concatenate([tri2, tri2, tri1, tri1], axis=1).astype(bfloat16)
    c["trimask2"] = np.concatenate([tri1, tri1], axis=1).astype(bfloat16)
    c["tri1one"] = tri1
    eden = np.zeros((128, 4), dtype=bfloat16)
    eden[:, 0] = bfloat16(1.0)
    eden[:, 3] = bfloat16(1.0)
    c["eden"] = eden
    en = np.zeros((4, 128, 16), dtype=bfloat16)
    for g in range(4):
        for q in range(128):
            en[g, q, 4 * g + q // 32] = bfloat16(1.0)
    c["enorm"] = en
    fn = np.zeros((4, 16, 128), dtype=np.float32)
    for g in range(4):
        for j in range(4):
            fn[g, 4 * g + j, 32 * j:32 * j + 32] = 1.0
    c["fnorm"] = fn
    fp = np.zeros((2, 128), dtype=np.float32)
    fp[0, 0:64] = 1.0
    fp[1, 64:128] = 1.0
    c["fpair"] = fp
    c["ones_col_bf"] = np.ones((128, 1), dtype=bfloat16)
    c["eps_col"] = np.full((128, 1), EPS, dtype=np.float32)
    c["ones_col_f32"] = np.ones((128, 1), dtype=np.float32)
    c["ones_row_f32"] = np.ones((1, 128), dtype=np.float32)
    return c


class Sem:
    __slots__ = ("h", "n", "name", "last_eng", "is_dma", "engs")

    def __init__(self, h, name, is_dma=False):
        self.h = h; self.n = 0; self.name = name
        self.last_eng = None; self.is_dma = is_dma
        self.engs = set()


class Plan:
    def __init__(self, nc, stack):
        self.nc = nc
        self.q = {e: [] for e in ENGS}
        self.maxwait = {e: {} for e in ENGS}
        self.stack = stack
        self.phase = "init"

    def mark(self, name):
        self.phase = name

    def sem(self, name, is_dma=False):
        return Sem(self.stack.enter_context(self.nc.semaphore(name)), name, is_dma)

    def op(self, eng, fn, waits=(), inc=None, inc_n=1):
        real = []
        mw = self.maxwait[eng]
        for s, v in waits:
            if s is None or v is None or v <= 0:
                continue
            if DEDUP_WAITS:
                if (not s.is_dma) and s.engs == {eng} and v <= s.n:
                    continue
                if mw.get(id(s), 0) >= v:
                    continue
                mw[id(s)] = v
            real.append((s, v))
        val = None
        if inc is not None:
            inc.n += inc_n
            inc.last_eng = eng
            inc.engs.add(eng)
            val = inc.n
        self.q[eng].append((tuple(real), fn, inc, inc_n, self.phase))
        return val

    def dma(self, eng, fn, waits=(), inc=None):
        return self.op(eng, fn, waits, inc, inc_n=16)

    def barrier(self, bar, dma_sems=()):
        for e in ENGS:
            self.op(e, lambda E: E.nop(), inc=bar)
        tgt = bar.n
        wl = tuple((s, s.n) for s in dma_sems)
        for e in ENGS:
            self.op(e, None, waits=((bar, tgt),) + wl)

    def emit(self):
        phase_of = self.nc._phase_of_inst = {}

        def body(name):
            def run(E):
                for waits, fn, inc, inc_n, phase in self.q[name]:
                    for s, v in waits:
                        E.wait_ge(s.h, v)
                    if fn is None:
                        continue
                    n0 = self.nc.n_instructions()
                    inst = fn(E)
                    phase_of[n0] = (phase, name)
                    if inc is not None:
                        inst.then_inc(inc.h, inc_n)
            return run
        with self.nc.Block() as block:
            block.tensor(body("tensor"))
            block.vector(body("vector"))
            block.scalar(body("scalar"))
            block.gpsimd(body("gpsimd"))
            block.sync(body("sync"))


class Tracker:
    def __init__(self):
        self.w = {}
        self.r = {}
        self.rb = {}

    def wait_read_bank(self, key, eng):
        out = list(self.wait_read(key))
        last = self.rb.get(key)
        if last is not None and last[0] != eng:
            out.append((last[1], last[2]))
        return out

    def note_read_bank(self, key, eng, sem, val):
        if val is None:
            return
        self.rb[key] = (eng, sem, val)
        self.note_read(key, sem, val)

    def wait_read(self, *keys):
        return [self.w[k] for k in keys if k in self.w]

    def wait_write(self, *keys):
        out = []
        for k in keys:
            if k in self.w:
                out.append(self.w[k])
            for s, v in self.r.get(k, {}).values():
                out.append((s, v))
        return out

    def note_write(self, key, sem, val):
        if val is None:
            return
        self.w[key] = (sem, val)
        self.r.pop(key, None)
        self.rb.pop(key, None)

    def note_read(self, key, sem, val):
        if val is None:
            return
        d = self.r.setdefault(key, {})
        old = d.get(id(sem))
        if old is None or old[1] < val:
            d[id(sem)] = (sem, val)


ARENA_BYTES = 195 * 1024


class Arena:
    def __init__(self, t):
        self.t = t

    def f32(self, off, n):
        assert off % 4 == 0 and off + 4 * n <= ARENA_BYTES, (off, n)
        return self.t.ap()[:, off // 4: off // 4 + n]

    def bf16(self, off, n):
        assert off % 4 == 0 and off + 2 * n <= ARENA_BYTES, (off, n)
        m = (n + 1) // 2
        return self.t.ap()[:, off // 4: off // 4 + m].bitcast(BF16)[:, :n]


def build_program(n_layers=DEPTH, n_batches=BLOC, dump_stage=None):
    nc = bass.Bass("TRN2", target_bir_lowering=False, debug=DEBUG_BUILD,
                   detect_race_conditions=False)
    stack = contextlib.ExitStack()
    nc._dev_stack = stack

    x_in = nc.dram_tensor("x_in", [BLOC, T, DM], F32, kind="ExternalInput").ap()
    out = nc.dram_tensor("out", [BLOC, T, DM], F32, kind="ExternalOutput").ap()
    gwq = nc.dram_tensor("gwq", [DM, DM], F32, kind="ExternalInput").ap()
    gwk = nc.dram_tensor("gwk", [DM, DM], F32, kind="ExternalInput").ap()
    gwv = nc.dram_tensor("gwv", [DM, DM], F32, kind="ExternalInput").ap()
    gwo = nc.dram_tensor("gwo", [DM, DM], F32, kind="ExternalInput").ap()
    iwqkv = nc.dram_tensor("iwqkv", [DEPTH, DM, 3 * DM], F32, kind="ExternalInput").ap()
    iwo = nc.dram_tensor("iwo", [DEPTH, DM, DM], F32, kind="ExternalInput").ap()
    iw1 = nc.dram_tensor("iw1", [DEPTH, DM, DI2], F32, kind="ExternalInput").ap()
    iw2 = nc.dram_tensor("iw2", [DEPTH, DI, DM], F32, kind="ExternalInput").ap()
    qscol = nc.dram_tensor("qscol", [DEPTH, 128, 4], F32, kind="ExternalInput").ap()
    lng_col = nc.dram_tensor("lng_col", [DEPTH, 128, NCH], F32, kind="ExternalInput").ap()
    lnb_col = nc.dram_tensor("lnb_col", [DEPTH, 128, NCH], F32, kind="ExternalInput").ap()
    b1_col = nc.dram_tensor("b1_col", [DEPTH, 128, 2 * NPAIR], F32, kind="ExternalInput").ap()
    b2_col = nc.dram_tensor("b2_col", [DEPTH, 128, NCH], F32, kind="ExternalInput").ap()
    tbl_g = nc.dram_tensor("tbl_g", [2, 128, T], F32, kind="ExternalInput").ap()
    tbl_i = nc.dram_tensor("tbl_i", [4, 128, T], F32, kind="ExternalInput").ap()
    identity_d = nc.dram_tensor("identity", [128, 128], F32, kind="ExternalInput").ap()
    trimask4_d = nc.dram_tensor("trimask4", [128, 512], BF16, kind="ExternalInput").ap()
    trimask2_d = nc.dram_tensor("trimask2", [128, 256], BF16, kind="ExternalInput").ap()
    tri1_d = nc.dram_tensor("tri1one", [128, 128], BF16, kind="ExternalInput").ap()
    eden_d = nc.dram_tensor("eden", [128, 4], BF16, kind="ExternalInput").ap()
    enorm_d = nc.dram_tensor("enorm", [4, 128, 16], BF16, kind="ExternalInput").ap()
    fnorm_d = nc.dram_tensor("fnorm", [4, 16, 128], F32, kind="ExternalInput").ap()
    fpair_d = nc.dram_tensor("fpair", [2, 128], F32, kind="ExternalInput").ap()
    onescb_d = nc.dram_tensor("ones_col_bf", [128, 1], BF16, kind="ExternalInput").ap()
    onescf_d = nc.dram_tensor("ones_col_f32", [128, 1], F32, kind="ExternalInput").ap()
    onesr_d = nc.dram_tensor("ones_row_f32", [1, 128], F32, kind="ExternalInput").ap()
    eps_d = nc.dram_tensor("eps_col", [128, 1], F32, kind="ExternalInput").ap()
    x1sp = nc.dram_tensor("x1sp", [DM, T], F32).ap()

    arena_t = stack.enter_context(nc.sbuf_tensor("arena", [128, ARENA_BYTES // 4], F32))
    A = Arena(arena_t)
    csb = stack.enter_context(nc.sbuf_tensor("csb", [128, 3072], F32))

    C_ID = 0
    C_TRIM4 = 128
    C_TRIM2 = 384
    C_TRI1 = 512
    C_EDEN = 576
    C_ENORM = 704
    C_FNORM = 736
    C_FPAIR = 1248
    C_ONESCB = 2272
    C_ONESCF = 2273
    C_ONESR = 2274
    C_EPS = 2528
    C_EPS2 = 2529
    C_QS = 2402
    C_GCOL = 2408
    C_BLNF = 2416
    C_BLNB = 2424
    C_B1 = 2428
    C_B2 = 2472
    C_U1 = 2480
    C_END = 2530
    assert C_END <= 3072

    def cs_f32(off, n):
        return csb.ap()[:, off:off + n]

    def cs_bf(off, n):
        m = (n + 1) // 2
        return csb.ap()[:, off:off + m].bitcast(BF16)[:, :n]

    P = Plan(nc, stack)
    TK = Tracker()
    s_bar = P.sem("bar")
    s_pe = P.sem("pe")
    s_dve = P.sem("dve")
    s_act = P.sem("act")
    s_pool = P.sem("pool")
    s_ld = P.sem("ld", is_dma=True)
    s_x = P.sem("x", is_dma=True)
    s_tb0 = P.sem("tb0", is_dma=True)
    s_tb1 = P.sem("tb1", is_dma=True)
    s_wA = P.sem("wA", is_dma=True)
    s_wB = P.sem("wB", is_dma=True)
    s_w1 = [P.sem("w1a", is_dma=True), P.sem("w1b", is_dma=True)]
    s_w2 = [P.sem("w2a", is_dma=True), P.sem("w2b", is_dma=True)]
    s_vec = P.sem("vec", is_dma=True)
    s_x1r = P.sem("x1r", is_dma=True)
    s_st = P.sem("st", is_dma=True)
    ALL_DMA_SEMS = (s_ld, s_x, s_tb0, s_tb1, s_wA, s_wB, s_w1[0], s_w1[1],
                    s_w2[0], s_w2[1], s_vec, s_x1r, s_st)
    s_tbs = (s_tb0, s_tb1)

    ps_t = stack.enter_context(nc.psum_tensor("ps", [128, 4096], F32))

    def PS(bank, rows=128, cols=512, coff=0, r0=0):
        return ps_t.ap()[r0:r0 + rows, 512 * bank + coff: 512 * bank + coff + cols]

    pend_r = []
    pend_w = []

    def mm(psum_ap, lhs_ap, rhs_ap, start, stop, wbank=(), rkeys=(), waits=(), inc=None,
           tile_position=None):
        wl = list(waits)
        if start:
            for b in wbank:
                wl += TK.wait_write(("ps", b))
        for k in rkeys:
            wl += TK.wait_read(k)
        pend_r.extend(rkeys)
        pend_w.extend(("ps", b) for b in wbank)

        def f(E, a=psum_ap, b=lhs_ap, c=rhs_ap, st=start, sp=stop, tp=tile_position):
            return E.matmul(a, b, c, start=st, stop=sp, tile_position=tp)
        v = P.op("tensor", f, waits=wl, inc=inc)
        if inc is not None:
            for k in pend_w:
                TK.note_write(k, s_pe, v)
            for k in pend_r:
                TK.note_read(k, s_pe, v)
            pend_r.clear()
            pend_w.clear()
        return v

    def generic(eng, sem, fn, rkeys=(), wkeys=(), rbanks=(), wbanks=(), waits=(), inc=True):
        wl = list(waits)
        for k in rkeys:
            wl += TK.wait_read(k)
        for b in rbanks:
            wl += TK.wait_read_bank(("ps", b), eng)
        for k in wkeys:
            wl += TK.wait_write(k)
        for b in wbanks:
            wl += TK.wait_write(("ps", b))
        v = P.op(eng, fn, waits=wl, inc=sem if inc else None)
        if inc:
            for k in rkeys:
                TK.note_read(k, sem, v)
            for b in rbanks:
                TK.note_read_bank(("ps", b), eng, sem, v)
            for k in wkeys:
                TK.note_write(k, sem, v)
            for b in wbanks:
                TK.note_write(("ps", b), sem, v)
        return v

    def v_tt(o, a, b, op, **kw):
        return generic("vector", s_dve, lambda E: E.tensor_tensor(o, a, b, op), **kw)

    def v_stt(o, a, sc, b, op0, op1, **kw):
        return generic("vector", s_dve, lambda E: E.scalar_tensor_tensor(o, a, sc, b, op0, op1), **kw)

    def v_ts(o, a, s1, s2, op0, op1=None, **kw):
        if op1 is None:
            return generic("vector", s_dve, lambda E: E.tensor_scalar(o, a, s1, None, op0), **kw)
        return generic("vector", s_dve, lambda E: E.tensor_scalar(o, a, s1, s2, op0, op1), **kw)

    def v_copy(o, a, **kw):
        return generic("vector", s_dve, lambda E: E.tensor_copy(o, a), **kw)

    def v_recip(o, a, **kw):
        return generic("vector", s_dve, lambda E: E.reciprocal(o, a), **kw)

    def v_memset(o, val, **kw):
        return generic("vector", s_dve, lambda E: E.memset(o, val), **kw)

    def a_act(o, a, func, bias=None, scale=None, accum_out=None, **kw):
        def f(E):
            kk = {}
            if bias is not None:
                kk["bias"] = bias
            if scale is not None:
                kk["scale"] = scale
            if accum_out is not None:
                kk["accum_out"] = accum_out
            return E.activation(o, a, func, **kk)
        return generic("scalar", s_act, f, **kw)

    def p_tt(o, a, b, op, **kw):
        return generic("gpsimd", s_pool, lambda E: E.tensor_tensor(o, a, b, op), **kw)

    def _dma(eng, sem, o, a, **kw):
        wl = list(kw.pop("waits", ()))
        rkeys = kw.pop("rkeys", ())
        wkeys = kw.pop("wkeys", ())
        for k in rkeys:
            wl += TK.wait_read(k)
        for k in wkeys:
            wl += TK.wait_write(k)
        v = P.dma(eng, lambda E: E.dma_start(out=o, in_=a), waits=wl, inc=sem)
        for k in rkeys:
            TK.note_read(k, sem, v)
        for k in wkeys:
            TK.note_write(k, sem, v)
        return v

    def dma_sync(o, a, sem, **kw):
        return _dma("sync", sem, o, a, **kw)

    def dma_pool(o, a, sem, **kw):
        return _dma("gpsimd", sem, o, a, **kw)

    def dma_store(o, a, **kw):
        wl = list(kw.pop("waits", ()))
        for k in kw.pop("rkeys", ()):
            wl += TK.wait_read(k)
        v = P.dma("sync", lambda E: E.dma_start(out=o, in_=a), waits=wl, inc=s_st)
        for k in kw.get("note_rkeys", ()):
            TK.note_read(k, s_st, v)
        return v

    def transpose(psum_ap, in_ap, bank, rkey=None, waits=(), start_clear=True,
                  stop=False):
        wl = list(waits)
        if start_clear:
            wl += TK.wait_write(("ps", bank))
        if rkey is not None:
            wl += TK.wait_read(rkey)
        ident = cs_f32(C_ID, 128)

        def f(E, o=psum_ap, i=in_ap, idn=ident, sp=stop):
            return E.matmul(o, i, idn, start=True, stop=sp, is_transpose=True,
                            skip_group_check=True)
        v = P.op("tensor", f, waits=wl, inc=s_pe)
        pend_w.append(("ps", bank))
        if rkey is not None:
            pend_r.append(rkey)
        for k in pend_w:
            TK.note_write(k, s_pe, v)
        for k in pend_r:
            TK.note_read(k, s_pe, v)
        pend_r.clear()
        pend_w.clear()
        return v

    def barrier():
        P.barrier(s_bar, dma_sems=ALL_DMA_SEMS)
        TK.w.clear()
        TK.r.clear()
        TK.rb.clear()

    dma_sync(cs_f32(C_ID, 128), identity_d, s_ld)
    dma_sync(cs_bf(C_TRIM4, 512), trimask4_d, s_ld)
    dma_sync(cs_bf(C_TRIM2, 256), trimask2_d, s_ld)
    dma_sync(cs_bf(C_TRI1, 128), tri1_d, s_ld)
    dma_sync(cs_bf(C_EDEN, 4), eden_d, s_ld)
    dma_sync(cs_bf(C_ENORM, 64).rearrange("p (g n) -> p g n", g=4),
             enorm_d.rearrange("g p n -> p g n"), s_ld)
    dma_sync(csb.ap()[0:16, C_FNORM:C_FNORM + 512].rearrange("p (g n) -> p g n", g=4),
             fnorm_d.rearrange("g p n -> p g n"), s_ld)
    dma_sync(csb.ap()[0:2, C_FPAIR:C_FPAIR + 128], fpair_d, s_ld)
    dma_sync(cs_bf(C_ONESCB, 1), onescb_d, s_ld)
    dma_sync(cs_f32(C_ONESCF, 1), onescf_d, s_ld)
    dma_sync(csb.ap()[0:1, C_ONESR:C_ONESR + 128], onesr_d, s_ld)
    dma_sync(cs_f32(C_EPS, 1), eps_d, s_ld)
    barrier()
    v_ts(cs_f32(C_EPS2, 1), cs_f32(C_EPS, 1), 1e-19, None, ALU.mult,
         wkeys=(("eps2",),))
    barrier()

    G_XBF, G_KH, G_VH = 0, 32768, 65536
    G_W1, G_W2 = 98304, 114688
    G_TBL = 131072
    G_TT = 139264
    G_QS = 147456
    G_PT = 155648
    G_XST = 159744
    G_OS = 176128
    G_BC = 184320
    G_RC = 186368
    G_MU = 188416
    G_M2 = 190464
    G_STG = 192512
    assert G_STG + 2048 <= ARENA_BYTES

    def xg_bf(ch, t0, n):
        return A.bf16(G_XBF + 2 * 2048 * ch, 2048)[:, t0:t0 + n]

    def khat_g(r0, rows, ch, c0, n):
        return A.bf16(G_KH + 2 * 2048 * ch, 2048)[r0:r0 + rows, c0:c0 + n]

    def vhat_g(tt, c0, n):
        return A.bf16(G_VH + 2 * 1024 * tt, 1024)[:, c0:c0 + n]

    def rope_pass_global(bA, bB, outA, outB, tb_c, tb_s, t1b, t2b, keys):
        for cc, ub, us, sgn, o in ((0, bA, bB, -1.0, outA), (1, bB, bA, 1.0, outB)):
            v_tt(t1b, PS(ub), tb_c, ALU.mult,
                 rbanks=(ub,), rkeys=(keys["tbl"],), wkeys=(keys["t1"],))
            v_stt(t2b, PS(us), sgn, tb_s, ALU.mult, ALU.mult,
                  rbanks=(us,), rkeys=(keys["tbl"],), wkeys=(keys["t2"],))
            p_tt(o, t1b, t2b, ALU.add,
                 rkeys=(keys["t1"], keys["t2"]), wkeys=(keys["out"],))

    def emit_global(b):
        P.mark("g0")
        stg = A.f32(G_STG, 512)
        for ch in range(NCH):
            for tg in range(4):
                src = x_in[b, 512 * tg:512 * (tg + 1), 128 * ch:128 * (ch + 1)] \
                    .rearrange("(tt p) c -> p tt c", p=128)
                dma_sync(stg.rearrange("p (tt c) -> p tt c", tt=4), src, s_x, wkeys=(("gstg",),))
                bank = tg % 2
                for i in range(4):
                    transpose(PS(bank, 128, 128, 128 * i), stg[:, 128 * i:128 * (i + 1)],
                              bank, rkey=("gstg",), start_clear=(i == 0), stop=(i == 3))
                v_copy(xg_bf(ch, 512 * tg, 512), PS(bank), rbanks=(bank,),
                       wkeys=(("gxbf", ch, tg),))
        barrier()

        P.mark("g1")
        wk_bf = A.bf16(G_W1, 8 * 1024)
        wv_bf = A.bf16(G_W2, 8 * 1024)
        dma_pool(wk_bf.rearrange("p (c n) -> p c n", c=8), gwk.rearrange("(c p) n -> p c n", p=128), s_wA, wkeys=(("gw1",),))
        dma_pool(wv_bf.rearrange("p (c n) -> p c n", c=8), gwv.rearrange("(c p) n -> p c n", p=128), s_wB, wkeys=(("gw2",),))
        tbc = [A.f32(G_TBL + 4096 * i, 512) for i in range(2)]
        tbs = [A.f32(G_TBL + 2048 + 4096 * i, 512) for i in range(2)]
        t1b = [A.f32(G_TT + 4096 * i, 512) for i in range(2)]
        t2b = [A.f32(G_TT + 2048 + 4096 * i, 512) for i in range(2)]
        for ss in range(4):
            t0 = 512 * ss
            sl = ss % 2
            dma_sync(tbc[sl], tbl_g[0][:, t0:t0 + 512], s_tbs[sl], wkeys=(("gtb", sl),))
            dma_sync(tbs[sl], tbl_g[1][:, t0:t0 + 512], s_tbs[sl], wkeys=(("gtb", sl),))
            for g in range(4):
                for cc, bank in ((g, 4), (4 + g, 5)):
                    for kc in range(NCH):
                        mm(PS(bank), wk_bf[:, 1024 * kc + 128 * cc:1024 * kc + 128 * cc + 128],
                           xg_bf(kc, t0, 512), start=(kc == 0), stop=(kc == NCH - 1),
                           wbank=(bank,), rkeys=(("gw1",), ("gxbf", kc, ss)),
                           inc=s_pe if kc == NCH - 1 else None)
                rope_pass_global(4, 5,
                                 khat_g(0, 128, g, t0, 512), khat_g(0, 128, 4 + g, t0, 512),
                                 tbc[sl], tbs[sl], t1b[sl], t2b[sl],
                                 {"t1": ("gt1", sl), "t2": ("gt2", sl),
                                  "tbl": ("gtb", sl), "out": ("gkh",)})
            for tt in range(4 * ss, 4 * ss + 4):
                for nt in range(2):
                    bank = 6 + nt
                    for kc in range(NCH):
                        mm(PS(bank), xg_bf(kc, 128 * tt - t0 + t0, 128) if False else
                           xg_bf(kc, 128 * tt, 128),
                           wv_bf[:, 1024 * kc + 512 * nt:1024 * kc + 512 * nt + 512],
                           start=(kc == 0), stop=(kc == NCH - 1),
                           wbank=(bank,), rkeys=(("gw2",), ("gxbf", kc, ss)),
                           inc=s_pe if kc == NCH - 1 else None)
                    v_copy(vhat_g(tt, 512 * nt, 512), PS(bank), rbanks=(bank,),
                           wkeys=(("gvh",),))
        barrier()

        P.mark("g2")
        wq_bf = A.bf16(G_W1, 8 * 1024)
        wo_bf = A.bf16(G_W2, 8 * 1024)
        dma_pool(wq_bf.rearrange("p (c n) -> p c n", c=8), gwq.rearrange("(c p) n -> p c n", p=128), s_wA, wkeys=(("gw1",),))
        dma_pool(wo_bf.rearrange("p (c n) -> p c n", c=8), gwo.rearrange("(c p) n -> p c n", p=128), s_wB, wkeys=(("gw2",),))
        qsb = A.bf16(G_QS, 8 * 512)
        ptb = A.bf16(G_PT, 2048)
        xst = A.f32(G_XST, 8 * 512)
        sqb = A.bf16(G_OS, 8 * 512)
        obuf = A.bf16(G_OS, 8 * 512)
        bce = A.f32(G_BC, 512)
        rcp = A.f32(G_RC, 512)[0:1, :]
        mu = A.f32(G_MU, 512)[0:1, :]
        m2 = A.f32(G_M2, 512)[0:1, :]
        for ss in range(4):
            t0 = 512 * ss
            sl = ss % 2
            dma_sync(tbc[sl], tbl_g[0][:, t0:t0 + 512], s_tbs[sl], wkeys=(("gtb", sl),))
            dma_sync(tbs[sl], tbl_g[1][:, t0:t0 + 512], s_tbs[sl], wkeys=(("gtb", sl),))
            for g in range(4):
                for cc, bank in ((g, 4), (4 + g, 5)):
                    for kc in range(NCH):
                        mm(PS(bank), wq_bf[:, 1024 * kc + 128 * cc:1024 * kc + 128 * cc + 128],
                           xg_bf(kc, t0, 512), start=(kc == 0), stop=(kc == NCH - 1),
                           wbank=(bank,), rkeys=(("gw1",),),
                           inc=s_pe if kc == NCH - 1 else None)
                rope_pass_global(4, 5,
                                 A.bf16(G_QS + 2 * 512 * g, 512),
                                 A.bf16(G_QS + 2 * 512 * (4 + g), 512),
                                 tbc[sl], tbs[sl], t1b[sl], t2b[sl],
                                 {"t1": ("gt1", sl), "t2": ("gt2", sl),
                                  "tbl": ("gtb", sl), "out": ("gqs",)})
            for ch in range(NCH):
                src = x_in[b, t0:t0 + 512, 128 * ch:128 * (ch + 1)] \
                    .rearrange("(tt p) c -> p tt c", p=128)
                dma_sync(stg.rearrange("p (tt c) -> p tt c", tt=4), src, s_x, wkeys=(("gstg",),))
                for i in range(4):
                    transpose(PS(6, 128, 128, 128 * i), stg[:, 128 * i:128 * (i + 1)],
                              6, rkey=("gstg",), start_clear=(i == 0), stop=(i == 3))
                v_copy(xst[:, 512 * ch:512 * (ch + 1)], PS(6), rbanks=(6,),
                       wkeys=(("gxst", ch),))
            nkt = 4 * (ss + 1)
            for h in range(GH):
                r0 = 64 * (h % 2)
                cA, cB = h // 2, 4 + h // 2
                pv_started = False
                dn_started = False
                for ktg in range(0, nkt, 4):
                    gsz = min(4, nkt - ktg)
                    for i in range(gsz):
                        kt = ktg + i
                        qlo = max(0, 128 * kt - t0)
                        nq = 512 - qlo
                        for half, cc in ((0, cA), (1, cB)):
                            mm(PS(i, 128, nq, qlo),
                               khat_g(r0, 64, cc, 128 * kt, 128),
                               A.bf16(G_QS + 2 * 512 * cc, 512)[r0:r0 + 64, qlo:qlo + nq],
                               start=(half == 0), stop=(half == 1),
                               wbank=(i,), rkeys=(("gkh",), ("gqs",)),
                               tile_position=(r0, 0),
                               inc=s_pe if half == 1 else None)
                    if 128 * (ktg + gsz) <= t0:
                        a_act(ptb[:, 0:512 * gsz], ps_t.ap()[:, 0:512 * gsz], AF.Exp,
                              scale=1.0 / 128.0,
                              rbanks=tuple(range(gsz)), wkeys=(("gpt",),))
                    else:
                        for i in range(gsz):
                            kt = ktg + i
                            qlo = max(0, 128 * kt - t0)
                            a_act(ptb[:, 512 * i + qlo:512 * (i + 1)],
                                  PS(i, 128, 512 - qlo, qlo), AF.Exp, scale=1.0 / 128.0,
                                  rbanks=(i,), wkeys=(("gpt",),))
                            if 128 * kt >= t0:
                                v_tt(ptb[:, 512 * i + qlo:512 * i + qlo + 128],
                                     ptb[:, 512 * i + qlo:512 * i + qlo + 128],
                                     cs_bf(C_TRI1, 128), ALU.mult,
                                     rkeys=(("gpt",),), wkeys=(("gpt",),))
                    for i in range(gsz):
                        kt = ktg + i
                        qlo = max(0, 128 * kt - t0)
                        nq = 512 - qlo
                        last = (kt == nkt - 1)
                        mm(PS(6, 1, nq, qlo), cs_bf(C_ONESCB, 1),
                           ptb[:, 512 * i + qlo:512 * (i + 1)],
                           start=not dn_started, stop=last,
                           wbank=(6,), rkeys=(("gpt",),),
                           inc=s_pe if last else None)
                        dn_started = True
                        mm(PS(7, 128, nq, qlo), vhat_g(kt, 128 * h, 128),
                           ptb[:, 512 * i + qlo:512 * (i + 1)],
                           start=not pv_started, stop=last,
                           wbank=(7,),
                           rkeys=(("gvh",), ("gpt",)),
                           inc=s_pe if last else None)
                        pv_started = True
                a_act(rcp, PS(6, 1, 512), AF.Ln, rbanks=(6,), wkeys=(("grcp",),))
                a_act(rcp, rcp, AF.Exp, scale=-1.0, rkeys=(("grcp",),), wkeys=(("grcp",),))
                mm(PS(5), csb.ap()[0:1, C_ONESR:C_ONESR + 128], rcp, start=True, stop=True,
                   wbank=(5,), rkeys=(("grcp",),), inc=s_pe)
                v_copy(bce, PS(5), rbanks=(5,), wkeys=(("gbce",),))
                v_tt(obuf[:, 512 * h:512 * (h + 1)], PS(7), bce, ALU.mult,
                     rbanks=(7,), rkeys=(("gbce",),), wkeys=(("gos", h),))
            for mt in range(NCH):
                for hc in range(NCH):
                    mm(PS(mt), wo_bf[:, 1024 * hc + 128 * mt:1024 * hc + 128 * mt + 128],
                       obuf[:, 512 * hc:512 * (hc + 1)],
                       start=(hc == 0), stop=(hc == NCH - 1),
                       wbank=(mt,), rkeys=(("gw2",), ("gos", hc)),
                       inc=s_pe if hc == NCH - 1 else None)
                v_tt(xst[:, 512 * mt:512 * (mt + 1)], PS(mt), xst[:, 512 * mt:512 * (mt + 1)],
                     ALU.add, rbanks=(mt,), rkeys=(("gxst", mt),), wkeys=(("gxst", mt),))
            if getattr(build_program, "_dbg", False) and ss in (0,):
                def _dbgx(E, t=xst, tag=(b, ss)):
                    import numpy as _np
                    E.dbg_aps(lambda a, g, tg=tag: print("XSTDBG", tg, _np.asarray(a), _np.asarray(g)),
                              t[0:1, 0:4], t[5:6, 512 * 3:512 * 3 + 4])
                    return E.nop()
                P.op("vector", _dbgx,
                     waits=sum((TK.wait_read(("gxst", c)) + TK.wait_write(("gxst", c))
                                for c in range(NCH)), []))
            for ch in range(NCH):
                a_act(sqb[:, 512 * ch:512 * (ch + 1)], xst[:, 512 * ch:512 * (ch + 1)],
                      AF.Square, rkeys=(("gxst", ch),), wkeys=(("gos", ch),))
            for ch in range(NCH):
                mm(PS(4, 1, 512), cs_f32(C_ONESCF, 1), xst[:, 512 * ch:512 * (ch + 1)],
                   start=(ch == 0), stop=(ch == NCH - 1),
                   wbank=(4,), rkeys=(("gxst", ch),),
                   inc=s_pe if ch == NCH - 1 else None)
            for ch in range(NCH):
                mm(PS(5, 1, 512), cs_bf(C_ONESCB, 1), sqb[:, 512 * ch:512 * (ch + 1)],
                   start=(ch == 0), stop=(ch == NCH - 1),
                   wbank=(5,), rkeys=(("gos", ch),),
                   inc=s_pe if ch == NCH - 1 else None)
            a_act(mu, PS(4, 1, 512), AF.Copy, scale=1.0 / DM, rbanks=(4,), wkeys=(("gmu",),))
            a_act(m2, PS(4, 1, 512), AF.Square, scale=1.0 / DM, rbanks=(4,), wkeys=(("gm2",),))
            v_stt(m2, PS(5, 1, 512), 1.0 / DM, m2, ALU.mult, ALU.subtract,
                  rbanks=(5,), rkeys=(("gm2",),), wkeys=(("gm2",),))
            a_act(m2, m2, AF.Ln, bias=cs_f32(C_EPS, 1)[0:1, :], rkeys=(("gm2",),), wkeys=(("gm2",),))
            a_act(m2, m2, AF.Exp, scale=-0.5, rkeys=(("gm2",),), wkeys=(("gm2",),))
            v_stt(mu, mu, -1.0, m2, ALU.mult, ALU.mult,
                  rkeys=(("gmu",), ("gm2",)), wkeys=(("gmu",),))
            mm(PS(4), csb.ap()[0:1, C_ONESR:C_ONESR + 128], m2, start=True, stop=True,
               wbank=(4,), rkeys=(("gm2",),), inc=s_pe)
            mm(PS(5), csb.ap()[0:1, C_ONESR:C_ONESR + 128], mu, start=True, stop=True,
               wbank=(5,), rkeys=(("gmu",),), inc=s_pe)
            for ch in range(NCH):
                v_tt(xst[:, 512 * ch:512 * (ch + 1)], xst[:, 512 * ch:512 * (ch + 1)],
                     PS(4), ALU.mult, rbanks=(4,), rkeys=(("gxst", ch),), wkeys=(("gxst", ch),))
                v_tt(xst[:, 512 * ch:512 * (ch + 1)], xst[:, 512 * ch:512 * (ch + 1)],
                     PS(5), ALU.add, rbanks=(5,), rkeys=(("gxst", ch),), wkeys=(("gxst", ch),))
            dma_store(x1sp[:, t0:t0 + 512].rearrange("(c p) t -> p c t", p=128),
                      xst.rearrange("p (c t) -> p c t", c=8),
                      rkeys=tuple(("gxst", ch) for ch in range(NCH)),
                      note_rkeys=tuple(("gxst", ch) for ch in range(NCH)))
        barrier()

    I_X = 0
    I_WQKV = 65568
    I_WO = 114720
    I_XBFW = 131104
    I_QPRE = 139328
    I_QBF = 147520
    I_KBF = 151616
    I_VBF = 159808
    I_USQ = 168000
    I_PT = 169024
    I_TBL = 175168
    I_TT = 183360
    I_OBUF = 185408
    I_BCE = 189504
    I_NRM = 190528
    I_END = 192576
    assert I_END <= ARENA_BYTES
    F_H = I_WQKV
    F_XSQ = I_WO
    F_W1 = I_WO + 8224
    F_XN = 139328
    F_W2 = 155712
    F_GEL = 163904
    F_STA = 168000
    F_END = 174144
    assert F_END <= ARENA_BYTES

    def x_sl(ch, c0, n):
        return A.f32(I_X + 4 * 2049 * ch, 2049)[:, c0:c0 + n]

    def inner_qk(ten, w, slot, outfn, sA, sB, nsB, wqkv, xbfw, tb, t1b, t2b, qpre, usq,
                 nrm, recd, layer_key):
        base = 1024 * ten
        t0 = WIN * w
        _full = "qkv" in ATTN_PARTS or "q" in ATTN_PARTS or "k" in ATTN_PARTS
        _p_mm = _full or "qk_mm" in ATTN_PARTS
        _p_sq = _p_mm and (_full or "qk_sq" in ATTN_PARTS)
        _p_rot = _p_mm and (_full or "qk_rot" in ATTN_PARTS)
        _p_bc = _p_sq and (_full or "qk_bc" in ATTN_PARTS)
        for g in range(4):
            if _p_mm:
                for cc, bank in ((g, 0), (4 + g, 1)):
                    for kc in range(NCH):
                        o = 257 * kc + (1 if kc < 4 else 0)
                        mm(PS(bank, 128, 256),
                           wqkv[:, 3072 * kc + base + 128 * cc:3072 * kc + base + 128 * cc + 128],
                           xbfw[slot][:, o:o + 256],
                           start=(kc == 0), stop=(kc == NCH - 1),
                           wbank=(bank,), rkeys=(layer_key, ("xbfw", slot)),
                           inc=s_pe if kc == NCH - 1 else None)
            if _p_sq:
                a_act(usq[:, 0:256], PS(0, 128, 256), AF.Square, rbanks=(0,), wkeys=(("usq", 0),))
                a_act(usq[:, 256:512], PS(1, 128, 256), AF.Square, rbanks=(1,), wkeys=(("usq", 1),))
                for j in range(2):
                    mm(PS(6, 16, 256), cs_bf(C_ENORM, 64)[:, 16 * g:16 * (g + 1)],
                       usq[:, 256 * j:256 * (j + 1)],
                       start=(g == 0 and j == 0), stop=(g == 3 and j == 1),
                       wbank=(6,), rkeys=(("usq", j),),
                       inc=s_pe if (g == 3 and j == 1) else None)
            if _p_rot:
                for cc, ub, us, s1, s2 in ((g, 0, 1, sA, nsB), (4 + g, 1, 0, sB, sA)):
                    v_stt(t1b, PS(ub, 128, 256), s1, tb[2 * ten], ALU.mult, ALU.mult,
                          rbanks=(ub,), rkeys=(("itb", slot),), wkeys=(("it1",),))
                    v_stt(t2b, PS(us, 128, 256), s2, tb[2 * ten + 1], ALU.mult, ALU.mult,
                          rbanks=(us,), rkeys=(("itb", slot),), wkeys=(("it2",),))
                    p_tt(qpre[:, 256 * cc:256 * (cc + 1)], t1b, t2b, ALU.add,
                         rkeys=(("it1",), ("it2",)), wkeys=(("qpre", cc),))
        if _p_sq:
            a_act(nrm, PS(6, 16, 256), AF.Ln, bias=cs_f32(C_EPS2, 1)[0:16, :],
                  rbanks=(6,), wkeys=(("nrm",),))
            a_act(recd, nrm, AF.Exp, scale=-0.5, rkeys=(("nrm",),), wkeys=(("recd",),))
        if _p_bc:
            for g in range(4):
                mm(PS(7, 128, 256), csb.ap()[0:16, C_FNORM + 128 * g:C_FNORM + 128 * (g + 1)],
                   recd, start=True, stop=True, wbank=(7,), rkeys=(("recd",),), inc=s_pe)
                for cc in (g, 4 + g):
                    v_tt(outfn(cc), qpre[:, 256 * cc:256 * (cc + 1)], PS(7, 128, 256), ALU.mult,
                         rbanks=(7,), rkeys=(("qpre", cc),), wkeys=(("qk_out", ten),))

    def emit_inner_attn(b, l):
        P.mark("L%da" % l)
        wqkv = A.bf16(I_WQKV, 8 * 3072)
        wo_bf = A.bf16(I_WO, 8 * 1024)
        lk = ("iw", l)
        dma_pool(wqkv.rearrange("p (c n) -> p c n", c=8),
                 iwqkv[l].rearrange("(c p) n -> p c n", p=128), s_wA, wkeys=(lk,))
        dma_pool(wo_bf.rearrange("p (c n) -> p c n", c=8),
                 iwo[l].rearrange("(c p) n -> p c n", p=128), s_wB, wkeys=(lk,))
        dma_sync(cs_f32(C_QS, 4), qscol[l], s_vec, wkeys=(("qs",),))
        v_ts(cs_f32(C_QS + 4, 1), cs_f32(C_QS + 1, 1), -1.0, None, ALU.mult,
             rkeys=(("qs",),), wkeys=(("qs",),))
        v_ts(cs_f32(C_QS + 5, 1), cs_f32(C_QS + 3, 1), -1.0, None, ALU.mult,
             rkeys=(("qs",),), wkeys=(("qs",),))
        qsA, qsB = cs_f32(C_QS, 1), cs_f32(C_QS + 1, 1)
        ksA, ksB = cs_f32(C_QS + 2, 1), cs_f32(C_QS + 3, 1)
        nqsB, nksB = cs_f32(C_QS + 4, 1), cs_f32(C_QS + 5, 1)

        xbfw = [A.bf16(I_XBFW + 4112 * i, 8 * 257) for i in range(2)]
        qpre = A.f32(I_QPRE, 8 * 256)
        qbf = A.bf16(I_QBF, 8 * 256)
        kbf = A.bf16(I_KBF, 8 * 512)
        vbf = A.bf16(I_VBF, 4 * 1024)
        usq = A.bf16(I_USQ, 512)
        ptb = [A.bf16(I_PT + 1536 * i, 768) for i in range(4)]
        tb = [[A.f32(I_TBL + 1024 * j + 4096 * i, 256) for j in range(4)] for i in range(2)]
        t1b = A.f32(I_TT, 256)
        t2b = A.f32(I_TT + 1024, 256)
        ob = A.bf16(I_OBUF, 8 * 256)
        bce = A.f32(I_BCE, 256)
        nrm = A.f32(I_NRM, 256)[0:16, :]
        recd = A.f32(I_NRM + 1024, 256)[0:16, :]

        def cast_win(w):
            slot = w % 2
            t0 = WIN * w
            xsrc = A.f32(I_X, 8 * 2049).rearrange("p (c t) -> p c t", c=8)[:, :, t0:t0 + 257]
            xdst = xbfw[slot].rearrange("p (c t) -> p c t", c=8)
            v_copy(xdst, xsrc,
                   rkeys=tuple(("x", ch) for ch in range(NCH)), wkeys=(("xbfw", slot),))

        _cast_on = "qkv" in ATTN_PARTS or "cast" in ATTN_PARTS
        _tbl_on = "qkv" in ATTN_PARTS or "tbl" in ATTN_PARTS

        def emit_qkv(w):
            slot = w % 2
            t0w = WIN * w
            for j in (range(4) if _tbl_on else ()):
                dma_sync(tb[slot][j], tbl_i[j][:, t0w:t0w + 256], s_tbs[slot],
                         wkeys=(("itb", slot),))
            if ("qkv" in ATTN_PARTS or "q" in ATTN_PARTS
                    or any(p.startswith("qk_") for p in ATTN_PARTS)):
                inner_qk(0, w, slot, lambda cc: A.bf16(I_QBF + 2 * 256 * cc, 256),
                         qsA, qsB, nqsB, wqkv, xbfw, tb[slot], t1b, t2b, qpre, usq, nrm, recd, lk)
            if "qkv" in ATTN_PARTS or "k" in ATTN_PARTS:
                inner_qk(1, w, slot, lambda cc: A.bf16(I_KBF + 2 * (512 * cc + 256 * slot), 256),
                         ksA, ksB, nksB, wqkv, xbfw, tb[slot], t1b, t2b, qpre, usq, nrm, recd, lk)
            if "qkv" in ATTN_PARTS or "v" in ATTN_PARTS:
                for tt in range(2):
                    for nt in range(2):
                        bank = 2 + nt
                        for kc in range(NCH):
                            o = 257 * kc + (1 if kc < 4 else 0) + 128 * tt
                            mm(PS(bank), xbfw[slot][:, o:o + 128],
                               wqkv[:, 3072 * kc + 2048 + 512 * nt:3072 * kc + 2048 + 512 * nt + 512],
                               start=(kc == 0), stop=(kc == NCH - 1),
                               wbank=(bank,), rkeys=(lk, ("xbfw", slot)),
                               inc=s_pe if kc == NCH - 1 else None)
                        v_copy(A.bf16(I_VBF + 2 * (1024 * (2 * slot + tt) + 512 * nt), 512),
                               PS(bank), rbanks=(bank,), wkeys=(("vbf", slot),))

        if _cast_on:
            cast_win(0)
        emit_qkv(0)
        if _cast_on and NWIN > 1:
            cast_win(1)
        for w in range(NWIN):
            t0 = WIN * w
            slot = w % 2
            pslot = (w - 1) % 2
            if w + 2 < NWIN and _cast_on:
                cast_win(w + 2)
            if w == 0:
                slots = [(0, (slot, 0), 1), (1, (slot, 1), 1), (1, (slot, 0), 0)]
                maskc = cs_bf(C_TRIM2, 256)
                nmask = 2
            else:
                slots = [(0, (pslot, 0), 2), (1, (pslot, 1), 2),
                         (0, (slot, 0), 1), (1, (slot, 1), 1),
                         (0, (pslot, 1), 0), (1, (slot, 0), 0)]
                maskc = cs_bf(C_TRIM4, 512)
                nmask = 4
            nslot = len(slots)

            def pair_ln(p):
                a_act(recd[0:2, :], PS(6, 2, 256), AF.Ln, rbanks=(6,), wkeys=(("recd",),))
                a_act(recd[0:2, :], recd[0:2, :], AF.Exp, scale=-1.0,
                      rkeys=(("recd",),), wkeys=(("recd",),))

            def pair_fin(p):
                pvb = 4 + (p % 2)
                mm(PS(7, 128, 256), csb.ap()[0:2, C_FPAIR:C_FPAIR + 128],
                   recd[0:2, :], start=True, stop=True, wbank=(7,), rkeys=(("recd",),),
                   inc=s_pe)
                v_copy(bce, PS(7, 128, 256), rbanks=(7,), wkeys=(("bce",),))
                v_tt(ob[:, 256 * p:256 * (p + 1)], PS(pvb, 128, 256), bce, ALU.mult,
                     rbanks=(pvb,), rkeys=(("bce",),), wkeys=(("ob", p),))

            for h in (range(IH) if "attn" in ATTN_PARTS else ()):
                r0 = 32 * (h % 4)
                g = h // 4
                cA, cB = g, 4 + g
                pts = ptb[h % 4]
                sb0 = 2 * (h % 2)
                pvb = 4 + ((h // 2) % 2)
                for si, (a, (ks, kt), mk) in enumerate(slots):
                    for half, cc in ((0, cA), (1, cB)):
                        kcol = 512 * cc + 256 * ks + 128 * kt
                        mm(ps_t.ap()[:, 512 * sb0 + 128 * si:512 * sb0 + 128 * (si + 1)],
                           kbf[r0:r0 + 32, kcol:kcol + 128],
                           A.bf16(I_QBF + 2 * 256 * cc, 256)[r0:r0 + 32, 128 * a:128 * (a + 1)],
                           start=(half == 0), stop=(half == 1),
                           wbank=(sb0, sb0 + 1),
                           rkeys=(("qk_out", 0), ("qk_out", 1)),
                           tile_position=(r0, 0),
                           inc=s_pe if half == 1 else None)
                a_act(pts[:, 0:128 * nslot], ps_t.ap()[:, 512 * sb0:512 * sb0 + 128 * nslot],
                      AF.Exp, scale=8.0,
                      rbanks=(sb0, sb0 + 1), wkeys=(("pt", h % 4),))
                v_tt(pts[:, 0:128 * nmask], pts[:, 0:128 * nmask], maskc, ALU.mult,
                     rkeys=(("pt", h % 4),), wkeys=(("pt", h % 4),))
                if h % 2 == 0 and h >= 2:
                    pair_ln((h - 2) // 2)
                for si, (a, (ks, kt), mk) in enumerate(slots):
                    mm(PS(6, 2, 128, 128 * a), cs_bf(C_EDEN, 4)[:, 2 * (h % 2):2 * (h % 2) + 2],
                       pts[:, 128 * si:128 * (si + 1)],
                       start=(si == 0 and h % 2 == 0), stop=(si == nslot - 1 and h % 2 == 1),
                       wbank=(6,),
                       rkeys=(("pt", h % 4),),
                       inc=s_pe if (si == nslot - 1 and h % 2 == 1) else None)
                for si, (a, (ks, kt), mk) in enumerate(slots):
                    vcol = 1024 * (2 * ks + kt) + 64 * h
                    mm(PS(pvb, 64, 128, 128 * a, r0=64 * (h % 2)),
                       A.bf16(I_VBF + 2 * vcol, 64),
                       pts[:, 128 * si:128 * (si + 1)],
                       start=(si == 0), stop=(si == nslot - 1),
                       wbank=(pvb,),
                       rkeys=(("vbf", ks), ("pt", h % 4)),
                       tile_position=(0, 64 * (h % 2)),
                       inc=s_pe if si == nslot - 1 else None)
                if h % 2 == 1 and h >= 3:
                    pair_fin((h - 3) // 2)
            if "attn" in ATTN_PARTS:
                pair_ln(7)
                pair_fin(7)
            if w + 1 < NWIN:
                emit_qkv(w + 1)
            wo_banks = (4, 5, 4, 5, 6, 7, 6, 7)
            for mt in (range(NCH) if "wo" in ATTN_PARTS else ()):
                bank = wo_banks[mt]
                for hc in range(NCH):
                    mm(PS(bank, 128, 256), wo_bf[:, 1024 * hc + 128 * mt:1024 * hc + 128 * mt + 128],
                       ob[:, 256 * hc:256 * (hc + 1)],
                       start=(hc == 0), stop=(hc == NCH - 1),
                       wbank=(bank,),
                       rkeys=(lk, ("ob", hc)),
                       inc=s_pe if hc == NCH - 1 else None)
                v_tt(x_sl(mt, t0 + 1, 256), PS(bank, 128, 256), x_sl(mt, t0 + 1, 256), ALU.add,
                     rbanks=(bank,), rkeys=(("x", mt),), wkeys=(("x", mt),))
        barrier()

    def emit_inner_ffn(b, l):
        P.mark("L%df" % l)
        hbf = A.bf16(F_H, NJT * 1024)
        xsq = A.bf16(F_XSQ, 8 * 513)
        w1s = [A.bf16(F_W1 + 8192 * i, 8 * 512) for i in range(2)]
        xn = [A.bf16(F_XN + 8192 * i, 8 * 512) for i in range(2)]
        w2s = [A.bf16(F_W2 + 4096 * i, 2 * 1024) for i in range(2)]
        gel = A.f32(F_GEL, 1024)
        stA = A.f32(F_STA, 512)[0:1, :]
        stB = A.f32(F_STA + 2048, 512)[0:1, :]
        stM = A.f32(F_STA + 4096, 512)[0:1, :]
        dma_sync(cs_f32(C_GCOL, 8), lng_col[l], s_vec, wkeys=(("lv",),))
        dma_sync(cs_f32(C_BLNF, 8), lnb_col[l], s_vec, wkeys=(("lv",),))
        dma_sync(cs_f32(C_B1, 44), b1_col[l], s_vec, wkeys=(("lv",),))
        dma_sync(cs_f32(C_B2, 8), b2_col[l], s_vec, wkeys=(("lv",),))
        v_copy(cs_bf(C_BLNB, 8), cs_f32(C_BLNF, 8), rkeys=(("lv",),), wkeys=(("lv",),))

        def w1cols(p):
            return (256 * p, 128 if p < 21 else 42) if p < 21 else (5376, 42)

        def emit_xn(qq):
            t0 = 512 * qq
            sl = qq % 2
            for ch in range(NCH):
                a_act(xsq[:, 513 * ch:513 * ch + 513], x_sl(ch, t0, 513), AF.Square,
                      rkeys=(("x", ch),), wkeys=(("xsq", ch),))
            for ch in range(NCH):
                o = 513 * ch + (1 if ch < 4 else 0)
                mm(PS(6, 1, 512), cs_f32(C_ONESCF, 1),
                   x_sl(ch, t0 + (1 if ch < 4 else 0), 512),
                   start=(ch == 0), stop=(ch == NCH - 1),
                   wbank=(6,), rkeys=(("x", ch),),
                   inc=s_pe if ch == NCH - 1 else None)
            for ch in range(NCH):
                o = 513 * ch + (1 if ch < 4 else 0)
                mm(PS(7, 1, 512), cs_bf(C_ONESCB, 1), xsq[:, o:o + 512],
                   start=(ch == 0), stop=(ch == NCH - 1),
                   wbank=(7,), rkeys=(("xsq", ch),),
                   inc=s_pe if ch == NCH - 1 else None)
            a_act(stM, PS(6, 1, 512), AF.Copy, scale=1.0 / DM, rbanks=(6,), wkeys=(("stM",),))
            a_act(stA, PS(6, 1, 512), AF.Square, scale=1.0 / DM, rbanks=(6,), wkeys=(("stA",),))
            v_stt(stA, PS(7, 1, 512), 1.0 / DM, stA, ALU.mult, ALU.subtract,
                  rbanks=(7,), rkeys=(("stA",),), wkeys=(("stA",),))
            a_act(stA, stA, AF.Ln, bias=cs_f32(C_EPS, 1)[0:1, :], rkeys=(("stA",),), wkeys=(("stA",),))
            a_act(stA, stA, AF.Exp, scale=-0.5, rkeys=(("stA",),), wkeys=(("stA",),))
            v_stt(stB, stM, -1.0, stA, ALU.mult, ALU.mult,
                  rkeys=(("stM",), ("stA",)), wkeys=(("stB",),))
            mm(PS(6), csb.ap()[0:1, C_ONESR:C_ONESR + 128], stA, start=True, stop=True,
               wbank=(6,), rkeys=(("stA",),), inc=s_pe)
            mm(PS(7), csb.ap()[0:1, C_ONESR:C_ONESR + 128], stB, start=True, stop=True,
               wbank=(7,), rkeys=(("stB",),), inc=s_pe)
            for ch in range(NCH):
                sh = 1 if ch < 4 else 0
                v_stt(gel[:, 0:512], x_sl(ch, t0 + sh, 512), cs_f32(C_GCOL + ch, 1),
                      PS(6), ALU.mult, ALU.mult,
                      rbanks=(6,), rkeys=(("x", ch), ("lv",)), wkeys=(("gel",),))
                v_stt(xn[sl][:, 512 * ch:512 * (ch + 1)], PS(7), cs_f32(C_GCOL + ch, 1),
                      gel[:, 0:512], ALU.mult, ALU.add,
                      rbanks=(7,), rkeys=(("gel",), ("lv",)), wkeys=(("xn", sl),))

        def emit_w1_half(hh, first):
            for jg in range(11):
                sl = jg % 2
                c0 = 512 * jg
                cw = 512 if jg < 10 else 340
                dma_pool(w1s[sl][:, 0:0 + 8 * 512] if False else w1s[sl],
                         None, wkeys=()) if False else None
                src = iw1[l][:, c0:c0 + cw].rearrange("(c p) n -> p c n", p=128)
                dma_pool(w1s[sl].rearrange("p (c n) -> p c n", c=8)[:, :, 0:cw],
                         src, s_w1[sl], wkeys=(("w1s", sl),))
                for pi in range(2):
                    p = 2 * jg + pi
                    if p >= NPAIR:
                        continue
                    wdt = 128 if p < 21 else 42
                    la = 256 * pi if jg < 10 else 256 * pi
                    if jg < 10:
                        loa, log = 256 * pi, 256 * pi + 128
                    else:
                        loa, log = 256 * pi, 256 * pi + wdt
                    if first:
                        for ti, lo in ((2 * p, loa), (2 * p + 1, log)):
                            for kc in range(NCH):
                                mm(PS(6, wdt, 1), w1s[sl][:, 512 * kc + lo:512 * kc + lo + wdt],
                                   cs_bf(C_BLNB + kc // 2, 0)[:, 0:0] if False else
                                   cs_bf(C_BLNB, 8)[:, kc:kc + 1],
                                   start=(kc == 0), stop=(kc == NCH - 1),
                                   wbank=(6,),
                                   rkeys=(("w1s", sl), ("lv",)),
                                   inc=s_pe if kc == NCH - 1 else None)
                            v_stt(cs_f32(C_U1 + ti, 1)[0:wdt, :], PS(6, wdt, 1),
                                  1.0, cs_f32(C_B1 + ti, 1)[0:wdt, :], ALU.mult, ALU.add,
                                  rbanks=(6,), rkeys=(("lv",),), wkeys=(("u1", ti),))
                    for bb, lo in ((0, loa), (2, log)):
                        for nt in range(2):
                            bank = bb + nt
                            for kc in range(NCH):
                                mm(PS(bank, wdt, 512), w1s[sl][:, 512 * kc + lo:512 * kc + lo + wdt],
                                   xn[nt][:, 512 * kc:512 * (kc + 1)],
                                   start=(kc == 0), stop=(kc == NCH - 1),
                                   wbank=(bank,),
                                   rkeys=(("w1s", sl), ("xn", nt)),
                                   inc=s_pe if kc == NCH - 1 else None)
                    a_act(gel[0:wdt, :], ps_t.ap()[0:wdt, 1024:2048], AF.Gelu,
                          bias=cs_f32(C_U1 + 2 * p + 1, 1)[0:wdt, :],
                          rbanks=(2, 3), rkeys=(("u1", 2 * p + 1),), wkeys=(("gel",),))
                    v_stt(hbf[0:wdt, 1024 * p:1024 * (p + 1)], ps_t.ap()[0:wdt, 0:1024],
                          cs_f32(C_U1 + 2 * p, 1)[0:wdt, :], gel[0:wdt, :],
                          ALU.add, ALU.mult,
                          rbanks=(0, 1), rkeys=(("gel",), ("u1", 2 * p)), wkeys=(("h", p),))

        def emit_w2_half(hh):
            for nt in range(2):
                t0 = 1024 * hh + 512 * nt
                for jg in range(11):
                    sl = jg % 2
                    r0 = 256 * jg
                    rh = 256 if jg < 10 else 170
                    if jg < 10:
                        dma_pool(w2s[sl].rearrange("p (j m) -> p j m", j=2),
                                 iw2[l][r0:r0 + 256, :].rearrange("(j p) m -> p j m", p=128),
                                 s_w2[sl], wkeys=(("w2s", sl),))
                    else:
                        dma_pool(w2s[sl][0:128, 0:1024],
                                 iw2[l][2560:2688, :], s_w2[sl], wkeys=(("w2s", sl),))
                        dma_pool(w2s[sl][0:42, 1024:2048],
                                 iw2[l][2688:2730, :], s_w2[sl], wkeys=(("w2s", sl),))
                    for ji in range(2):
                        j = 2 * jg + ji
                        if j >= NJT:
                            continue
                        kj = 128 if j < 21 else 42
                        for mt in range(NCH):
                            mm(PS(mt, 128, 512),
                               w2s[sl][0:kj, 1024 * ji + 128 * mt:1024 * ji + 128 * (mt + 1)],
                               hbf[0:kj, 1024 * j + 512 * nt:1024 * j + 512 * nt + 512],
                               start=(j == 0), stop=(j == NJT - 1),
                               wbank=(mt,),
                               rkeys=(("w2s", sl), ("h", j)),
                               inc=s_pe if (ji == 1 and mt == NCH - 1) else None)
                for mt in range(NCH):
                    v_stt(x_sl(mt, t0 + 1, 512), PS(mt, 128, 512), cs_f32(C_B2 + mt, 1),
                          x_sl(mt, t0 + 1, 512), ALU.add, ALU.add,
                          rbanks=(mt,), rkeys=(("x", mt), ("lv",)), wkeys=(("x", mt),))

        emit_xn(0)
        emit_xn(1)
        emit_w1_half(0, first=True)
        emit_xn(2)
        emit_xn(3)
        emit_w2_half(0)
        emit_w1_half(1, first=False)
        emit_w2_half(1)
        barrier()

    def emit_final(b):
        P.mark("final")
        xst = A.f32(G_XST, 8 * 512)
        sqb = A.bf16(G_OS, 8 * 512)
        mu = A.f32(G_MU, 512)[0:1, :]
        m2 = A.f32(G_M2, 512)[0:1, :]
        tmst = A.f32(G_BC, 1024)
        for ss in range(4):
            t0 = 512 * ss
            dma_sync(xst.rearrange("p (c t) -> p c t", c=8),
                     x1sp[:, t0:t0 + 512].rearrange("(c p) t -> p c t", p=128),
                     s_x1r, waits=((s_st, s_st.n),),
                     wkeys=tuple(("fx1", ch) for ch in range(NCH)))
            for ch in range(NCH):
                v_tt(xst[:, 512 * ch:512 * (ch + 1)], xst[:, 512 * ch:512 * (ch + 1)],
                     x_sl(ch, t0 + 1, 512), ALU.add,
                     rkeys=(("x", ch), ("fx1", ch)), wkeys=(("fx1", ch),))
            for ch in range(NCH):
                a_act(sqb[:, 512 * ch:512 * (ch + 1)], xst[:, 512 * ch:512 * (ch + 1)],
                      AF.Square, rkeys=(("fx1", ch),), wkeys=(("fsq", ch),))
            for ch in range(NCH):
                mm(PS(4, 1, 512), cs_f32(C_ONESCF, 1), xst[:, 512 * ch:512 * (ch + 1)],
                   start=(ch == 0), stop=(ch == NCH - 1),
                   wbank=(4,), rkeys=(("fx1", ch),),
                   inc=s_pe if ch == NCH - 1 else None)
            for ch in range(NCH):
                mm(PS(5, 1, 512), cs_bf(C_ONESCB, 1), sqb[:, 512 * ch:512 * (ch + 1)],
                   start=(ch == 0), stop=(ch == NCH - 1),
                   wbank=(5,), rkeys=(("fsq", ch),),
                   inc=s_pe if ch == NCH - 1 else None)
            a_act(mu, PS(4, 1, 512), AF.Copy, scale=1.0 / DM, rbanks=(4,), wkeys=(("gmu",),))
            a_act(m2, PS(4, 1, 512), AF.Square, scale=1.0 / DM, rbanks=(4,), wkeys=(("gm2",),))
            v_stt(m2, PS(5, 1, 512), 1.0 / DM, m2, ALU.mult, ALU.subtract,
                  rbanks=(5,), rkeys=(("gm2",),), wkeys=(("gm2",),))
            a_act(m2, m2, AF.Ln, bias=cs_f32(C_EPS, 1)[0:1, :], rkeys=(("gm2",),), wkeys=(("gm2",),))
            a_act(m2, m2, AF.Exp, scale=-0.5, rkeys=(("gm2",),), wkeys=(("gm2",),))
            v_stt(mu, mu, -1.0, m2, ALU.mult, ALU.mult,
                  rkeys=(("gmu",), ("gm2",)), wkeys=(("gmu",),))
            mm(PS(4), csb.ap()[0:1, C_ONESR:C_ONESR + 128], m2, start=True, stop=True,
               wbank=(4,), rkeys=(("gm2",),), inc=s_pe)
            mm(PS(5), csb.ap()[0:1, C_ONESR:C_ONESR + 128], mu, start=True, stop=True,
               wbank=(5,), rkeys=(("gmu",),), inc=s_pe)
            for ch in range(NCH):
                v_tt(xst[:, 512 * ch:512 * (ch + 1)], xst[:, 512 * ch:512 * (ch + 1)],
                     PS(4), ALU.mult, rbanks=(4,), rkeys=(("fx1", ch),), wkeys=(("fx1", ch),))
                v_tt(xst[:, 512 * ch:512 * (ch + 1)], xst[:, 512 * ch:512 * (ch + 1)],
                     PS(5), ALU.add, rbanks=(5,), rkeys=(("fx1", ch),), wkeys=(("fx1", ch),))
            for tt in range(4):
                bank = tt % 2
                for ch in range(NCH):
                    transpose(PS(bank, 128, 128, 128 * (ch % 4), r0=0)
                              if False else
                              ps_t.ap()[:, 512 * (2 * bank + ch // 4) + 128 * (ch % 4):
                                        512 * (2 * bank + ch // 4) + 128 * (ch % 4) + 128],
                              xst[:, 512 * ch + 128 * tt:512 * ch + 128 * (tt + 1)],
                              2 * bank + ch // 4, rkey=("fx1", ch),
                              start_clear=(ch % 4 == 0))
                v_copy(tmst, ps_t.ap()[:, 1024 * bank:1024 * bank + 1024],
                       rbanks=(2 * bank, 2 * bank + 1), wkeys=(("ftm",),))
                dma_store(out[b, t0 + 128 * tt:t0 + 128 * (tt + 1), :], tmst,
                          waits=TK.wait_read(("ftm",)) + TK.wait_write(("ftm",)),
                          note_rkeys=())
                TK.note_read(("ftm",), s_st, s_st.n)
        barrier()

    for b in range(n_batches):
        emit_global(b)
        if dump_stage == "x1":
            xst = A.f32(G_XST, 8 * 512)
            tmst = A.f32(G_BC, 1024)
            for ss in range(4):
                t0 = 512 * ss
                dma_sync(xst.rearrange("p (c t) -> p c t", c=8),
                         x1sp[:, t0:t0 + 512].rearrange("(c p) t -> p c t", p=128),
                         s_x1r, waits=((s_st, s_st.n),),
                         wkeys=tuple(("dx", ch) for ch in range(NCH)))
                for tt in range(4):
                    bank = tt % 2
                    for ch in range(NCH):
                        transpose(ps_t.ap()[:, 512 * (2 * bank + ch // 4) + 128 * (ch % 4):
                                            512 * (2 * bank + ch // 4) + 128 * (ch % 4) + 128],
                                  xst[:, 512 * ch + 128 * tt:512 * ch + 128 * (tt + 1)],
                                  2 * bank + ch // 4, rkey=("dx", ch),
                                  start_clear=(ch % 4 == 0), stop=(ch % 4 == 3))
                    v_copy(tmst, ps_t.ap()[:, 1024 * bank:1024 * bank + 1024],
                           rbanks=(2 * bank, 2 * bank + 1), wkeys=(("ftm",),))
                    dma_store(out[b, t0 + 128 * tt:t0 + 128 * (tt + 1), :], tmst,
                              waits=TK.wait_read(("ftm",)) + [TK.w[("ftm",)]])
                    TK.note_read(("ftm",), s_st, s_st.n)
            barrier()
            continue
        P.mark("reload")
        for ch in range(NCH):
            v_memset(x_sl(ch, 0, 1), 0.0, wkeys=(("x", ch),))
        for ch in range(NCH):
            dma_sync(x_sl(ch, 1, 2048), x1sp[128 * ch:128 * (ch + 1), :],
                     s_x1r, wkeys=(("x", ch),), waits=((s_st, s_st.n),))
        barrier()
        for l in range(n_layers):
            emit_inner_attn(b, l)
            if dump_stage == ("L%da" % (l + 1)):
                break
            emit_inner_ffn(b, l)
            if dump_stage == ("L%d" % (l + 1)):
                break
        if dump_stage and dump_stage != "full" and dump_stage.startswith("L"):
            tmst = A.f32(G_BC, 1024)
            for ss in range(4):
                t0 = 512 * ss
                for tt in range(4):
                    bank = tt % 2
                    for ch in range(NCH):
                        transpose(ps_t.ap()[:, 512 * (2 * bank + ch // 4) + 128 * (ch % 4):
                                            512 * (2 * bank + ch // 4) + 128 * (ch % 4) + 128],
                                  x_sl(ch, t0 + 128 * tt + 1, 128),
                                  2 * bank + ch // 4, rkey=("x", ch),
                                  start_clear=(ch % 4 == 0), stop=(ch % 4 == 3))
                    v_copy(tmst, ps_t.ap()[:, 1024 * bank:1024 * bank + 1024],
                           rbanks=(2 * bank, 2 * bank + 1), wkeys=(("ftm",),))
                    dma_store(out[b, t0 + 128 * tt:t0 + 128 * (tt + 1), :], tmst,
                              waits=TK.wait_read(("ftm",)) + [TK.w[("ftm",)]])
                    TK.note_read(("ftm",), s_st, s_st.n)
            barrier()
        elif dump_stage is None or dump_stage == "full":
            emit_final(b)

    P.op("sync", None, waits=((s_st, s_st.n),))
    P.emit()
    return nc


def _prep_inputs(inputs):
    c = host_constants()
    pg = _perm_global_qk()
    pi = _perm_inner_qk()
    pw1, tcols = _w1_tile_cols()
    wq = np.ascontiguousarray(inputs["L0_attn_wq"].reshape(DM, DM)[:, pg], dtype=np.float32)
    wk = np.ascontiguousarray(inputs["L0_attn_wk"].reshape(DM, DM)[:, pg], dtype=np.float32)
    wv = np.ascontiguousarray(inputs["L0_attn_wv"].reshape(DM, DM), dtype=np.float32)
    wo = np.ascontiguousarray(inputs["L0_attn_wo"].reshape(DM, DM), dtype=np.float32)
    Wqkv = np.asarray(inputs["L0_ffn_Wqkv"], dtype=np.float32)
    Wqkv_p = np.concatenate([Wqkv[:, :, :DM][:, :, pi], Wqkv[:, :, DM:2 * DM][:, :, pi],
                             Wqkv[:, :, 2 * DM:]], axis=2)
    W1p = np.ascontiguousarray(np.asarray(inputs["L0_ffn_W1"], dtype=np.float32)[:, :, pw1])
    qs = np.asarray(inputs["L0_ffn_q_scale"], dtype=np.float32)
    ks = np.asarray(inputs["L0_ffn_k_scale"], dtype=np.float32)
    qscol_a = np.zeros((DEPTH, 128, 4), dtype=np.float32)
    for l in range(DEPTH):
        qscol_a[l, :, 0] = np.tile(qs[l, 0:32], 4)
        qscol_a[l, :, 1] = np.tile(qs[l, 32:64], 4)
        qscol_a[l, :, 2] = np.tile(ks[l, 0:32], 4)
        qscol_a[l, :, 3] = np.tile(ks[l, 32:64], 4)
    b1p = np.asarray(inputs["L0_ffn_b1"], dtype=np.float32)[:, pw1]
    b1c = np.zeros((DEPTH, 128, 2 * NPAIR), dtype=np.float32)
    off = 0
    tlist = []
    for p in range(NPAIR):
        w = 128 if p < 21 else 42
        tlist.append((off, w)); off += w
        tlist.append((off, w)); off += w
    for l in range(DEPTH):
        for ti, (o, w) in enumerate(tlist):
            b1c[l, :w, ti] = b1p[l, o:o + w]
    lng_c = np.zeros((DEPTH, 128, NCH), dtype=np.float32)
    lnb_c = np.zeros((DEPTH, 128, NCH), dtype=np.float32)
    b2_c = np.zeros((DEPTH, 128, NCH), dtype=np.float32)
    for l in range(DEPTH):
        lng_c[l] = np.asarray(inputs["L0_ffn_ln_g"][l], dtype=np.float32).reshape(NCH, 128).T
        lnb_c[l] = np.asarray(inputs["L0_ffn_ln_b"][l], dtype=np.float32).reshape(NCH, 128).T
        b2_c[l] = np.asarray(inputs["L0_ffn_b2"][l], dtype=np.float32).reshape(NCH, 128).T
    return {
        "gwq": wq, "gwk": wk, "gwv": wv, "gwo": wo,
        "iwqkv": np.ascontiguousarray(Wqkv_p),
        "iwo": np.ascontiguousarray(np.asarray(inputs["L0_ffn_Wo"], dtype=np.float32)),
        "iw1": W1p,
        "iw2": np.ascontiguousarray(np.asarray(inputs["L0_ffn_W2"], dtype=np.float32)),
        "qscol": qscol_a, "lng_col": lng_c, "lnb_col": lnb_c, "b1_col": b1c, "b2_col": b2_c,
        "tbl_g": c["tbl_g"], "tbl_i": c["tbl_i"], "identity": c["identity"],
        "trimask4": c["trimask4"], "trimask2": c["trimask2"], "tri1one": c["tri1one"],
        "eden": c["eden"], "enorm": c["enorm"], "fnorm": c["fnorm"], "fpair": c["fpair"],
        "ones_col_bf": c["ones_col_bf"], "ones_col_f32": c["ones_col_f32"],
        "ones_row_f32": c["ones_row_f32"], "eps_col": c["eps_col"],
    }


def kernel(**inputs):
    x = np.ascontiguousarray(inputs["L0_attn_x"], dtype=np.float32)
    base = _prep_inputs(inputs)
    nc = build_program()
    in_maps = []
    for cidx in range(NCORE):
        m = dict(base)
        m["x_in"] = np.ascontiguousarray(x[BLOC * cidx:BLOC * (cidx + 1)])
        in_maps.append(m)
    res = run_bass_kernel_spmd(nc, in_maps, core_ids=list(range(NCORE)))
    return np.concatenate([r["out"] for r in res.results], axis=0)


# version 05
# speedup vs baseline: 1.2810x; 1.2189x over previous
import contextlib
import numpy as np
import ml_dtypes
import concourse.bass as bass
import concourse.mybir as mybir
from concourse.bass_utils import run_bass_kernel_spmd

F32 = mybir.dt.float32
BF16 = mybir.dt.bfloat16
AF = mybir.ActivationFunctionType
ALU = mybir.AluOpType
bfloat16 = ml_dtypes.bfloat16

T = 2048
NCORE = 8
BLOC = 2
NCH = 8
DM = 1024
GH = 8
IH = 16
WIN = 256
NWIN = T // WIN
DEPTH = 6
DI2 = 5460
DI = 2730
NPAIR = 22
NJT = 22
EPS = 1e-5

ENGS = ("tensor", "vector", "scalar", "gpsimd", "sync")
DEBUG_BUILD = False
DEDUP_WAITS = True
ATTN_PARTS = ("qkv", "attn", "wo")


def _rope_tables_global():
    half = 64
    af = np.power(np.float32(10000.0), -(np.arange(half, dtype=np.float32) / np.float32(half)))
    pos = np.arange(T, dtype=np.float32)[:, None]
    rad = (pos * af[None, :]).astype(np.float32)
    cg = np.tile(np.cos(rad).astype(np.float32).T, (2, 1))
    sg = np.tile(np.sin(rad).astype(np.float32).T, (2, 1))
    return cg.astype(np.float32), sg.astype(np.float32)


def _rope_tables_inner():
    half = np.arange(0, 64, 2, dtype=np.float32)
    inv_freq = (np.float32(1.0) / np.power(np.float32(10000.0), half / np.float32(64))).astype(np.float32)
    t = np.arange(T, dtype=np.float32)
    freqs = (t[:, None] * inv_freq[None, :]).astype(np.float32)
    base_scale = ((half + np.float32(0.4 * 64)) / np.float32(1.4 * 64)).astype(np.float32)
    power = ((t - T // 2) / np.float32(WIN // 2)).astype(np.float32)
    scale = np.power(base_scale[None, :], power[:, None]).astype(np.float32)
    cos = np.cos(freqs).astype(np.float32)
    sin = np.sin(freqs).astype(np.float32)
    inv = (np.float32(1.0) / scale).astype(np.float32)

    def pat(m):
        return np.tile(m.T, (4, 1)).astype(np.float32)
    return pat(cos * scale), pat(sin * scale), pat(cos * inv), pat(sin * inv)


def _perm_global_qk():
    perm = np.zeros(1024, dtype=np.int64)
    for h in range(GH):
        for d in range(128):
            newc = (d // 64) * 512 + (h // 2) * 128 + (h % 2) * 64 + (d % 64)
            perm[newc] = h * 128 + d
    return perm


def _perm_inner_qk():
    perm = np.zeros(1024, dtype=np.int64)
    for h in range(IH):
        for d in range(64):
            newc = (d // 32) * 512 + (h // 4) * 128 + (h % 4) * 32 + (d % 32)
            perm[newc] = h * 64 + d
    return perm


def _w1_tile_cols():
    idx = []
    tcols = []
    off = 0
    for p in range(NPAIR):
        w = 128 if p < 21 else 42
        idx.extend(range(128 * p, 128 * p + w))
        tcols.append((off, w)); off += w
        idx.extend(range(DI + 128 * p, DI + 128 * p + w))
        tcols.append((off, w)); off += w
    return np.array(idx, dtype=np.int64), tcols


def host_constants():
    c = {}
    cg, sg = _rope_tables_global()
    c["tbl_g"] = np.stack([cg, sg])
    csq, ssq, csk, ssk = _rope_tables_inner()
    c["tbl_i"] = np.stack([csq, ssq, csk, ssk])
    c["identity"] = np.eye(128, dtype=np.float32)
    p = np.arange(128)[:, None]
    f = np.arange(128)[None, :]
    tri1 = (p <= f).astype(bfloat16)
    tri2 = (f <= p).astype(bfloat16)
    c["trimask4"] = np.concatenate([tri2, tri2, tri1, tri1], axis=1).astype(bfloat16)
    c["trimask2"] = np.concatenate([tri1, tri1], axis=1).astype(bfloat16)
    c["tri1one"] = tri1
    eden = np.zeros((128, 4), dtype=bfloat16)
    eden[:, 0] = bfloat16(1.0)
    eden[:, 3] = bfloat16(1.0)
    c["eden"] = eden
    en = np.zeros((4, 128, 16), dtype=bfloat16)
    for g in range(4):
        for q in range(128):
            en[g, q, 4 * g + q // 32] = bfloat16(1.0)
    c["enorm"] = en
    fn = np.zeros((4, 16, 128), dtype=np.float32)
    for g in range(4):
        for j in range(4):
            fn[g, 4 * g + j, 32 * j:32 * j + 32] = 1.0
    c["fnorm"] = fn
    fp = np.zeros((2, 128), dtype=np.float32)
    fp[0, 0:64] = 1.0
    fp[1, 64:128] = 1.0
    c["fpair"] = fp
    c["ones_col_bf"] = np.ones((128, 1), dtype=bfloat16)
    c["eps_col"] = np.full((128, 1), EPS, dtype=np.float32)
    c["ones_col_f32"] = np.ones((128, 1), dtype=np.float32)
    c["ones_row_f32"] = np.ones((1, 128), dtype=np.float32)
    return c


class Sem:
    __slots__ = ("h", "n", "name", "last_eng", "is_dma", "engs")

    def __init__(self, h, name, is_dma=False):
        self.h = h; self.n = 0; self.name = name
        self.last_eng = None; self.is_dma = is_dma
        self.engs = set()


class Plan:
    def __init__(self, nc, stack):
        self.nc = nc
        self.q = {e: [] for e in ENGS}
        self.maxwait = {e: {} for e in ENGS}
        self.stack = stack
        self.phase = "init"

    def mark(self, name):
        self.phase = name

    def sem(self, name, is_dma=False):
        return Sem(self.stack.enter_context(self.nc.semaphore(name)), name, is_dma)

    def op(self, eng, fn, waits=(), inc=None, inc_n=1):
        real = []
        mw = self.maxwait[eng]
        for s, v in waits:
            if s is None or v is None or v <= 0:
                continue
            if DEDUP_WAITS:
                if (not s.is_dma) and s.engs == {eng} and v <= s.n:
                    continue
                if mw.get(id(s), 0) >= v:
                    continue
                mw[id(s)] = v
            real.append((s, v))
        val = None
        if inc is not None:
            inc.n += inc_n
            inc.last_eng = eng
            inc.engs.add(eng)
            val = inc.n
        self.q[eng].append((tuple(real), fn, inc, inc_n, self.phase))
        return val

    def dma(self, eng, fn, waits=(), inc=None):
        return self.op(eng, fn, waits, inc, inc_n=16)

    def barrier(self, bar, dma_sems=()):
        for e in ENGS:
            self.op(e, lambda E: E.nop(), inc=bar)
        tgt = bar.n
        wl = tuple((s, s.n) for s in dma_sems)
        for e in ENGS:
            self.op(e, None, waits=((bar, tgt),) + wl)

    def emit(self):
        phase_of = self.nc._phase_of_inst = {}

        def body(name):
            def run(E):
                for waits, fn, inc, inc_n, phase in self.q[name]:
                    for s, v in waits:
                        E.wait_ge(s.h, v)
                    if fn is None:
                        continue
                    n0 = self.nc.n_instructions()
                    inst = fn(E)
                    phase_of[n0] = (phase, name)
                    if inc is not None:
                        inst.then_inc(inc.h, inc_n)
            return run
        with self.nc.Block() as block:
            block.tensor(body("tensor"))
            block.vector(body("vector"))
            block.scalar(body("scalar"))
            block.gpsimd(body("gpsimd"))
            block.sync(body("sync"))


class Tracker:
    def __init__(self):
        self.w = {}
        self.r = {}
        self.rb = {}

    def wait_read_bank(self, key, eng):
        out = list(self.wait_read(key))
        last = self.rb.get(key)
        if last is not None and last[0] != eng:
            out.append((last[1], last[2]))
        return out

    def note_read_bank(self, key, eng, sem, val):
        if val is None:
            return
        self.rb[key] = (eng, sem, val)
        self.note_read(key, sem, val)

    def wait_read(self, *keys):
        return [self.w[k] for k in keys if k in self.w]

    def wait_write(self, *keys):
        out = []
        for k in keys:
            if k in self.w:
                out.append(self.w[k])
            for s, v in self.r.get(k, {}).values():
                out.append((s, v))
        return out

    def note_write(self, key, sem, val):
        if val is None:
            return
        self.w[key] = (sem, val)
        self.r.pop(key, None)
        self.rb.pop(key, None)

    def note_read(self, key, sem, val):
        if val is None:
            return
        d = self.r.setdefault(key, {})
        old = d.get(id(sem))
        if old is None or old[1] < val:
            d[id(sem)] = (sem, val)


ARENA_BYTES = 195 * 1024


class Arena:
    def __init__(self, t):
        self.t = t

    def f32(self, off, n):
        assert off % 4 == 0 and off + 4 * n <= ARENA_BYTES, (off, n)
        return self.t.ap()[:, off // 4: off // 4 + n]

    def bf16(self, off, n):
        assert off % 4 == 0 and off + 2 * n <= ARENA_BYTES, (off, n)
        m = (n + 1) // 2
        return self.t.ap()[:, off // 4: off // 4 + m].bitcast(BF16)[:, :n]


def build_program(n_layers=DEPTH, n_batches=BLOC, dump_stage=None):
    nc = bass.Bass("TRN2", target_bir_lowering=False, debug=DEBUG_BUILD,
                   detect_race_conditions=False)
    stack = contextlib.ExitStack()
    nc._dev_stack = stack

    x_in = nc.dram_tensor("x_in", [BLOC, T, DM], F32, kind="ExternalInput").ap()
    out = nc.dram_tensor("out", [BLOC, T, DM], F32, kind="ExternalOutput").ap()
    gwq = nc.dram_tensor("gwq", [DM, DM], F32, kind="ExternalInput").ap()
    gwk = nc.dram_tensor("gwk", [DM, DM], F32, kind="ExternalInput").ap()
    gwv = nc.dram_tensor("gwv", [DM, DM], F32, kind="ExternalInput").ap()
    gwo = nc.dram_tensor("gwo", [DM, DM], F32, kind="ExternalInput").ap()
    iwqkv = nc.dram_tensor("iwqkv", [DEPTH, DM, 3 * DM], F32, kind="ExternalInput").ap()
    iwo = nc.dram_tensor("iwo", [DEPTH, DM, DM], F32, kind="ExternalInput").ap()
    iw1 = nc.dram_tensor("iw1", [DEPTH, DM, DI2], F32, kind="ExternalInput").ap()
    iw2 = nc.dram_tensor("iw2", [DEPTH, DI, DM], F32, kind="ExternalInput").ap()
    qscol = nc.dram_tensor("qscol", [DEPTH, 128, 4], F32, kind="ExternalInput").ap()
    lng_col = nc.dram_tensor("lng_col", [DEPTH, 128, NCH], F32, kind="ExternalInput").ap()
    lnb_col = nc.dram_tensor("lnb_col", [DEPTH, 128, NCH], F32, kind="ExternalInput").ap()
    b1_col = nc.dram_tensor("b1_col", [DEPTH, 128, 2 * NPAIR], F32, kind="ExternalInput").ap()
    b2_col = nc.dram_tensor("b2_col", [DEPTH, 128, NCH], F32, kind="ExternalInput").ap()
    tbl_g = nc.dram_tensor("tbl_g", [2, 128, T], F32, kind="ExternalInput").ap()
    tbl_i = nc.dram_tensor("tbl_i", [4, 128, T], F32, kind="ExternalInput").ap()
    identity_d = nc.dram_tensor("identity", [128, 128], F32, kind="ExternalInput").ap()
    trimask4_d = nc.dram_tensor("trimask4", [128, 512], BF16, kind="ExternalInput").ap()
    trimask2_d = nc.dram_tensor("trimask2", [128, 256], BF16, kind="ExternalInput").ap()
    tri1_d = nc.dram_tensor("tri1one", [128, 128], BF16, kind="ExternalInput").ap()
    eden_d = nc.dram_tensor("eden", [128, 4], BF16, kind="ExternalInput").ap()
    enorm_d = nc.dram_tensor("enorm", [4, 128, 16], BF16, kind="ExternalInput").ap()
    fnorm_d = nc.dram_tensor("fnorm", [4, 16, 128], F32, kind="ExternalInput").ap()
    fpair_d = nc.dram_tensor("fpair", [2, 128], F32, kind="ExternalInput").ap()
    onescb_d = nc.dram_tensor("ones_col_bf", [128, 1], BF16, kind="ExternalInput").ap()
    onescf_d = nc.dram_tensor("ones_col_f32", [128, 1], F32, kind="ExternalInput").ap()
    onesr_d = nc.dram_tensor("ones_row_f32", [1, 128], F32, kind="ExternalInput").ap()
    eps_d = nc.dram_tensor("eps_col", [128, 1], F32, kind="ExternalInput").ap()
    x1sp = nc.dram_tensor("x1sp", [DM, T], F32).ap()

    arena_t = stack.enter_context(nc.sbuf_tensor("arena", [128, ARENA_BYTES // 4], F32))
    A = Arena(arena_t)
    csb = stack.enter_context(nc.sbuf_tensor("csb", [128, 3072], F32))

    C_ID = 0
    C_TRIM4 = 128
    C_TRIM2 = 384
    C_TRI1 = 512
    C_EDEN = 576
    C_ENORM = 704
    C_FNORM = 736
    C_FPAIR = 1248
    C_ONESCB = 2272
    C_ONESCF = 2273
    C_ONESR = 2274
    C_EPS = 2528
    C_EPS2 = 2529
    C_QS = 2402
    C_GCOL = 2408
    C_BLNF = 2416
    C_BLNB = 2424
    C_B1 = 2428
    C_B2 = 2472
    C_U1 = 2480
    C_END = 2530
    assert C_END <= 3072

    def cs_f32(off, n):
        return csb.ap()[:, off:off + n]

    def cs_bf(off, n):
        m = (n + 1) // 2
        return csb.ap()[:, off:off + m].bitcast(BF16)[:, :n]

    P = Plan(nc, stack)
    TK = Tracker()
    s_bar = P.sem("bar")
    s_pe = P.sem("pe")
    s_dve = P.sem("dve")
    s_act = P.sem("act")
    s_pool = P.sem("pool")
    s_ld = P.sem("ld", is_dma=True)
    s_x = P.sem("x", is_dma=True)
    s_tb0 = P.sem("tb0", is_dma=True)
    s_tb1 = P.sem("tb1", is_dma=True)
    s_wA = P.sem("wA", is_dma=True)
    s_wB = P.sem("wB", is_dma=True)
    s_w1 = [P.sem("w1a", is_dma=True), P.sem("w1b", is_dma=True)]
    s_w2 = [P.sem("w2a", is_dma=True), P.sem("w2b", is_dma=True)]
    s_vec = P.sem("vec", is_dma=True)
    s_x1r = P.sem("x1r", is_dma=True)
    s_st = P.sem("st", is_dma=True)
    ALL_DMA_SEMS = (s_ld, s_x, s_tb0, s_tb1, s_wA, s_wB, s_w1[0], s_w1[1],
                    s_w2[0], s_w2[1], s_vec, s_x1r, s_st)
    s_tbs = (s_tb0, s_tb1)

    ps_t = stack.enter_context(nc.psum_tensor("ps", [128, 4096], F32))

    def PS(bank, rows=128, cols=512, coff=0, r0=0):
        return ps_t.ap()[r0:r0 + rows, 512 * bank + coff: 512 * bank + coff + cols]

    pend_r = []
    pend_w = []

    def mm(psum_ap, lhs_ap, rhs_ap, start, stop, wbank=(), rkeys=(), waits=(), inc=None,
           tile_position=None):
        wl = list(waits)
        if start:
            for b in wbank:
                wl += TK.wait_write(("ps", b))
        for k in rkeys:
            wl += TK.wait_read(k)
        pend_r.extend(rkeys)
        pend_w.extend(("ps", b) for b in wbank)

        def f(E, a=psum_ap, b=lhs_ap, c=rhs_ap, st=start, sp=stop, tp=tile_position):
            return E.matmul(a, b, c, start=st, stop=sp, tile_position=tp)
        v = P.op("tensor", f, waits=wl, inc=inc)
        if inc is not None:
            for k in pend_w:
                TK.note_write(k, s_pe, v)
            for k in pend_r:
                TK.note_read(k, s_pe, v)
            pend_r.clear()
            pend_w.clear()
        return v

    def generic(eng, sem, fn, rkeys=(), wkeys=(), rbanks=(), wbanks=(), waits=(), inc=True):
        wl = list(waits)
        for k in rkeys:
            wl += TK.wait_read(k)
        for b in rbanks:
            wl += TK.wait_read_bank(("ps", b), eng)
        for k in wkeys:
            wl += TK.wait_write(k)
        for b in wbanks:
            wl += TK.wait_write(("ps", b))
        v = P.op(eng, fn, waits=wl, inc=sem if inc else None)
        if inc:
            for k in rkeys:
                TK.note_read(k, sem, v)
            for b in rbanks:
                TK.note_read_bank(("ps", b), eng, sem, v)
            for k in wkeys:
                TK.note_write(k, sem, v)
            for b in wbanks:
                TK.note_write(("ps", b), sem, v)
        return v

    def v_tt(o, a, b, op, **kw):
        return generic("vector", s_dve, lambda E: E.tensor_tensor(o, a, b, op), **kw)

    def v_stt(o, a, sc, b, op0, op1, **kw):
        return generic("vector", s_dve, lambda E: E.scalar_tensor_tensor(o, a, sc, b, op0, op1), **kw)

    def v_ts(o, a, s1, s2, op0, op1=None, **kw):
        if op1 is None:
            return generic("vector", s_dve, lambda E: E.tensor_scalar(o, a, s1, None, op0), **kw)
        return generic("vector", s_dve, lambda E: E.tensor_scalar(o, a, s1, s2, op0, op1), **kw)

    def v_copy(o, a, **kw):
        return generic("vector", s_dve, lambda E: E.tensor_copy(o, a), **kw)

    def v_recip(o, a, **kw):
        return generic("vector", s_dve, lambda E: E.reciprocal(o, a), **kw)

    def v_memset(o, val, **kw):
        return generic("vector", s_dve, lambda E: E.memset(o, val), **kw)

    def a_act(o, a, func, bias=None, scale=None, accum_out=None, **kw):
        def f(E):
            kk = {}
            if bias is not None:
                kk["bias"] = bias
            if scale is not None:
                kk["scale"] = scale
            if accum_out is not None:
                kk["accum_out"] = accum_out
            return E.activation(o, a, func, **kk)
        return generic("scalar", s_act, f, **kw)

    def p_tt(o, a, b, op, **kw):
        return generic("gpsimd", s_pool, lambda E: E.tensor_tensor(o, a, b, op), **kw)

    def _dma(eng, sem, o, a, **kw):
        wl = list(kw.pop("waits", ()))
        rkeys = kw.pop("rkeys", ())
        wkeys = kw.pop("wkeys", ())
        for k in rkeys:
            wl += TK.wait_read(k)
        for k in wkeys:
            wl += TK.wait_write(k)
        v = P.dma(eng, lambda E: E.dma_start(out=o, in_=a), waits=wl, inc=sem)
        for k in rkeys:
            TK.note_read(k, sem, v)
        for k in wkeys:
            TK.note_write(k, sem, v)
        return v

    def dma_sync(o, a, sem, **kw):
        return _dma("sync", sem, o, a, **kw)

    def dma_pool(o, a, sem, **kw):
        return _dma("gpsimd", sem, o, a, **kw)

    def dma_store(o, a, **kw):
        wl = list(kw.pop("waits", ()))
        for k in kw.pop("rkeys", ()):
            wl += TK.wait_read(k)
        v = P.dma("sync", lambda E: E.dma_start(out=o, in_=a), waits=wl, inc=s_st)
        for k in kw.get("note_rkeys", ()):
            TK.note_read(k, s_st, v)
        return v

    def transpose(psum_ap, in_ap, bank, rkey=None, waits=(), start_clear=True,
                  stop=False):
        wl = list(waits)
        if start_clear:
            wl += TK.wait_write(("ps", bank))
        if rkey is not None:
            wl += TK.wait_read(rkey)
        ident = cs_f32(C_ID, 128)

        def f(E, o=psum_ap, i=in_ap, idn=ident, sp=stop):
            return E.matmul(o, i, idn, start=True, stop=sp, is_transpose=True,
                            skip_group_check=True)
        v = P.op("tensor", f, waits=wl, inc=s_pe)
        pend_w.append(("ps", bank))
        if rkey is not None:
            pend_r.append(rkey)
        for k in pend_w:
            TK.note_write(k, s_pe, v)
        for k in pend_r:
            TK.note_read(k, s_pe, v)
        pend_r.clear()
        pend_w.clear()
        return v

    def barrier():
        P.barrier(s_bar, dma_sems=ALL_DMA_SEMS)
        TK.w.clear()
        TK.r.clear()
        TK.rb.clear()

    dma_sync(cs_f32(C_ID, 128), identity_d, s_ld)
    dma_sync(cs_bf(C_TRIM4, 512), trimask4_d, s_ld)
    dma_sync(cs_bf(C_TRIM2, 256), trimask2_d, s_ld)
    dma_sync(cs_bf(C_TRI1, 128), tri1_d, s_ld)
    dma_sync(cs_bf(C_EDEN, 4), eden_d, s_ld)
    dma_sync(cs_bf(C_ENORM, 64).rearrange("p (g n) -> p g n", g=4),
             enorm_d.rearrange("g p n -> p g n"), s_ld)
    dma_sync(csb.ap()[0:16, C_FNORM:C_FNORM + 512].rearrange("p (g n) -> p g n", g=4),
             fnorm_d.rearrange("g p n -> p g n"), s_ld)
    dma_sync(csb.ap()[0:2, C_FPAIR:C_FPAIR + 128], fpair_d, s_ld)
    dma_sync(cs_bf(C_ONESCB, 1), onescb_d, s_ld)
    dma_sync(cs_f32(C_ONESCF, 1), onescf_d, s_ld)
    dma_sync(csb.ap()[0:1, C_ONESR:C_ONESR + 128], onesr_d, s_ld)
    dma_sync(cs_f32(C_EPS, 1), eps_d, s_ld)
    barrier()
    v_ts(cs_f32(C_EPS2, 1), cs_f32(C_EPS, 1), 1e-19, None, ALU.mult,
         wkeys=(("eps2",),))
    barrier()

    G_XBF, G_KH, G_VH = 0, 32768, 65536
    G_W1, G_W2 = 98304, 114688
    G_TBL = 131072
    G_TT = 139264
    G_QS = 147456
    G_PT = 155648
    G_XST = 159744
    G_OS = 176128
    G_BC = 184320
    G_RC = 186368
    G_MU = 188416
    G_M2 = 190464
    G_STG = 192512
    assert G_STG + 2048 <= ARENA_BYTES

    def xg_bf(ch, t0, n):
        return A.bf16(G_XBF + 2 * 2048 * ch, 2048)[:, t0:t0 + n]

    def khat_g(r0, rows, ch, c0, n):
        return A.bf16(G_KH + 2 * 2048 * ch, 2048)[r0:r0 + rows, c0:c0 + n]

    def vhat_g(tt, c0, n):
        return A.bf16(G_VH + 2 * 1024 * tt, 1024)[:, c0:c0 + n]

    def rope_pass_global(bA, bB, outA, outB, tb_c, tb_s, t1b, t2b, keys):
        for cc, ub, us, sgn, o in ((0, bA, bB, -1.0, outA), (1, bB, bA, 1.0, outB)):
            v_tt(t1b, PS(ub), tb_c, ALU.mult,
                 rbanks=(ub,), rkeys=(keys["tbl"],), wkeys=(keys["t1"],))
            v_stt(t2b, PS(us), sgn, tb_s, ALU.mult, ALU.mult,
                  rbanks=(us,), rkeys=(keys["tbl"],), wkeys=(keys["t2"],))
            p_tt(o, t1b, t2b, ALU.add,
                 rkeys=(keys["t1"], keys["t2"]), wkeys=(keys["out"],))

    def emit_global(b):
        P.mark("g0")
        stg = A.f32(G_STG, 512)
        for ch in range(NCH):
            for tg in range(4):
                src = x_in[b, 512 * tg:512 * (tg + 1), 128 * ch:128 * (ch + 1)] \
                    .rearrange("(tt p) c -> p tt c", p=128)
                dma_sync(stg.rearrange("p (tt c) -> p tt c", tt=4), src, s_x, wkeys=(("gstg",),))
                bank = tg % 2
                for i in range(4):
                    transpose(PS(bank, 128, 128, 128 * i), stg[:, 128 * i:128 * (i + 1)],
                              bank, rkey=("gstg",), start_clear=(i == 0), stop=(i == 3))
                v_copy(xg_bf(ch, 512 * tg, 512), PS(bank), rbanks=(bank,),
                       wkeys=(("gxbf", ch, tg),))
        barrier()

        P.mark("g1")
        wk_bf = A.bf16(G_W1, 8 * 1024)
        wv_bf = A.bf16(G_W2, 8 * 1024)
        dma_pool(wk_bf.rearrange("p (c n) -> p c n", c=8), gwk.rearrange("(c p) n -> p c n", p=128), s_wA, wkeys=(("gw1",),))
        dma_pool(wv_bf.rearrange("p (c n) -> p c n", c=8), gwv.rearrange("(c p) n -> p c n", p=128), s_wB, wkeys=(("gw2",),))
        tbc = [A.f32(G_TBL + 4096 * i, 512) for i in range(2)]
        tbs = [A.f32(G_TBL + 2048 + 4096 * i, 512) for i in range(2)]
        t1b = [A.f32(G_TT + 4096 * i, 512) for i in range(2)]
        t2b = [A.f32(G_TT + 2048 + 4096 * i, 512) for i in range(2)]
        for ss in range(4):
            t0 = 512 * ss
            sl = ss % 2
            dma_sync(tbc[sl], tbl_g[0][:, t0:t0 + 512], s_tbs[sl], wkeys=(("gtb", sl),))
            dma_sync(tbs[sl], tbl_g[1][:, t0:t0 + 512], s_tbs[sl], wkeys=(("gtb", sl),))
            for g in range(4):
                for cc, bank in ((g, 4), (4 + g, 5)):
                    for kc in range(NCH):
                        mm(PS(bank), wk_bf[:, 1024 * kc + 128 * cc:1024 * kc + 128 * cc + 128],
                           xg_bf(kc, t0, 512), start=(kc == 0), stop=(kc == NCH - 1),
                           wbank=(bank,), rkeys=(("gw1",), ("gxbf", kc, ss)),
                           inc=s_pe if kc == NCH - 1 else None)
                rope_pass_global(4, 5,
                                 khat_g(0, 128, g, t0, 512), khat_g(0, 128, 4 + g, t0, 512),
                                 tbc[sl], tbs[sl], t1b[sl], t2b[sl],
                                 {"t1": ("gt1", sl), "t2": ("gt2", sl),
                                  "tbl": ("gtb", sl), "out": ("gkh",)})
            for tt in range(4 * ss, 4 * ss + 4):
                for nt in range(2):
                    bank = 6 + nt
                    for kc in range(NCH):
                        mm(PS(bank), xg_bf(kc, 128 * tt - t0 + t0, 128) if False else
                           xg_bf(kc, 128 * tt, 128),
                           wv_bf[:, 1024 * kc + 512 * nt:1024 * kc + 512 * nt + 512],
                           start=(kc == 0), stop=(kc == NCH - 1),
                           wbank=(bank,), rkeys=(("gw2",), ("gxbf", kc, ss)),
                           inc=s_pe if kc == NCH - 1 else None)
                    v_copy(vhat_g(tt, 512 * nt, 512), PS(bank), rbanks=(bank,),
                           wkeys=(("gvh",),))
        barrier()

        P.mark("g2")
        wq_bf = A.bf16(G_W1, 8 * 1024)
        wo_bf = A.bf16(G_W2, 8 * 1024)
        dma_pool(wq_bf.rearrange("p (c n) -> p c n", c=8), gwq.rearrange("(c p) n -> p c n", p=128), s_wA, wkeys=(("gw1",),))
        dma_pool(wo_bf.rearrange("p (c n) -> p c n", c=8), gwo.rearrange("(c p) n -> p c n", p=128), s_wB, wkeys=(("gw2",),))
        qsb = A.bf16(G_QS, 8 * 512)
        ptb = A.bf16(G_PT, 2048)
        xst = A.f32(G_XST, 8 * 512)
        sqb = A.bf16(G_OS, 8 * 512)
        obuf = A.bf16(G_OS, 8 * 512)
        bce = A.f32(G_BC, 512)
        rcp = A.f32(G_RC, 512)[0:1, :]
        mu = A.f32(G_MU, 512)[0:1, :]
        m2 = A.f32(G_M2, 512)[0:1, :]
        for ss in range(4):
            t0 = 512 * ss
            sl = ss % 2
            dma_sync(tbc[sl], tbl_g[0][:, t0:t0 + 512], s_tbs[sl], wkeys=(("gtb", sl),))
            dma_sync(tbs[sl], tbl_g[1][:, t0:t0 + 512], s_tbs[sl], wkeys=(("gtb", sl),))
            for g in range(4):
                for cc, bank in ((g, 4), (4 + g, 5)):
                    for kc in range(NCH):
                        mm(PS(bank), wq_bf[:, 1024 * kc + 128 * cc:1024 * kc + 128 * cc + 128],
                           xg_bf(kc, t0, 512), start=(kc == 0), stop=(kc == NCH - 1),
                           wbank=(bank,), rkeys=(("gw1",),),
                           inc=s_pe if kc == NCH - 1 else None)
                rope_pass_global(4, 5,
                                 A.bf16(G_QS + 2 * 512 * g, 512),
                                 A.bf16(G_QS + 2 * 512 * (4 + g), 512),
                                 tbc[sl], tbs[sl], t1b[sl], t2b[sl],
                                 {"t1": ("gt1", sl), "t2": ("gt2", sl),
                                  "tbl": ("gtb", sl), "out": ("gqs",)})
            for ch in range(NCH):
                src = x_in[b, t0:t0 + 512, 128 * ch:128 * (ch + 1)] \
                    .rearrange("(tt p) c -> p tt c", p=128)
                dma_sync(stg.rearrange("p (tt c) -> p tt c", tt=4), src, s_x, wkeys=(("gstg",),))
                for i in range(4):
                    transpose(PS(6, 128, 128, 128 * i), stg[:, 128 * i:128 * (i + 1)],
                              6, rkey=("gstg",), start_clear=(i == 0), stop=(i == 3))
                v_copy(xst[:, 512 * ch:512 * (ch + 1)], PS(6), rbanks=(6,),
                       wkeys=(("gxst", ch),))
            nkt = 4 * (ss + 1)
            for h in range(GH):
                r0 = 64 * (h % 2)
                cA, cB = h // 2, 4 + h // 2
                pv_started = False
                dn_started = False
                for ktg in range(0, nkt, 4):
                    gsz = min(4, nkt - ktg)
                    for i in range(gsz):
                        kt = ktg + i
                        qlo = max(0, 128 * kt - t0)
                        nq = 512 - qlo
                        for half, cc in ((0, cA), (1, cB)):
                            mm(PS(i, 128, nq, qlo),
                               khat_g(r0, 64, cc, 128 * kt, 128),
                               A.bf16(G_QS + 2 * 512 * cc, 512)[r0:r0 + 64, qlo:qlo + nq],
                               start=(half == 0), stop=(half == 1),
                               wbank=(i,), rkeys=(("gkh",), ("gqs",)),
                               tile_position=(r0, 0),
                               inc=s_pe if half == 1 else None)
                    if 128 * (ktg + gsz) <= t0:
                        a_act(ptb[:, 0:512 * gsz], ps_t.ap()[:, 0:512 * gsz], AF.Exp,
                              scale=1.0 / 128.0,
                              rbanks=tuple(range(gsz)), wkeys=(("gpt",),))
                    else:
                        for i in range(gsz):
                            kt = ktg + i
                            qlo = max(0, 128 * kt - t0)
                            a_act(ptb[:, 512 * i + qlo:512 * (i + 1)],
                                  PS(i, 128, 512 - qlo, qlo), AF.Exp, scale=1.0 / 128.0,
                                  rbanks=(i,), wkeys=(("gpt",),))
                            if 128 * kt >= t0:
                                v_tt(ptb[:, 512 * i + qlo:512 * i + qlo + 128],
                                     ptb[:, 512 * i + qlo:512 * i + qlo + 128],
                                     cs_bf(C_TRI1, 128), ALU.mult,
                                     rkeys=(("gpt",),), wkeys=(("gpt",),))
                    for i in range(gsz):
                        kt = ktg + i
                        qlo = max(0, 128 * kt - t0)
                        nq = 512 - qlo
                        last = (kt == nkt - 1)
                        mm(PS(6, 1, nq, qlo), cs_bf(C_ONESCB, 1),
                           ptb[:, 512 * i + qlo:512 * (i + 1)],
                           start=not dn_started, stop=last,
                           wbank=(6,), rkeys=(("gpt",),),
                           inc=s_pe if last else None)
                        dn_started = True
                        mm(PS(7, 128, nq, qlo), vhat_g(kt, 128 * h, 128),
                           ptb[:, 512 * i + qlo:512 * (i + 1)],
                           start=not pv_started, stop=last,
                           wbank=(7,),
                           rkeys=(("gvh",), ("gpt",)),
                           inc=s_pe if last else None)
                        pv_started = True
                a_act(rcp, PS(6, 1, 512), AF.Ln, rbanks=(6,), wkeys=(("grcp",),))
                a_act(rcp, rcp, AF.Exp, scale=-1.0, rkeys=(("grcp",),), wkeys=(("grcp",),))
                mm(PS(5), csb.ap()[0:1, C_ONESR:C_ONESR + 128], rcp, start=True, stop=True,
                   wbank=(5,), rkeys=(("grcp",),), inc=s_pe)
                v_copy(bce, PS(5), rbanks=(5,), wkeys=(("gbce",),))
                v_tt(obuf[:, 512 * h:512 * (h + 1)], PS(7), bce, ALU.mult,
                     rbanks=(7,), rkeys=(("gbce",),), wkeys=(("gos", h),))
            for mt in range(NCH):
                for hc in range(NCH):
                    mm(PS(mt), wo_bf[:, 1024 * hc + 128 * mt:1024 * hc + 128 * mt + 128],
                       obuf[:, 512 * hc:512 * (hc + 1)],
                       start=(hc == 0), stop=(hc == NCH - 1),
                       wbank=(mt,), rkeys=(("gw2",), ("gos", hc)),
                       inc=s_pe if hc == NCH - 1 else None)
                v_tt(xst[:, 512 * mt:512 * (mt + 1)], PS(mt), xst[:, 512 * mt:512 * (mt + 1)],
                     ALU.add, rbanks=(mt,), rkeys=(("gxst", mt),), wkeys=(("gxst", mt),))
            if getattr(build_program, "_dbg", False) and ss in (0,):
                def _dbgx(E, t=xst, tag=(b, ss)):
                    import numpy as _np
                    E.dbg_aps(lambda a, g, tg=tag: print("XSTDBG", tg, _np.asarray(a), _np.asarray(g)),
                              t[0:1, 0:4], t[5:6, 512 * 3:512 * 3 + 4])
                    return E.nop()
                P.op("vector", _dbgx,
                     waits=sum((TK.wait_read(("gxst", c)) + TK.wait_write(("gxst", c))
                                for c in range(NCH)), []))
            for ch in range(NCH):
                a_act(sqb[:, 512 * ch:512 * (ch + 1)], xst[:, 512 * ch:512 * (ch + 1)],
                      AF.Square, rkeys=(("gxst", ch),), wkeys=(("gos", ch),))
            for ch in range(NCH):
                mm(PS(4, 1, 512), cs_f32(C_ONESCF, 1), xst[:, 512 * ch:512 * (ch + 1)],
                   start=(ch == 0), stop=(ch == NCH - 1),
                   wbank=(4,), rkeys=(("gxst", ch),),
                   inc=s_pe if ch == NCH - 1 else None)
            for ch in range(NCH):
                mm(PS(5, 1, 512), cs_bf(C_ONESCB, 1), sqb[:, 512 * ch:512 * (ch + 1)],
                   start=(ch == 0), stop=(ch == NCH - 1),
                   wbank=(5,), rkeys=(("gos", ch),),
                   inc=s_pe if ch == NCH - 1 else None)
            a_act(mu, PS(4, 1, 512), AF.Copy, scale=1.0 / DM, rbanks=(4,), wkeys=(("gmu",),))
            a_act(m2, PS(4, 1, 512), AF.Square, scale=1.0 / DM, rbanks=(4,), wkeys=(("gm2",),))
            v_stt(m2, PS(5, 1, 512), 1.0 / DM, m2, ALU.mult, ALU.subtract,
                  rbanks=(5,), rkeys=(("gm2",),), wkeys=(("gm2",),))
            a_act(m2, m2, AF.Ln, bias=cs_f32(C_EPS, 1)[0:1, :], rkeys=(("gm2",),), wkeys=(("gm2",),))
            a_act(m2, m2, AF.Exp, scale=-0.5, rkeys=(("gm2",),), wkeys=(("gm2",),))
            v_stt(mu, mu, -1.0, m2, ALU.mult, ALU.mult,
                  rkeys=(("gmu",), ("gm2",)), wkeys=(("gmu",),))
            mm(PS(4), csb.ap()[0:1, C_ONESR:C_ONESR + 128], m2, start=True, stop=True,
               wbank=(4,), rkeys=(("gm2",),), inc=s_pe)
            mm(PS(5), csb.ap()[0:1, C_ONESR:C_ONESR + 128], mu, start=True, stop=True,
               wbank=(5,), rkeys=(("gmu",),), inc=s_pe)
            for ch in range(NCH):
                v_tt(xst[:, 512 * ch:512 * (ch + 1)], xst[:, 512 * ch:512 * (ch + 1)],
                     PS(4), ALU.mult, rbanks=(4,), rkeys=(("gxst", ch),), wkeys=(("gxst", ch),))
                v_tt(xst[:, 512 * ch:512 * (ch + 1)], xst[:, 512 * ch:512 * (ch + 1)],
                     PS(5), ALU.add, rbanks=(5,), rkeys=(("gxst", ch),), wkeys=(("gxst", ch),))
            dma_store(x1sp[:, t0:t0 + 512].rearrange("(c p) t -> p c t", p=128),
                      xst.rearrange("p (c t) -> p c t", c=8),
                      rkeys=tuple(("gxst", ch) for ch in range(NCH)),
                      note_rkeys=tuple(("gxst", ch) for ch in range(NCH)))
        barrier()

    I_X = 0
    I_WQKV = 65568
    I_WO = 114720
    I_XBFW = 131104
    I_QPRE = 139328
    I_QBF = 147520
    I_KBF = 151616
    I_VBF = 159808
    I_USQ = 168000
    I_PT = 169024
    I_TBL = 175168
    I_TT = 183360
    I_OBUF = 185408
    I_BCE = 189504
    I_NRM = 190528
    I_END = 192576
    assert I_END <= ARENA_BYTES
    F_H = I_WQKV
    F_XSQ = I_WO
    F_W1 = I_WO + 8224
    F_XN = 139328
    F_W2 = 155712
    F_GEL = 163904
    F_STA = 168000
    F_GEL2 = 174144
    F_END = 178240
    assert F_END <= ARENA_BYTES

    def x_sl(ch, c0, n):
        return A.f32(I_X + 4 * 2049 * ch, 2049)[:, c0:c0 + n]

    def inner_qk(ten, w, slot, outfn, sA, sB, nsB, wqkv, xbfw, tb, t1b, t2b, qpre, usq,
                 nrm, recd, layer_key):
        base = 1024 * ten
        t0 = WIN * w
        _full = "qkv" in ATTN_PARTS or "q" in ATTN_PARTS or "k" in ATTN_PARTS
        _p_mm = _full or "qk_mm" in ATTN_PARTS
        _p_sq = _p_mm and (_full or "qk_sq" in ATTN_PARTS)
        _p_rot = _p_mm and (_full or "qk_rot" in ATTN_PARTS)
        _p_bc = _p_sq and (_full or "qk_bc" in ATTN_PARTS)
        enb = 6 + ten
        for st in range(5):
            if st < 4 and _p_mm:
                g = st
                b0 = 2 * (g % 2)
                for cc, bank in ((g, b0), (4 + g, b0 + 1)):
                    for kc in range(NCH):
                        o = 257 * kc + (1 if kc < 4 else 0)
                        mm(PS(bank, 128, 256),
                           wqkv[:, 3072 * kc + base + 128 * cc:3072 * kc + base + 128 * cc + 128],
                           xbfw[slot][:, o:o + 256],
                           start=(kc == 0), stop=(kc == NCH - 1),
                           wbank=(bank,), rkeys=(layer_key, ("xbfw", slot)),
                           inc=s_pe if kc == NCH - 1 else None)
            if st >= 1:
                g = st - 1
                b0 = 2 * (g % 2)
                if _p_sq:
                    a_act(usq[:, 0:256], PS(b0, 128, 256), AF.Square, rbanks=(b0,),
                          wkeys=(("usq", 0),))
                    a_act(usq[:, 256:512], PS(b0 + 1, 128, 256), AF.Square, rbanks=(b0 + 1,),
                          wkeys=(("usq", 1),))
                    for j in range(2):
                        mm(PS(enb, 16, 256), cs_bf(C_ENORM, 64)[:, 16 * g:16 * (g + 1)],
                           usq[:, 256 * j:256 * (j + 1)],
                           start=(g == 0 and j == 0), stop=(g == 3 and j == 1),
                           wbank=(enb,), rkeys=(("usq", j),),
                           inc=s_pe if (g == 3 and j == 1) else None)
                if _p_rot:
                    for cc, ub, us, s1, s2 in ((g, b0, b0 + 1, sA, nsB), (4 + g, b0 + 1, b0, sB, sA)):
                        v_stt(t1b, PS(ub, 128, 256), s1, tb[2 * ten], ALU.mult, ALU.mult,
                              rbanks=(ub,), rkeys=(("itb", slot),), wkeys=(("it1",),))
                        v_stt(t2b, PS(us, 128, 256), s2, tb[2 * ten + 1], ALU.mult, ALU.mult,
                              rbanks=(us,), rkeys=(("itb", slot),), wkeys=(("it2",),))
                        p_tt(qpre[:, 256 * cc:256 * (cc + 1)], t1b, t2b, ALU.add,
                             rkeys=(("it1",), ("it2",)), wkeys=(("qpre", cc),))
        if _p_sq:
            a_act(nrm, PS(enb, 16, 256), AF.Ln, bias=cs_f32(C_EPS2, 1)[0:16, :],
                  rbanks=(enb,), wkeys=(("nrm",),))
            a_act(recd, nrm, AF.Exp, scale=-0.5, rkeys=(("nrm",),), wkeys=(("recd",),))
        if _p_bc:
            for g in range(4):
                fb = 6 + ((ten + g) % 2)
                mm(PS(fb, 128, 256), csb.ap()[0:16, C_FNORM + 128 * g:C_FNORM + 128 * (g + 1)],
                   recd, start=True, stop=True, wbank=(fb,), rkeys=(("recd",),), inc=s_pe)
                for cc in (g, 4 + g):
                    v_tt(outfn(cc), qpre[:, 256 * cc:256 * (cc + 1)], PS(fb, 128, 256), ALU.mult,
                         rbanks=(fb,), rkeys=(("qpre", cc),), wkeys=(("qk_out", ten),))

    def emit_inner_attn(b, l):
        P.mark("L%da" % l)
        wqkv = A.bf16(I_WQKV, 8 * 3072)
        wo_bf = A.bf16(I_WO, 8 * 1024)
        lk = ("iw", l)
        dma_pool(wqkv.rearrange("p (c n) -> p c n", c=8),
                 iwqkv[l].rearrange("(c p) n -> p c n", p=128), s_wA, wkeys=(lk,))
        dma_pool(wo_bf.rearrange("p (c n) -> p c n", c=8),
                 iwo[l].rearrange("(c p) n -> p c n", p=128), s_wB, wkeys=(lk,))
        dma_sync(cs_f32(C_QS, 4), qscol[l], s_vec, wkeys=(("qs",),))
        v_ts(cs_f32(C_QS + 4, 1), cs_f32(C_QS + 1, 1), -1.0, None, ALU.mult,
             rkeys=(("qs",),), wkeys=(("qs",),))
        v_ts(cs_f32(C_QS + 5, 1), cs_f32(C_QS + 3, 1), -1.0, None, ALU.mult,
             rkeys=(("qs",),), wkeys=(("qs",),))
        qsA, qsB = cs_f32(C_QS, 1), cs_f32(C_QS + 1, 1)
        ksA, ksB = cs_f32(C_QS + 2, 1), cs_f32(C_QS + 3, 1)
        nqsB, nksB = cs_f32(C_QS + 4, 1), cs_f32(C_QS + 5, 1)

        xbfw = [A.bf16(I_XBFW + 4112 * i, 8 * 257) for i in range(2)]
        qpre = A.f32(I_QPRE, 8 * 256)
        qbf = A.bf16(I_QBF, 8 * 256)
        kbf = A.bf16(I_KBF, 8 * 512)
        vbf = A.bf16(I_VBF, 4 * 1024)
        usq = A.bf16(I_USQ, 512)
        ptb = [A.bf16(I_PT + 1536 * i, 768) for i in range(4)]
        tb = [[A.f32(I_TBL + 1024 * j + 4096 * i, 256) for j in range(4)] for i in range(2)]
        t1b = A.f32(I_TT, 256)
        t2b = A.f32(I_TT + 1024, 256)
        ob = A.bf16(I_OBUF, 8 * 256)
        bce = A.f32(I_BCE, 256)
        nrm = A.f32(I_NRM, 256)[0:16, :]
        recd = A.f32(I_NRM + 1024, 256)[0:16, :]

        def cast_win(w):
            slot = w % 2
            t0 = WIN * w
            xsrc = A.f32(I_X, 8 * 2049).rearrange("p (c t) -> p c t", c=8)[:, :, t0:t0 + 257]
            xdst = xbfw[slot].rearrange("p (c t) -> p c t", c=8)
            v_copy(xdst, xsrc,
                   rkeys=tuple(("x", ch) for ch in range(NCH)), wkeys=(("xbfw", slot),))

        _cast_on = "qkv" in ATTN_PARTS or "cast" in ATTN_PARTS
        _tbl_on = "qkv" in ATTN_PARTS or "tbl" in ATTN_PARTS

        def emit_qkv(w):
            slot = w % 2
            t0w = WIN * w
            for j in (range(4) if _tbl_on else ()):
                dma_sync(tb[slot][j], tbl_i[j][:, t0w:t0w + 256], s_tbs[slot],
                         wkeys=(("itb", slot),))
            if ("qkv" in ATTN_PARTS or "q" in ATTN_PARTS
                    or any(p.startswith("qk_") for p in ATTN_PARTS)):
                inner_qk(0, w, slot, lambda cc: A.bf16(I_QBF + 2 * 256 * cc, 256),
                         qsA, qsB, nqsB, wqkv, xbfw, tb[slot], t1b, t2b, qpre, usq, nrm, recd, lk)
            if "qkv" in ATTN_PARTS or "k" in ATTN_PARTS:
                inner_qk(1, w, slot, lambda cc: A.bf16(I_KBF + 2 * (512 * cc + 256 * slot), 256),
                         ksA, ksB, nksB, wqkv, xbfw, tb[slot], t1b, t2b, qpre, usq, nrm, recd, lk)
            if "qkv" in ATTN_PARTS or "v" in ATTN_PARTS:
                for tt in range(2):
                    for nt in range(2):
                        bank = 2 + nt
                        for kc in range(NCH):
                            o = 257 * kc + (1 if kc < 4 else 0) + 128 * tt
                            mm(PS(bank), xbfw[slot][:, o:o + 128],
                               wqkv[:, 3072 * kc + 2048 + 512 * nt:3072 * kc + 2048 + 512 * nt + 512],
                               start=(kc == 0), stop=(kc == NCH - 1),
                               wbank=(bank,), rkeys=(lk, ("xbfw", slot)),
                               inc=s_pe if kc == NCH - 1 else None)
                        v_copy(A.bf16(I_VBF + 2 * (1024 * (2 * slot + tt) + 512 * nt), 512),
                               PS(bank), rbanks=(bank,), wkeys=(("vbf", slot),))

        if _cast_on:
            cast_win(0)
        emit_qkv(0)
        if _cast_on and NWIN > 1:
            cast_win(1)
        for w in range(NWIN):
            t0 = WIN * w
            slot = w % 2
            pslot = (w - 1) % 2
            if w + 2 < NWIN and _cast_on:
                cast_win(w + 2)
            if w == 0:
                slots = [(0, (slot, 0), 1), (1, (slot, 1), 1), (1, (slot, 0), 0)]
                maskc = cs_bf(C_TRIM2, 256)
                nmask = 2
            else:
                slots = [(0, (pslot, 0), 2), (1, (pslot, 1), 2),
                         (0, (slot, 0), 1), (1, (slot, 1), 1),
                         (0, (pslot, 1), 0), (1, (slot, 0), 0)]
                maskc = cs_bf(C_TRIM4, 512)
                nmask = 4
            nslot = len(slots)

            def pair_ln(p):
                a_act(recd[0:2, :], PS(6, 2, 256), AF.Ln, rbanks=(6,), wkeys=(("recd",),))
                a_act(recd[0:2, :], recd[0:2, :], AF.Exp, scale=-1.0,
                      rkeys=(("recd",),), wkeys=(("recd",),))

            def pair_fin(p):
                pvb = 4 + (p % 2)
                mm(PS(7, 128, 256), csb.ap()[0:2, C_FPAIR:C_FPAIR + 128],
                   recd[0:2, :], start=True, stop=True, wbank=(7,), rkeys=(("recd",),),
                   inc=s_pe)
                v_copy(bce, PS(7, 128, 256), rbanks=(7,), wkeys=(("bce",),))
                v_tt(ob[:, 256 * p:256 * (p + 1)], PS(pvb, 128, 256), bce, ALU.mult,
                     rbanks=(pvb,), rkeys=(("bce",),), wkeys=(("ob", p),))

            for st in (range(IH + 1) if "attn" in ATTN_PARTS else ()):
                if st < IH:
                    h = st
                    r0 = 32 * (h % 4)
                    g = h // 4
                    cA, cB = g, 4 + g
                    pts = ptb[h % 4]
                    sb0 = 2 * (h % 2)
                    for si, (a, (ks, kt), mk) in enumerate(slots):
                        for half, cc in ((0, cA), (1, cB)):
                            kcol = 512 * cc + 256 * ks + 128 * kt
                            mm(ps_t.ap()[:, 512 * sb0 + 128 * si:512 * sb0 + 128 * (si + 1)],
                               kbf[r0:r0 + 32, kcol:kcol + 128],
                               A.bf16(I_QBF + 2 * 256 * cc, 256)[r0:r0 + 32, 128 * a:128 * (a + 1)],
                               start=(half == 0), stop=(half == 1),
                               wbank=(sb0, sb0 + 1),
                               rkeys=(("qk_out", 0), ("qk_out", 1)),
                               tile_position=(r0, 0),
                               inc=s_pe if half == 1 else None)
                    a_act(pts[:, 0:128 * nslot], ps_t.ap()[:, 512 * sb0:512 * sb0 + 128 * nslot],
                          AF.Exp, scale=8.0,
                          rbanks=(sb0, sb0 + 1), wkeys=(("pt", h % 4),))
                    v_tt(pts[:, 0:128 * nmask], pts[:, 0:128 * nmask], maskc, ALU.mult,
                         rkeys=(("pt", h % 4),), wkeys=(("pt", h % 4),))
                if st >= 1:
                    h = st - 1
                    pts = ptb[h % 4]
                    pvb = 4 + ((h // 2) % 2)
                    if h % 2 == 0 and h >= 2:
                        pair_ln((h - 2) // 2)
                    for si, (a, (ks, kt), mk) in enumerate(slots):
                        mm(PS(6, 2, 128, 128 * a), cs_bf(C_EDEN, 4)[:, 2 * (h % 2):2 * (h % 2) + 2],
                           pts[:, 128 * si:128 * (si + 1)],
                           start=(si == 0 and h % 2 == 0), stop=(si == nslot - 1 and h % 2 == 1),
                           wbank=(6,),
                           rkeys=(("pt", h % 4),),
                           inc=s_pe if (si == nslot - 1 and h % 2 == 1) else None)
                    for si, (a, (ks, kt), mk) in enumerate(slots):
                        vcol = 1024 * (2 * ks + kt) + 64 * h
                        mm(PS(pvb, 64, 128, 128 * a, r0=64 * (h % 2)),
                           A.bf16(I_VBF + 2 * vcol, 64),
                           pts[:, 128 * si:128 * (si + 1)],
                           start=(si == 0), stop=(si == nslot - 1),
                           wbank=(pvb,),
                           rkeys=(("vbf", ks), ("pt", h % 4)),
                           tile_position=(0, 64 * (h % 2)),
                           inc=s_pe if si == nslot - 1 else None)
                    if h % 2 == 1 and h >= 3:
                        pair_fin((h - 3) // 2)
            if "attn" in ATTN_PARTS:
                pair_ln(7)
                pair_fin(7)
            if w + 1 < NWIN:
                emit_qkv(w + 1)
            wo_banks = (4, 5, 4, 5, 6, 7, 6, 7)
            for mt in (range(NCH) if "wo" in ATTN_PARTS else ()):
                bank = wo_banks[mt]
                for hc in range(NCH):
                    mm(PS(bank, 128, 256), wo_bf[:, 1024 * hc + 128 * mt:1024 * hc + 128 * mt + 128],
                       ob[:, 256 * hc:256 * (hc + 1)],
                       start=(hc == 0), stop=(hc == NCH - 1),
                       wbank=(bank,),
                       rkeys=(lk, ("ob", hc)),
                       inc=s_pe if hc == NCH - 1 else None)
                v_tt(x_sl(mt, t0 + 1, 256), PS(bank, 128, 256), x_sl(mt, t0 + 1, 256), ALU.add,
                     rbanks=(bank,), rkeys=(("x", mt),), wkeys=(("x", mt),))
        barrier()

    def emit_inner_ffn(b, l):
        P.mark("L%df" % l)
        hbf = A.bf16(F_H, NJT * 1024)
        xsq = A.bf16(F_XSQ, 8 * 513)
        w1s = [A.bf16(F_W1 + 8192 * i, 8 * 512) for i in range(2)]
        xn = [A.bf16(F_XN + 8192 * i, 8 * 512) for i in range(2)]
        w2s = [A.bf16(F_W2 + 4096 * i, 2 * 1024) for i in range(2)]
        gel2 = [A.f32(F_GEL, 1024), A.f32(F_GEL2, 1024)]
        stA = A.f32(F_STA, 512)[0:1, :]
        stB = A.f32(F_STA + 2048, 512)[0:1, :]
        stM = A.f32(F_STA + 4096, 512)[0:1, :]
        dma_sync(cs_f32(C_GCOL, 8), lng_col[l], s_vec, wkeys=(("lv",),))
        dma_sync(cs_f32(C_BLNF, 8), lnb_col[l], s_vec, wkeys=(("lv",),))
        dma_sync(cs_f32(C_B1, 44), b1_col[l], s_vec, wkeys=(("lv",),))
        dma_sync(cs_f32(C_B2, 8), b2_col[l], s_vec, wkeys=(("lv",),))
        v_copy(cs_bf(C_BLNB, 8), cs_f32(C_BLNF, 8), rkeys=(("lv",),), wkeys=(("lv",),))

        def w1cols(p):
            return (256 * p, 128 if p < 21 else 42) if p < 21 else (5376, 42)

        def emit_xn(qq):
            t0 = 512 * qq
            sl = qq % 2
            for ch in range(NCH):
                a_act(xsq[:, 513 * ch:513 * ch + 513], x_sl(ch, t0, 513), AF.Square,
                      rkeys=(("x", ch),), wkeys=(("xsq", ch),))
            for ch in range(NCH):
                o = 513 * ch + (1 if ch < 4 else 0)
                mm(PS(6, 1, 512), cs_f32(C_ONESCF, 1),
                   x_sl(ch, t0 + (1 if ch < 4 else 0), 512),
                   start=(ch == 0), stop=(ch == NCH - 1),
                   wbank=(6,), rkeys=(("x", ch),),
                   inc=s_pe if ch == NCH - 1 else None)
            for ch in range(NCH):
                o = 513 * ch + (1 if ch < 4 else 0)
                mm(PS(7, 1, 512), cs_bf(C_ONESCB, 1), xsq[:, o:o + 512],
                   start=(ch == 0), stop=(ch == NCH - 1),
                   wbank=(7,), rkeys=(("xsq", ch),),
                   inc=s_pe if ch == NCH - 1 else None)
            a_act(stM, PS(6, 1, 512), AF.Copy, scale=1.0 / DM, rbanks=(6,), wkeys=(("stM",),))
            a_act(stA, PS(6, 1, 512), AF.Square, scale=1.0 / DM, rbanks=(6,), wkeys=(("stA",),))
            v_stt(stA, PS(7, 1, 512), 1.0 / DM, stA, ALU.mult, ALU.subtract,
                  rbanks=(7,), rkeys=(("stA",),), wkeys=(("stA",),))
            a_act(stA, stA, AF.Ln, bias=cs_f32(C_EPS, 1)[0:1, :], rkeys=(("stA",),), wkeys=(("stA",),))
            a_act(stA, stA, AF.Exp, scale=-0.5, rkeys=(("stA",),), wkeys=(("stA",),))
            v_stt(stB, stM, -1.0, stA, ALU.mult, ALU.mult,
                  rkeys=(("stM",), ("stA",)), wkeys=(("stB",),))
            mm(PS(6), csb.ap()[0:1, C_ONESR:C_ONESR + 128], stA, start=True, stop=True,
               wbank=(6,), rkeys=(("stA",),), inc=s_pe)
            mm(PS(7), csb.ap()[0:1, C_ONESR:C_ONESR + 128], stB, start=True, stop=True,
               wbank=(7,), rkeys=(("stB",),), inc=s_pe)
            for ch in range(NCH):
                sh = 1 if ch < 4 else 0
                v_stt(gel2[0][:, 0:512], x_sl(ch, t0 + sh, 512), cs_f32(C_GCOL + ch, 1),
                      PS(6), ALU.mult, ALU.mult,
                      rbanks=(6,), rkeys=(("x", ch), ("lv",)), wkeys=(("gel", 0),))
                v_stt(xn[sl][:, 512 * ch:512 * (ch + 1)], PS(7), cs_f32(C_GCOL + ch, 1),
                      gel2[0][:, 0:512], ALU.mult, ALU.add,
                      rbanks=(7,), rkeys=(("gel", 0), ("lv",)), wkeys=(("xn", sl),))

        def emit_w1_half(hh, first):
            for jg in range(11):
                sl = jg % 2
                c0 = 512 * jg
                cw = 512 if jg < 10 else 340
                dma_pool(w1s[sl][:, 0:0 + 8 * 512] if False else w1s[sl],
                         None, wkeys=()) if False else None
                src = iw1[l][:, c0:c0 + cw].rearrange("(c p) n -> p c n", p=128)
                dma_pool(w1s[sl].rearrange("p (c n) -> p c n", c=8)[:, :, 0:cw],
                         src, s_w1[sl], wkeys=(("w1s", sl),))
                for pi in range(2):
                    p = 2 * jg + pi
                    if p >= NPAIR:
                        continue
                    wdt = 128 if p < 21 else 42
                    la = 256 * pi if jg < 10 else 256 * pi
                    if jg < 10:
                        loa, log = 256 * pi, 256 * pi + 128
                    else:
                        loa, log = 256 * pi, 256 * pi + wdt
                    pb = 4 * (p % 2)
                    if first:
                        for ti, lo in ((2 * p, loa), (2 * p + 1, log)):
                            for kc in range(NCH):
                                mm(PS(pb + 2, wdt, 1), w1s[sl][:, 512 * kc + lo:512 * kc + lo + wdt],
                                   cs_bf(C_BLNB, 8)[:, kc:kc + 1],
                                   start=(kc == 0), stop=(kc == NCH - 1),
                                   wbank=(pb + 2,),
                                   rkeys=(("w1s", sl), ("lv",)),
                                   inc=s_pe if kc == NCH - 1 else None)
                            v_stt(cs_f32(C_U1 + ti, 1)[0:wdt, :], PS(pb + 2, wdt, 1),
                                  1.0, cs_f32(C_B1 + ti, 1)[0:wdt, :], ALU.mult, ALU.add,
                                  rbanks=(pb + 2,), rkeys=(("lv",),), wkeys=(("u1", ti),))
                    for bb, lo in ((pb, loa), (pb + 2, log)):
                        for nt in range(2):
                            bank = bb + nt
                            for kc in range(NCH):
                                mm(PS(bank, wdt, 512), w1s[sl][:, 512 * kc + lo:512 * kc + lo + wdt],
                                   xn[nt][:, 512 * kc:512 * (kc + 1)],
                                   start=(kc == 0), stop=(kc == NCH - 1),
                                   wbank=(bank,),
                                   rkeys=(("w1s", sl), ("xn", nt)),
                                   inc=s_pe if kc == NCH - 1 else None)
                    a_act(gel2[p % 2][0:wdt, :], ps_t.ap()[0:wdt, 512 * (pb + 2):512 * (pb + 2) + 1024],
                          AF.Gelu,
                          bias=cs_f32(C_U1 + 2 * p + 1, 1)[0:wdt, :],
                          rbanks=(pb + 2, pb + 3), rkeys=(("u1", 2 * p + 1),), wkeys=(("gel", p % 2),))
                    v_stt(hbf[0:wdt, 1024 * p:1024 * (p + 1)],
                          ps_t.ap()[0:wdt, 512 * pb:512 * pb + 1024],
                          cs_f32(C_U1 + 2 * p, 1)[0:wdt, :], gel2[p % 2][0:wdt, :],
                          ALU.add, ALU.mult,
                          rbanks=(pb, pb + 1), rkeys=(("gel", p % 2), ("u1", 2 * p)),
                          wkeys=(("h", p),))

        def emit_w2_half(hh):
            for nt in range(2):
                t0 = 1024 * hh + 512 * nt
                for jg in range(11):
                    sl = jg % 2
                    r0 = 256 * jg
                    rh = 256 if jg < 10 else 170
                    if jg < 10:
                        dma_pool(w2s[sl].rearrange("p (j m) -> p j m", j=2),
                                 iw2[l][r0:r0 + 256, :].rearrange("(j p) m -> p j m", p=128),
                                 s_w2[sl], wkeys=(("w2s", sl),))
                    else:
                        dma_pool(w2s[sl][0:128, 0:1024],
                                 iw2[l][2560:2688, :], s_w2[sl], wkeys=(("w2s", sl),))
                        dma_pool(w2s[sl][0:42, 1024:2048],
                                 iw2[l][2688:2730, :], s_w2[sl], wkeys=(("w2s", sl),))
                    for ji in range(2):
                        j = 2 * jg + ji
                        if j >= NJT:
                            continue
                        kj = 128 if j < 21 else 42
                        for mt in range(NCH):
                            mm(PS(mt, 128, 512),
                               w2s[sl][0:kj, 1024 * ji + 128 * mt:1024 * ji + 128 * (mt + 1)],
                               hbf[0:kj, 1024 * j + 512 * nt:1024 * j + 512 * nt + 512],
                               start=(j == 0), stop=(j == NJT - 1),
                               wbank=(mt,),
                               rkeys=(("w2s", sl), ("h", j)),
                               inc=s_pe if (ji == 1 and mt == NCH - 1) else None)
                for mt in range(NCH):
                    v_stt(x_sl(mt, t0 + 1, 512), PS(mt, 128, 512), cs_f32(C_B2 + mt, 1),
                          x_sl(mt, t0 + 1, 512), ALU.add, ALU.add,
                          rbanks=(mt,), rkeys=(("x", mt), ("lv",)), wkeys=(("x", mt),))

        emit_xn(0)
        emit_xn(1)
        emit_w1_half(0, first=True)
        emit_xn(2)
        emit_xn(3)
        emit_w2_half(0)
        emit_w1_half(1, first=False)
        emit_w2_half(1)
        barrier()

    def emit_final(b):
        P.mark("final")
        xst = A.f32(G_XST, 8 * 512)
        sqb = A.bf16(G_OS, 8 * 512)
        mu = A.f32(G_MU, 512)[0:1, :]
        m2 = A.f32(G_M2, 512)[0:1, :]
        tmst = A.f32(G_BC, 1024)
        for ss in range(4):
            t0 = 512 * ss
            dma_sync(xst.rearrange("p (c t) -> p c t", c=8),
                     x1sp[:, t0:t0 + 512].rearrange("(c p) t -> p c t", p=128),
                     s_x1r, waits=((s_st, s_st.n),),
                     wkeys=tuple(("fx1", ch) for ch in range(NCH)))
            for ch in range(NCH):
                v_tt(xst[:, 512 * ch:512 * (ch + 1)], xst[:, 512 * ch:512 * (ch + 1)],
                     x_sl(ch, t0 + 1, 512), ALU.add,
                     rkeys=(("x", ch), ("fx1", ch)), wkeys=(("fx1", ch),))
            for ch in range(NCH):
                a_act(sqb[:, 512 * ch:512 * (ch + 1)], xst[:, 512 * ch:512 * (ch + 1)],
                      AF.Square, rkeys=(("fx1", ch),), wkeys=(("fsq", ch),))
            for ch in range(NCH):
                mm(PS(4, 1, 512), cs_f32(C_ONESCF, 1), xst[:, 512 * ch:512 * (ch + 1)],
                   start=(ch == 0), stop=(ch == NCH - 1),
                   wbank=(4,), rkeys=(("fx1", ch),),
                   inc=s_pe if ch == NCH - 1 else None)
            for ch in range(NCH):
                mm(PS(5, 1, 512), cs_bf(C_ONESCB, 1), sqb[:, 512 * ch:512 * (ch + 1)],
                   start=(ch == 0), stop=(ch == NCH - 1),
                   wbank=(5,), rkeys=(("fsq", ch),),
                   inc=s_pe if ch == NCH - 1 else None)
            a_act(mu, PS(4, 1, 512), AF.Copy, scale=1.0 / DM, rbanks=(4,), wkeys=(("gmu",),))
            a_act(m2, PS(4, 1, 512), AF.Square, scale=1.0 / DM, rbanks=(4,), wkeys=(("gm2",),))
            v_stt(m2, PS(5, 1, 512), 1.0 / DM, m2, ALU.mult, ALU.subtract,
                  rbanks=(5,), rkeys=(("gm2",),), wkeys=(("gm2",),))
            a_act(m2, m2, AF.Ln, bias=cs_f32(C_EPS, 1)[0:1, :], rkeys=(("gm2",),), wkeys=(("gm2",),))
            a_act(m2, m2, AF.Exp, scale=-0.5, rkeys=(("gm2",),), wkeys=(("gm2",),))
            v_stt(mu, mu, -1.0, m2, ALU.mult, ALU.mult,
                  rkeys=(("gmu",), ("gm2",)), wkeys=(("gmu",),))
            mm(PS(4), csb.ap()[0:1, C_ONESR:C_ONESR + 128], m2, start=True, stop=True,
               wbank=(4,), rkeys=(("gm2",),), inc=s_pe)
            mm(PS(5), csb.ap()[0:1, C_ONESR:C_ONESR + 128], mu, start=True, stop=True,
               wbank=(5,), rkeys=(("gmu",),), inc=s_pe)
            for ch in range(NCH):
                v_tt(xst[:, 512 * ch:512 * (ch + 1)], xst[:, 512 * ch:512 * (ch + 1)],
                     PS(4), ALU.mult, rbanks=(4,), rkeys=(("fx1", ch),), wkeys=(("fx1", ch),))
                v_tt(xst[:, 512 * ch:512 * (ch + 1)], xst[:, 512 * ch:512 * (ch + 1)],
                     PS(5), ALU.add, rbanks=(5,), rkeys=(("fx1", ch),), wkeys=(("fx1", ch),))
            for tt in range(4):
                bank = tt % 2
                for ch in range(NCH):
                    transpose(PS(bank, 128, 128, 128 * (ch % 4), r0=0)
                              if False else
                              ps_t.ap()[:, 512 * (2 * bank + ch // 4) + 128 * (ch % 4):
                                        512 * (2 * bank + ch // 4) + 128 * (ch % 4) + 128],
                              xst[:, 512 * ch + 128 * tt:512 * ch + 128 * (tt + 1)],
                              2 * bank + ch // 4, rkey=("fx1", ch),
                              start_clear=(ch % 4 == 0))
                v_copy(tmst, ps_t.ap()[:, 1024 * bank:1024 * bank + 1024],
                       rbanks=(2 * bank, 2 * bank + 1), wkeys=(("ftm",),))
                dma_store(out[b, t0 + 128 * tt:t0 + 128 * (tt + 1), :], tmst,
                          waits=TK.wait_read(("ftm",)) + TK.wait_write(("ftm",)),
                          note_rkeys=())
                TK.note_read(("ftm",), s_st, s_st.n)
        barrier()

    for b in range(n_batches):
        emit_global(b)
        if dump_stage == "x1":
            xst = A.f32(G_XST, 8 * 512)
            tmst = A.f32(G_BC, 1024)
            for ss in range(4):
                t0 = 512 * ss
                dma_sync(xst.rearrange("p (c t) -> p c t", c=8),
                         x1sp[:, t0:t0 + 512].rearrange("(c p) t -> p c t", p=128),
                         s_x1r, waits=((s_st, s_st.n),),
                         wkeys=tuple(("dx", ch) for ch in range(NCH)))
                for tt in range(4):
                    bank = tt % 2
                    for ch in range(NCH):
                        transpose(ps_t.ap()[:, 512 * (2 * bank + ch // 4) + 128 * (ch % 4):
                                            512 * (2 * bank + ch // 4) + 128 * (ch % 4) + 128],
                                  xst[:, 512 * ch + 128 * tt:512 * ch + 128 * (tt + 1)],
                                  2 * bank + ch // 4, rkey=("dx", ch),
                                  start_clear=(ch % 4 == 0), stop=(ch % 4 == 3))
                    v_copy(tmst, ps_t.ap()[:, 1024 * bank:1024 * bank + 1024],
                           rbanks=(2 * bank, 2 * bank + 1), wkeys=(("ftm",),))
                    dma_store(out[b, t0 + 128 * tt:t0 + 128 * (tt + 1), :], tmst,
                              waits=TK.wait_read(("ftm",)) + [TK.w[("ftm",)]])
                    TK.note_read(("ftm",), s_st, s_st.n)
            barrier()
            continue
        P.mark("reload")
        for ch in range(NCH):
            v_memset(x_sl(ch, 0, 1), 0.0, wkeys=(("x", ch),))
        for ch in range(NCH):
            dma_sync(x_sl(ch, 1, 2048), x1sp[128 * ch:128 * (ch + 1), :],
                     s_x1r, wkeys=(("x", ch),), waits=((s_st, s_st.n),))
        barrier()
        for l in range(n_layers):
            emit_inner_attn(b, l)
            if dump_stage == ("L%da" % (l + 1)):
                break
            emit_inner_ffn(b, l)
            if dump_stage == ("L%d" % (l + 1)):
                break
        if dump_stage and dump_stage != "full" and dump_stage.startswith("L"):
            tmst = A.f32(G_BC, 1024)
            for ss in range(4):
                t0 = 512 * ss
                for tt in range(4):
                    bank = tt % 2
                    for ch in range(NCH):
                        transpose(ps_t.ap()[:, 512 * (2 * bank + ch // 4) + 128 * (ch % 4):
                                            512 * (2 * bank + ch // 4) + 128 * (ch % 4) + 128],
                                  x_sl(ch, t0 + 128 * tt + 1, 128),
                                  2 * bank + ch // 4, rkey=("x", ch),
                                  start_clear=(ch % 4 == 0), stop=(ch % 4 == 3))
                    v_copy(tmst, ps_t.ap()[:, 1024 * bank:1024 * bank + 1024],
                           rbanks=(2 * bank, 2 * bank + 1), wkeys=(("ftm",),))
                    dma_store(out[b, t0 + 128 * tt:t0 + 128 * (tt + 1), :], tmst,
                              waits=TK.wait_read(("ftm",)) + [TK.w[("ftm",)]])
                    TK.note_read(("ftm",), s_st, s_st.n)
            barrier()
        elif dump_stage is None or dump_stage == "full":
            emit_final(b)

    P.op("sync", None, waits=((s_st, s_st.n),))
    P.emit()
    return nc


def _prep_inputs(inputs):
    c = host_constants()
    pg = _perm_global_qk()
    pi = _perm_inner_qk()
    pw1, tcols = _w1_tile_cols()
    wq = np.ascontiguousarray(inputs["L0_attn_wq"].reshape(DM, DM)[:, pg], dtype=np.float32)
    wk = np.ascontiguousarray(inputs["L0_attn_wk"].reshape(DM, DM)[:, pg], dtype=np.float32)
    wv = np.ascontiguousarray(inputs["L0_attn_wv"].reshape(DM, DM), dtype=np.float32)
    wo = np.ascontiguousarray(inputs["L0_attn_wo"].reshape(DM, DM), dtype=np.float32)
    Wqkv = np.asarray(inputs["L0_ffn_Wqkv"], dtype=np.float32)
    Wqkv_p = np.concatenate([Wqkv[:, :, :DM][:, :, pi], Wqkv[:, :, DM:2 * DM][:, :, pi],
                             Wqkv[:, :, 2 * DM:]], axis=2)
    W1p = np.ascontiguousarray(np.asarray(inputs["L0_ffn_W1"], dtype=np.float32)[:, :, pw1])
    qs = np.asarray(inputs["L0_ffn_q_scale"], dtype=np.float32)
    ks = np.asarray(inputs["L0_ffn_k_scale"], dtype=np.float32)
    qscol_a = np.zeros((DEPTH, 128, 4), dtype=np.float32)
    for l in range(DEPTH):
        qscol_a[l, :, 0] = np.tile(qs[l, 0:32], 4)
        qscol_a[l, :, 1] = np.tile(qs[l, 32:64], 4)
        qscol_a[l, :, 2] = np.tile(ks[l, 0:32], 4)
        qscol_a[l, :, 3] = np.tile(ks[l, 32:64], 4)
    b1p = np.asarray(inputs["L0_ffn_b1"], dtype=np.float32)[:, pw1]
    b1c = np.zeros((DEPTH, 128, 2 * NPAIR), dtype=np.float32)
    off = 0
    tlist = []
    for p in range(NPAIR):
        w = 128 if p < 21 else 42
        tlist.append((off, w)); off += w
        tlist.append((off, w)); off += w
    for l in range(DEPTH):
        for ti, (o, w) in enumerate(tlist):
            b1c[l, :w, ti] = b1p[l, o:o + w]
    lng_c = np.zeros((DEPTH, 128, NCH), dtype=np.float32)
    lnb_c = np.zeros((DEPTH, 128, NCH), dtype=np.float32)
    b2_c = np.zeros((DEPTH, 128, NCH), dtype=np.float32)
    for l in range(DEPTH):
        lng_c[l] = np.asarray(inputs["L0_ffn_ln_g"][l], dtype=np.float32).reshape(NCH, 128).T
        lnb_c[l] = np.asarray(inputs["L0_ffn_ln_b"][l], dtype=np.float32).reshape(NCH, 128).T
        b2_c[l] = np.asarray(inputs["L0_ffn_b2"][l], dtype=np.float32).reshape(NCH, 128).T
    return {
        "gwq": wq, "gwk": wk, "gwv": wv, "gwo": wo,
        "iwqkv": np.ascontiguousarray(Wqkv_p),
        "iwo": np.ascontiguousarray(np.asarray(inputs["L0_ffn_Wo"], dtype=np.float32)),
        "iw1": W1p,
        "iw2": np.ascontiguousarray(np.asarray(inputs["L0_ffn_W2"], dtype=np.float32)),
        "qscol": qscol_a, "lng_col": lng_c, "lnb_col": lnb_c, "b1_col": b1c, "b2_col": b2_c,
        "tbl_g": c["tbl_g"], "tbl_i": c["tbl_i"], "identity": c["identity"],
        "trimask4": c["trimask4"], "trimask2": c["trimask2"], "tri1one": c["tri1one"],
        "eden": c["eden"], "enorm": c["enorm"], "fnorm": c["fnorm"], "fpair": c["fpair"],
        "ones_col_bf": c["ones_col_bf"], "ones_col_f32": c["ones_col_f32"],
        "ones_row_f32": c["ones_row_f32"], "eps_col": c["eps_col"],
    }


def kernel(**inputs):
    x = np.ascontiguousarray(inputs["L0_attn_x"], dtype=np.float32)
    base = _prep_inputs(inputs)
    nc = build_program()
    in_maps = []
    for cidx in range(NCORE):
        m = dict(base)
        m["x_in"] = np.ascontiguousarray(x[BLOC * cidx:BLOC * (cidx + 1)])
        in_maps.append(m)
    res = run_bass_kernel_spmd(nc, in_maps, core_ids=list(range(NCORE)))
    return np.concatenate([r["out"] for r in res.results], axis=0)


# version 06
# speedup vs baseline: 1.2925x; 1.0090x over previous
import contextlib
import numpy as np
import ml_dtypes
import concourse.bass as bass
import concourse.mybir as mybir
from concourse.bass_utils import run_bass_kernel_spmd

F32 = mybir.dt.float32
BF16 = mybir.dt.bfloat16
AF = mybir.ActivationFunctionType
ALU = mybir.AluOpType
bfloat16 = ml_dtypes.bfloat16

T = 2048
NCORE = 8
BLOC = 2
NCH = 8
DM = 1024
GH = 8
IH = 16
WIN = 256
NWIN = T // WIN
DEPTH = 6
DI2 = 5460
DI = 2730
NPAIR = 22
NJT = 22
EPS = 1e-5

ENGS = ("tensor", "vector", "scalar", "gpsimd", "sync")
DEBUG_BUILD = False
DEDUP_WAITS = True
ATTN_PARTS = ("qkv", "attn", "wo")
DEFER_HEAD = True
DEFER_QKV = False
QKV_ALT = True
DEFER_SQ = True
DEFER_ROT = True
FN_ALT = True
FFN_PB = True


def _rope_tables_global():
    half = 64
    af = np.power(np.float32(10000.0), -(np.arange(half, dtype=np.float32) / np.float32(half)))
    pos = np.arange(T, dtype=np.float32)[:, None]
    rad = (pos * af[None, :]).astype(np.float32)
    cg = np.tile(np.cos(rad).astype(np.float32).T, (2, 1))
    sg = np.tile(np.sin(rad).astype(np.float32).T, (2, 1))
    return cg.astype(np.float32), sg.astype(np.float32)


def _rope_tables_inner():
    half = np.arange(0, 64, 2, dtype=np.float32)
    inv_freq = (np.float32(1.0) / np.power(np.float32(10000.0), half / np.float32(64))).astype(np.float32)
    t = np.arange(T, dtype=np.float32)
    freqs = (t[:, None] * inv_freq[None, :]).astype(np.float32)
    base_scale = ((half + np.float32(0.4 * 64)) / np.float32(1.4 * 64)).astype(np.float32)
    power = ((t - T // 2) / np.float32(WIN // 2)).astype(np.float32)
    scale = np.power(base_scale[None, :], power[:, None]).astype(np.float32)
    cos = np.cos(freqs).astype(np.float32)
    sin = np.sin(freqs).astype(np.float32)
    inv = (np.float32(1.0) / scale).astype(np.float32)

    def pat(m):
        return np.tile(m.T, (4, 1)).astype(np.float32)
    return pat(cos * scale), pat(sin * scale), pat(cos * inv), pat(sin * inv)


def _perm_global_qk():
    perm = np.zeros(1024, dtype=np.int64)
    for h in range(GH):
        for d in range(128):
            newc = (d // 64) * 512 + (h // 2) * 128 + (h % 2) * 64 + (d % 64)
            perm[newc] = h * 128 + d
    return perm


def _perm_inner_qk():
    perm = np.zeros(1024, dtype=np.int64)
    for h in range(IH):
        for d in range(64):
            newc = (d // 32) * 512 + (h // 4) * 128 + (h % 4) * 32 + (d % 32)
            perm[newc] = h * 64 + d
    return perm


def _w1_tile_cols():
    idx = []
    tcols = []
    off = 0
    for p in range(NPAIR):
        w = 128 if p < 21 else 42
        idx.extend(range(128 * p, 128 * p + w))
        tcols.append((off, w)); off += w
        idx.extend(range(DI + 128 * p, DI + 128 * p + w))
        tcols.append((off, w)); off += w
    return np.array(idx, dtype=np.int64), tcols


def host_constants():
    c = {}
    cg, sg = _rope_tables_global()
    c["tbl_g"] = np.stack([cg, sg])
    csq, ssq, csk, ssk = _rope_tables_inner()
    c["tbl_i"] = np.stack([csq, ssq, csk, ssk])
    c["identity"] = np.eye(128, dtype=np.float32)
    p = np.arange(128)[:, None]
    f = np.arange(128)[None, :]
    tri1 = (p <= f).astype(bfloat16)
    tri2 = (f <= p).astype(bfloat16)
    c["trimask4"] = np.concatenate([tri2, tri2, tri1, tri1], axis=1).astype(bfloat16)
    c["trimask2"] = np.concatenate([tri1, tri1], axis=1).astype(bfloat16)
    c["tri1one"] = tri1
    eden = np.zeros((128, 4), dtype=bfloat16)
    eden[:, 0] = bfloat16(1.0)
    eden[:, 3] = bfloat16(1.0)
    c["eden"] = eden
    en = np.zeros((4, 128, 16), dtype=bfloat16)
    for g in range(4):
        for q in range(128):
            en[g, q, 4 * g + q // 32] = bfloat16(1.0)
    c["enorm"] = en
    fn = np.zeros((4, 16, 128), dtype=np.float32)
    for g in range(4):
        for j in range(4):
            fn[g, 4 * g + j, 32 * j:32 * j + 32] = 1.0
    c["fnorm"] = fn
    fp = np.zeros((2, 128), dtype=np.float32)
    fp[0, 0:64] = 1.0
    fp[1, 64:128] = 1.0
    c["fpair"] = fp
    c["ones_col_bf"] = np.ones((128, 1), dtype=bfloat16)
    c["eps_col"] = np.full((128, 1), EPS, dtype=np.float32)
    c["ones_col_f32"] = np.ones((128, 1), dtype=np.float32)
    c["ones_row_f32"] = np.ones((1, 128), dtype=np.float32)
    return c


class Sem:
    __slots__ = ("h", "n", "name", "last_eng", "is_dma", "engs")

    def __init__(self, h, name, is_dma=False):
        self.h = h; self.n = 0; self.name = name
        self.last_eng = None; self.is_dma = is_dma
        self.engs = set()


class Plan:
    def __init__(self, nc, stack):
        self.nc = nc
        self.q = {e: [] for e in ENGS}
        self.maxwait = {e: {} for e in ENGS}
        self.stack = stack
        self.phase = "init"

    def mark(self, name):
        self.phase = name

    def sem(self, name, is_dma=False):
        return Sem(self.stack.enter_context(self.nc.semaphore(name)), name, is_dma)

    def op(self, eng, fn, waits=(), inc=None, inc_n=1):
        real = []
        mw = self.maxwait[eng]
        for s, v in waits:
            if s is None or v is None or v <= 0:
                continue
            if DEDUP_WAITS:
                if (not s.is_dma) and s.engs == {eng} and v <= s.n:
                    continue
                if mw.get(id(s), 0) >= v:
                    continue
                mw[id(s)] = v
            real.append((s, v))
        val = None
        if inc is not None:
            inc.n += inc_n
            inc.last_eng = eng
            inc.engs.add(eng)
            val = inc.n
        self.q[eng].append((tuple(real), fn, inc, inc_n, self.phase))
        return val

    def dma(self, eng, fn, waits=(), inc=None):
        return self.op(eng, fn, waits, inc, inc_n=16)

    def barrier(self, bar, dma_sems=()):
        for e in ENGS:
            self.op(e, lambda E: E.nop(), inc=bar)
        tgt = bar.n
        wl = tuple((s, s.n) for s in dma_sems)
        for e in ENGS:
            self.op(e, None, waits=((bar, tgt),) + wl)

    def emit(self):
        phase_of = self.nc._phase_of_inst = {}

        def body(name):
            def run(E):
                for waits, fn, inc, inc_n, phase in self.q[name]:
                    for s, v in waits:
                        E.wait_ge(s.h, v)
                    if fn is None:
                        continue
                    n0 = self.nc.n_instructions()
                    inst = fn(E)
                    phase_of[n0] = (phase, name)
                    if inc is not None:
                        inst.then_inc(inc.h, inc_n)
            return run
        with self.nc.Block() as block:
            block.tensor(body("tensor"))
            block.vector(body("vector"))
            block.scalar(body("scalar"))
            block.gpsimd(body("gpsimd"))
            block.sync(body("sync"))


class Tracker:
    def __init__(self):
        self.w = {}
        self.r = {}
        self.rb = {}

    def wait_read_bank(self, key, eng):
        out = list(self.wait_read(key))
        last = self.rb.get(key)
        if last is not None and last[0] != eng:
            out.append((last[1], last[2]))
        return out

    def note_read_bank(self, key, eng, sem, val):
        if val is None:
            return
        self.rb[key] = (eng, sem, val)
        self.note_read(key, sem, val)

    def wait_read(self, *keys):
        return [self.w[k] for k in keys if k in self.w]

    def wait_write(self, *keys):
        out = []
        for k in keys:
            if k in self.w:
                out.append(self.w[k])
            for s, v in self.r.get(k, {}).values():
                out.append((s, v))
        return out

    def note_write(self, key, sem, val):
        if val is None:
            return
        self.w[key] = (sem, val)
        self.r.pop(key, None)
        self.rb.pop(key, None)

    def note_read(self, key, sem, val):
        if val is None:
            return
        d = self.r.setdefault(key, {})
        old = d.get(id(sem))
        if old is None or old[1] < val:
            d[id(sem)] = (sem, val)


ARENA_BYTES = 195 * 1024


class Arena:
    def __init__(self, t):
        self.t = t

    def f32(self, off, n):
        assert off % 4 == 0 and off + 4 * n <= ARENA_BYTES, (off, n)
        return self.t.ap()[:, off // 4: off // 4 + n]

    def bf16(self, off, n):
        assert off % 4 == 0 and off + 2 * n <= ARENA_BYTES, (off, n)
        m = (n + 1) // 2
        return self.t.ap()[:, off // 4: off // 4 + m].bitcast(BF16)[:, :n]


def build_program(n_layers=DEPTH, n_batches=BLOC, dump_stage=None):
    nc = bass.Bass("TRN2", target_bir_lowering=False, debug=DEBUG_BUILD,
                   detect_race_conditions=False)
    stack = contextlib.ExitStack()
    nc._dev_stack = stack

    x_in = nc.dram_tensor("x_in", [BLOC, T, DM], F32, kind="ExternalInput").ap()
    out = nc.dram_tensor("out", [BLOC, T, DM], F32, kind="ExternalOutput").ap()
    gwq = nc.dram_tensor("gwq", [DM, DM], F32, kind="ExternalInput").ap()
    gwk = nc.dram_tensor("gwk", [DM, DM], F32, kind="ExternalInput").ap()
    gwv = nc.dram_tensor("gwv", [DM, DM], F32, kind="ExternalInput").ap()
    gwo = nc.dram_tensor("gwo", [DM, DM], F32, kind="ExternalInput").ap()
    iwqkv = nc.dram_tensor("iwqkv", [DEPTH, DM, 3 * DM], F32, kind="ExternalInput").ap()
    iwo = nc.dram_tensor("iwo", [DEPTH, DM, DM], F32, kind="ExternalInput").ap()
    iw1 = nc.dram_tensor("iw1", [DEPTH, DM, DI2], F32, kind="ExternalInput").ap()
    iw2 = nc.dram_tensor("iw2", [DEPTH, DI, DM], F32, kind="ExternalInput").ap()
    qscol = nc.dram_tensor("qscol", [DEPTH, 128, 4], F32, kind="ExternalInput").ap()
    lng_col = nc.dram_tensor("lng_col", [DEPTH, 128, NCH], F32, kind="ExternalInput").ap()
    lnb_col = nc.dram_tensor("lnb_col", [DEPTH, 128, NCH], F32, kind="ExternalInput").ap()
    b1_col = nc.dram_tensor("b1_col", [DEPTH, 128, 2 * NPAIR], F32, kind="ExternalInput").ap()
    b2_col = nc.dram_tensor("b2_col", [DEPTH, 128, NCH], F32, kind="ExternalInput").ap()
    tbl_g = nc.dram_tensor("tbl_g", [2, 128, T], F32, kind="ExternalInput").ap()
    tbl_i = nc.dram_tensor("tbl_i", [4, 128, T], F32, kind="ExternalInput").ap()
    identity_d = nc.dram_tensor("identity", [128, 128], F32, kind="ExternalInput").ap()
    trimask4_d = nc.dram_tensor("trimask4", [128, 512], BF16, kind="ExternalInput").ap()
    trimask2_d = nc.dram_tensor("trimask2", [128, 256], BF16, kind="ExternalInput").ap()
    tri1_d = nc.dram_tensor("tri1one", [128, 128], BF16, kind="ExternalInput").ap()
    eden_d = nc.dram_tensor("eden", [128, 4], BF16, kind="ExternalInput").ap()
    enorm_d = nc.dram_tensor("enorm", [4, 128, 16], BF16, kind="ExternalInput").ap()
    fnorm_d = nc.dram_tensor("fnorm", [4, 16, 128], F32, kind="ExternalInput").ap()
    fpair_d = nc.dram_tensor("fpair", [2, 128], F32, kind="ExternalInput").ap()
    onescb_d = nc.dram_tensor("ones_col_bf", [128, 1], BF16, kind="ExternalInput").ap()
    onescf_d = nc.dram_tensor("ones_col_f32", [128, 1], F32, kind="ExternalInput").ap()
    onesr_d = nc.dram_tensor("ones_row_f32", [1, 128], F32, kind="ExternalInput").ap()
    eps_d = nc.dram_tensor("eps_col", [128, 1], F32, kind="ExternalInput").ap()
    x1sp = nc.dram_tensor("x1sp", [DM, T], F32).ap()

    arena_t = stack.enter_context(nc.sbuf_tensor("arena", [128, ARENA_BYTES // 4], F32))
    A = Arena(arena_t)
    csb = stack.enter_context(nc.sbuf_tensor("csb", [128, 3072], F32))

    C_ID = 0
    C_TRIM4 = 128
    C_TRIM2 = 384
    C_TRI1 = 512
    C_EDEN = 576
    C_ENORM = 704
    C_FNORM = 736
    C_FPAIR = 1248
    C_ONESCB = 2272
    C_ONESCF = 2273
    C_ONESR = 2274
    C_EPS = 2528
    C_EPS2 = 2529
    C_QS = 2402
    C_GCOL = 2408
    C_BLNF = 2416
    C_BLNB = 2424
    C_B1 = 2428
    C_B2 = 2472
    C_U1 = 2480
    C_END = 2530
    assert C_END <= 3072

    def cs_f32(off, n):
        return csb.ap()[:, off:off + n]

    def cs_bf(off, n):
        m = (n + 1) // 2
        return csb.ap()[:, off:off + m].bitcast(BF16)[:, :n]

    P = Plan(nc, stack)
    TK = Tracker()
    s_bar = P.sem("bar")
    s_pe = P.sem("pe")
    s_dve = P.sem("dve")
    s_act = P.sem("act")
    s_pool = P.sem("pool")
    s_ld = P.sem("ld", is_dma=True)
    s_x = P.sem("x", is_dma=True)
    s_tb0 = P.sem("tb0", is_dma=True)
    s_tb1 = P.sem("tb1", is_dma=True)
    s_wA = P.sem("wA", is_dma=True)
    s_wB = P.sem("wB", is_dma=True)
    s_w1 = [P.sem("w1a", is_dma=True), P.sem("w1b", is_dma=True)]
    s_w2 = [P.sem("w2a", is_dma=True), P.sem("w2b", is_dma=True)]
    s_vec = P.sem("vec", is_dma=True)
    s_x1r = P.sem("x1r", is_dma=True)
    s_st = P.sem("st", is_dma=True)
    ALL_DMA_SEMS = (s_ld, s_x, s_tb0, s_tb1, s_wA, s_wB, s_w1[0], s_w1[1],
                    s_w2[0], s_w2[1], s_vec, s_x1r, s_st)
    s_tbs = (s_tb0, s_tb1)

    ps_t = stack.enter_context(nc.psum_tensor("ps", [128, 4096], F32))

    def PS(bank, rows=128, cols=512, coff=0, r0=0):
        return ps_t.ap()[r0:r0 + rows, 512 * bank + coff: 512 * bank + coff + cols]

    pend_r = []
    pend_w = []

    def mm(psum_ap, lhs_ap, rhs_ap, start, stop, wbank=(), rkeys=(), waits=(), inc=None,
           tile_position=None):
        wl = list(waits)
        if start:
            for b in wbank:
                wl += TK.wait_write(("ps", b))
        for k in rkeys:
            wl += TK.wait_read(k)
        pend_r.extend(rkeys)
        pend_w.extend(("ps", b) for b in wbank)

        def f(E, a=psum_ap, b=lhs_ap, c=rhs_ap, st=start, sp=stop, tp=tile_position):
            return E.matmul(a, b, c, start=st, stop=sp, tile_position=tp)
        v = P.op("tensor", f, waits=wl, inc=inc)
        if inc is not None:
            for k in pend_w:
                TK.note_write(k, s_pe, v)
            for k in pend_r:
                TK.note_read(k, s_pe, v)
            pend_r.clear()
            pend_w.clear()
        return v

    def generic(eng, sem, fn, rkeys=(), wkeys=(), rbanks=(), wbanks=(), waits=(), inc=True):
        wl = list(waits)
        for k in rkeys:
            wl += TK.wait_read(k)
        for b in rbanks:
            wl += TK.wait_read_bank(("ps", b), eng)
        for k in wkeys:
            wl += TK.wait_write(k)
        for b in wbanks:
            wl += TK.wait_write(("ps", b))
        v = P.op(eng, fn, waits=wl, inc=sem if inc else None)
        if inc:
            for k in rkeys:
                TK.note_read(k, sem, v)
            for b in rbanks:
                TK.note_read_bank(("ps", b), eng, sem, v)
            for k in wkeys:
                TK.note_write(k, sem, v)
            for b in wbanks:
                TK.note_write(("ps", b), sem, v)
        return v

    def v_tt(o, a, b, op, **kw):
        return generic("vector", s_dve, lambda E: E.tensor_tensor(o, a, b, op), **kw)

    def v_stt(o, a, sc, b, op0, op1, **kw):
        return generic("vector", s_dve, lambda E: E.scalar_tensor_tensor(o, a, sc, b, op0, op1), **kw)

    def v_ts(o, a, s1, s2, op0, op1=None, **kw):
        if op1 is None:
            return generic("vector", s_dve, lambda E: E.tensor_scalar(o, a, s1, None, op0), **kw)
        return generic("vector", s_dve, lambda E: E.tensor_scalar(o, a, s1, s2, op0, op1), **kw)

    def v_copy(o, a, **kw):
        return generic("vector", s_dve, lambda E: E.tensor_copy(o, a), **kw)

    def v_recip(o, a, **kw):
        return generic("vector", s_dve, lambda E: E.reciprocal(o, a), **kw)

    def v_memset(o, val, **kw):
        return generic("vector", s_dve, lambda E: E.memset(o, val), **kw)

    def a_act(o, a, func, bias=None, scale=None, accum_out=None, **kw):
        def f(E):
            kk = {}
            if bias is not None:
                kk["bias"] = bias
            if scale is not None:
                kk["scale"] = scale
            if accum_out is not None:
                kk["accum_out"] = accum_out
            return E.activation(o, a, func, **kk)
        return generic("scalar", s_act, f, **kw)

    def p_tt(o, a, b, op, **kw):
        return generic("gpsimd", s_pool, lambda E: E.tensor_tensor(o, a, b, op), **kw)

    def _dma(eng, sem, o, a, **kw):
        wl = list(kw.pop("waits", ()))
        rkeys = kw.pop("rkeys", ())
        wkeys = kw.pop("wkeys", ())
        for k in rkeys:
            wl += TK.wait_read(k)
        for k in wkeys:
            wl += TK.wait_write(k)
        v = P.dma(eng, lambda E: E.dma_start(out=o, in_=a), waits=wl, inc=sem)
        for k in rkeys:
            TK.note_read(k, sem, v)
        for k in wkeys:
            TK.note_write(k, sem, v)
        return v

    def dma_sync(o, a, sem, **kw):
        return _dma("sync", sem, o, a, **kw)

    def dma_pool(o, a, sem, **kw):
        return _dma("gpsimd", sem, o, a, **kw)

    def dma_store(o, a, **kw):
        wl = list(kw.pop("waits", ()))
        for k in kw.pop("rkeys", ()):
            wl += TK.wait_read(k)
        v = P.dma("sync", lambda E: E.dma_start(out=o, in_=a), waits=wl, inc=s_st)
        for k in kw.get("note_rkeys", ()):
            TK.note_read(k, s_st, v)
        return v

    def transpose(psum_ap, in_ap, bank, rkey=None, waits=(), start_clear=True,
                  stop=False):
        wl = list(waits)
        if start_clear:
            wl += TK.wait_write(("ps", bank))
        if rkey is not None:
            wl += TK.wait_read(rkey)
        ident = cs_f32(C_ID, 128)

        def f(E, o=psum_ap, i=in_ap, idn=ident, sp=stop):
            return E.matmul(o, i, idn, start=True, stop=sp, is_transpose=True,
                            skip_group_check=True)
        v = P.op("tensor", f, waits=wl, inc=s_pe)
        pend_w.append(("ps", bank))
        if rkey is not None:
            pend_r.append(rkey)
        for k in pend_w:
            TK.note_write(k, s_pe, v)
        for k in pend_r:
            TK.note_read(k, s_pe, v)
        pend_r.clear()
        pend_w.clear()
        return v

    def barrier():
        P.barrier(s_bar, dma_sems=ALL_DMA_SEMS)
        TK.w.clear()
        TK.r.clear()
        TK.rb.clear()

    dma_sync(cs_f32(C_ID, 128), identity_d, s_ld)
    dma_sync(cs_bf(C_TRIM4, 512), trimask4_d, s_ld)
    dma_sync(cs_bf(C_TRIM2, 256), trimask2_d, s_ld)
    dma_sync(cs_bf(C_TRI1, 128), tri1_d, s_ld)
    dma_sync(cs_bf(C_EDEN, 4), eden_d, s_ld)
    dma_sync(cs_bf(C_ENORM, 64).rearrange("p (g n) -> p g n", g=4),
             enorm_d.rearrange("g p n -> p g n"), s_ld)
    dma_sync(csb.ap()[0:16, C_FNORM:C_FNORM + 512].rearrange("p (g n) -> p g n", g=4),
             fnorm_d.rearrange("g p n -> p g n"), s_ld)
    dma_sync(csb.ap()[0:2, C_FPAIR:C_FPAIR + 128], fpair_d, s_ld)
    dma_sync(cs_bf(C_ONESCB, 1), onescb_d, s_ld)
    dma_sync(cs_f32(C_ONESCF, 1), onescf_d, s_ld)
    dma_sync(csb.ap()[0:1, C_ONESR:C_ONESR + 128], onesr_d, s_ld)
    dma_sync(cs_f32(C_EPS, 1), eps_d, s_ld)
    barrier()
    v_ts(cs_f32(C_EPS2, 1), cs_f32(C_EPS, 1), 1e-19, None, ALU.mult,
         wkeys=(("eps2",),))
    barrier()

    G_XBF, G_KH, G_VH = 0, 32768, 65536
    G_W1, G_W2 = 98304, 114688
    G_TBL = 131072
    G_TT = 139264
    G_QS = 147456
    G_PT = 155648
    G_XST = 159744
    G_OS = 176128
    G_BC = 184320
    G_RC = 186368
    G_MU = 188416
    G_M2 = 190464
    G_STG = 192512
    assert G_STG + 2048 <= ARENA_BYTES

    def xg_bf(ch, t0, n):
        return A.bf16(G_XBF + 2 * 2048 * ch, 2048)[:, t0:t0 + n]

    def khat_g(r0, rows, ch, c0, n):
        return A.bf16(G_KH + 2 * 2048 * ch, 2048)[r0:r0 + rows, c0:c0 + n]

    def vhat_g(tt, c0, n):
        return A.bf16(G_VH + 2 * 1024 * tt, 1024)[:, c0:c0 + n]

    def rope_pass_global(bA, bB, outA, outB, tb_c, tb_s, t1b, t2b, keys):
        for cc, ub, us, sgn, o in ((0, bA, bB, -1.0, outA), (1, bB, bA, 1.0, outB)):
            v_tt(t1b, PS(ub), tb_c, ALU.mult,
                 rbanks=(ub,), rkeys=(keys["tbl"],), wkeys=(keys["t1"],))
            v_stt(t2b, PS(us), sgn, tb_s, ALU.mult, ALU.mult,
                  rbanks=(us,), rkeys=(keys["tbl"],), wkeys=(keys["t2"],))
            p_tt(o, t1b, t2b, ALU.add,
                 rkeys=(keys["t1"], keys["t2"]), wkeys=(keys["out"],))

    def emit_global(b):
        P.mark("g0")
        stg = A.f32(G_STG, 512)
        for ch in range(NCH):
            for tg in range(4):
                src = x_in[b, 512 * tg:512 * (tg + 1), 128 * ch:128 * (ch + 1)] \
                    .rearrange("(tt p) c -> p tt c", p=128)
                dma_sync(stg.rearrange("p (tt c) -> p tt c", tt=4), src, s_x, wkeys=(("gstg",),))
                bank = tg % 2
                for i in range(4):
                    transpose(PS(bank, 128, 128, 128 * i), stg[:, 128 * i:128 * (i + 1)],
                              bank, rkey=("gstg",), start_clear=(i == 0), stop=(i == 3))
                v_copy(xg_bf(ch, 512 * tg, 512), PS(bank), rbanks=(bank,),
                       wkeys=(("gxbf", ch, tg),))
        barrier()

        P.mark("g1")
        wk_bf = A.bf16(G_W1, 8 * 1024)
        wv_bf = A.bf16(G_W2, 8 * 1024)
        dma_pool(wk_bf.rearrange("p (c n) -> p c n", c=8), gwk.rearrange("(c p) n -> p c n", p=128), s_wA, wkeys=(("gw1",),))
        dma_pool(wv_bf.rearrange("p (c n) -> p c n", c=8), gwv.rearrange("(c p) n -> p c n", p=128), s_wB, wkeys=(("gw2",),))
        tbc = [A.f32(G_TBL + 4096 * i, 512) for i in range(2)]
        tbs = [A.f32(G_TBL + 2048 + 4096 * i, 512) for i in range(2)]
        t1b = [A.f32(G_TT + 4096 * i, 512) for i in range(2)]
        t2b = [A.f32(G_TT + 2048 + 4096 * i, 512) for i in range(2)]
        for ss in range(4):
            t0 = 512 * ss
            sl = ss % 2
            dma_sync(tbc[sl], tbl_g[0][:, t0:t0 + 512], s_tbs[sl], wkeys=(("gtb", sl),))
            dma_sync(tbs[sl], tbl_g[1][:, t0:t0 + 512], s_tbs[sl], wkeys=(("gtb", sl),))
            for g in range(4):
                for cc, bank in ((g, 4), (4 + g, 5)):
                    for kc in range(NCH):
                        mm(PS(bank), wk_bf[:, 1024 * kc + 128 * cc:1024 * kc + 128 * cc + 128],
                           xg_bf(kc, t0, 512), start=(kc == 0), stop=(kc == NCH - 1),
                           wbank=(bank,), rkeys=(("gw1",), ("gxbf", kc, ss)),
                           inc=s_pe if kc == NCH - 1 else None)
                rope_pass_global(4, 5,
                                 khat_g(0, 128, g, t0, 512), khat_g(0, 128, 4 + g, t0, 512),
                                 tbc[sl], tbs[sl], t1b[sl], t2b[sl],
                                 {"t1": ("gt1", sl), "t2": ("gt2", sl),
                                  "tbl": ("gtb", sl), "out": ("gkh",)})
            for tt in range(4 * ss, 4 * ss + 4):
                for nt in range(2):
                    bank = 6 + nt
                    for kc in range(NCH):
                        mm(PS(bank), xg_bf(kc, 128 * tt - t0 + t0, 128) if False else
                           xg_bf(kc, 128 * tt, 128),
                           wv_bf[:, 1024 * kc + 512 * nt:1024 * kc + 512 * nt + 512],
                           start=(kc == 0), stop=(kc == NCH - 1),
                           wbank=(bank,), rkeys=(("gw2",), ("gxbf", kc, ss)),
                           inc=s_pe if kc == NCH - 1 else None)
                    v_copy(vhat_g(tt, 512 * nt, 512), PS(bank), rbanks=(bank,),
                           wkeys=(("gvh",),))
        barrier()

        P.mark("g2")
        wq_bf = A.bf16(G_W1, 8 * 1024)
        wo_bf = A.bf16(G_W2, 8 * 1024)
        dma_pool(wq_bf.rearrange("p (c n) -> p c n", c=8), gwq.rearrange("(c p) n -> p c n", p=128), s_wA, wkeys=(("gw1",),))
        dma_pool(wo_bf.rearrange("p (c n) -> p c n", c=8), gwo.rearrange("(c p) n -> p c n", p=128), s_wB, wkeys=(("gw2",),))
        qsb = A.bf16(G_QS, 8 * 512)
        ptb = A.bf16(G_PT, 2048)
        xst = A.f32(G_XST, 8 * 512)
        sqb = A.bf16(G_OS, 8 * 512)
        obuf = A.bf16(G_OS, 8 * 512)
        bce = A.f32(G_BC, 512)
        rcp = A.f32(G_RC, 512)[0:1, :]
        mu = A.f32(G_MU, 512)[0:1, :]
        m2 = A.f32(G_M2, 512)[0:1, :]
        for ss in range(4):
            t0 = 512 * ss
            sl = ss % 2
            dma_sync(tbc[sl], tbl_g[0][:, t0:t0 + 512], s_tbs[sl], wkeys=(("gtb", sl),))
            dma_sync(tbs[sl], tbl_g[1][:, t0:t0 + 512], s_tbs[sl], wkeys=(("gtb", sl),))
            for g in range(4):
                for cc, bank in ((g, 4), (4 + g, 5)):
                    for kc in range(NCH):
                        mm(PS(bank), wq_bf[:, 1024 * kc + 128 * cc:1024 * kc + 128 * cc + 128],
                           xg_bf(kc, t0, 512), start=(kc == 0), stop=(kc == NCH - 1),
                           wbank=(bank,), rkeys=(("gw1",),),
                           inc=s_pe if kc == NCH - 1 else None)
                rope_pass_global(4, 5,
                                 A.bf16(G_QS + 2 * 512 * g, 512),
                                 A.bf16(G_QS + 2 * 512 * (4 + g), 512),
                                 tbc[sl], tbs[sl], t1b[sl], t2b[sl],
                                 {"t1": ("gt1", sl), "t2": ("gt2", sl),
                                  "tbl": ("gtb", sl), "out": ("gqs",)})
            for ch in range(NCH):
                src = x_in[b, t0:t0 + 512, 128 * ch:128 * (ch + 1)] \
                    .rearrange("(tt p) c -> p tt c", p=128)
                dma_sync(stg.rearrange("p (tt c) -> p tt c", tt=4), src, s_x, wkeys=(("gstg",),))
                for i in range(4):
                    transpose(PS(6, 128, 128, 128 * i), stg[:, 128 * i:128 * (i + 1)],
                              6, rkey=("gstg",), start_clear=(i == 0), stop=(i == 3))
                v_copy(xst[:, 512 * ch:512 * (ch + 1)], PS(6), rbanks=(6,),
                       wkeys=(("gxst", ch),))
            nkt = 4 * (ss + 1)
            for h in range(GH):
                r0 = 64 * (h % 2)
                cA, cB = h // 2, 4 + h // 2
                pv_started = False
                dn_started = False
                for ktg in range(0, nkt, 4):
                    gsz = min(4, nkt - ktg)
                    for i in range(gsz):
                        kt = ktg + i
                        qlo = max(0, 128 * kt - t0)
                        nq = 512 - qlo
                        for half, cc in ((0, cA), (1, cB)):
                            mm(PS(i, 128, nq, qlo),
                               khat_g(r0, 64, cc, 128 * kt, 128),
                               A.bf16(G_QS + 2 * 512 * cc, 512)[r0:r0 + 64, qlo:qlo + nq],
                               start=(half == 0), stop=(half == 1),
                               wbank=(i,), rkeys=(("gkh",), ("gqs",)),
                               tile_position=(r0, 0),
                               inc=s_pe if half == 1 else None)
                    if 128 * (ktg + gsz) <= t0:
                        a_act(ptb[:, 0:512 * gsz], ps_t.ap()[:, 0:512 * gsz], AF.Exp,
                              scale=1.0 / 128.0,
                              rbanks=tuple(range(gsz)), wkeys=(("gpt",),))
                    else:
                        for i in range(gsz):
                            kt = ktg + i
                            qlo = max(0, 128 * kt - t0)
                            a_act(ptb[:, 512 * i + qlo:512 * (i + 1)],
                                  PS(i, 128, 512 - qlo, qlo), AF.Exp, scale=1.0 / 128.0,
                                  rbanks=(i,), wkeys=(("gpt",),))
                            if 128 * kt >= t0:
                                v_tt(ptb[:, 512 * i + qlo:512 * i + qlo + 128],
                                     ptb[:, 512 * i + qlo:512 * i + qlo + 128],
                                     cs_bf(C_TRI1, 128), ALU.mult,
                                     rkeys=(("gpt",),), wkeys=(("gpt",),))
                    for i in range(gsz):
                        kt = ktg + i
                        qlo = max(0, 128 * kt - t0)
                        nq = 512 - qlo
                        last = (kt == nkt - 1)
                        mm(PS(6, 1, nq, qlo), cs_bf(C_ONESCB, 1),
                           ptb[:, 512 * i + qlo:512 * (i + 1)],
                           start=not dn_started, stop=last,
                           wbank=(6,), rkeys=(("gpt",),),
                           inc=s_pe if last else None)
                        dn_started = True
                        mm(PS(7, 128, nq, qlo), vhat_g(kt, 128 * h, 128),
                           ptb[:, 512 * i + qlo:512 * (i + 1)],
                           start=not pv_started, stop=last,
                           wbank=(7,),
                           rkeys=(("gvh",), ("gpt",)),
                           inc=s_pe if last else None)
                        pv_started = True
                a_act(rcp, PS(6, 1, 512), AF.Ln, rbanks=(6,), wkeys=(("grcp",),))
                a_act(rcp, rcp, AF.Exp, scale=-1.0, rkeys=(("grcp",),), wkeys=(("grcp",),))
                mm(PS(5), csb.ap()[0:1, C_ONESR:C_ONESR + 128], rcp, start=True, stop=True,
                   wbank=(5,), rkeys=(("grcp",),), inc=s_pe)
                v_copy(bce, PS(5), rbanks=(5,), wkeys=(("gbce",),))
                v_tt(obuf[:, 512 * h:512 * (h + 1)], PS(7), bce, ALU.mult,
                     rbanks=(7,), rkeys=(("gbce",),), wkeys=(("gos", h),))
            for mt in range(NCH):
                for hc in range(NCH):
                    mm(PS(mt), wo_bf[:, 1024 * hc + 128 * mt:1024 * hc + 128 * mt + 128],
                       obuf[:, 512 * hc:512 * (hc + 1)],
                       start=(hc == 0), stop=(hc == NCH - 1),
                       wbank=(mt,), rkeys=(("gw2",), ("gos", hc)),
                       inc=s_pe if hc == NCH - 1 else None)
                v_tt(xst[:, 512 * mt:512 * (mt + 1)], PS(mt), xst[:, 512 * mt:512 * (mt + 1)],
                     ALU.add, rbanks=(mt,), rkeys=(("gxst", mt),), wkeys=(("gxst", mt),))
            if getattr(build_program, "_dbg", False) and ss in (0,):
                def _dbgx(E, t=xst, tag=(b, ss)):
                    import numpy as _np
                    E.dbg_aps(lambda a, g, tg=tag: print("XSTDBG", tg, _np.asarray(a), _np.asarray(g)),
                              t[0:1, 0:4], t[5:6, 512 * 3:512 * 3 + 4])
                    return E.nop()
                P.op("vector", _dbgx,
                     waits=sum((TK.wait_read(("gxst", c)) + TK.wait_write(("gxst", c))
                                for c in range(NCH)), []))
            for ch in range(NCH):
                a_act(sqb[:, 512 * ch:512 * (ch + 1)], xst[:, 512 * ch:512 * (ch + 1)],
                      AF.Square, rkeys=(("gxst", ch),), wkeys=(("gos", ch),))
            for ch in range(NCH):
                mm(PS(4, 1, 512), cs_f32(C_ONESCF, 1), xst[:, 512 * ch:512 * (ch + 1)],
                   start=(ch == 0), stop=(ch == NCH - 1),
                   wbank=(4,), rkeys=(("gxst", ch),),
                   inc=s_pe if ch == NCH - 1 else None)
            for ch in range(NCH):
                mm(PS(5, 1, 512), cs_bf(C_ONESCB, 1), sqb[:, 512 * ch:512 * (ch + 1)],
                   start=(ch == 0), stop=(ch == NCH - 1),
                   wbank=(5,), rkeys=(("gos", ch),),
                   inc=s_pe if ch == NCH - 1 else None)
            a_act(mu, PS(4, 1, 512), AF.Copy, scale=1.0 / DM, rbanks=(4,), wkeys=(("gmu",),))
            a_act(m2, PS(4, 1, 512), AF.Square, scale=1.0 / DM, rbanks=(4,), wkeys=(("gm2",),))
            v_stt(m2, PS(5, 1, 512), 1.0 / DM, m2, ALU.mult, ALU.subtract,
                  rbanks=(5,), rkeys=(("gm2",),), wkeys=(("gm2",),))
            a_act(m2, m2, AF.Ln, bias=cs_f32(C_EPS, 1)[0:1, :], rkeys=(("gm2",),), wkeys=(("gm2",),))
            a_act(m2, m2, AF.Exp, scale=-0.5, rkeys=(("gm2",),), wkeys=(("gm2",),))
            v_stt(mu, mu, -1.0, m2, ALU.mult, ALU.mult,
                  rkeys=(("gmu",), ("gm2",)), wkeys=(("gmu",),))
            mm(PS(4), csb.ap()[0:1, C_ONESR:C_ONESR + 128], m2, start=True, stop=True,
               wbank=(4,), rkeys=(("gm2",),), inc=s_pe)
            mm(PS(5), csb.ap()[0:1, C_ONESR:C_ONESR + 128], mu, start=True, stop=True,
               wbank=(5,), rkeys=(("gmu",),), inc=s_pe)
            for ch in range(NCH):
                v_tt(xst[:, 512 * ch:512 * (ch + 1)], xst[:, 512 * ch:512 * (ch + 1)],
                     PS(4), ALU.mult, rbanks=(4,), rkeys=(("gxst", ch),), wkeys=(("gxst", ch),))
                v_tt(xst[:, 512 * ch:512 * (ch + 1)], xst[:, 512 * ch:512 * (ch + 1)],
                     PS(5), ALU.add, rbanks=(5,), rkeys=(("gxst", ch),), wkeys=(("gxst", ch),))
            dma_store(x1sp[:, t0:t0 + 512].rearrange("(c p) t -> p c t", p=128),
                      xst.rearrange("p (c t) -> p c t", c=8),
                      rkeys=tuple(("gxst", ch) for ch in range(NCH)),
                      note_rkeys=tuple(("gxst", ch) for ch in range(NCH)))
        barrier()

    I_X = 0
    I_WQKV = 65568
    I_WO = 114720
    I_XBFW = 131104
    I_QPRE = 139328
    I_QBF = 147520
    I_KBF = 151616
    I_VBF = 159808
    I_USQ = 168000
    I_PT = 169024
    I_TBL = 175168
    I_TT = 183360
    I_OBUF = 185408
    I_BCE = 189504
    I_NRM = 190528
    I_END = 192576
    assert I_END <= ARENA_BYTES
    F_H = I_WQKV
    F_XSQ = I_WO
    F_W1 = I_WO + 8224
    F_XN = 139328
    F_W2 = 155712
    F_GEL = 163904
    F_STA = 168000
    F_GEL2 = 174144
    F_END = 178240
    assert F_END <= ARENA_BYTES

    def x_sl(ch, c0, n):
        return A.f32(I_X + 4 * 2049 * ch, 2049)[:, c0:c0 + n]

    def inner_qk(ten, w, slot, outfn, sA, sB, nsB, wqkv, xbfw, tb, t1b, t2b, qpre, usq,
                 nrm, recd, layer_key):
        base = 1024 * ten
        t0 = WIN * w
        _full = "qkv" in ATTN_PARTS or "q" in ATTN_PARTS or "k" in ATTN_PARTS
        _p_mm = _full or "qk_mm" in ATTN_PARTS
        _p_sq = _p_mm and (_full or "qk_sq" in ATTN_PARTS)
        _p_rot = _p_mm and (_full or "qk_rot" in ATTN_PARTS)
        _p_bc = _p_sq and (_full or "qk_bc" in ATTN_PARTS)
        enb = 6 + ten
        for st in range(5 if DEFER_QKV else 4):
            if st < 4 and _p_mm:
                g = st
                b0 = 2 * (g % 2) if QKV_ALT else 0
                for cc, bank in ((g, b0), (4 + g, b0 + 1)):
                    for kc in range(NCH):
                        o = 257 * kc + (1 if kc < 4 else 0)
                        mm(PS(bank, 128, 256),
                           wqkv[:, 3072 * kc + base + 128 * cc:3072 * kc + base + 128 * cc + 128],
                           xbfw[slot][:, o:o + 256],
                           start=(kc == 0), stop=(kc == NCH - 1),
                           wbank=(bank,), rkeys=(layer_key, ("xbfw", slot)),
                           inc=s_pe if kc == NCH - 1 else None)
            gi = st if st < 4 else -1
            gd = st - 1 if st >= 1 else -1
            g = gd if (DEFER_QKV and DEFER_SQ) else gi
            if g >= 0:
                b0 = 2 * (g % 2) if QKV_ALT else 0
                if _p_sq:
                    a_act(usq[:, 0:256], PS(b0, 128, 256), AF.Square, rbanks=(b0,),
                          wkeys=(("usq", 0),))
                    a_act(usq[:, 256:512], PS(b0 + 1, 128, 256), AF.Square, rbanks=(b0 + 1,),
                          wkeys=(("usq", 1),))
                    for j in range(2):
                        mm(PS(enb, 16, 256), cs_bf(C_ENORM, 64)[:, 16 * g:16 * (g + 1)],
                           usq[:, 256 * j:256 * (j + 1)],
                           start=(g == 0 and j == 0), stop=(g == 3 and j == 1),
                           wbank=(enb,), rkeys=(("usq", j),),
                           inc=s_pe if (g == 3 and j == 1) else None)
            g = gd if (DEFER_QKV and DEFER_ROT) else gi
            if g >= 0:
                b0 = 2 * (g % 2) if QKV_ALT else 0
                if _p_rot:
                    for cc, ub, us, s1, s2 in ((g, b0, b0 + 1, sA, nsB), (4 + g, b0 + 1, b0, sB, sA)):
                        v_stt(t1b, PS(ub, 128, 256), s1, tb[2 * ten], ALU.mult, ALU.mult,
                              rbanks=(ub,), rkeys=(("itb", slot),), wkeys=(("it1",),))
                        v_stt(t2b, PS(us, 128, 256), s2, tb[2 * ten + 1], ALU.mult, ALU.mult,
                              rbanks=(us,), rkeys=(("itb", slot),), wkeys=(("it2",),))
                        p_tt(qpre[:, 256 * cc:256 * (cc + 1)], t1b, t2b, ALU.add,
                             rkeys=(("it1",), ("it2",)), wkeys=(("qpre", cc),))
        if _p_sq:
            a_act(nrm, PS(enb, 16, 256), AF.Ln, bias=cs_f32(C_EPS2, 1)[0:16, :],
                  rbanks=(enb,), wkeys=(("nrm",),))
            a_act(recd, nrm, AF.Exp, scale=-0.5, rkeys=(("nrm",),), wkeys=(("recd",),))
        if _p_bc:
            for g in range(4):
                fb = (6 + ((ten + g) % 2)) if FN_ALT else enb
                mm(PS(fb, 128, 256), csb.ap()[0:16, C_FNORM + 128 * g:C_FNORM + 128 * (g + 1)],
                   recd, start=True, stop=True, wbank=(fb,), rkeys=(("recd",),), inc=s_pe)
                for cc in (g, 4 + g):
                    v_tt(outfn(cc), qpre[:, 256 * cc:256 * (cc + 1)], PS(fb, 128, 256), ALU.mult,
                         rbanks=(fb,), rkeys=(("qpre", cc),), wkeys=(("qk_out", ten),))

    def emit_inner_attn(b, l):
        P.mark("L%da" % l)
        wqkv = A.bf16(I_WQKV, 8 * 3072)
        wo_bf = A.bf16(I_WO, 8 * 1024)
        lk = ("iw", l)
        dma_pool(wqkv.rearrange("p (c n) -> p c n", c=8),
                 iwqkv[l].rearrange("(c p) n -> p c n", p=128), s_wA, wkeys=(lk,))
        dma_pool(wo_bf.rearrange("p (c n) -> p c n", c=8),
                 iwo[l].rearrange("(c p) n -> p c n", p=128), s_wB, wkeys=(lk,))
        dma_sync(cs_f32(C_QS, 4), qscol[l], s_vec, wkeys=(("qs",),))
        v_ts(cs_f32(C_QS + 4, 1), cs_f32(C_QS + 1, 1), -1.0, None, ALU.mult,
             rkeys=(("qs",),), wkeys=(("qs",),))
        v_ts(cs_f32(C_QS + 5, 1), cs_f32(C_QS + 3, 1), -1.0, None, ALU.mult,
             rkeys=(("qs",),), wkeys=(("qs",),))
        qsA, qsB = cs_f32(C_QS, 1), cs_f32(C_QS + 1, 1)
        ksA, ksB = cs_f32(C_QS + 2, 1), cs_f32(C_QS + 3, 1)
        nqsB, nksB = cs_f32(C_QS + 4, 1), cs_f32(C_QS + 5, 1)

        xbfw = [A.bf16(I_XBFW + 4112 * i, 8 * 257) for i in range(2)]
        qpre = A.f32(I_QPRE, 8 * 256)
        qbf = A.bf16(I_QBF, 8 * 256)
        kbf = A.bf16(I_KBF, 8 * 512)
        vbf = A.bf16(I_VBF, 4 * 1024)
        usq = A.bf16(I_USQ, 512)
        ptb = [A.bf16(I_PT + 1536 * i, 768) for i in range(4)]
        tb = [[A.f32(I_TBL + 1024 * j + 4096 * i, 256) for j in range(4)] for i in range(2)]
        t1b = A.f32(I_TT, 256)
        t2b = A.f32(I_TT + 1024, 256)
        ob = A.bf16(I_OBUF, 8 * 256)
        bce = A.f32(I_BCE, 256)
        nrm = A.f32(I_NRM, 256)[0:16, :]
        recd = A.f32(I_NRM + 1024, 256)[0:16, :]

        def cast_win(w):
            slot = w % 2
            t0 = WIN * w
            xsrc = A.f32(I_X, 8 * 2049).rearrange("p (c t) -> p c t", c=8)[:, :, t0:t0 + 257]
            xdst = xbfw[slot].rearrange("p (c t) -> p c t", c=8)
            v_copy(xdst, xsrc,
                   rkeys=tuple(("x", ch) for ch in range(NCH)), wkeys=(("xbfw", slot),))

        _cast_on = "qkv" in ATTN_PARTS or "cast" in ATTN_PARTS
        _tbl_on = "qkv" in ATTN_PARTS or "tbl" in ATTN_PARTS

        def emit_qkv(w):
            slot = w % 2
            t0w = WIN * w
            for j in (range(4) if _tbl_on else ()):
                dma_sync(tb[slot][j], tbl_i[j][:, t0w:t0w + 256], s_tbs[slot],
                         wkeys=(("itb", slot),))
            if ("qkv" in ATTN_PARTS or "q" in ATTN_PARTS
                    or any(p.startswith("qk_") for p in ATTN_PARTS)):
                inner_qk(0, w, slot, lambda cc: A.bf16(I_QBF + 2 * 256 * cc, 256),
                         qsA, qsB, nqsB, wqkv, xbfw, tb[slot], t1b, t2b, qpre, usq, nrm, recd, lk)
            if "qkv" in ATTN_PARTS or "k" in ATTN_PARTS:
                inner_qk(1, w, slot, lambda cc: A.bf16(I_KBF + 2 * (512 * cc + 256 * slot), 256),
                         ksA, ksB, nksB, wqkv, xbfw, tb[slot], t1b, t2b, qpre, usq, nrm, recd, lk)
            if "qkv" in ATTN_PARTS or "v" in ATTN_PARTS:
                for tt in range(2):
                    for nt in range(2):
                        bank = 2 + nt
                        for kc in range(NCH):
                            o = 257 * kc + (1 if kc < 4 else 0) + 128 * tt
                            mm(PS(bank), xbfw[slot][:, o:o + 128],
                               wqkv[:, 3072 * kc + 2048 + 512 * nt:3072 * kc + 2048 + 512 * nt + 512],
                               start=(kc == 0), stop=(kc == NCH - 1),
                               wbank=(bank,), rkeys=(lk, ("xbfw", slot)),
                               inc=s_pe if kc == NCH - 1 else None)
                        v_copy(A.bf16(I_VBF + 2 * (1024 * (2 * slot + tt) + 512 * nt), 512),
                               PS(bank), rbanks=(bank,), wkeys=(("vbf", slot),))

        if _cast_on:
            cast_win(0)
        emit_qkv(0)
        if _cast_on and NWIN > 1:
            cast_win(1)
        for w in range(NWIN):
            t0 = WIN * w
            slot = w % 2
            pslot = (w - 1) % 2
            if w + 2 < NWIN and _cast_on:
                cast_win(w + 2)
            if w == 0:
                slots = [(0, (slot, 0), 1), (1, (slot, 1), 1), (1, (slot, 0), 0)]
                maskc = cs_bf(C_TRIM2, 256)
                nmask = 2
            else:
                slots = [(0, (pslot, 0), 2), (1, (pslot, 1), 2),
                         (0, (slot, 0), 1), (1, (slot, 1), 1),
                         (0, (pslot, 1), 0), (1, (slot, 0), 0)]
                maskc = cs_bf(C_TRIM4, 512)
                nmask = 4
            nslot = len(slots)

            def pair_ln(p):
                a_act(recd[0:2, :], PS(6, 2, 256), AF.Ln, rbanks=(6,), wkeys=(("recd",),))
                a_act(recd[0:2, :], recd[0:2, :], AF.Exp, scale=-1.0,
                      rkeys=(("recd",),), wkeys=(("recd",),))

            def pair_fin(p):
                pvb = 4 + (p % 2)
                mm(PS(7, 128, 256), csb.ap()[0:2, C_FPAIR:C_FPAIR + 128],
                   recd[0:2, :], start=True, stop=True, wbank=(7,), rkeys=(("recd",),),
                   inc=s_pe)
                v_copy(bce, PS(7, 128, 256), rbanks=(7,), wkeys=(("bce",),))
                v_tt(ob[:, 256 * p:256 * (p + 1)], PS(pvb, 128, 256), bce, ALU.mult,
                     rbanks=(pvb,), rkeys=(("bce",),), wkeys=(("ob", p),))

            for st in (range(IH + 1) if "attn" in ATTN_PARTS else ()):
                if st < IH:
                    h = st
                    r0 = 32 * (h % 4)
                    g = h // 4
                    cA, cB = g, 4 + g
                    pts = ptb[h % 4]
                    sb0 = 2 * (h % 2)
                    for si, (a, (ks, kt), mk) in enumerate(slots):
                        for half, cc in ((0, cA), (1, cB)):
                            kcol = 512 * cc + 256 * ks + 128 * kt
                            mm(ps_t.ap()[:, 512 * sb0 + 128 * si:512 * sb0 + 128 * (si + 1)],
                               kbf[r0:r0 + 32, kcol:kcol + 128],
                               A.bf16(I_QBF + 2 * 256 * cc, 256)[r0:r0 + 32, 128 * a:128 * (a + 1)],
                               start=(half == 0), stop=(half == 1),
                               wbank=(sb0, sb0 + 1),
                               rkeys=(("qk_out", 0), ("qk_out", 1)),
                               tile_position=(r0, 0),
                               inc=s_pe if half == 1 else None)
                    a_act(pts[:, 0:128 * nslot], ps_t.ap()[:, 512 * sb0:512 * sb0 + 128 * nslot],
                          AF.Exp, scale=8.0,
                          rbanks=(sb0, sb0 + 1), wkeys=(("pt", h % 4),))
                    v_tt(pts[:, 0:128 * nmask], pts[:, 0:128 * nmask], maskc, ALU.mult,
                         rkeys=(("pt", h % 4),), wkeys=(("pt", h % 4),))
                if (st >= 1) if DEFER_HEAD else (st < IH):
                    h = (st - 1) if DEFER_HEAD else st
                    pts = ptb[h % 4]
                    pvb = 4 + ((h // 2) % 2)
                    if h % 2 == 0 and h >= 2:
                        pair_ln((h - 2) // 2)
                    for si, (a, (ks, kt), mk) in enumerate(slots):
                        mm(PS(6, 2, 128, 128 * a), cs_bf(C_EDEN, 4)[:, 2 * (h % 2):2 * (h % 2) + 2],
                           pts[:, 128 * si:128 * (si + 1)],
                           start=(si == 0 and h % 2 == 0), stop=(si == nslot - 1 and h % 2 == 1),
                           wbank=(6,),
                           rkeys=(("pt", h % 4),),
                           inc=s_pe if (si == nslot - 1 and h % 2 == 1) else None)
                    for si, (a, (ks, kt), mk) in enumerate(slots):
                        vcol = 1024 * (2 * ks + kt) + 64 * h
                        mm(PS(pvb, 64, 128, 128 * a, r0=64 * (h % 2)),
                           A.bf16(I_VBF + 2 * vcol, 64),
                           pts[:, 128 * si:128 * (si + 1)],
                           start=(si == 0), stop=(si == nslot - 1),
                           wbank=(pvb,),
                           rkeys=(("vbf", ks), ("pt", h % 4)),
                           tile_position=(0, 64 * (h % 2)),
                           inc=s_pe if si == nslot - 1 else None)
                    if h % 2 == 1 and h >= 3:
                        pair_fin((h - 3) // 2)
            if "attn" in ATTN_PARTS:
                pair_ln(7)
                pair_fin(7)
            if w + 1 < NWIN:
                emit_qkv(w + 1)
            wo_banks = (4, 5, 4, 5, 6, 7, 6, 7)
            for mt in (range(NCH) if "wo" in ATTN_PARTS else ()):
                bank = wo_banks[mt]
                for hc in range(NCH):
                    mm(PS(bank, 128, 256), wo_bf[:, 1024 * hc + 128 * mt:1024 * hc + 128 * mt + 128],
                       ob[:, 256 * hc:256 * (hc + 1)],
                       start=(hc == 0), stop=(hc == NCH - 1),
                       wbank=(bank,),
                       rkeys=(lk, ("ob", hc)),
                       inc=s_pe if hc == NCH - 1 else None)
                v_tt(x_sl(mt, t0 + 1, 256), PS(bank, 128, 256), x_sl(mt, t0 + 1, 256), ALU.add,
                     rbanks=(bank,), rkeys=(("x", mt),), wkeys=(("x", mt),))
        barrier()

    def emit_inner_ffn(b, l):
        P.mark("L%df" % l)
        hbf = A.bf16(F_H, NJT * 1024)
        xsq = A.bf16(F_XSQ, 8 * 513)
        w1s = [A.bf16(F_W1 + 8192 * i, 8 * 512) for i in range(2)]
        xn = [A.bf16(F_XN + 8192 * i, 8 * 512) for i in range(2)]
        w2s = [A.bf16(F_W2 + 4096 * i, 2 * 1024) for i in range(2)]
        gel2 = [A.f32(F_GEL, 1024), A.f32(F_GEL2, 1024)]
        stA = A.f32(F_STA, 512)[0:1, :]
        stB = A.f32(F_STA + 2048, 512)[0:1, :]
        stM = A.f32(F_STA + 4096, 512)[0:1, :]
        dma_sync(cs_f32(C_GCOL, 8), lng_col[l], s_vec, wkeys=(("lv",),))
        dma_sync(cs_f32(C_BLNF, 8), lnb_col[l], s_vec, wkeys=(("lv",),))
        dma_sync(cs_f32(C_B1, 44), b1_col[l], s_vec, wkeys=(("lv",),))
        dma_sync(cs_f32(C_B2, 8), b2_col[l], s_vec, wkeys=(("lv",),))
        v_copy(cs_bf(C_BLNB, 8), cs_f32(C_BLNF, 8), rkeys=(("lv",),), wkeys=(("lv",),))

        def w1cols(p):
            return (256 * p, 128 if p < 21 else 42) if p < 21 else (5376, 42)

        def emit_xn(qq):
            t0 = 512 * qq
            sl = qq % 2
            for ch in range(NCH):
                a_act(xsq[:, 513 * ch:513 * ch + 513], x_sl(ch, t0, 513), AF.Square,
                      rkeys=(("x", ch),), wkeys=(("xsq", ch),))
            for ch in range(NCH):
                o = 513 * ch + (1 if ch < 4 else 0)
                mm(PS(6, 1, 512), cs_f32(C_ONESCF, 1),
                   x_sl(ch, t0 + (1 if ch < 4 else 0), 512),
                   start=(ch == 0), stop=(ch == NCH - 1),
                   wbank=(6,), rkeys=(("x", ch),),
                   inc=s_pe if ch == NCH - 1 else None)
            for ch in range(NCH):
                o = 513 * ch + (1 if ch < 4 else 0)
                mm(PS(7, 1, 512), cs_bf(C_ONESCB, 1), xsq[:, o:o + 512],
                   start=(ch == 0), stop=(ch == NCH - 1),
                   wbank=(7,), rkeys=(("xsq", ch),),
                   inc=s_pe if ch == NCH - 1 else None)
            a_act(stM, PS(6, 1, 512), AF.Copy, scale=1.0 / DM, rbanks=(6,), wkeys=(("stM",),))
            a_act(stA, PS(6, 1, 512), AF.Square, scale=1.0 / DM, rbanks=(6,), wkeys=(("stA",),))
            v_stt(stA, PS(7, 1, 512), 1.0 / DM, stA, ALU.mult, ALU.subtract,
                  rbanks=(7,), rkeys=(("stA",),), wkeys=(("stA",),))
            a_act(stA, stA, AF.Ln, bias=cs_f32(C_EPS, 1)[0:1, :], rkeys=(("stA",),), wkeys=(("stA",),))
            a_act(stA, stA, AF.Exp, scale=-0.5, rkeys=(("stA",),), wkeys=(("stA",),))
            v_stt(stB, stM, -1.0, stA, ALU.mult, ALU.mult,
                  rkeys=(("stM",), ("stA",)), wkeys=(("stB",),))
            mm(PS(6), csb.ap()[0:1, C_ONESR:C_ONESR + 128], stA, start=True, stop=True,
               wbank=(6,), rkeys=(("stA",),), inc=s_pe)
            mm(PS(7), csb.ap()[0:1, C_ONESR:C_ONESR + 128], stB, start=True, stop=True,
               wbank=(7,), rkeys=(("stB",),), inc=s_pe)
            for ch in range(NCH):
                sh = 1 if ch < 4 else 0
                v_stt(gel2[0][:, 0:512], x_sl(ch, t0 + sh, 512), cs_f32(C_GCOL + ch, 1),
                      PS(6), ALU.mult, ALU.mult,
                      rbanks=(6,), rkeys=(("x", ch), ("lv",)), wkeys=(("gel", 0),))
                v_stt(xn[sl][:, 512 * ch:512 * (ch + 1)], PS(7), cs_f32(C_GCOL + ch, 1),
                      gel2[0][:, 0:512], ALU.mult, ALU.add,
                      rbanks=(7,), rkeys=(("gel", 0), ("lv",)), wkeys=(("xn", sl),))

        def emit_w1_half(hh, first):
            for jg in range(11):
                sl = jg % 2
                c0 = 512 * jg
                cw = 512 if jg < 10 else 340
                dma_pool(w1s[sl][:, 0:0 + 8 * 512] if False else w1s[sl],
                         None, wkeys=()) if False else None
                src = iw1[l][:, c0:c0 + cw].rearrange("(c p) n -> p c n", p=128)
                dma_pool(w1s[sl].rearrange("p (c n) -> p c n", c=8)[:, :, 0:cw],
                         src, s_w1[sl], wkeys=(("w1s", sl),))
                for pi in range(2):
                    p = 2 * jg + pi
                    if p >= NPAIR:
                        continue
                    wdt = 128 if p < 21 else 42
                    la = 256 * pi if jg < 10 else 256 * pi
                    if jg < 10:
                        loa, log = 256 * pi, 256 * pi + 128
                    else:
                        loa, log = 256 * pi, 256 * pi + wdt
                    pb = 4 * (p % 2) if FFN_PB else 0
                    if first:
                        for ti, lo in ((2 * p, loa), (2 * p + 1, log)):
                            for kc in range(NCH):
                                mm(PS(pb + 2, wdt, 1), w1s[sl][:, 512 * kc + lo:512 * kc + lo + wdt],
                                   cs_bf(C_BLNB, 8)[:, kc:kc + 1],
                                   start=(kc == 0), stop=(kc == NCH - 1),
                                   wbank=(pb + 2,),
                                   rkeys=(("w1s", sl), ("lv",)),
                                   inc=s_pe if kc == NCH - 1 else None)
                            v_stt(cs_f32(C_U1 + ti, 1)[0:wdt, :], PS(pb + 2, wdt, 1),
                                  1.0, cs_f32(C_B1 + ti, 1)[0:wdt, :], ALU.mult, ALU.add,
                                  rbanks=(pb + 2,), rkeys=(("lv",),), wkeys=(("u1", ti),))
                    for bb, lo in ((pb, loa), (pb + 2, log)):
                        for nt in range(2):
                            bank = bb + nt
                            for kc in range(NCH):
                                mm(PS(bank, wdt, 512), w1s[sl][:, 512 * kc + lo:512 * kc + lo + wdt],
                                   xn[nt][:, 512 * kc:512 * (kc + 1)],
                                   start=(kc == 0), stop=(kc == NCH - 1),
                                   wbank=(bank,),
                                   rkeys=(("w1s", sl), ("xn", nt)),
                                   inc=s_pe if kc == NCH - 1 else None)
                    a_act(gel2[p % 2][0:wdt, :], ps_t.ap()[0:wdt, 512 * (pb + 2):512 * (pb + 2) + 1024],
                          AF.Gelu,
                          bias=cs_f32(C_U1 + 2 * p + 1, 1)[0:wdt, :],
                          rbanks=(pb + 2, pb + 3), rkeys=(("u1", 2 * p + 1),), wkeys=(("gel", p % 2),))
                    v_stt(hbf[0:wdt, 1024 * p:1024 * (p + 1)],
                          ps_t.ap()[0:wdt, 512 * pb:512 * pb + 1024],
                          cs_f32(C_U1 + 2 * p, 1)[0:wdt, :], gel2[p % 2][0:wdt, :],
                          ALU.add, ALU.mult,
                          rbanks=(pb, pb + 1), rkeys=(("gel", p % 2), ("u1", 2 * p)),
                          wkeys=(("h", p),))

        def emit_w2_half(hh):
            for nt in range(2):
                t0 = 1024 * hh + 512 * nt
                for jg in range(11):
                    sl = jg % 2
                    r0 = 256 * jg
                    rh = 256 if jg < 10 else 170
                    if jg < 10:
                        dma_pool(w2s[sl].rearrange("p (j m) -> p j m", j=2),
                                 iw2[l][r0:r0 + 256, :].rearrange("(j p) m -> p j m", p=128),
                                 s_w2[sl], wkeys=(("w2s", sl),))
                    else:
                        dma_pool(w2s[sl][0:128, 0:1024],
                                 iw2[l][2560:2688, :], s_w2[sl], wkeys=(("w2s", sl),))
                        dma_pool(w2s[sl][0:42, 1024:2048],
                                 iw2[l][2688:2730, :], s_w2[sl], wkeys=(("w2s", sl),))
                    for ji in range(2):
                        j = 2 * jg + ji
                        if j >= NJT:
                            continue
                        kj = 128 if j < 21 else 42
                        for mt in range(NCH):
                            mm(PS(mt, 128, 512),
                               w2s[sl][0:kj, 1024 * ji + 128 * mt:1024 * ji + 128 * (mt + 1)],
                               hbf[0:kj, 1024 * j + 512 * nt:1024 * j + 512 * nt + 512],
                               start=(j == 0), stop=(j == NJT - 1),
                               wbank=(mt,),
                               rkeys=(("w2s", sl), ("h", j)),
                               inc=s_pe if (ji == 1 and mt == NCH - 1) else None)
                for mt in range(NCH):
                    v_stt(x_sl(mt, t0 + 1, 512), PS(mt, 128, 512), cs_f32(C_B2 + mt, 1),
                          x_sl(mt, t0 + 1, 512), ALU.add, ALU.add,
                          rbanks=(mt,), rkeys=(("x", mt), ("lv",)), wkeys=(("x", mt),))

        emit_xn(0)
        emit_xn(1)
        emit_w1_half(0, first=True)
        emit_xn(2)
        emit_xn(3)
        emit_w2_half(0)
        emit_w1_half(1, first=False)
        emit_w2_half(1)
        barrier()

    def emit_final(b):
        P.mark("final")
        xst = A.f32(G_XST, 8 * 512)
        sqb = A.bf16(G_OS, 8 * 512)
        mu = A.f32(G_MU, 512)[0:1, :]
        m2 = A.f32(G_M2, 512)[0:1, :]
        tmst = A.f32(G_BC, 1024)
        for ss in range(4):
            t0 = 512 * ss
            dma_sync(xst.rearrange("p (c t) -> p c t", c=8),
                     x1sp[:, t0:t0 + 512].rearrange("(c p) t -> p c t", p=128),
                     s_x1r, waits=((s_st, s_st.n),),
                     wkeys=tuple(("fx1", ch) for ch in range(NCH)))
            for ch in range(NCH):
                v_tt(xst[:, 512 * ch:512 * (ch + 1)], xst[:, 512 * ch:512 * (ch + 1)],
                     x_sl(ch, t0 + 1, 512), ALU.add,
                     rkeys=(("x", ch), ("fx1", ch)), wkeys=(("fx1", ch),))
            for ch in range(NCH):
                a_act(sqb[:, 512 * ch:512 * (ch + 1)], xst[:, 512 * ch:512 * (ch + 1)],
                      AF.Square, rkeys=(("fx1", ch),), wkeys=(("fsq", ch),))
            for ch in range(NCH):
                mm(PS(4, 1, 512), cs_f32(C_ONESCF, 1), xst[:, 512 * ch:512 * (ch + 1)],
                   start=(ch == 0), stop=(ch == NCH - 1),
                   wbank=(4,), rkeys=(("fx1", ch),),
                   inc=s_pe if ch == NCH - 1 else None)
            for ch in range(NCH):
                mm(PS(5, 1, 512), cs_bf(C_ONESCB, 1), sqb[:, 512 * ch:512 * (ch + 1)],
                   start=(ch == 0), stop=(ch == NCH - 1),
                   wbank=(5,), rkeys=(("fsq", ch),),
                   inc=s_pe if ch == NCH - 1 else None)
            a_act(mu, PS(4, 1, 512), AF.Copy, scale=1.0 / DM, rbanks=(4,), wkeys=(("gmu",),))
            a_act(m2, PS(4, 1, 512), AF.Square, scale=1.0 / DM, rbanks=(4,), wkeys=(("gm2",),))
            v_stt(m2, PS(5, 1, 512), 1.0 / DM, m2, ALU.mult, ALU.subtract,
                  rbanks=(5,), rkeys=(("gm2",),), wkeys=(("gm2",),))
            a_act(m2, m2, AF.Ln, bias=cs_f32(C_EPS, 1)[0:1, :], rkeys=(("gm2",),), wkeys=(("gm2",),))
            a_act(m2, m2, AF.Exp, scale=-0.5, rkeys=(("gm2",),), wkeys=(("gm2",),))
            v_stt(mu, mu, -1.0, m2, ALU.mult, ALU.mult,
                  rkeys=(("gmu",), ("gm2",)), wkeys=(("gmu",),))
            mm(PS(4), csb.ap()[0:1, C_ONESR:C_ONESR + 128], m2, start=True, stop=True,
               wbank=(4,), rkeys=(("gm2",),), inc=s_pe)
            mm(PS(5), csb.ap()[0:1, C_ONESR:C_ONESR + 128], mu, start=True, stop=True,
               wbank=(5,), rkeys=(("gmu",),), inc=s_pe)
            for ch in range(NCH):
                v_tt(xst[:, 512 * ch:512 * (ch + 1)], xst[:, 512 * ch:512 * (ch + 1)],
                     PS(4), ALU.mult, rbanks=(4,), rkeys=(("fx1", ch),), wkeys=(("fx1", ch),))
                v_tt(xst[:, 512 * ch:512 * (ch + 1)], xst[:, 512 * ch:512 * (ch + 1)],
                     PS(5), ALU.add, rbanks=(5,), rkeys=(("fx1", ch),), wkeys=(("fx1", ch),))
            for tt in range(4):
                bank = tt % 2
                for ch in range(NCH):
                    transpose(PS(bank, 128, 128, 128 * (ch % 4), r0=0)
                              if False else
                              ps_t.ap()[:, 512 * (2 * bank + ch // 4) + 128 * (ch % 4):
                                        512 * (2 * bank + ch // 4) + 128 * (ch % 4) + 128],
                              xst[:, 512 * ch + 128 * tt:512 * ch + 128 * (tt + 1)],
                              2 * bank + ch // 4, rkey=("fx1", ch),
                              start_clear=(ch % 4 == 0))
                v_copy(tmst, ps_t.ap()[:, 1024 * bank:1024 * bank + 1024],
                       rbanks=(2 * bank, 2 * bank + 1), wkeys=(("ftm",),))
                dma_store(out[b, t0 + 128 * tt:t0 + 128 * (tt + 1), :], tmst,
                          waits=TK.wait_read(("ftm",)) + TK.wait_write(("ftm",)),
                          note_rkeys=())
                TK.note_read(("ftm",), s_st, s_st.n)
        barrier()

    for b in range(n_batches):
        emit_global(b)
        if dump_stage == "x1":
            xst = A.f32(G_XST, 8 * 512)
            tmst = A.f32(G_BC, 1024)
            for ss in range(4):
                t0 = 512 * ss
                dma_sync(xst.rearrange("p (c t) -> p c t", c=8),
                         x1sp[:, t0:t0 + 512].rearrange("(c p) t -> p c t", p=128),
                         s_x1r, waits=((s_st, s_st.n),),
                         wkeys=tuple(("dx", ch) for ch in range(NCH)))
                for tt in range(4):
                    bank = tt % 2
                    for ch in range(NCH):
                        transpose(ps_t.ap()[:, 512 * (2 * bank + ch // 4) + 128 * (ch % 4):
                                            512 * (2 * bank + ch // 4) + 128 * (ch % 4) + 128],
                                  xst[:, 512 * ch + 128 * tt:512 * ch + 128 * (tt + 1)],
                                  2 * bank + ch // 4, rkey=("dx", ch),
                                  start_clear=(ch % 4 == 0), stop=(ch % 4 == 3))
                    v_copy(tmst, ps_t.ap()[:, 1024 * bank:1024 * bank + 1024],
                           rbanks=(2 * bank, 2 * bank + 1), wkeys=(("ftm",),))
                    dma_store(out[b, t0 + 128 * tt:t0 + 128 * (tt + 1), :], tmst,
                              waits=TK.wait_read(("ftm",)) + [TK.w[("ftm",)]])
                    TK.note_read(("ftm",), s_st, s_st.n)
            barrier()
            continue
        P.mark("reload")
        for ch in range(NCH):
            v_memset(x_sl(ch, 0, 1), 0.0, wkeys=(("x", ch),))
        for ch in range(NCH):
            dma_sync(x_sl(ch, 1, 2048), x1sp[128 * ch:128 * (ch + 1), :],
                     s_x1r, wkeys=(("x", ch),), waits=((s_st, s_st.n),))
        barrier()
        for l in range(n_layers):
            emit_inner_attn(b, l)
            if dump_stage == ("L%da" % (l + 1)):
                break
            emit_inner_ffn(b, l)
            if dump_stage == ("L%d" % (l + 1)):
                break
        if dump_stage and dump_stage != "full" and dump_stage.startswith("L"):
            tmst = A.f32(G_BC, 1024)
            for ss in range(4):
                t0 = 512 * ss
                for tt in range(4):
                    bank = tt % 2
                    for ch in range(NCH):
                        transpose(ps_t.ap()[:, 512 * (2 * bank + ch // 4) + 128 * (ch % 4):
                                            512 * (2 * bank + ch // 4) + 128 * (ch % 4) + 128],
                                  x_sl(ch, t0 + 128 * tt + 1, 128),
                                  2 * bank + ch // 4, rkey=("x", ch),
                                  start_clear=(ch % 4 == 0), stop=(ch % 4 == 3))
                    v_copy(tmst, ps_t.ap()[:, 1024 * bank:1024 * bank + 1024],
                           rbanks=(2 * bank, 2 * bank + 1), wkeys=(("ftm",),))
                    dma_store(out[b, t0 + 128 * tt:t0 + 128 * (tt + 1), :], tmst,
                              waits=TK.wait_read(("ftm",)) + [TK.w[("ftm",)]])
                    TK.note_read(("ftm",), s_st, s_st.n)
            barrier()
        elif dump_stage is None or dump_stage == "full":
            emit_final(b)

    P.op("sync", None, waits=((s_st, s_st.n),))
    P.emit()
    return nc


def _prep_inputs(inputs):
    c = host_constants()
    pg = _perm_global_qk()
    pi = _perm_inner_qk()
    pw1, tcols = _w1_tile_cols()
    wq = np.ascontiguousarray(inputs["L0_attn_wq"].reshape(DM, DM)[:, pg], dtype=np.float32)
    wk = np.ascontiguousarray(inputs["L0_attn_wk"].reshape(DM, DM)[:, pg], dtype=np.float32)
    wv = np.ascontiguousarray(inputs["L0_attn_wv"].reshape(DM, DM), dtype=np.float32)
    wo = np.ascontiguousarray(inputs["L0_attn_wo"].reshape(DM, DM), dtype=np.float32)
    Wqkv = np.asarray(inputs["L0_ffn_Wqkv"], dtype=np.float32)
    Wqkv_p = np.concatenate([Wqkv[:, :, :DM][:, :, pi], Wqkv[:, :, DM:2 * DM][:, :, pi],
                             Wqkv[:, :, 2 * DM:]], axis=2)
    W1p = np.ascontiguousarray(np.asarray(inputs["L0_ffn_W1"], dtype=np.float32)[:, :, pw1])
    qs = np.asarray(inputs["L0_ffn_q_scale"], dtype=np.float32)
    ks = np.asarray(inputs["L0_ffn_k_scale"], dtype=np.float32)
    qscol_a = np.zeros((DEPTH, 128, 4), dtype=np.float32)
    for l in range(DEPTH):
        qscol_a[l, :, 0] = np.tile(qs[l, 0:32], 4)
        qscol_a[l, :, 1] = np.tile(qs[l, 32:64], 4)
        qscol_a[l, :, 2] = np.tile(ks[l, 0:32], 4)
        qscol_a[l, :, 3] = np.tile(ks[l, 32:64], 4)
    b1p = np.asarray(inputs["L0_ffn_b1"], dtype=np.float32)[:, pw1]
    b1c = np.zeros((DEPTH, 128, 2 * NPAIR), dtype=np.float32)
    off = 0
    tlist = []
    for p in range(NPAIR):
        w = 128 if p < 21 else 42
        tlist.append((off, w)); off += w
        tlist.append((off, w)); off += w
    for l in range(DEPTH):
        for ti, (o, w) in enumerate(tlist):
            b1c[l, :w, ti] = b1p[l, o:o + w]
    lng_c = np.zeros((DEPTH, 128, NCH), dtype=np.float32)
    lnb_c = np.zeros((DEPTH, 128, NCH), dtype=np.float32)
    b2_c = np.zeros((DEPTH, 128, NCH), dtype=np.float32)
    for l in range(DEPTH):
        lng_c[l] = np.asarray(inputs["L0_ffn_ln_g"][l], dtype=np.float32).reshape(NCH, 128).T
        lnb_c[l] = np.asarray(inputs["L0_ffn_ln_b"][l], dtype=np.float32).reshape(NCH, 128).T
        b2_c[l] = np.asarray(inputs["L0_ffn_b2"][l], dtype=np.float32).reshape(NCH, 128).T
    return {
        "gwq": wq, "gwk": wk, "gwv": wv, "gwo": wo,
        "iwqkv": np.ascontiguousarray(Wqkv_p),
        "iwo": np.ascontiguousarray(np.asarray(inputs["L0_ffn_Wo"], dtype=np.float32)),
        "iw1": W1p,
        "iw2": np.ascontiguousarray(np.asarray(inputs["L0_ffn_W2"], dtype=np.float32)),
        "qscol": qscol_a, "lng_col": lng_c, "lnb_col": lnb_c, "b1_col": b1c, "b2_col": b2_c,
        "tbl_g": c["tbl_g"], "tbl_i": c["tbl_i"], "identity": c["identity"],
        "trimask4": c["trimask4"], "trimask2": c["trimask2"], "tri1one": c["tri1one"],
        "eden": c["eden"], "enorm": c["enorm"], "fnorm": c["fnorm"], "fpair": c["fpair"],
        "ones_col_bf": c["ones_col_bf"], "ones_col_f32": c["ones_col_f32"],
        "ones_row_f32": c["ones_row_f32"], "eps_col": c["eps_col"],
    }


def kernel(**inputs):
    x = np.ascontiguousarray(inputs["L0_attn_x"], dtype=np.float32)
    base = _prep_inputs(inputs)
    nc = build_program()
    in_maps = []
    for cidx in range(NCORE):
        m = dict(base)
        m["x_in"] = np.ascontiguousarray(x[BLOC * cidx:BLOC * (cidx + 1)])
        in_maps.append(m)
    res = run_bass_kernel_spmd(nc, in_maps, core_ids=list(range(NCORE)))
    return np.concatenate([r["out"] for r in res.results], axis=0)


# version 07 (final)
# speedup vs baseline: 1.3656x; 1.0565x over previous
import contextlib
import numpy as np
import ml_dtypes
import concourse.bass as bass
import concourse.mybir as mybir
from concourse.bass_utils import run_bass_kernel_spmd

F32 = mybir.dt.float32
BF16 = mybir.dt.bfloat16
AF = mybir.ActivationFunctionType
ALU = mybir.AluOpType
bfloat16 = ml_dtypes.bfloat16

T = 2048
NCORE = 8
BLOC = 2
NCH = 8
DM = 1024
GH = 8
IH = 16
WIN = 256
NWIN = T // WIN
DEPTH = 6
DI2 = 5460
DI = 2730
NPAIR = 22
NJT = 22
EPS = 1e-5

ENGS = ("tensor", "vector", "scalar", "gpsimd", "sync")
DEBUG_BUILD = False
DEDUP_WAITS = True
ATTN_PARTS = ("qkv", "attn", "wo")
DEFER_HEAD = True
DEFER_QKV = False
QKV_ALT = True
DEFER_SQ = True
DEFER_ROT = True
FN_ALT = True
FFN_PB = True


def _rope_tables_global():
    half = 64
    af = np.power(np.float32(10000.0), -(np.arange(half, dtype=np.float32) / np.float32(half)))
    pos = np.arange(T, dtype=np.float32)[:, None]
    rad = (pos * af[None, :]).astype(np.float32)
    cg = np.tile(np.cos(rad).astype(np.float32).T, (2, 1))
    sg = np.tile(np.sin(rad).astype(np.float32).T, (2, 1))
    return cg.astype(np.float32), sg.astype(np.float32)


def _rope_tables_inner():
    half = np.arange(0, 64, 2, dtype=np.float32)
    inv_freq = (np.float32(1.0) / np.power(np.float32(10000.0), half / np.float32(64))).astype(np.float32)
    t = np.arange(T, dtype=np.float32)
    freqs = (t[:, None] * inv_freq[None, :]).astype(np.float32)
    base_scale = ((half + np.float32(0.4 * 64)) / np.float32(1.4 * 64)).astype(np.float32)
    power = ((t - T // 2) / np.float32(WIN // 2)).astype(np.float32)
    scale = np.power(base_scale[None, :], power[:, None]).astype(np.float32)
    cos = np.cos(freqs).astype(np.float32)
    sin = np.sin(freqs).astype(np.float32)
    inv = (np.float32(1.0) / scale).astype(np.float32)

    def pat(m):
        return np.tile(m.T, (4, 1)).astype(np.float32)
    return pat(cos * scale), pat(sin * scale), pat(cos * inv), pat(sin * inv)


def _perm_global_qk():
    perm = np.zeros(1024, dtype=np.int64)
    for h in range(GH):
        for d in range(128):
            newc = (d // 64) * 512 + (h // 2) * 128 + (h % 2) * 64 + (d % 64)
            perm[newc] = h * 128 + d
    return perm


def _perm_inner_qk():
    perm = np.zeros(1024, dtype=np.int64)
    for h in range(IH):
        for d in range(64):
            newc = (d // 32) * 512 + (h // 4) * 128 + (h % 4) * 32 + (d % 32)
            perm[newc] = h * 64 + d
    return perm


def _w1_tile_cols():
    idx = []
    tcols = []
    off = 0
    for p in range(NPAIR):
        w = 128 if p < 21 else 42
        idx.extend(range(128 * p, 128 * p + w))
        tcols.append((off, w)); off += w
        idx.extend(range(DI + 128 * p, DI + 128 * p + w))
        tcols.append((off, w)); off += w
    return np.array(idx, dtype=np.int64), tcols


def host_constants():
    c = {}
    cg, sg = _rope_tables_global()
    c["tbl_g"] = np.stack([cg, sg])
    csq, ssq, csk, ssk = _rope_tables_inner()
    c["tbl_i"] = np.stack([csq, ssq, csk, ssk])
    c["identity"] = np.eye(128, dtype=np.float32)
    p = np.arange(128)[:, None]
    f = np.arange(128)[None, :]
    tri1 = (p <= f).astype(bfloat16)
    tri2 = (f <= p).astype(bfloat16)
    c["trimask4"] = np.concatenate([tri2, tri2, tri1, tri1], axis=1).astype(bfloat16)
    c["trimask2"] = np.concatenate([tri1, tri1], axis=1).astype(bfloat16)
    c["tri1one"] = tri1
    eden = np.zeros((128, 4), dtype=bfloat16)
    eden[:, 0] = bfloat16(1.0)
    eden[:, 3] = bfloat16(1.0)
    c["eden"] = eden
    en = np.zeros((4, 128, 16), dtype=bfloat16)
    for g in range(4):
        for q in range(128):
            en[g, q, 4 * g + q // 32] = bfloat16(1.0)
    c["enorm"] = en
    fn = np.zeros((4, 16, 128), dtype=np.float32)
    for g in range(4):
        for j in range(4):
            fn[g, 4 * g + j, 32 * j:32 * j + 32] = 1.0
    c["fnorm"] = fn
    fp = np.zeros((2, 128), dtype=np.float32)
    fp[0, 0:64] = 1.0
    fp[1, 64:128] = 1.0
    c["fpair"] = fp
    c["ones_col_bf"] = np.ones((128, 1), dtype=bfloat16)
    c["eps_col"] = np.full((128, 1), EPS, dtype=np.float32)
    c["ones_col_f32"] = np.ones((128, 1), dtype=np.float32)
    c["ones_row_f32"] = np.ones((1, 128), dtype=np.float32)
    return c


class Sem:
    __slots__ = ("h", "n", "name", "last_eng", "is_dma", "engs")

    def __init__(self, h, name, is_dma=False):
        self.h = h; self.n = 0; self.name = name
        self.last_eng = None; self.is_dma = is_dma
        self.engs = set()


class Plan:
    def __init__(self, nc, stack):
        self.nc = nc
        self.q = {e: [] for e in ENGS}
        self.maxwait = {e: {} for e in ENGS}
        self.stack = stack
        self.phase = "init"

    def mark(self, name):
        self.phase = name

    def sem(self, name, is_dma=False):
        return Sem(self.stack.enter_context(self.nc.semaphore(name)), name, is_dma)

    def op(self, eng, fn, waits=(), inc=None, inc_n=1):
        real = []
        mw = self.maxwait[eng]
        for s, v in waits:
            if s is None or v is None or v <= 0:
                continue
            if DEDUP_WAITS:
                if (not s.is_dma) and s.engs == {eng} and v <= s.n:
                    continue
                if mw.get(id(s), 0) >= v:
                    continue
                mw[id(s)] = v
            real.append((s, v))
        val = None
        if inc is not None:
            inc.n += inc_n
            inc.last_eng = eng
            inc.engs.add(eng)
            val = inc.n
        self.q[eng].append((tuple(real), fn, inc, inc_n, self.phase))
        return val

    def dma(self, eng, fn, waits=(), inc=None):
        return self.op(eng, fn, waits, inc, inc_n=16)

    def barrier(self, bar, dma_sems=()):
        for e in ENGS:
            self.op(e, lambda E: E.nop(), inc=bar)
        tgt = bar.n
        wl = tuple((s, s.n) for s in dma_sems)
        for e in ENGS:
            self.op(e, None, waits=((bar, tgt),) + wl)

    def emit(self):
        phase_of = self.nc._phase_of_inst = {}

        def body(name):
            def run(E):
                for waits, fn, inc, inc_n, phase in self.q[name]:
                    for s, v in waits:
                        E.wait_ge(s.h, v)
                    if fn is None:
                        continue
                    n0 = self.nc.n_instructions()
                    inst = fn(E)
                    phase_of[n0] = (phase, name)
                    if inc is not None:
                        inst.then_inc(inc.h, inc_n)
            return run
        with self.nc.Block() as block:
            block.tensor(body("tensor"))
            block.vector(body("vector"))
            block.scalar(body("scalar"))
            block.gpsimd(body("gpsimd"))
            block.sync(body("sync"))


class Tracker:
    def __init__(self):
        self.w = {}
        self.r = {}
        self.rb = {}

    def wait_read_bank(self, key, eng):
        out = list(self.wait_read(key))
        last = self.rb.get(key)
        if last is not None and last[0] != eng:
            out.append((last[1], last[2]))
        return out

    def note_read_bank(self, key, eng, sem, val):
        if val is None:
            return
        self.rb[key] = (eng, sem, val)
        self.note_read(key, sem, val)

    def wait_read(self, *keys):
        return [self.w[k] for k in keys if k in self.w]

    def wait_write(self, *keys):
        out = []
        for k in keys:
            if k in self.w:
                out.append(self.w[k])
            for s, v in self.r.get(k, {}).values():
                out.append((s, v))
        return out

    def note_write(self, key, sem, val):
        if val is None:
            return
        self.w[key] = (sem, val)
        self.r.pop(key, None)
        self.rb.pop(key, None)

    def note_read(self, key, sem, val):
        if val is None:
            return
        d = self.r.setdefault(key, {})
        old = d.get(id(sem))
        if old is None or old[1] < val:
            d[id(sem)] = (sem, val)


ARENA_BYTES = 195 * 1024


class Arena:
    def __init__(self, t):
        self.t = t

    def f32(self, off, n):
        assert off % 4 == 0 and off + 4 * n <= ARENA_BYTES, (off, n)
        return self.t.ap()[:, off // 4: off // 4 + n]

    def bf16(self, off, n):
        assert off % 4 == 0 and off + 2 * n <= ARENA_BYTES, (off, n)
        m = (n + 1) // 2
        return self.t.ap()[:, off // 4: off // 4 + m].bitcast(BF16)[:, :n]


def build_program(n_layers=DEPTH, n_batches=BLOC, dump_stage=None):
    nc = bass.Bass("TRN2", target_bir_lowering=False, debug=DEBUG_BUILD,
                   detect_race_conditions=False)
    stack = contextlib.ExitStack()
    nc._dev_stack = stack

    x_in = nc.dram_tensor("x_in", [BLOC, T, DM], F32, kind="ExternalInput").ap()
    out = nc.dram_tensor("out", [BLOC, T, DM], F32, kind="ExternalOutput").ap()
    gwq = nc.dram_tensor("gwq", [DM, DM], F32, kind="ExternalInput").ap()
    gwk = nc.dram_tensor("gwk", [DM, DM], F32, kind="ExternalInput").ap()
    gwv = nc.dram_tensor("gwv", [DM, DM], F32, kind="ExternalInput").ap()
    gwo = nc.dram_tensor("gwo", [DM, DM], F32, kind="ExternalInput").ap()
    iwqkv = nc.dram_tensor("iwqkv", [DEPTH, DM, 3 * DM], F32, kind="ExternalInput").ap()
    iwo = nc.dram_tensor("iwo", [DEPTH, DM, DM], F32, kind="ExternalInput").ap()
    iw1 = nc.dram_tensor("iw1", [DEPTH, DM, DI2], F32, kind="ExternalInput").ap()
    iw2 = nc.dram_tensor("iw2", [DEPTH, DI, DM], F32, kind="ExternalInput").ap()
    qscol = nc.dram_tensor("qscol", [DEPTH, 128, 4], F32, kind="ExternalInput").ap()
    lng_col = nc.dram_tensor("lng_col", [DEPTH, 128, NCH], F32, kind="ExternalInput").ap()
    lnb_col = nc.dram_tensor("lnb_col", [DEPTH, 128, NCH], F32, kind="ExternalInput").ap()
    b1_col = nc.dram_tensor("b1_col", [DEPTH, 128, 2 * NPAIR], F32, kind="ExternalInput").ap()
    b2_col = nc.dram_tensor("b2_col", [DEPTH, 128, NCH], F32, kind="ExternalInput").ap()
    tbl_g = nc.dram_tensor("tbl_g", [2, 128, T], F32, kind="ExternalInput").ap()
    tbl_i = nc.dram_tensor("tbl_i", [4, 128, T], F32, kind="ExternalInput").ap()
    identity_d = nc.dram_tensor("identity", [128, 128], F32, kind="ExternalInput").ap()
    trimask4_d = nc.dram_tensor("trimask4", [128, 512], BF16, kind="ExternalInput").ap()
    trimask2_d = nc.dram_tensor("trimask2", [128, 256], BF16, kind="ExternalInput").ap()
    tri1_d = nc.dram_tensor("tri1one", [128, 128], BF16, kind="ExternalInput").ap()
    eden_d = nc.dram_tensor("eden", [128, 4], BF16, kind="ExternalInput").ap()
    enorm_d = nc.dram_tensor("enorm", [4, 128, 16], BF16, kind="ExternalInput").ap()
    fnorm_d = nc.dram_tensor("fnorm", [4, 16, 128], F32, kind="ExternalInput").ap()
    fpair_d = nc.dram_tensor("fpair", [2, 128], F32, kind="ExternalInput").ap()
    onescb_d = nc.dram_tensor("ones_col_bf", [128, 1], BF16, kind="ExternalInput").ap()
    onescf_d = nc.dram_tensor("ones_col_f32", [128, 1], F32, kind="ExternalInput").ap()
    onesr_d = nc.dram_tensor("ones_row_f32", [1, 128], F32, kind="ExternalInput").ap()
    eps_d = nc.dram_tensor("eps_col", [128, 1], F32, kind="ExternalInput").ap()
    x1sp = nc.dram_tensor("x1sp", [DM, T], F32).ap()

    arena_t = stack.enter_context(nc.sbuf_tensor("arena", [128, ARENA_BYTES // 4], F32))
    A = Arena(arena_t)
    csb = stack.enter_context(nc.sbuf_tensor("csb", [128, 3072], F32))

    C_ID = 0
    C_TRIM4 = 128
    C_TRIM2 = 384
    C_TRI1 = 512
    C_EDEN = 576
    C_ENORM = 704
    C_FNORM = 736
    C_FPAIR = 1248
    C_ONESCB = 2272
    C_ONESCF = 2273
    C_ONESR = 2274
    C_EPS = 2528
    C_EPS2 = 2529
    C_QS = 2402
    C_GCOL = 2408
    C_BLNF = 2416
    C_BLNB = 2424
    C_B1 = 2428
    C_B2 = 2472
    C_U1 = 2480
    C_END = 2530
    assert C_END <= 3072

    def cs_f32(off, n):
        return csb.ap()[:, off:off + n]

    def cs_bf(off, n):
        m = (n + 1) // 2
        return csb.ap()[:, off:off + m].bitcast(BF16)[:, :n]

    P = Plan(nc, stack)
    TK = Tracker()
    s_bar = P.sem("bar")
    s_pe = P.sem("pe")
    s_dve = P.sem("dve")
    s_act = P.sem("act")
    s_pool = P.sem("pool")
    s_ld = P.sem("ld", is_dma=True)
    s_x = P.sem("x", is_dma=True)
    s_tb0 = P.sem("tb0", is_dma=True)
    s_tb1 = P.sem("tb1", is_dma=True)
    s_wA = P.sem("wA", is_dma=True)
    s_wB = P.sem("wB", is_dma=True)
    s_w1 = [P.sem("w1a", is_dma=True), P.sem("w1b", is_dma=True)]
    s_w2 = [P.sem("w2a", is_dma=True), P.sem("w2b", is_dma=True),
            P.sem("w2c", is_dma=True), P.sem("w2d", is_dma=True)]
    s_vec = P.sem("vec", is_dma=True)
    s_x1r = P.sem("x1r", is_dma=True)
    s_st = P.sem("st", is_dma=True)
    ALL_DMA_SEMS = (s_ld, s_x, s_tb0, s_tb1, s_wA, s_wB, s_w1[0], s_w1[1],
                    s_w2[0], s_w2[1], s_w2[2], s_w2[3], s_vec, s_x1r, s_st)
    s_tbs = (s_tb0, s_tb1)

    ps_t = stack.enter_context(nc.psum_tensor("ps", [128, 4096], F32))

    def PS(bank, rows=128, cols=512, coff=0, r0=0):
        return ps_t.ap()[r0:r0 + rows, 512 * bank + coff: 512 * bank + coff + cols]

    pend_r = []
    pend_w = []

    def mm(psum_ap, lhs_ap, rhs_ap, start, stop, wbank=(), rkeys=(), waits=(), inc=None,
           tile_position=None):
        wl = list(waits)
        if start:
            for b in wbank:
                wl += TK.wait_write(("ps", b))
        for k in rkeys:
            wl += TK.wait_read(k)
        pend_r.extend(rkeys)
        pend_w.extend(("ps", b) for b in wbank)

        def f(E, a=psum_ap, b=lhs_ap, c=rhs_ap, st=start, sp=stop, tp=tile_position):
            return E.matmul(a, b, c, start=st, stop=sp, tile_position=tp)
        v = P.op("tensor", f, waits=wl, inc=inc)
        if inc is not None:
            for k in pend_w:
                TK.note_write(k, s_pe, v)
            for k in pend_r:
                TK.note_read(k, s_pe, v)
            pend_r.clear()
            pend_w.clear()
        return v

    def generic(eng, sem, fn, rkeys=(), wkeys=(), rbanks=(), wbanks=(), waits=(), inc=True):
        wl = list(waits)
        for k in rkeys:
            wl += TK.wait_read(k)
        for b in rbanks:
            wl += TK.wait_read_bank(("ps", b), eng)
        for k in wkeys:
            wl += TK.wait_write(k)
        for b in wbanks:
            wl += TK.wait_write(("ps", b))
        v = P.op(eng, fn, waits=wl, inc=sem if inc else None)
        if inc:
            for k in rkeys:
                TK.note_read(k, sem, v)
            for b in rbanks:
                TK.note_read_bank(("ps", b), eng, sem, v)
            for k in wkeys:
                TK.note_write(k, sem, v)
            for b in wbanks:
                TK.note_write(("ps", b), sem, v)
        return v

    def v_tt(o, a, b, op, **kw):
        return generic("vector", s_dve, lambda E: E.tensor_tensor(o, a, b, op), **kw)

    def v_stt(o, a, sc, b, op0, op1, **kw):
        return generic("vector", s_dve, lambda E: E.scalar_tensor_tensor(o, a, sc, b, op0, op1), **kw)

    def v_ts(o, a, s1, s2, op0, op1=None, **kw):
        if op1 is None:
            return generic("vector", s_dve, lambda E: E.tensor_scalar(o, a, s1, None, op0), **kw)
        return generic("vector", s_dve, lambda E: E.tensor_scalar(o, a, s1, s2, op0, op1), **kw)

    def v_copy(o, a, **kw):
        return generic("vector", s_dve, lambda E: E.tensor_copy(o, a), **kw)

    def v_recip(o, a, **kw):
        return generic("vector", s_dve, lambda E: E.reciprocal(o, a), **kw)

    def v_memset(o, val, **kw):
        return generic("vector", s_dve, lambda E: E.memset(o, val), **kw)

    def a_act(o, a, func, bias=None, scale=None, accum_out=None, **kw):
        def f(E):
            kk = {}
            if bias is not None:
                kk["bias"] = bias
            if scale is not None:
                kk["scale"] = scale
            if accum_out is not None:
                kk["accum_out"] = accum_out
            return E.activation(o, a, func, **kk)
        return generic("scalar", s_act, f, **kw)

    def p_tt(o, a, b, op, **kw):
        return generic("gpsimd", s_pool, lambda E: E.tensor_tensor(o, a, b, op), **kw)

    def _dma(eng, sem, o, a, **kw):
        wl = list(kw.pop("waits", ()))
        rkeys = kw.pop("rkeys", ())
        wkeys = kw.pop("wkeys", ())
        for k in rkeys:
            wl += TK.wait_read(k)
        for k in wkeys:
            wl += TK.wait_write(k)
        v = P.dma(eng, lambda E: E.dma_start(out=o, in_=a), waits=wl, inc=sem)
        for k in rkeys:
            TK.note_read(k, sem, v)
        for k in wkeys:
            TK.note_write(k, sem, v)
        return v

    def dma_sync(o, a, sem, **kw):
        return _dma("sync", sem, o, a, **kw)

    def dma_pool(o, a, sem, **kw):
        return _dma("gpsimd", sem, o, a, **kw)

    def dma_store(o, a, **kw):
        wl = list(kw.pop("waits", ()))
        for k in kw.pop("rkeys", ()):
            wl += TK.wait_read(k)
        v = P.dma("sync", lambda E: E.dma_start(out=o, in_=a), waits=wl, inc=s_st)
        for k in kw.get("note_rkeys", ()):
            TK.note_read(k, s_st, v)
        return v

    def transpose(psum_ap, in_ap, bank, rkey=None, waits=(), start_clear=True,
                  stop=False):
        wl = list(waits)
        if start_clear:
            wl += TK.wait_write(("ps", bank))
        if rkey is not None:
            wl += TK.wait_read(rkey)
        ident = cs_f32(C_ID, 128)

        def f(E, o=psum_ap, i=in_ap, idn=ident, sp=stop):
            return E.matmul(o, i, idn, start=True, stop=sp, is_transpose=True,
                            skip_group_check=True)
        v = P.op("tensor", f, waits=wl, inc=s_pe)
        pend_w.append(("ps", bank))
        if rkey is not None:
            pend_r.append(rkey)
        for k in pend_w:
            TK.note_write(k, s_pe, v)
        for k in pend_r:
            TK.note_read(k, s_pe, v)
        pend_r.clear()
        pend_w.clear()
        return v

    def barrier():
        P.barrier(s_bar, dma_sems=ALL_DMA_SEMS)
        TK.w.clear()
        TK.r.clear()
        TK.rb.clear()

    dma_sync(cs_f32(C_ID, 128), identity_d, s_ld)
    dma_sync(cs_bf(C_TRIM4, 512), trimask4_d, s_ld)
    dma_sync(cs_bf(C_TRIM2, 256), trimask2_d, s_ld)
    dma_sync(cs_bf(C_TRI1, 128), tri1_d, s_ld)
    dma_sync(cs_bf(C_EDEN, 4), eden_d, s_ld)
    dma_sync(cs_bf(C_ENORM, 64).rearrange("p (g n) -> p g n", g=4),
             enorm_d.rearrange("g p n -> p g n"), s_ld)
    dma_sync(csb.ap()[0:16, C_FNORM:C_FNORM + 512].rearrange("p (g n) -> p g n", g=4),
             fnorm_d.rearrange("g p n -> p g n"), s_ld)
    dma_sync(csb.ap()[0:2, C_FPAIR:C_FPAIR + 128], fpair_d, s_ld)
    dma_sync(cs_bf(C_ONESCB, 1), onescb_d, s_ld)
    dma_sync(cs_f32(C_ONESCF, 1), onescf_d, s_ld)
    dma_sync(csb.ap()[0:1, C_ONESR:C_ONESR + 128], onesr_d, s_ld)
    dma_sync(cs_f32(C_EPS, 1), eps_d, s_ld)
    barrier()
    v_ts(cs_f32(C_EPS2, 1), cs_f32(C_EPS, 1), 1e-19, None, ALU.mult,
         wkeys=(("eps2",),))
    barrier()

    G_XBF, G_KH, G_VH = 0, 32768, 65536
    G_W1, G_W2 = 98304, 114688
    G_TBL = 131072
    G_TT = 139264
    G_QS = 147456
    G_PT = 155648
    G_XST = 159744
    G_OS = 176128
    G_BC = 184320
    G_RC = 186368
    G_MU = 188416
    G_M2 = 190464
    G_STG = 192512
    assert G_STG + 2048 <= ARENA_BYTES

    def xg_bf(ch, t0, n):
        return A.bf16(G_XBF + 2 * 2048 * ch, 2048)[:, t0:t0 + n]

    def khat_g(r0, rows, ch, c0, n):
        return A.bf16(G_KH + 2 * 2048 * ch, 2048)[r0:r0 + rows, c0:c0 + n]

    def vhat_g(tt, c0, n):
        return A.bf16(G_VH + 2 * 1024 * tt, 1024)[:, c0:c0 + n]

    def rope_pass_global(bA, bB, outA, outB, tb_c, tb_s, t1b, t2b, keys):
        for cc, ub, us, sgn, o in ((0, bA, bB, -1.0, outA), (1, bB, bA, 1.0, outB)):
            v_tt(t1b, PS(ub), tb_c, ALU.mult,
                 rbanks=(ub,), rkeys=(keys["tbl"],), wkeys=(keys["t1"],))
            v_stt(t2b, PS(us), sgn, tb_s, ALU.mult, ALU.mult,
                  rbanks=(us,), rkeys=(keys["tbl"],), wkeys=(keys["t2"],))
            p_tt(o, t1b, t2b, ALU.add,
                 rkeys=(keys["t1"], keys["t2"]), wkeys=(keys["out"],))

    def emit_global(b):
        P.mark("g0")
        stg = A.f32(G_STG, 512)
        for ch in range(NCH):
            for tg in range(4):
                src = x_in[b, 512 * tg:512 * (tg + 1), 128 * ch:128 * (ch + 1)] \
                    .rearrange("(tt p) c -> p tt c", p=128)
                dma_sync(stg.rearrange("p (tt c) -> p tt c", tt=4), src, s_x, wkeys=(("gstg",),))
                bank = tg % 2
                for i in range(4):
                    transpose(PS(bank, 128, 128, 128 * i), stg[:, 128 * i:128 * (i + 1)],
                              bank, rkey=("gstg",), start_clear=(i == 0), stop=(i == 3))
                v_copy(xg_bf(ch, 512 * tg, 512), PS(bank), rbanks=(bank,),
                       wkeys=(("gxbf", ch, tg),))
        barrier()

        P.mark("g1")
        wk_bf = A.bf16(G_W1, 8 * 1024)
        wv_bf = A.bf16(G_W2, 8 * 1024)
        dma_pool(wk_bf.rearrange("p (c n) -> p c n", c=8), gwk.rearrange("(c p) n -> p c n", p=128), s_wA, wkeys=(("gw1",),))
        dma_pool(wv_bf.rearrange("p (c n) -> p c n", c=8), gwv.rearrange("(c p) n -> p c n", p=128), s_wB, wkeys=(("gw2",),))
        tbc = [A.f32(G_TBL + 4096 * i, 512) for i in range(2)]
        tbs = [A.f32(G_TBL + 2048 + 4096 * i, 512) for i in range(2)]
        t1b = [A.f32(G_TT + 4096 * i, 512) for i in range(2)]
        t2b = [A.f32(G_TT + 2048 + 4096 * i, 512) for i in range(2)]
        for ss in range(4):
            t0 = 512 * ss
            sl = ss % 2
            dma_sync(tbc[sl], tbl_g[0][:, t0:t0 + 512], s_tbs[sl], wkeys=(("gtb", sl),))
            dma_sync(tbs[sl], tbl_g[1][:, t0:t0 + 512], s_tbs[sl], wkeys=(("gtb", sl),))
            for g in range(4):
                for cc, bank in ((g, 4), (4 + g, 5)):
                    for kc in range(NCH):
                        mm(PS(bank), wk_bf[:, 1024 * kc + 128 * cc:1024 * kc + 128 * cc + 128],
                           xg_bf(kc, t0, 512), start=(kc == 0), stop=(kc == NCH - 1),
                           wbank=(bank,), rkeys=(("gw1",), ("gxbf", kc, ss)),
                           inc=s_pe if kc == NCH - 1 else None)
                rope_pass_global(4, 5,
                                 khat_g(0, 128, g, t0, 512), khat_g(0, 128, 4 + g, t0, 512),
                                 tbc[sl], tbs[sl], t1b[sl], t2b[sl],
                                 {"t1": ("gt1", sl), "t2": ("gt2", sl),
                                  "tbl": ("gtb", sl), "out": ("gkh",)})
            for tt in range(4 * ss, 4 * ss + 4):
                for nt in range(2):
                    bank = 6 + nt
                    for kc in range(NCH):
                        mm(PS(bank), xg_bf(kc, 128 * tt - t0 + t0, 128) if False else
                           xg_bf(kc, 128 * tt, 128),
                           wv_bf[:, 1024 * kc + 512 * nt:1024 * kc + 512 * nt + 512],
                           start=(kc == 0), stop=(kc == NCH - 1),
                           wbank=(bank,), rkeys=(("gw2",), ("gxbf", kc, ss)),
                           inc=s_pe if kc == NCH - 1 else None)
                    v_copy(vhat_g(tt, 512 * nt, 512), PS(bank), rbanks=(bank,),
                           wkeys=(("gvh",),))
        barrier()

        P.mark("g2")
        wq_bf = A.bf16(G_W1, 8 * 1024)
        wo_bf = A.bf16(G_W2, 8 * 1024)
        dma_pool(wq_bf.rearrange("p (c n) -> p c n", c=8), gwq.rearrange("(c p) n -> p c n", p=128), s_wA, wkeys=(("gw1",),))
        dma_pool(wo_bf.rearrange("p (c n) -> p c n", c=8), gwo.rearrange("(c p) n -> p c n", p=128), s_wB, wkeys=(("gw2",),))
        qsb = A.bf16(G_QS, 8 * 512)
        ptb = A.bf16(G_PT, 2048)
        xst = A.f32(G_XST, 8 * 512)
        sqb = A.bf16(G_OS, 8 * 512)
        obuf = A.bf16(G_OS, 8 * 512)
        bce = A.f32(G_BC, 512)
        rcp = A.f32(G_RC, 512)[0:1, :]
        mu = A.f32(G_MU, 512)[0:1, :]
        m2 = A.f32(G_M2, 512)[0:1, :]
        for ss in range(4):
            t0 = 512 * ss
            sl = ss % 2
            dma_sync(tbc[sl], tbl_g[0][:, t0:t0 + 512], s_tbs[sl], wkeys=(("gtb", sl),))
            dma_sync(tbs[sl], tbl_g[1][:, t0:t0 + 512], s_tbs[sl], wkeys=(("gtb", sl),))
            for g in range(4):
                for cc, bank in ((g, 4), (4 + g, 5)):
                    for kc in range(NCH):
                        mm(PS(bank), wq_bf[:, 1024 * kc + 128 * cc:1024 * kc + 128 * cc + 128],
                           xg_bf(kc, t0, 512), start=(kc == 0), stop=(kc == NCH - 1),
                           wbank=(bank,), rkeys=(("gw1",),),
                           inc=s_pe if kc == NCH - 1 else None)
                rope_pass_global(4, 5,
                                 A.bf16(G_QS + 2 * 512 * g, 512),
                                 A.bf16(G_QS + 2 * 512 * (4 + g), 512),
                                 tbc[sl], tbs[sl], t1b[sl], t2b[sl],
                                 {"t1": ("gt1", sl), "t2": ("gt2", sl),
                                  "tbl": ("gtb", sl), "out": ("gqs",)})
            for ch in range(NCH):
                src = x_in[b, t0:t0 + 512, 128 * ch:128 * (ch + 1)] \
                    .rearrange("(tt p) c -> p tt c", p=128)
                dma_sync(stg.rearrange("p (tt c) -> p tt c", tt=4), src, s_x, wkeys=(("gstg",),))
                for i in range(4):
                    transpose(PS(6, 128, 128, 128 * i), stg[:, 128 * i:128 * (i + 1)],
                              6, rkey=("gstg",), start_clear=(i == 0), stop=(i == 3))
                v_copy(xst[:, 512 * ch:512 * (ch + 1)], PS(6), rbanks=(6,),
                       wkeys=(("gxst", ch),))
            nkt = 4 * (ss + 1)
            for h in range(GH):
                r0 = 64 * (h % 2)
                cA, cB = h // 2, 4 + h // 2
                pv_started = False
                dn_started = False
                for ktg in range(0, nkt, 4):
                    gsz = min(4, nkt - ktg)
                    for i in range(gsz):
                        kt = ktg + i
                        qlo = max(0, 128 * kt - t0)
                        nq = 512 - qlo
                        for half, cc in ((0, cA), (1, cB)):
                            mm(PS(i, 128, nq, qlo),
                               khat_g(r0, 64, cc, 128 * kt, 128),
                               A.bf16(G_QS + 2 * 512 * cc, 512)[r0:r0 + 64, qlo:qlo + nq],
                               start=(half == 0), stop=(half == 1),
                               wbank=(i,), rkeys=(("gkh",), ("gqs",)),
                               tile_position=(r0, 0),
                               inc=s_pe if half == 1 else None)
                    if 128 * (ktg + gsz) <= t0:
                        a_act(ptb[:, 0:512 * gsz], ps_t.ap()[:, 0:512 * gsz], AF.Exp,
                              scale=1.0 / 128.0,
                              rbanks=tuple(range(gsz)), wkeys=(("gpt",),))
                    else:
                        for i in range(gsz):
                            kt = ktg + i
                            qlo = max(0, 128 * kt - t0)
                            a_act(ptb[:, 512 * i + qlo:512 * (i + 1)],
                                  PS(i, 128, 512 - qlo, qlo), AF.Exp, scale=1.0 / 128.0,
                                  rbanks=(i,), wkeys=(("gpt",),))
                            if 128 * kt >= t0:
                                v_tt(ptb[:, 512 * i + qlo:512 * i + qlo + 128],
                                     ptb[:, 512 * i + qlo:512 * i + qlo + 128],
                                     cs_bf(C_TRI1, 128), ALU.mult,
                                     rkeys=(("gpt",),), wkeys=(("gpt",),))
                    for i in range(gsz):
                        kt = ktg + i
                        qlo = max(0, 128 * kt - t0)
                        nq = 512 - qlo
                        last = (kt == nkt - 1)
                        mm(PS(6, 1, nq, qlo), cs_bf(C_ONESCB, 1),
                           ptb[:, 512 * i + qlo:512 * (i + 1)],
                           start=not dn_started, stop=last,
                           wbank=(6,), rkeys=(("gpt",),),
                           inc=s_pe if last else None)
                        dn_started = True
                        mm(PS(7, 128, nq, qlo), vhat_g(kt, 128 * h, 128),
                           ptb[:, 512 * i + qlo:512 * (i + 1)],
                           start=not pv_started, stop=last,
                           wbank=(7,),
                           rkeys=(("gvh",), ("gpt",)),
                           inc=s_pe if last else None)
                        pv_started = True
                a_act(rcp, PS(6, 1, 512), AF.Ln, rbanks=(6,), wkeys=(("grcp",),))
                a_act(rcp, rcp, AF.Exp, scale=-1.0, rkeys=(("grcp",),), wkeys=(("grcp",),))
                mm(PS(5), csb.ap()[0:1, C_ONESR:C_ONESR + 128], rcp, start=True, stop=True,
                   wbank=(5,), rkeys=(("grcp",),), inc=s_pe)
                v_copy(bce, PS(5), rbanks=(5,), wkeys=(("gbce",),))
                v_tt(obuf[:, 512 * h:512 * (h + 1)], PS(7), bce, ALU.mult,
                     rbanks=(7,), rkeys=(("gbce",),), wkeys=(("gos", h),))
            for mt in range(NCH):
                for hc in range(NCH):
                    mm(PS(mt), wo_bf[:, 1024 * hc + 128 * mt:1024 * hc + 128 * mt + 128],
                       obuf[:, 512 * hc:512 * (hc + 1)],
                       start=(hc == 0), stop=(hc == NCH - 1),
                       wbank=(mt,), rkeys=(("gw2",), ("gos", hc)),
                       inc=s_pe if hc == NCH - 1 else None)
                v_tt(xst[:, 512 * mt:512 * (mt + 1)], PS(mt), xst[:, 512 * mt:512 * (mt + 1)],
                     ALU.add, rbanks=(mt,), rkeys=(("gxst", mt),), wkeys=(("gxst", mt),))
            if getattr(build_program, "_dbg", False) and ss in (0,):
                def _dbgx(E, t=xst, tag=(b, ss)):
                    import numpy as _np
                    E.dbg_aps(lambda a, g, tg=tag: print("XSTDBG", tg, _np.asarray(a), _np.asarray(g)),
                              t[0:1, 0:4], t[5:6, 512 * 3:512 * 3 + 4])
                    return E.nop()
                P.op("vector", _dbgx,
                     waits=sum((TK.wait_read(("gxst", c)) + TK.wait_write(("gxst", c))
                                for c in range(NCH)), []))
            for ch in range(NCH):
                a_act(sqb[:, 512 * ch:512 * (ch + 1)], xst[:, 512 * ch:512 * (ch + 1)],
                      AF.Square, rkeys=(("gxst", ch),), wkeys=(("gos", ch),))
            for ch in range(NCH):
                mm(PS(4, 1, 512), cs_f32(C_ONESCF, 1), xst[:, 512 * ch:512 * (ch + 1)],
                   start=(ch == 0), stop=(ch == NCH - 1),
                   wbank=(4,), rkeys=(("gxst", ch),),
                   inc=s_pe if ch == NCH - 1 else None)
            for ch in range(NCH):
                mm(PS(5, 1, 512), cs_bf(C_ONESCB, 1), sqb[:, 512 * ch:512 * (ch + 1)],
                   start=(ch == 0), stop=(ch == NCH - 1),
                   wbank=(5,), rkeys=(("gos", ch),),
                   inc=s_pe if ch == NCH - 1 else None)
            a_act(mu, PS(4, 1, 512), AF.Copy, scale=1.0 / DM, rbanks=(4,), wkeys=(("gmu",),))
            a_act(m2, PS(4, 1, 512), AF.Square, scale=1.0 / DM, rbanks=(4,), wkeys=(("gm2",),))
            v_stt(m2, PS(5, 1, 512), 1.0 / DM, m2, ALU.mult, ALU.subtract,
                  rbanks=(5,), rkeys=(("gm2",),), wkeys=(("gm2",),))
            a_act(m2, m2, AF.Ln, bias=cs_f32(C_EPS, 1)[0:1, :], rkeys=(("gm2",),), wkeys=(("gm2",),))
            a_act(m2, m2, AF.Exp, scale=-0.5, rkeys=(("gm2",),), wkeys=(("gm2",),))
            v_stt(mu, mu, -1.0, m2, ALU.mult, ALU.mult,
                  rkeys=(("gmu",), ("gm2",)), wkeys=(("gmu",),))
            mm(PS(4), csb.ap()[0:1, C_ONESR:C_ONESR + 128], m2, start=True, stop=True,
               wbank=(4,), rkeys=(("gm2",),), inc=s_pe)
            mm(PS(5), csb.ap()[0:1, C_ONESR:C_ONESR + 128], mu, start=True, stop=True,
               wbank=(5,), rkeys=(("gmu",),), inc=s_pe)
            for ch in range(NCH):
                v_tt(xst[:, 512 * ch:512 * (ch + 1)], xst[:, 512 * ch:512 * (ch + 1)],
                     PS(4), ALU.mult, rbanks=(4,), rkeys=(("gxst", ch),), wkeys=(("gxst", ch),))
                v_tt(xst[:, 512 * ch:512 * (ch + 1)], xst[:, 512 * ch:512 * (ch + 1)],
                     PS(5), ALU.add, rbanks=(5,), rkeys=(("gxst", ch),), wkeys=(("gxst", ch),))
            dma_store(x1sp[:, t0:t0 + 512].rearrange("(c p) t -> p c t", p=128),
                      xst.rearrange("p (c t) -> p c t", c=8),
                      rkeys=tuple(("gxst", ch) for ch in range(NCH)),
                      note_rkeys=tuple(("gxst", ch) for ch in range(NCH)))
        barrier()

    I_X = 0
    I_WQKV = 65568
    I_WO = 114720
    I_XBFW = 131104
    I_QPRE = 139328
    I_QBF = 147520
    I_KBF = 151616
    I_VBF = 159808
    I_USQ = 168000
    I_PT = 169024
    I_TBL = 175168
    I_TT = 183360
    I_OBUF = 185408
    I_BCE = 189504
    I_NRM = 190528
    I_END = 192576
    assert I_END <= ARENA_BYTES
    F_H = I_WQKV
    F_XSQ = I_WO
    F_W1 = I_WO + 8224
    F_XN = 139328
    F_W2 = 155712
    F_STA = 172096
    F_GEL = 178240
    F_GEL2 = 182336
    F_END = 186432
    assert F_END <= ARENA_BYTES

    def x_sl(ch, c0, n):
        return A.f32(I_X + 4 * 2049 * ch, 2049)[:, c0:c0 + n]

    def inner_qk(ten, w, slot, outfn, sA, sB, nsB, wqkv, xbfw, tb, t1b, t2b, qpre, usq,
                 nrm, recd, layer_key):
        base = 1024 * ten
        t0 = WIN * w
        _full = "qkv" in ATTN_PARTS or "q" in ATTN_PARTS or "k" in ATTN_PARTS
        _p_mm = _full or "qk_mm" in ATTN_PARTS
        _p_sq = _p_mm and (_full or "qk_sq" in ATTN_PARTS)
        _p_rot = _p_mm and (_full or "qk_rot" in ATTN_PARTS)
        _p_bc = _p_sq and (_full or "qk_bc" in ATTN_PARTS)
        enb = 6 + ten
        for st in range(5 if DEFER_QKV else 4):
            if st < 4 and _p_mm:
                g = st
                b0 = 2 * (g % 2) if QKV_ALT else 0
                for cc, bank in ((g, b0), (4 + g, b0 + 1)):
                    for kc in range(NCH):
                        o = 257 * kc + (1 if kc < 4 else 0)
                        mm(PS(bank, 128, 256),
                           wqkv[:, 3072 * kc + base + 128 * cc:3072 * kc + base + 128 * cc + 128],
                           xbfw[slot][:, o:o + 256],
                           start=(kc == 0), stop=(kc == NCH - 1),
                           wbank=(bank,), rkeys=(layer_key, ("xbfw", slot)),
                           inc=s_pe if kc == NCH - 1 else None)
            gi = st if st < 4 else -1
            gd = st - 1 if st >= 1 else -1
            g = gd if (DEFER_QKV and DEFER_SQ) else gi
            if g >= 0:
                b0 = 2 * (g % 2) if QKV_ALT else 0
                if _p_sq:
                    a_act(usq[:, 0:256], PS(b0, 128, 256), AF.Square, rbanks=(b0,),
                          wkeys=(("usq", 0),))
                    a_act(usq[:, 256:512], PS(b0 + 1, 128, 256), AF.Square, rbanks=(b0 + 1,),
                          wkeys=(("usq", 1),))
                    for j in range(2):
                        mm(PS(enb, 16, 256), cs_bf(C_ENORM, 64)[:, 16 * g:16 * (g + 1)],
                           usq[:, 256 * j:256 * (j + 1)],
                           start=(g == 0 and j == 0), stop=(g == 3 and j == 1),
                           wbank=(enb,), rkeys=(("usq", j),),
                           inc=s_pe if (g == 3 and j == 1) else None)
            g = gd if (DEFER_QKV and DEFER_ROT) else gi
            if g >= 0:
                b0 = 2 * (g % 2) if QKV_ALT else 0
                if _p_rot:
                    for cc, ub, us, s1, s2 in ((g, b0, b0 + 1, sA, nsB), (4 + g, b0 + 1, b0, sB, sA)):
                        v_stt(t1b, PS(ub, 128, 256), s1, tb[2 * ten], ALU.mult, ALU.mult,
                              rbanks=(ub,), rkeys=(("itb", slot),), wkeys=(("it1",),))
                        v_stt(t2b, PS(us, 128, 256), s2, tb[2 * ten + 1], ALU.mult, ALU.mult,
                              rbanks=(us,), rkeys=(("itb", slot),), wkeys=(("it2",),))
                        p_tt(qpre[:, 256 * cc:256 * (cc + 1)], t1b, t2b, ALU.add,
                             rkeys=(("it1",), ("it2",)), wkeys=(("qpre", cc),))
        if _p_sq:
            a_act(nrm, PS(enb, 16, 256), AF.Ln, bias=cs_f32(C_EPS2, 1)[0:16, :],
                  rbanks=(enb,), wkeys=(("nrm",),))
            a_act(recd, nrm, AF.Exp, scale=-0.5, rkeys=(("nrm",),), wkeys=(("recd",),))
        if _p_bc:
            for g in range(4):
                fb = (6 + ((ten + g) % 2)) if FN_ALT else enb
                mm(PS(fb, 128, 256), csb.ap()[0:16, C_FNORM + 128 * g:C_FNORM + 128 * (g + 1)],
                   recd, start=True, stop=True, wbank=(fb,), rkeys=(("recd",),), inc=s_pe)
                for cc in (g, 4 + g):
                    v_tt(outfn(cc), qpre[:, 256 * cc:256 * (cc + 1)], PS(fb, 128, 256), ALU.mult,
                         rbanks=(fb,), rkeys=(("qpre", cc),), wkeys=(("qk_out", ten),))

    def emit_inner_attn(b, l):
        P.mark("L%da" % l)
        wqkv = A.bf16(I_WQKV, 8 * 3072)
        wo_bf = A.bf16(I_WO, 8 * 1024)
        lk = ("iw", l)
        dma_pool(wqkv.rearrange("p (c n) -> p c n", c=8),
                 iwqkv[l].rearrange("(c p) n -> p c n", p=128), s_wA, wkeys=(lk,))
        dma_pool(wo_bf.rearrange("p (c n) -> p c n", c=8),
                 iwo[l].rearrange("(c p) n -> p c n", p=128), s_wB, wkeys=(lk,))
        dma_sync(cs_f32(C_QS, 4), qscol[l], s_vec, wkeys=(("qs",),))
        v_ts(cs_f32(C_QS + 4, 1), cs_f32(C_QS + 1, 1), -1.0, None, ALU.mult,
             rkeys=(("qs",),), wkeys=(("qs",),))
        v_ts(cs_f32(C_QS + 5, 1), cs_f32(C_QS + 3, 1), -1.0, None, ALU.mult,
             rkeys=(("qs",),), wkeys=(("qs",),))
        qsA, qsB = cs_f32(C_QS, 1), cs_f32(C_QS + 1, 1)
        ksA, ksB = cs_f32(C_QS + 2, 1), cs_f32(C_QS + 3, 1)
        nqsB, nksB = cs_f32(C_QS + 4, 1), cs_f32(C_QS + 5, 1)

        xbfw = [A.bf16(I_XBFW + 4112 * i, 8 * 257) for i in range(2)]
        qpre = A.f32(I_QPRE, 8 * 256)
        qbf = A.bf16(I_QBF, 8 * 256)
        kbf = A.bf16(I_KBF, 8 * 512)
        vbf = A.bf16(I_VBF, 4 * 1024)
        usq = A.bf16(I_USQ, 512)
        ptb = [A.bf16(I_PT + 1536 * i, 768) for i in range(4)]
        tb = [[A.f32(I_TBL + 1024 * j + 4096 * i, 256) for j in range(4)] for i in range(2)]
        t1b = A.f32(I_TT, 256)
        t2b = A.f32(I_TT + 1024, 256)
        ob = A.bf16(I_OBUF, 8 * 256)
        bce = A.f32(I_BCE, 256)
        nrm = A.f32(I_NRM, 256)[0:16, :]
        recd = A.f32(I_NRM + 1024, 256)[0:16, :]

        def cast_win(w):
            slot = w % 2
            t0 = WIN * w
            xsrc = A.f32(I_X, 8 * 2049).rearrange("p (c t) -> p c t", c=8)[:, :, t0:t0 + 257]
            xdst = xbfw[slot].rearrange("p (c t) -> p c t", c=8)
            v_copy(xdst, xsrc,
                   rkeys=tuple(("x", ch) for ch in range(NCH)), wkeys=(("xbfw", slot),))

        _cast_on = "qkv" in ATTN_PARTS or "cast" in ATTN_PARTS
        _tbl_on = "qkv" in ATTN_PARTS or "tbl" in ATTN_PARTS

        def emit_qkv(w):
            slot = w % 2
            t0w = WIN * w
            for j in (range(4) if _tbl_on else ()):
                dma_sync(tb[slot][j], tbl_i[j][:, t0w:t0w + 256], s_tbs[slot],
                         wkeys=(("itb", slot),))
            if ("qkv" in ATTN_PARTS or "q" in ATTN_PARTS
                    or any(p.startswith("qk_") for p in ATTN_PARTS)):
                inner_qk(0, w, slot, lambda cc: A.bf16(I_QBF + 2 * 256 * cc, 256),
                         qsA, qsB, nqsB, wqkv, xbfw, tb[slot], t1b, t2b, qpre, usq, nrm, recd, lk)
            if "qkv" in ATTN_PARTS or "k" in ATTN_PARTS:
                inner_qk(1, w, slot, lambda cc: A.bf16(I_KBF + 2 * (512 * cc + 256 * slot), 256),
                         ksA, ksB, nksB, wqkv, xbfw, tb[slot], t1b, t2b, qpre, usq, nrm, recd, lk)
            if "qkv" in ATTN_PARTS or "v" in ATTN_PARTS:
                for tt in range(2):
                    for nt in range(2):
                        bank = 2 + nt
                        for kc in range(NCH):
                            o = 257 * kc + (1 if kc < 4 else 0) + 128 * tt
                            mm(PS(bank), xbfw[slot][:, o:o + 128],
                               wqkv[:, 3072 * kc + 2048 + 512 * nt:3072 * kc + 2048 + 512 * nt + 512],
                               start=(kc == 0), stop=(kc == NCH - 1),
                               wbank=(bank,), rkeys=(lk, ("xbfw", slot)),
                               inc=s_pe if kc == NCH - 1 else None)
                        v_copy(A.bf16(I_VBF + 2 * (1024 * (2 * slot + tt) + 512 * nt), 512),
                               PS(bank), rbanks=(bank,), wkeys=(("vbf", slot),))

        if _cast_on:
            cast_win(0)
        emit_qkv(0)
        if _cast_on and NWIN > 1:
            cast_win(1)
        for w in range(NWIN):
            t0 = WIN * w
            slot = w % 2
            pslot = (w - 1) % 2
            if w + 2 < NWIN and _cast_on:
                cast_win(w + 2)
            if w == 0:
                slots = [(0, (slot, 0), 1), (1, (slot, 1), 1), (1, (slot, 0), 0)]
                maskc = cs_bf(C_TRIM2, 256)
                nmask = 2
            else:
                slots = [(0, (pslot, 0), 2), (1, (pslot, 1), 2),
                         (0, (slot, 0), 1), (1, (slot, 1), 1),
                         (0, (pslot, 1), 0), (1, (slot, 0), 0)]
                maskc = cs_bf(C_TRIM4, 512)
                nmask = 4
            nslot = len(slots)

            def pair_ln(p):
                a_act(recd[0:2, :], PS(6, 2, 256), AF.Ln, rbanks=(6,), wkeys=(("recd",),))
                a_act(recd[0:2, :], recd[0:2, :], AF.Exp, scale=-1.0,
                      rkeys=(("recd",),), wkeys=(("recd",),))

            def pair_fin(p):
                pvb = 4 + (p % 2)
                mm(PS(7, 128, 256), csb.ap()[0:2, C_FPAIR:C_FPAIR + 128],
                   recd[0:2, :], start=True, stop=True, wbank=(7,), rkeys=(("recd",),),
                   inc=s_pe)
                v_copy(bce, PS(7, 128, 256), rbanks=(7,), wkeys=(("bce",),))
                v_tt(ob[:, 256 * p:256 * (p + 1)], PS(pvb, 128, 256), bce, ALU.mult,
                     rbanks=(pvb,), rkeys=(("bce",),), wkeys=(("ob", p),))

            for st in (range(IH + 1) if "attn" in ATTN_PARTS else ()):
                if st < IH:
                    h = st
                    r0 = 32 * (h % 4)
                    g = h // 4
                    cA, cB = g, 4 + g
                    pts = ptb[h % 4]
                    sb0 = 2 * (h % 2)
                    for si, (a, (ks, kt), mk) in enumerate(slots):
                        for half, cc in ((0, cA), (1, cB)):
                            kcol = 512 * cc + 256 * ks + 128 * kt
                            mm(ps_t.ap()[:, 512 * sb0 + 128 * si:512 * sb0 + 128 * (si + 1)],
                               kbf[r0:r0 + 32, kcol:kcol + 128],
                               A.bf16(I_QBF + 2 * 256 * cc, 256)[r0:r0 + 32, 128 * a:128 * (a + 1)],
                               start=(half == 0), stop=(half == 1),
                               wbank=(sb0, sb0 + 1),
                               rkeys=(("qk_out", 0), ("qk_out", 1)),
                               tile_position=(r0, 0),
                               inc=s_pe if half == 1 else None)
                    a_act(pts[:, 0:128 * nslot], ps_t.ap()[:, 512 * sb0:512 * sb0 + 128 * nslot],
                          AF.Exp, scale=8.0,
                          rbanks=(sb0, sb0 + 1), wkeys=(("pt", h % 4),))
                    v_tt(pts[:, 0:128 * nmask], pts[:, 0:128 * nmask], maskc, ALU.mult,
                         rkeys=(("pt", h % 4),), wkeys=(("pt", h % 4),))
                if (st >= 1) if DEFER_HEAD else (st < IH):
                    h = (st - 1) if DEFER_HEAD else st
                    pts = ptb[h % 4]
                    pvb = 4 + ((h // 2) % 2)
                    if h % 2 == 0 and h >= 2:
                        pair_ln((h - 2) // 2)
                    for si, (a, (ks, kt), mk) in enumerate(slots):
                        mm(PS(6, 2, 128, 128 * a), cs_bf(C_EDEN, 4)[:, 2 * (h % 2):2 * (h % 2) + 2],
                           pts[:, 128 * si:128 * (si + 1)],
                           start=(si == 0 and h % 2 == 0), stop=(si == nslot - 1 and h % 2 == 1),
                           wbank=(6,),
                           rkeys=(("pt", h % 4),),
                           inc=s_pe if (si == nslot - 1 and h % 2 == 1) else None)
                    for si, (a, (ks, kt), mk) in enumerate(slots):
                        vcol = 1024 * (2 * ks + kt) + 64 * h
                        mm(PS(pvb, 64, 128, 128 * a, r0=64 * (h % 2)),
                           A.bf16(I_VBF + 2 * vcol, 64),
                           pts[:, 128 * si:128 * (si + 1)],
                           start=(si == 0), stop=(si == nslot - 1),
                           wbank=(pvb,),
                           rkeys=(("vbf", ks), ("pt", h % 4)),
                           tile_position=(0, 64 * (h % 2)),
                           inc=s_pe if si == nslot - 1 else None)
                    if h % 2 == 1 and h >= 3:
                        pair_fin((h - 3) // 2)
            if "attn" in ATTN_PARTS:
                pair_ln(7)
                pair_fin(7)
            if w + 1 < NWIN:
                emit_qkv(w + 1)
            wo_banks = (4, 5, 4, 5, 6, 7, 6, 7)
            for mt in (range(NCH) if "wo" in ATTN_PARTS else ()):
                bank = wo_banks[mt]
                for hc in range(NCH):
                    mm(PS(bank, 128, 256), wo_bf[:, 1024 * hc + 128 * mt:1024 * hc + 128 * mt + 128],
                       ob[:, 256 * hc:256 * (hc + 1)],
                       start=(hc == 0), stop=(hc == NCH - 1),
                       wbank=(bank,),
                       rkeys=(lk, ("ob", hc)),
                       inc=s_pe if hc == NCH - 1 else None)
                v_tt(x_sl(mt, t0 + 1, 256), PS(bank, 128, 256), x_sl(mt, t0 + 1, 256), ALU.add,
                     rbanks=(bank,), rkeys=(("x", mt),), wkeys=(("x", mt),))
        barrier()

    def emit_inner_ffn(b, l):
        P.mark("L%df" % l)
        hbf = A.bf16(F_H, NJT * 1024)
        xsq = A.bf16(F_XSQ, 8 * 513)
        w1s = [A.bf16(F_W1 + 8192 * i, 8 * 512) for i in range(2)]
        xn = [A.bf16(F_XN + 8192 * i, 8 * 512) for i in range(2)]
        w2s = [A.bf16(F_W2 + 4096 * i, 2 * 1024) for i in range(4)]
        gel2 = [A.f32(F_GEL, 1024), A.f32(F_GEL2, 1024)]
        stA = A.f32(F_STA, 512)[0:1, :]
        stB = A.f32(F_STA + 2048, 512)[0:1, :]
        stM = A.f32(F_STA + 4096, 512)[0:1, :]
        dma_sync(cs_f32(C_GCOL, 8), lng_col[l], s_vec, wkeys=(("lv",),))
        dma_sync(cs_f32(C_BLNF, 8), lnb_col[l], s_vec, wkeys=(("lv",),))
        dma_sync(cs_f32(C_B1, 44), b1_col[l], s_vec, wkeys=(("lv",),))
        dma_sync(cs_f32(C_B2, 8), b2_col[l], s_vec, wkeys=(("lv",),))
        v_copy(cs_bf(C_BLNB, 8), cs_f32(C_BLNF, 8), rkeys=(("lv",),), wkeys=(("lv",),))

        def w1cols(p):
            return (256 * p, 128 if p < 21 else 42) if p < 21 else (5376, 42)

        def emit_xn(qq):
            t0 = 512 * qq
            sl = qq % 2
            for ch in range(NCH):
                a_act(xsq[:, 513 * ch:513 * ch + 513], x_sl(ch, t0, 513), AF.Square,
                      rkeys=(("x", ch),), wkeys=(("xsq", ch),))
            for ch in range(NCH):
                o = 513 * ch + (1 if ch < 4 else 0)
                mm(PS(6, 1, 512), cs_f32(C_ONESCF, 1),
                   x_sl(ch, t0 + (1 if ch < 4 else 0), 512),
                   start=(ch == 0), stop=(ch == NCH - 1),
                   wbank=(6,), rkeys=(("x", ch),),
                   inc=s_pe if ch == NCH - 1 else None)
            for ch in range(NCH):
                o = 513 * ch + (1 if ch < 4 else 0)
                mm(PS(7, 1, 512), cs_bf(C_ONESCB, 1), xsq[:, o:o + 512],
                   start=(ch == 0), stop=(ch == NCH - 1),
                   wbank=(7,), rkeys=(("xsq", ch),),
                   inc=s_pe if ch == NCH - 1 else None)
            a_act(stM, PS(6, 1, 512), AF.Copy, scale=1.0 / DM, rbanks=(6,), wkeys=(("stM",),))
            a_act(stA, PS(6, 1, 512), AF.Square, scale=1.0 / DM, rbanks=(6,), wkeys=(("stA",),))
            v_stt(stA, PS(7, 1, 512), 1.0 / DM, stA, ALU.mult, ALU.subtract,
                  rbanks=(7,), rkeys=(("stA",),), wkeys=(("stA",),))
            a_act(stA, stA, AF.Ln, bias=cs_f32(C_EPS, 1)[0:1, :], rkeys=(("stA",),), wkeys=(("stA",),))
            a_act(stA, stA, AF.Exp, scale=-0.5, rkeys=(("stA",),), wkeys=(("stA",),))
            v_stt(stB, stM, -1.0, stA, ALU.mult, ALU.mult,
                  rkeys=(("stM",), ("stA",)), wkeys=(("stB",),))
            mm(PS(6), csb.ap()[0:1, C_ONESR:C_ONESR + 128], stA, start=True, stop=True,
               wbank=(6,), rkeys=(("stA",),), inc=s_pe)
            mm(PS(7), csb.ap()[0:1, C_ONESR:C_ONESR + 128], stB, start=True, stop=True,
               wbank=(7,), rkeys=(("stB",),), inc=s_pe)
            for ch in range(NCH):
                sh = 1 if ch < 4 else 0
                v_stt(gel2[0][:, 0:512], x_sl(ch, t0 + sh, 512), cs_f32(C_GCOL + ch, 1),
                      PS(6), ALU.mult, ALU.mult,
                      rbanks=(6,), rkeys=(("x", ch), ("lv",)), wkeys=(("gel", 0),))
                v_stt(xn[sl][:, 512 * ch:512 * (ch + 1)], PS(7), cs_f32(C_GCOL + ch, 1),
                      gel2[0][:, 0:512], ALU.mult, ALU.add,
                      rbanks=(7,), rkeys=(("gel", 0), ("lv",)), wkeys=(("xn", sl),))

        def emit_w1_half(hh, first):
            for jg in range(11):
                sl = jg % 2
                c0 = 512 * jg
                cw = 512 if jg < 10 else 340
                dma_pool(w1s[sl][:, 0:0 + 8 * 512] if False else w1s[sl],
                         None, wkeys=()) if False else None
                src = iw1[l][:, c0:c0 + cw].rearrange("(c p) n -> p c n", p=128)
                dma_pool(w1s[sl].rearrange("p (c n) -> p c n", c=8)[:, :, 0:cw],
                         src, s_w1[sl], wkeys=(("w1s", sl),))
                for pi in range(2):
                    p = 2 * jg + pi
                    if p >= NPAIR:
                        continue
                    wdt = 128 if p < 21 else 42
                    la = 256 * pi if jg < 10 else 256 * pi
                    if jg < 10:
                        loa, log = 256 * pi, 256 * pi + 128
                    else:
                        loa, log = 256 * pi, 256 * pi + wdt
                    pb = 4 * (p % 2) if FFN_PB else 0
                    if first:
                        for ti, lo in ((2 * p, loa), (2 * p + 1, log)):
                            for kc in range(NCH):
                                mm(PS(pb + 2, wdt, 1), w1s[sl][:, 512 * kc + lo:512 * kc + lo + wdt],
                                   cs_bf(C_BLNB, 8)[:, kc:kc + 1],
                                   start=(kc == 0), stop=(kc == NCH - 1),
                                   wbank=(pb + 2,),
                                   rkeys=(("w1s", sl), ("lv",)),
                                   inc=s_pe if kc == NCH - 1 else None)
                            v_stt(cs_f32(C_U1 + ti, 1)[0:wdt, :], PS(pb + 2, wdt, 1),
                                  1.0, cs_f32(C_B1 + ti, 1)[0:wdt, :], ALU.mult, ALU.add,
                                  rbanks=(pb + 2,), rkeys=(("lv",),), wkeys=(("u1", ti),))
                    for bb, lo in ((pb, loa), (pb + 2, log)):
                        for nt in range(2):
                            bank = bb + nt
                            for kc in range(NCH):
                                mm(PS(bank, wdt, 512), w1s[sl][:, 512 * kc + lo:512 * kc + lo + wdt],
                                   xn[nt][:, 512 * kc:512 * (kc + 1)],
                                   start=(kc == 0), stop=(kc == NCH - 1),
                                   wbank=(bank,),
                                   rkeys=(("w1s", sl), ("xn", nt)),
                                   inc=s_pe if kc == NCH - 1 else None)
                    a_act(gel2[p % 2][0:wdt, :], ps_t.ap()[0:wdt, 512 * (pb + 2):512 * (pb + 2) + 1024],
                          AF.Gelu,
                          bias=cs_f32(C_U1 + 2 * p + 1, 1)[0:wdt, :],
                          rbanks=(pb + 2, pb + 3), rkeys=(("u1", 2 * p + 1),), wkeys=(("gel", p % 2),))
                    v_stt(hbf[0:wdt, 1024 * p:1024 * (p + 1)],
                          ps_t.ap()[0:wdt, 512 * pb:512 * pb + 1024],
                          cs_f32(C_U1 + 2 * p, 1)[0:wdt, :], gel2[p % 2][0:wdt, :],
                          ALU.add, ALU.mult,
                          rbanks=(pb, pb + 1), rkeys=(("gel", p % 2), ("u1", 2 * p)),
                          wkeys=(("h", p),))

        def emit_w2_half(hh):
            for nt in range(2):
                t0 = 1024 * hh + 512 * nt
                for jg in range(11):
                    sl = (nt * 11 + jg) % 4
                    r0 = 256 * jg
                    rh = 256 if jg < 10 else 170
                    if jg < 10:
                        dma_pool(w2s[sl].rearrange("p (j m) -> p j m", j=2),
                                 iw2[l][r0:r0 + 256, :].rearrange("(j p) m -> p j m", p=128),
                                 s_w2[sl], wkeys=(("w2s", sl),))
                    else:
                        dma_pool(w2s[sl][0:128, 0:1024],
                                 iw2[l][2560:2688, :], s_w2[sl], wkeys=(("w2s", sl),))
                        dma_pool(w2s[sl][0:42, 1024:2048],
                                 iw2[l][2688:2730, :], s_w2[sl], wkeys=(("w2s", sl),))
                    for ji in range(2):
                        j = 2 * jg + ji
                        if j >= NJT:
                            continue
                        kj = 128 if j < 21 else 42
                        for mt in range(NCH):
                            mm(PS(mt, 128, 512),
                               w2s[sl][0:kj, 1024 * ji + 128 * mt:1024 * ji + 128 * (mt + 1)],
                               hbf[0:kj, 1024 * j + 512 * nt:1024 * j + 512 * nt + 512],
                               start=(j == 0), stop=(j == NJT - 1),
                               wbank=(mt,),
                               rkeys=(("w2s", sl), ("h", j)),
                               inc=s_pe if (ji == 1 and mt == NCH - 1) else None)
                for mt in range(NCH):
                    v_stt(x_sl(mt, t0 + 1, 512), PS(mt, 128, 512), cs_f32(C_B2 + mt, 1),
                          x_sl(mt, t0 + 1, 512), ALU.add, ALU.add,
                          rbanks=(mt,), rkeys=(("x", mt), ("lv",)), wkeys=(("x", mt),))

        emit_xn(0)
        emit_xn(1)
        emit_w1_half(0, first=True)
        emit_xn(2)
        emit_xn(3)
        emit_w2_half(0)
        emit_w1_half(1, first=False)
        emit_w2_half(1)
        barrier()

    def emit_final(b):
        P.mark("final")
        xst = A.f32(G_XST, 8 * 512)
        sqb = A.bf16(G_OS, 8 * 512)
        mu = A.f32(G_MU, 512)[0:1, :]
        m2 = A.f32(G_M2, 512)[0:1, :]
        tmst = A.f32(G_BC, 1024)
        for ss in range(4):
            t0 = 512 * ss
            dma_sync(xst.rearrange("p (c t) -> p c t", c=8),
                     x1sp[:, t0:t0 + 512].rearrange("(c p) t -> p c t", p=128),
                     s_x1r, waits=((s_st, s_st.n),),
                     wkeys=tuple(("fx1", ch) for ch in range(NCH)))
            for ch in range(NCH):
                v_tt(xst[:, 512 * ch:512 * (ch + 1)], xst[:, 512 * ch:512 * (ch + 1)],
                     x_sl(ch, t0 + 1, 512), ALU.add,
                     rkeys=(("x", ch), ("fx1", ch)), wkeys=(("fx1", ch),))
            for ch in range(NCH):
                a_act(sqb[:, 512 * ch:512 * (ch + 1)], xst[:, 512 * ch:512 * (ch + 1)],
                      AF.Square, rkeys=(("fx1", ch),), wkeys=(("fsq", ch),))
            for ch in range(NCH):
                mm(PS(4, 1, 512), cs_f32(C_ONESCF, 1), xst[:, 512 * ch:512 * (ch + 1)],
                   start=(ch == 0), stop=(ch == NCH - 1),
                   wbank=(4,), rkeys=(("fx1", ch),),
                   inc=s_pe if ch == NCH - 1 else None)
            for ch in range(NCH):
                mm(PS(5, 1, 512), cs_bf(C_ONESCB, 1), sqb[:, 512 * ch:512 * (ch + 1)],
                   start=(ch == 0), stop=(ch == NCH - 1),
                   wbank=(5,), rkeys=(("fsq", ch),),
                   inc=s_pe if ch == NCH - 1 else None)
            a_act(mu, PS(4, 1, 512), AF.Copy, scale=1.0 / DM, rbanks=(4,), wkeys=(("gmu",),))
            a_act(m2, PS(4, 1, 512), AF.Square, scale=1.0 / DM, rbanks=(4,), wkeys=(("gm2",),))
            v_stt(m2, PS(5, 1, 512), 1.0 / DM, m2, ALU.mult, ALU.subtract,
                  rbanks=(5,), rkeys=(("gm2",),), wkeys=(("gm2",),))
            a_act(m2, m2, AF.Ln, bias=cs_f32(C_EPS, 1)[0:1, :], rkeys=(("gm2",),), wkeys=(("gm2",),))
            a_act(m2, m2, AF.Exp, scale=-0.5, rkeys=(("gm2",),), wkeys=(("gm2",),))
            v_stt(mu, mu, -1.0, m2, ALU.mult, ALU.mult,
                  rkeys=(("gmu",), ("gm2",)), wkeys=(("gmu",),))
            mm(PS(4), csb.ap()[0:1, C_ONESR:C_ONESR + 128], m2, start=True, stop=True,
               wbank=(4,), rkeys=(("gm2",),), inc=s_pe)
            mm(PS(5), csb.ap()[0:1, C_ONESR:C_ONESR + 128], mu, start=True, stop=True,
               wbank=(5,), rkeys=(("gmu",),), inc=s_pe)
            for ch in range(NCH):
                v_tt(xst[:, 512 * ch:512 * (ch + 1)], xst[:, 512 * ch:512 * (ch + 1)],
                     PS(4), ALU.mult, rbanks=(4,), rkeys=(("fx1", ch),), wkeys=(("fx1", ch),))
                v_tt(xst[:, 512 * ch:512 * (ch + 1)], xst[:, 512 * ch:512 * (ch + 1)],
                     PS(5), ALU.add, rbanks=(5,), rkeys=(("fx1", ch),), wkeys=(("fx1", ch),))
            for tt in range(4):
                bank = tt % 2
                for ch in range(NCH):
                    transpose(PS(bank, 128, 128, 128 * (ch % 4), r0=0)
                              if False else
                              ps_t.ap()[:, 512 * (2 * bank + ch // 4) + 128 * (ch % 4):
                                        512 * (2 * bank + ch // 4) + 128 * (ch % 4) + 128],
                              xst[:, 512 * ch + 128 * tt:512 * ch + 128 * (tt + 1)],
                              2 * bank + ch // 4, rkey=("fx1", ch),
                              start_clear=(ch % 4 == 0))
                v_copy(tmst, ps_t.ap()[:, 1024 * bank:1024 * bank + 1024],
                       rbanks=(2 * bank, 2 * bank + 1), wkeys=(("ftm",),))
                dma_store(out[b, t0 + 128 * tt:t0 + 128 * (tt + 1), :], tmst,
                          waits=TK.wait_read(("ftm",)) + TK.wait_write(("ftm",)),
                          note_rkeys=())
                TK.note_read(("ftm",), s_st, s_st.n)
        barrier()

    for b in range(n_batches):
        emit_global(b)
        if dump_stage == "x1":
            xst = A.f32(G_XST, 8 * 512)
            tmst = A.f32(G_BC, 1024)
            for ss in range(4):
                t0 = 512 * ss
                dma_sync(xst.rearrange("p (c t) -> p c t", c=8),
                         x1sp[:, t0:t0 + 512].rearrange("(c p) t -> p c t", p=128),
                         s_x1r, waits=((s_st, s_st.n),),
                         wkeys=tuple(("dx", ch) for ch in range(NCH)))
                for tt in range(4):
                    bank = tt % 2
                    for ch in range(NCH):
                        transpose(ps_t.ap()[:, 512 * (2 * bank + ch // 4) + 128 * (ch % 4):
                                            512 * (2 * bank + ch // 4) + 128 * (ch % 4) + 128],
                                  xst[:, 512 * ch + 128 * tt:512 * ch + 128 * (tt + 1)],
                                  2 * bank + ch // 4, rkey=("dx", ch),
                                  start_clear=(ch % 4 == 0), stop=(ch % 4 == 3))
                    v_copy(tmst, ps_t.ap()[:, 1024 * bank:1024 * bank + 1024],
                           rbanks=(2 * bank, 2 * bank + 1), wkeys=(("ftm",),))
                    dma_store(out[b, t0 + 128 * tt:t0 + 128 * (tt + 1), :], tmst,
                              waits=TK.wait_read(("ftm",)) + [TK.w[("ftm",)]])
                    TK.note_read(("ftm",), s_st, s_st.n)
            barrier()
            continue
        P.mark("reload")
        for ch in range(NCH):
            v_memset(x_sl(ch, 0, 1), 0.0, wkeys=(("x", ch),))
        for ch in range(NCH):
            dma_sync(x_sl(ch, 1, 2048), x1sp[128 * ch:128 * (ch + 1), :],
                     s_x1r, wkeys=(("x", ch),), waits=((s_st, s_st.n),))
        barrier()
        for l in range(n_layers):
            emit_inner_attn(b, l)
            if dump_stage == ("L%da" % (l + 1)):
                break
            emit_inner_ffn(b, l)
            if dump_stage == ("L%d" % (l + 1)):
                break
        if dump_stage and dump_stage != "full" and dump_stage.startswith("L"):
            tmst = A.f32(G_BC, 1024)
            for ss in range(4):
                t0 = 512 * ss
                for tt in range(4):
                    bank = tt % 2
                    for ch in range(NCH):
                        transpose(ps_t.ap()[:, 512 * (2 * bank + ch // 4) + 128 * (ch % 4):
                                            512 * (2 * bank + ch // 4) + 128 * (ch % 4) + 128],
                                  x_sl(ch, t0 + 128 * tt + 1, 128),
                                  2 * bank + ch // 4, rkey=("x", ch),
                                  start_clear=(ch % 4 == 0), stop=(ch % 4 == 3))
                    v_copy(tmst, ps_t.ap()[:, 1024 * bank:1024 * bank + 1024],
                           rbanks=(2 * bank, 2 * bank + 1), wkeys=(("ftm",),))
                    dma_store(out[b, t0 + 128 * tt:t0 + 128 * (tt + 1), :], tmst,
                              waits=TK.wait_read(("ftm",)) + [TK.w[("ftm",)]])
                    TK.note_read(("ftm",), s_st, s_st.n)
            barrier()
        elif dump_stage is None or dump_stage == "full":
            emit_final(b)

    P.op("sync", None, waits=((s_st, s_st.n),))
    P.emit()
    return nc


def _prep_inputs(inputs):
    c = host_constants()
    pg = _perm_global_qk()
    pi = _perm_inner_qk()
    pw1, tcols = _w1_tile_cols()
    wq = np.ascontiguousarray(inputs["L0_attn_wq"].reshape(DM, DM)[:, pg], dtype=np.float32)
    wk = np.ascontiguousarray(inputs["L0_attn_wk"].reshape(DM, DM)[:, pg], dtype=np.float32)
    wv = np.ascontiguousarray(inputs["L0_attn_wv"].reshape(DM, DM), dtype=np.float32)
    wo = np.ascontiguousarray(inputs["L0_attn_wo"].reshape(DM, DM), dtype=np.float32)
    Wqkv = np.asarray(inputs["L0_ffn_Wqkv"], dtype=np.float32)
    Wqkv_p = np.concatenate([Wqkv[:, :, :DM][:, :, pi], Wqkv[:, :, DM:2 * DM][:, :, pi],
                             Wqkv[:, :, 2 * DM:]], axis=2)
    W1p = np.ascontiguousarray(np.asarray(inputs["L0_ffn_W1"], dtype=np.float32)[:, :, pw1])
    qs = np.asarray(inputs["L0_ffn_q_scale"], dtype=np.float32)
    ks = np.asarray(inputs["L0_ffn_k_scale"], dtype=np.float32)
    qscol_a = np.zeros((DEPTH, 128, 4), dtype=np.float32)
    for l in range(DEPTH):
        qscol_a[l, :, 0] = np.tile(qs[l, 0:32], 4)
        qscol_a[l, :, 1] = np.tile(qs[l, 32:64], 4)
        qscol_a[l, :, 2] = np.tile(ks[l, 0:32], 4)
        qscol_a[l, :, 3] = np.tile(ks[l, 32:64], 4)
    b1p = np.asarray(inputs["L0_ffn_b1"], dtype=np.float32)[:, pw1]
    b1c = np.zeros((DEPTH, 128, 2 * NPAIR), dtype=np.float32)
    off = 0
    tlist = []
    for p in range(NPAIR):
        w = 128 if p < 21 else 42
        tlist.append((off, w)); off += w
        tlist.append((off, w)); off += w
    for l in range(DEPTH):
        for ti, (o, w) in enumerate(tlist):
            b1c[l, :w, ti] = b1p[l, o:o + w]
    lng_c = np.zeros((DEPTH, 128, NCH), dtype=np.float32)
    lnb_c = np.zeros((DEPTH, 128, NCH), dtype=np.float32)
    b2_c = np.zeros((DEPTH, 128, NCH), dtype=np.float32)
    for l in range(DEPTH):
        lng_c[l] = np.asarray(inputs["L0_ffn_ln_g"][l], dtype=np.float32).reshape(NCH, 128).T
        lnb_c[l] = np.asarray(inputs["L0_ffn_ln_b"][l], dtype=np.float32).reshape(NCH, 128).T
        b2_c[l] = np.asarray(inputs["L0_ffn_b2"][l], dtype=np.float32).reshape(NCH, 128).T
    return {
        "gwq": wq, "gwk": wk, "gwv": wv, "gwo": wo,
        "iwqkv": np.ascontiguousarray(Wqkv_p),
        "iwo": np.ascontiguousarray(np.asarray(inputs["L0_ffn_Wo"], dtype=np.float32)),
        "iw1": W1p,
        "iw2": np.ascontiguousarray(np.asarray(inputs["L0_ffn_W2"], dtype=np.float32)),
        "qscol": qscol_a, "lng_col": lng_c, "lnb_col": lnb_c, "b1_col": b1c, "b2_col": b2_c,
        "tbl_g": c["tbl_g"], "tbl_i": c["tbl_i"], "identity": c["identity"],
        "trimask4": c["trimask4"], "trimask2": c["trimask2"], "tri1one": c["tri1one"],
        "eden": c["eden"], "enorm": c["enorm"], "fnorm": c["fnorm"], "fpair": c["fpair"],
        "ones_col_bf": c["ones_col_bf"], "ones_col_f32": c["ones_col_f32"],
        "ones_row_f32": c["ones_row_f32"], "eps_col": c["eps_col"],
    }


def kernel(**inputs):
    x = np.ascontiguousarray(inputs["L0_attn_x"], dtype=np.float32)
    base = _prep_inputs(inputs)
    nc = build_program()
    in_maps = []
    for cidx in range(NCORE):
        m = dict(base)
        m["x_in"] = np.ascontiguousarray(x[BLOC * cidx:BLOC * (cidx + 1)])
        in_maps.append(m)
    res = run_bass_kernel_spmd(nc, in_maps, core_ids=list(range(NCORE)))
    return np.concatenate([r["out"] for r in res.results], axis=0)
